# version 09
# baseline (speedup 1.0000x reference)
import numpy as np
from contextlib import ExitStack

import concourse.bass as bass
import concourse.mybir as mybir
from concourse.bass_utils import run_bass_kernel_spmd

F32 = mybir.dt.float32
BF16 = mybir.dt.bfloat16
I32 = mybir.dt.int32
AF = mybir.ActivationFunctionType
OP = mybir.AluOpType
AX = mybir.AxisListType

DEBUG_TAPS = False
PHASE_LIMIT = 4
SUBPHASE = 9
EMIT_TILES = None
N_CORES = 8
B, T, D = 16, 4096, 1024
SEQ_PER_CORE = 2
TT = 512
NTC = TT // 128
TILES_PER_SEQ = T // TT
H = 8
HC = 2048
FF = 4096
INV_SQRT_DK = float(1.0 / np.sqrt(128.0))

CONSTS = [256.0, 1.0 / 256.0, 1.0 / 64.0, 0.25, -64.0, -63.75, 0.5, 1e-5, 127.0,
          1.0 / 1024.0, 64.0, 2.0, 16.0, -0.5, 1.0, 0.0]
C256, CI256, CI64, C025, CN64, CN6375, C05, CEPS, C127, CI1024, C64, C2, C16, CN05, C1, C0 = range(16)


class Sched:
    ENGS = ("pe", "dve", "sc", "gp", "sync")

    def __init__(self):
        self.ops = {e: [] for e in self.ENGS}
        self.W = {}
        self.R = {}

    def _record(self, eng, fn, reads, writes, extra, is_dma):
        deps = []
        for b in reads:
            t = self.W.get(b)
            if t is not None:
                deps.append(t)
        for b in writes:
            deps.extend(self.R.get(b, ()))
            t = self.W.get(b)
            if t is not None:
                deps.append(t)
        deps.extend(extra)
        tok = (eng, len(self.ops[eng]))
        self.ops[eng].append(dict(fn=fn, deps=set(deps), is_dma=is_dma,
                                  awaited=False))
        for b in reads:
            self.R.setdefault(b, []).append(tok)
        for b in writes:
            self.W[b] = tok
            self.R[b] = []
        return tok

    def op(self, eng, fn, reads=(), writes=(), extra=()):
        return self._record(eng, fn, reads, writes, extra, False)

    def dma(self, eng, fn, reads=(), writes=(), extra=()):
        return self._record(eng, fn, reads, writes, extra, True)

    NPOOL = 24

    def finalize(self, nc, es):
        self.sems = {}
        for e in self.ENGS:
            if any(not o["is_dma"] for o in self.ops[e]):
                self.sems[e] = es.enter_context(nc.semaphore(f"s_{e}"))
        for p in range(self.NPOOL):
            self.sems[f"d{p}"] = es.enter_context(nc.semaphore(f"d{p}"))
        SELF_SYNC = ("dve", "sc", "gp")
        for e in self.ENGS:
            for o in self.ops[e]:
                for (de, di) in o["deps"]:
                    dep = self.ops[de][di]
                    if dep["is_dma"] or de != e or de in SELF_SYNC:
                        dep["awaited"] = True
        self.val = {}
        pool_use = [0] * self.NPOOL
        dma_ord = 0
        for e in self.ENGS:
            cc_ = 0
            for i, o in enumerate(self.ops[e]):
                if o["is_dma"]:
                    p = dma_ord % self.NPOOL
                    dma_ord += 1
                    pool_use[p] += 1
                    sem_name = f"d{p}"
                    o["pre_wait"] = (sem_name, 16 * (pool_use[p] - 1))
                    o["inc"] = (sem_name, 16)
                    self.val[(e, i)] = (sem_name, 16 * pool_use[p])
                elif o["awaited"]:
                    cc_ += 1
                    o["inc"] = (e, 1)
                    self.val[(e, i)] = (e, cc_)
                else:
                    o["inc"] = None

    def emit(self, ename, eng):
        clock = {}
        for i, o in enumerate(self.ops[ename]):
            want = {}
            for (de, di) in sorted(o["deps"]):
                dep = self.ops[de][di]
                if (not dep["is_dma"] and de == ename
                        and de not in ("dve", "sc", "gp")):
                    continue
                sem_name, v = self.val[(de, di)]
                if v > want.get(sem_name, 0):
                    want[sem_name] = v
            pw = o.get("pre_wait")
            if pw is not None and pw[1] > want.get(pw[0], 0):
                want[pw[0]] = pw[1]
            for sem_name, v in sorted(want.items()):
                if v > clock.get(sem_name, 0):
                    eng.wait_ge(self.sems[sem_name], v)
                    clock[sem_name] = v
            inst = o["fn"](eng) if o["fn"] is not None else None
            if o["inc"] is not None:
                assert inst is not None, f"awaited op with no inst on {ename}:{i}"
                sem_name, amt = o["inc"]
                inst.then_inc(self.sems[sem_name], amt)


class TBuf:
    def __init__(self, t, nm, blk=256):
        self.t = t
        self.nm = nm
        self.blk = blk

    def ap(self, lo, n):
        return self.t[:, lo:lo + n]

    def k(self, lo=0, n=None):
        if n is None:
            n = self.t.shape[1] - lo
        b = self.blk
        return [f"{self.nm}:{i}" for i in range(lo // b, (lo + n + b - 1) // b)]


def build_program(debug_taps=False):
    nc = bass.Bass("TRN2", target_bir_lowering=False, detect_race_conditions=False)
    S = Sched()
    es = ExitStack()

    din = {}
    def dram_in(name, shape, dt=F32):
        din[name] = nc.dram_tensor(name, list(shape), dt, kind="ExternalInput").ap()
        return din[name]

    xin = dram_in("xin", [SEQ_PER_CORE, T, D])
    wqkv = dram_in("wqkv", [D, 4096])
    wg_in = dram_in("wg", [D, 2048])
    wproj_in = dram_in("wproj", [2048, D])
    decay_in = dram_in("decay", [1, H])
    packed_in = dram_in("packed", [256, 1024], I32)
    ws_in = dram_in("wscale", [1, 1])
    bias0_in = dram_in("bias0", [1, D])
    gamma_in = dram_in("gamma", [D])
    beta_in = dram_in("beta", [D])
    inw_in = dram_in("in_w", [D, 2 * HC])
    inb_in = dram_in("in_b", [2 * HC])
    dww_in = dram_in("dw_w", [HC, 7])
    dwb_in = dram_in("dw_b", [HC])
    logd_in = dram_in("log_decay", [HC])
    mixw_in = dram_in("mix_w", [HC, HC])
    mixb_in = dram_in("mix_b", [HC])
    outw_in = dram_in("out_w", [HC, D])
    outb_in = dram_in("out_b", [1, D])
    w1_in = dram_in("w1", [D, FF])
    b1_in = dram_in("b1", [FF])
    w2_in = dram_in("w2", [FF, D])
    b2_in = dram_in("b2", [1, D])
    consts_in = dram_in("consts", [128, len(CONSTS)])
    cq_in = dram_in("cq_t", [TILES_PER_SEQ, 8, 128, TT])
    sq_in = dram_in("sq_t", [TILES_PER_SEQ, 8, 128, TT])
    ck_in = dram_in("ck_t", [TILES_PER_SEQ, 8, 128, TT])
    sk_in = dram_in("sk_t", [TILES_PER_SEQ, 8, 128, TT])
    idm_in = dram_in("idm", [128, 128])

    out_d = nc.dram_tensor("out", [SEQ_PER_CORE, T, D], F32, kind="ExternalOutput").ap()

    taps = {}
    def tap(name, shape, dt=F32):
        if name not in taps:
            taps[name] = nc.dram_tensor("tap_" + name, list(shape), dt,
                                        kind="ExternalOutput").ap()
        return taps[name]

    def scr(name, n_el, dt=BF16):
        return nc.dram_tensor(name, [n_el], dt, kind="Internal").ap()

    sQ = scr("sQ", 128 * 8 * 8 * 128)
    sK = scr("sK", 128 * 8 * 8 * 128)
    sV = scr("sV", 128 * 4 * 8 * 512)
    sGw = scr("sGw", 128 * 4 * 8 * 512)
    sP = scr("sP", 128 * 2 * 16 * 512)
    sTn = scr("sTn", 128 * 2 * 8 * 512)
    sI = scr("sI", 128 * 32 * 8 * 128)
    sM = scr("sM", 128 * 16 * 16 * 128)
    sOw = scr("sOw", 128 * 2 * 16 * 512)
    sW1 = scr("sW1", 128 * 32 * 8 * 128)
    sW2 = scr("sW2", 128 * 2 * 32 * 512)

    def sb(nm, shape, dt=F32, blk=256):
        t = es.enter_context(nc.sbuf_tensor(nm, list(shape), dt))
        return TBuf(t, nm, blk)

    x_tm = sb("x_tm", [128, NTC * 1024])
    arena = sb("arena", [128, 32768], BF16, blk=512)
    fmbuf = sb("fmbuf", [128, 8 * 512], BF16, blk=512)
    scf = sb("scf", [128, 7680])
    sci = sb("sci", [128, 2048], I32, blk=512)
    ret_state = sb("ret_state", [128, 16 * 256])
    ret_sm = sb("ret_sm", [128, 1536], BF16, blk=128)
    masks = sb("masks", [128, H * 512], BF16, blk=512)
    rotb = sb("rotb", [128, 1024], F32, blk=512)
    wst = sb("wst", [128, 16384], BF16, blk=8192)
    brep = sb("brep", [128, 1024], F32, blk=1024)
    ones_bf = sb("ones_bf", [1, 128], BF16, blk=128)
    brow = sb("brow", [1, 4096], BF16, blk=1024)
    prm = sb("prm", [128, 232], F32, blk=8)
    cst = sb("cst", [128, len(CONSTS)], F32, blk=16)
    stat = sb("stat", [128, 64], F32, blk=8)
    ident_f = sb("ident_f", [128, 128], F32, blk=128)
    ident_b = sb("ident_b", [128, 128], BF16, blk=128)
    vcs = sb("vcs", [128, 712], F32, blk=64)
    dwwt = sb("dwwt", [128, 112], F32, blk=112)

    PRM_LND, PRM_D256, PRM_WR, PRM_WK, PRM_A, PRM_CARRY = 0, 8, 16, 32, 48, 64
    PRM_INB, PRM_MIXB, PRM_DWB = 96, 128, 144
    PRM_GAM, PRM_BET, PRM_B1S, PRM_WS = 160, 168, 176, 224
    VTAILS = 520

    psA = es.enter_context(nc.psum_tensor("psA", [128, 1536], F32))
    psB = es.enter_context(nc.psum_tensor("psB", [128, 1536], F32))
    psT = es.enter_context(nc.psum_tensor("psT", [128, 512], F32))
    psTb = es.enter_context(nc.psum_tensor("psTb", [128, 1024], BF16))
    TPF32 = [(psT, 0, "psT"), (psA, 0, "psA:0"), (psA, 512, "psA:1"),
             (psA, 1024, "psA:2")]
    TPBF = [(psTb, 0, "psTb"), (psB, 1024, "psB:2"), (psB, 0, "psB:0"),
            (psB, 512, "psB:1")]
    def tp_f32(i):
        t, off, k = TPF32[i % 4]
        return t[:, off:off + 128], k
    def tp_bf(i):
        t, off, k = TPBF[i % 4]
        if k == "psTb":
            return t[:, off:off + 128], k
        return t[:, off:off + 64].bitcast(BF16), k

    cAP = lambda i: cst.t[:, i:i + 1]
    CSTK = cst.k(0)

    def dmain(dst_ap, src_ap, reads=(), writes=(), slow=False):
        return S.dma("sync", lambda e, d=dst_ap, s=src_ap, sl=slow: e.dma_start(
            out=d, in_=s, allow_slow_non_contiguous=sl),
            reads=reads, writes=writes)

    def repl_row(buf, lo, hi):
        n = 1
        while n < 128:
            m = min(n, 128 - n)
            S.dma("sync", lambda e, n=n, m=m: e.dma_start(
                out=buf.t[n:n + m, lo:hi], in_=buf.t[0:m, lo:hi]),
                reads=buf.k(lo, hi - lo), writes=buf.k(lo, hi - lo))
            n += m

    wslot = [0]
    def load_wblock(src3d_ap, n_el, src_key):
        slot = wslot[0]
        wslot[0] ^= 1
        base = slot * 8192
        assert n_el <= 8192
        dmain(wst.ap(base, n_el), src3d_ap, reads=[src_key],
              writes=wst.k(base, n_el))
        return base, wst.k(base, n_el)

    vQ = sQ.rearrange("(p oc kc m) -> p oc kc m", p=128, oc=8, kc=8)
    vK = sK.rearrange("(p oc kc m) -> p oc kc m", p=128, oc=8, kc=8)
    vV = sV.rearrange("(p ncd kc n) -> p ncd kc n", p=128, ncd=4, kc=8)
    vG = sGw.rearrange("(p ncd kc n) -> p ncd kc n", p=128, ncd=4, kc=8)
    vP = sP.rearrange("(p ncd kc n) -> p ncd kc n", p=128, ncd=2, kc=16)
    vTn = sTn.rearrange("(p ncd kc n) -> p ncd kc n", p=128, ncd=2, kc=8)
    vI = sI.rearrange("(p oc kc m) -> p oc kc m", p=128, oc=32, kc=8)
    vM = sM.rearrange("(p oc kc m) -> p oc kc m", p=128, oc=16, kc=16)
    vO = sOw.rearrange("(p ncd kc n) -> p ncd kc n", p=128, ncd=2, kc=16)
    vW1 = sW1.rearrange("(p oc kc m) -> p oc kc m", p=128, oc=32, kc=8)
    vW2 = sW2.rearrange("(p ncd kc n) -> p ncd kc n", p=128, ncd=2, kc=32)

    dmain(cst.ap(0, len(CONSTS)), consts_in, writes=CSTK)
    dmain(ident_f.ap(0, 128), idm_in, writes=ident_f.k())
    S.op("dve", lambda e: e.tensor_copy(ident_b.ap(0, 128), ident_f.ap(0, 128)),
         reads=ident_f.k(), writes=ident_b.k())

    def prep_weight(src_ap, n_rows, n_cols, view, kind, scr_key):
        n_kc = n_rows // 128
        for kc in range(n_kc):
            dmain(scf.ap(0, n_cols), src_ap[kc * 128:(kc + 1) * 128, :],
                  writes=scf.k(0, n_cols))
            S.op("dve", lambda e, c=n_cols: e.tensor_copy(arena.ap(0, c),
                                                          scf.ap(0, c)),
                 reads=scf.k(0, n_cols), writes=arena.k(0, n_cols))
            if kind == "lhsT":
                dst = view[:, :, kc, :]
                src = arena.ap(0, n_cols).rearrange("p (oc m) -> p oc m", m=128)
            else:
                dst = view[:, :, kc, :]
                src = arena.ap(0, n_cols).rearrange("p (ncd n) -> p ncd n", n=512)
            S.dma("sync", lambda e, dst=dst, src=src: e.dma_start(out=dst, in_=src),
                  reads=arena.k(0, n_cols), writes=[scr_key])

    prep_weight(wqkv[:, 0:1024], 1024, 1024, vQ, "lhsT", "sQ")
    prep_weight(wqkv[:, 1024:2048], 1024, 1024, vK, "lhsT", "sK")
    prep_weight(wqkv[:, 2048:4096], 1024, 2048, vV, "rhs", "sV")
    prep_weight(wg_in, 1024, 2048, vG, "rhs", "sGw")
    prep_weight(wproj_in, 2048, 1024, vP, "rhs", "sP")
    prep_weight(inw_in, 1024, 4096, vI, "lhsT", "sI")
    prep_weight(mixw_in, 2048, 2048, vM, "lhsT", "sM")
    prep_weight(outw_in, 2048, 1024, vO, "rhs", "sOw")
    prep_weight(w1_in, 1024, 4096, vW1, "lhsT", "sW1")
    prep_weight(w2_in, 4096, 1024, vW2, "rhs", "sW2")

    def floor_fixup(u_ap, u_keys, c_ap, c_keys, m_ap, m_keys, f_ap, f_keys):
        S.op("sc", lambda e: e.activation(c_ap, u_ap, AF.Identity),
             reads=u_keys, writes=c_keys)
        S.op("dve", lambda e: e.tensor_tensor(m_ap, c_ap, u_ap, OP.is_le),
             reads=c_keys + u_keys, writes=m_keys)
        S.op("dve", lambda e: e.scalar_tensor_tensor(f_ap, c_ap, 1, m_ap,
                                                     OP.subtract, OP.add),
             reads=c_keys + m_keys, writes=f_keys)

    for oc2 in range(8):
        j = oc2 // 2
        r0 = (oc2 % 2) * 128
        dmain(sci.ap(0, 1024), packed_in[r0:r0 + 128, :], writes=sci.k(0, 1024))
        S.op("dve", lambda e: e.tensor_copy(scf.ap(0, 1024), sci.ap(0, 1024)),
             reads=sci.k(0, 1024), writes=scf.k(0, 1024))
        def floor_scaled(dst_off, scale_val):
            S.op("dve", lambda e, s=scale_val, d=dst_off: e.tensor_scalar(
                scf.ap(d, 1024), scf.ap(0, 1024), s, None, OP.mult),
                reads=scf.k(0, 1024), writes=scf.k(dst_off, 1024))
            floor_fixup(scf.ap(dst_off, 1024), scf.k(dst_off, 1024),
                        sci.ap(1024, 1024), sci.k(1024, 1024),
                        scf.ap(4096, 1024), scf.k(4096, 1024),
                        sci.ap(1024, 1024), sci.k(1024, 1024))
            S.op("dve", lambda e, d=dst_off: e.tensor_copy(scf.ap(d, 1024),
                                                           sci.ap(1024, 1024)),
                 reads=sci.k(1024, 1024), writes=scf.k(dst_off, 1024))
        if j == 0:
            floor_scaled(1024, 0.25)
            S.op("dve", lambda e: e.scalar_tensor_tensor(
                scf.ap(2048, 1024), scf.ap(1024, 1024), -4.0, scf.ap(0, 1024),
                OP.mult, OP.add),
                reads=scf.k(0, 2048), writes=scf.k(2048, 1024))
        else:
            floor_scaled(1024, 0.25 ** j)
            floor_scaled(3072, 0.25 ** (j + 1))
            S.op("dve", lambda e: e.scalar_tensor_tensor(
                scf.ap(2048, 1024), scf.ap(3072, 1024), -4.0, scf.ap(1024, 1024),
                OP.mult, OP.add),
                reads=scf.k(1024, 1024) + scf.k(3072, 1024),
                writes=scf.k(2048, 1024))
        S.op("dve", lambda e: e.tensor_scalar(arena.ap(0, 1024), scf.ap(2048, 1024),
                                              1.0, None, OP.subtract),
             reads=scf.k(2048, 1024), writes=arena.k(0, 1024))
        for ic in range(8):
            S.op("pe", lambda e, ic=ic: e.transpose(
                psTb[:, 0:128], arena.ap(ic * 128, 128), ident_b.ap(0, 128)),
                reads=arena.k(0, 1024) + ident_b.k(), writes=["psTb"])
            S.op("dve", lambda e: e.tensor_copy(arena.ap(4096, 128), psTb[:, 0:128]),
                 reads=["psTb"], writes=arena.k(4096, 128))
            dstv = vTn[:, oc2 // 4, ic, (oc2 % 4) * 128:(oc2 % 4) * 128 + 128]
            S.dma("sync", lambda e, dstv=dstv: e.dma_start(out=dstv,
                                                           in_=arena.ap(4096, 128)),
                  reads=arena.k(4096, 128), writes=["sTn"])

    dmain(scf.t[0:1, 2048:2056], decay_in, writes=scf.k(2048, 256))
    S.op("sc", lambda e: e.activation(prm.t[0:1, PRM_LND:PRM_LND + 8],
                                      scf.t[0:1, 2048:2056], AF.Ln),
         reads=scf.k(2048, 256), writes=prm.k(PRM_LND, 8))
    repl_row(prm, PRM_LND, PRM_LND + 8)
    S.op("dve", lambda e: e.tensor_scalar(scf.ap(2048, 8), prm.ap(PRM_LND, 8),
                                          256.0, None, OP.mult),
         reads=prm.k(PRM_LND, 8), writes=scf.k(2048, 256))
    S.op("sc", lambda e: e.activation(prm.ap(PRM_D256, 8), scf.ap(2048, 8),
                                      AF.Exp),
         reads=scf.k(2048, 256), writes=prm.k(PRM_D256, 8))
    S.op("gp", lambda e: e.iota(sci.ap(0, 512), [[-128, 2], [1, 256]], base=0,
                                channel_multiplier=-1),
         writes=sci.k(0, 512))
    S.op("dve", lambda e: e.tensor_copy(scf.ap(0, 512), sci.ap(0, 512)),
         reads=sci.k(0, 512), writes=scf.k(0, 512))
    S.op("dve", lambda e: e.tensor_single_scalar(scf.ap(512, 512), scf.ap(0, 512),
                                                 0.0, OP.is_ge),
         reads=scf.k(0, 512), writes=scf.k(512, 512))
    S.op("dve", lambda e: e.tensor_scalar(scf.ap(1536, 512), scf.ap(512, 512),
                                          INV_SQRT_DK, None, OP.mult),
         reads=scf.k(512, 512), writes=scf.k(1536, 512))
    for h in range(H):
        S.op("dve", lambda e, h=h: e.tensor_scalar(
            scf.ap(1024, 512), scf.ap(0, 512),
            prm.t[:, PRM_LND + h:PRM_LND + h + 1], None, OP.mult),
            reads=scf.k(0, 512) + prm.k(PRM_LND, 8), writes=scf.k(1024, 512))
        S.op("sc", lambda e: e.activation(scf.ap(1024, 512), scf.ap(1024, 512),
                                          AF.Exp),
             reads=scf.k(1024, 512), writes=scf.k(1024, 512))
        S.op("dve", lambda e, h=h: e.tensor_tensor(
            masks.ap(h * 512, 512), scf.ap(1024, 512), scf.ap(1536, 512), OP.mult),
            reads=scf.k(1024, 512) + scf.k(1536, 512),
            writes=masks.k(h * 512, 512))
    S.op("gp", lambda e: e.iota(sci.ap(0, 2), [[128, 2]], base=1,
                                channel_multiplier=1),
         writes=sci.k(0, 512))
    S.op("dve", lambda e: e.tensor_copy(scf.ap(0, 2), sci.ap(0, 2)),
         reads=sci.k(0, 512), writes=scf.k(0, 256))
    S.op("gp", lambda e: e.iota(sci.ap(512, 2), [[-128, 2]], base=255,
                                channel_multiplier=-1),
         writes=sci.k(512, 512))
    S.op("dve", lambda e: e.tensor_copy(scf.ap(256, 2), sci.ap(512, 2)),
         reads=sci.k(512, 512), writes=scf.k(256, 256))
    for h in range(H):
        for (so, dst) in ((0, PRM_WR), (256, PRM_WK)):
            S.op("dve", lambda e, h=h, so=so: e.tensor_scalar(
                scf.ap(2048, 2), scf.ap(so, 2),
                prm.t[:, PRM_LND + h:PRM_LND + h + 1], None, OP.mult),
                reads=scf.k(so, 256) + prm.k(PRM_LND, 8),
                writes=scf.k(2048, 256))
            S.op("sc", lambda e: e.activation(
                scf.ap(2304, 2), scf.ap(2048, 2), AF.Exp),
                reads=scf.k(2048, 256), writes=scf.k(2304, 256))
            if dst == PRM_WR:
                S.op("dve", lambda e, h=h, dst=dst: e.tensor_scalar(
                    prm.ap(dst + 2 * h, 2), scf.ap(2304, 2),
                    INV_SQRT_DK, None, OP.mult),
                    reads=scf.k(2304, 256), writes=prm.k(dst + 2 * h, 2))
            else:
                S.op("dve", lambda e, h=h, dst=dst: e.tensor_copy(
                    prm.ap(dst + 2 * h, 2), scf.ap(2304, 2)),
                    reads=scf.k(2304, 256), writes=prm.k(dst + 2 * h, 2))
    dmain(scf.ap(2048, 16), logd_in.rearrange("(c p) -> p c", p=128),
          writes=scf.k(2048, 256), slow=True)
    S.op("sc", lambda e: e.activation(prm.ap(PRM_A, 16), scf.ap(2048, 16),
                                      AF.Sigmoid),
         reads=scf.k(2048, 256), writes=prm.k(PRM_A, 16))
    S.op("dve", lambda e: e.memset(prm.ap(PRM_CARRY, 32), 0.0),
         writes=prm.k(PRM_CARRY, 32))
    dmain(prm.ap(PRM_INB, 32), inb_in.rearrange("(c p) -> p c", p=128),
          writes=prm.k(PRM_INB, 32), slow=True)
    dmain(prm.ap(PRM_MIXB, 16), mixb_in.rearrange("(c p) -> p c", p=128),
          writes=prm.k(PRM_MIXB, 16), slow=True)
    dmain(prm.ap(PRM_DWB, 16), dwb_in.rearrange("(c p) -> p c", p=128),
          writes=prm.k(PRM_DWB, 16), slow=True)
    dmain(prm.ap(PRM_GAM, 8), gamma_in.rearrange("(c p) -> p c", p=128),
          writes=prm.k(PRM_GAM, 8), slow=True)
    dmain(prm.ap(PRM_BET, 8), beta_in.rearrange("(c p) -> p c", p=128),
          writes=prm.k(PRM_BET, 8), slow=True)
    dmain(prm.ap(PRM_B1S, 32), b1_in.rearrange("(c p) -> p c", p=128),
          writes=prm.k(PRM_B1S, 32), slow=True)
    S.op("dve", lambda e: e.tensor_scalar(scf.ap(2048, 32), prm.ap(PRM_B1S, 32),
                                          256.0, 0.5, OP.mult, OP.add),
         reads=prm.k(PRM_B1S, 32), writes=scf.k(2048, 256))
    S.op("dve", lambda e: e.tensor_copy(prm.ap(PRM_B1S, 32), scf.ap(2048, 32)),
         reads=scf.k(2048, 256), writes=prm.k(PRM_B1S, 32))
    dmain(dwwt.ap(0, 112).rearrange("p (c j) -> p c j", j=7),
          dww_in.rearrange("(c p) j -> p c j", p=128), writes=dwwt.k(), slow=True)
    dmain(prm.t[0:1, PRM_WS:PRM_WS + 1], ws_in, writes=prm.k(PRM_WS, 1))
    repl_row(prm, PRM_WS, PRM_WS + 1)
    dmain(brep.t[0:1, 0:1024], bias0_in, writes=brep.k(0, 1024))
    repl_row(brep, 0, 1024)
    S.op("dve", lambda e: e.memset(ones_bf.ap(0, 128), 1.0), writes=ones_bf.k())
    for bi, (bsrc, blo) in enumerate(((outb_in, 0), (b2_in, 2048))):
        dmain(scf.t[0:1, 0:1024], bsrc, writes=scf.k(0, 1024))
        S.op("dve", lambda e, blo=blo: e.tensor_copy(brow.t[0:1, blo:blo + 1024],
                                                     scf.t[0:1, 0:1024]),
             reads=scf.k(0, 1024), writes=brow.k(blo, 1024))
        S.op("dve", lambda e, blo=blo: e.tensor_copy(scf.t[0:1, 1024:2048],
                                                     brow.t[0:1, blo:blo + 1024]),
             reads=brow.k(blo, 1024), writes=scf.k(1024, 1024))
        S.op("dve", lambda e: e.tensor_tensor(scf.t[0:1, 0:1024],
                                              scf.t[0:1, 0:1024],
                                              scf.t[0:1, 1024:2048], OP.subtract),
             reads=scf.k(0, 2048), writes=scf.k(0, 1024))
        S.op("dve", lambda e, blo=blo: e.tensor_copy(
            brow.t[0:1, blo + 1024:blo + 2048], scf.t[0:1, 0:1024]),
            reads=scf.k(0, 1024), writes=brow.k(blo + 1024, 1024))
    S.op("dve", lambda e: e.memset(ret_state.ap(0, 4096), 0.0),
         writes=ret_state.k())
    S.op("dve", lambda e: e.memset(vcs.ap(VTAILS, 192), 0.0),
         writes=vcs.k(VTAILS, 192))

    AQ, AK, AV, AG0, AGN = 0, 4096, 8192, 16384, 24576
    AL1G, AL1S, AL1C, AL1V = 0, 8192, 16384, 24576
    AH = 0

    def layer_norm(s_lo, tc):
        sb_ = 32 + (tc % 2) * 8
        sk = scf.k(s_lo, 1024)
        sap = scf.ap(s_lo, 1024)
        ST = lambda i: stat.t[:, sb_ + i:sb_ + i + 1]
        STA = lambda i: stat.ap(sb_ + i, 1)
        STK = stat.k(sb_, 8)
        S.op("dve", lambda e: e.reduce_sum(STA(0), sap, axis=AX.X),
             reads=sk, writes=STK)
        S.op("sc", lambda e: e.activation(scf.ap(1024, 1024), sap, AF.Square,
                                          accum_out=STA(1)),
             reads=sk, writes=scf.k(1024, 1024) + STK)
        S.op("dve", lambda e: e.tensor_scalar(STA(2), STA(0), cAP(CI1024), None,
                                              OP.mult),
             reads=STK + CSTK, writes=STK)
        S.op("dve", lambda e: e.tensor_tensor(STA(3), STA(2), STA(2), OP.mult),
             reads=STK, writes=STK)
        S.op("dve", lambda e: e.scalar_tensor_tensor(
            STA(4), STA(1), 1.0 / 1024.0, STA(3), OP.mult, OP.subtract),
            reads=STK, writes=STK)
        S.op("sc", lambda e: e.activation(STA(0), STA(4), AF.Ln, bias=cAP(CEPS),
                                          scale=1.0),
             reads=STK + CSTK, writes=STK)
        S.op("sc", lambda e: e.activation(STA(5), STA(0), AF.Exp,
                                          bias=0.0, scale=cAP(CN05)),
             reads=STK + CSTK, writes=STK)
        S.op("dve", lambda e: e.scalar_tensor_tensor(
            STA(6), STA(2), -1.0, STA(5), OP.mult, OP.mult),
            reads=STK, writes=STK)
        S.op("sc", lambda e: e.activation(
            x_tm.ap(tc * 1024, 1024), sap, AF.Identity,
            bias=ST(6), scale=ST(5)),
            reads=sk + STK, writes=x_tm.k(tc * 1024, 1024))

    def transpose_tm_to_fm(src, src_keys_fn, cast_gb=False):
        for tc in range(NTC):
            for dc in range(8):
                pap, pk = tp_f32(tc * 8 + dc)
                S.op("pe", lambda e, tc=tc, dc=dc, pap=pap: e.transpose(
                    pap, src.ap(tc * 1024 + dc * 128, 128), ident_f.ap(0, 128)),
                    reads=src_keys_fn(tc, dc) + ident_f.k(),
                    writes=[pk])
                dsta = fmbuf.ap(dc * 512 + tc * 128, 128)
                dkey = fmbuf.k(dc * 512 + tc * 128, 128)
                if cast_gb:
                    S.op("dve", lambda e, dsta=dsta, pap=pap, dc=dc:
                         e.tensor_scalar(dsta, pap,
                                         prm.t[:, PRM_GAM + dc:PRM_GAM + dc + 1],
                                         prm.t[:, PRM_BET + dc:PRM_BET + dc + 1],
                                         OP.mult, OP.add),
                         reads=[pk] + prm.k(PRM_GAM, 16), writes=dkey)
                else:
                    S.op("dve", lambda e, dsta=dsta, pap=pap: e.tensor_copy(
                        dsta, pap),
                        reads=[pk], writes=dkey)

    def emit_tile(seq, tt, tile_idx):
        t0 = tt * TT
        dbg = debug_taps and tile_idx == 0

        src = xin[seq, t0:t0 + TT, :].rearrange("(tc p) d -> p tc d", p=128)
        dmain(x_tm.ap(0, 4096).rearrange("p (tc d) -> p tc d", d=1024), src,
              writes=x_tm.k())

        if SUBPHASE < 0.7:
            dst = out_d[seq, t0:t0 + TT, :].rearrange("(tc p) d -> p tc d", p=128)
            S.dma("sync", lambda e, dst=dst: e.dma_start(
                out=dst, in_=x_tm.ap(0, 4096).rearrange("p (tc d) -> p tc d",
                                                        d=1024)),
                reads=x_tm.k(), writes=[f"outp:{tile_idx}"])
            return
        transpose_tm_to_fm(x_tm, lambda tc, dc: x_tm.k(tc * 1024 + dc * 128, 128))
        if SUBPHASE < 1:
            dst = out_d[seq, t0:t0 + TT, :].rearrange("(tc p) d -> p tc d", p=128)
            S.dma("sync", lambda e, dst=dst: e.dma_start(
                out=dst, in_=x_tm.ap(0, 4096).rearrange("p (tc d) -> p tc d",
                                                        d=1024)),
                reads=x_tm.k(), writes=[f"outp:{tile_idx}"])
            return

        for which, aoff, view, skey, ctab, stab in (
                ("q", AQ, vQ, "sQ", cq_in, sq_in),
                ("k", AK, vK, "sK", ck_in, sk_in)):
            base, wkeys = load_wblock(
                (sQ if which == "q" else sK).rearrange("(p x) -> p x", p=128),
                8192, skey)
            def rot_ops(oc, po, pkey, rtb, rlo, aoff=aoff):
                sf = 2048 * (oc % 2)
                ops = []
                ops.append(lambda: S.op("dve", lambda e: e.tensor_tensor(
                    scf.ap(sf + 512, 512), po, rtb.ap(rlo, 512), OP.mult),
                    reads=[pkey] + rtb.k(rlo, 512),
                    writes=scf.k(sf + 512, 512)))
                ops.append(lambda: S.op("dve", lambda e: e.stream_shuffle(
                    scf.ap(sf + 1024, 512), po, [i ^ 1 for i in range(32)]),
                    reads=[pkey], writes=scf.k(sf + 1024, 512)))
                ops.append(lambda: S.op("dve", lambda e: e.tensor_tensor(
                    scf.ap(sf + 1536, 512), scf.ap(sf + 1024, 512),
                    rtb.ap(rlo + 512, 512), OP.mult),
                    reads=scf.k(sf + 1024, 512) + rtb.k(rlo + 512, 512),
                    writes=scf.k(sf + 1536, 512)))
                ops.append(lambda: S.op("dve", lambda e: e.tensor_tensor(
                    arena.ap(aoff + oc * 512, 512), scf.ap(sf + 512, 512),
                    scf.ap(sf + 1536, 512), OP.add),
                    reads=scf.k(sf + 512, 512) + scf.k(sf + 1536, 512),
                    writes=arena.k(aoff + oc * 512, 512)))
                return ops

            import itertools as _it2
            for og in range(4):
                chains = []
                for oc in (2 * og, 2 * og + 1):
                    if oc % 2 == 0:
                        rtb, rlo = rotb, 0
                    else:
                        rtb, rlo = scf, 4096
                    dmain(rtb.ap(rlo, 512), ctab[tt, oc], writes=rtb.k(rlo, 512))
                    dmain(rtb.ap(rlo + 512, 512), stab[tt, oc],
                          writes=rtb.k(rlo + 512, 512))
                    pslot = oc % 3
                    po = psB[:, pslot * 512:pslot * 512 + 512]
                    for kc in range(8):
                        S.op("pe", lambda e, oc=oc, kc=kc, po=po, base=base:
                             e.matmul(
                                 po, wst.ap(base + oc * 1024 + kc * 128, 128),
                                 fmbuf.ap(kc * 512, 512),
                                 start=(kc == 0), stop=(kc == 7)),
                             reads=wkeys + fmbuf.k(kc * 512, 512),
                             writes=[f"psB:{pslot}"])
                    chains.append(rot_ops(oc, po, f"psB:{pslot}", rtb, rlo))
                for pair in _it2.zip_longest(*chains):
                    for fn in pair:
                        if fn is not None:
                            fn()
        if dbg:
            dmain(tap("q_fm", [128, 4096], BF16), arena.ap(AQ, 4096),
                  reads=arena.k(AQ, 4096))
            dmain(tap("k_fm", [128, 4096], BF16), arena.ap(AK, 4096),
                  reads=arena.k(AK, 4096))

        for which, aoff, view, skey, act in (("v", AV, vV, "sV", False),
                                             ("g0", AG0, vG, "sGw", True)):
            for ncd in range(4):
                base, wkeys = load_wblock(view[:, ncd].rearrange("p kc n -> p (kc n)"),
                                          4096, skey)
                for tc in range(NTC):
                    pslot = tc % 3
                    po = psB[:, pslot * 512:pslot * 512 + 512]
                    for kc in range(8):
                        S.op("pe", lambda e, tc=tc, kc=kc, po=po, base=base:
                             e.matmul(po, fmbuf.ap(kc * 512 + tc * 128, 128),
                                      wst.ap(base + kc * 512, 512),
                                      start=(kc == 0), stop=(kc == 7)),
                             reads=wkeys + fmbuf.k(kc * 512 + tc * 128, 128),
                             writes=[f"psB:{pslot}"])
                    dsta = arena.ap(aoff + tc * 2048 + ncd * 512, 512)
                    dkey = arena.k(aoff + tc * 2048 + ncd * 512, 512)
                    if act:
                        S.op("sc", lambda e, dsta=dsta, po=po: e.activation(
                            dsta, po, AF.Silu),
                            reads=[f"psB:{pslot}"], writes=dkey)
                    else:
                        S.op("sc", lambda e, dsta=dsta, po=po: e.activation(
                            dsta, po, AF.Identity),
                            reads=[f"psB:{pslot}"], writes=dkey)
        if dbg:
            dmain(tap("v_tm", [128, 8192], BF16), arena.ap(AV, 8192),
                  reads=arena.k(AV, 8192))

        if SUBPHASE < 3:
            dst = out_d[seq, t0:t0 + TT, :].rearrange("(tc p) d -> p tc d", p=128)
            S.dma("sync", lambda e, dst=dst: e.dma_start(
                out=dst, in_=x_tm.ap(0, 4096).rearrange("p (tc d) -> p tc d",
                                                        d=1024)),
                reads=x_tm.k(), writes=[f"outp:{tile_idx}"])
            return
        SC_AP = [psA[:, 0:256], psA[:, 512:768]]
        SC_K = ["psA:0", "psA:1"]
        O_AP = [psA[:, 1024:1280], psT[:, 0:256]]
        O_K = ["psA:2", "psT"]
        X_AP = psB[:, 0:256]
        X_K = "psB:0"
        U_AP = psB[:, 512:768]
        U_K = "psB:1"
        KT_AP = [psTb[:, 0:128], psB[:, 1024:1088].bitcast(BF16)]
        KT_K = ["psTb", "psB:2"]
        for c in range(2):
            for h in range(H):
                kq = arena.k(AQ + h * 512 + c * 256, 256)
                kk = arena.k(AK + h * 512 + c * 256, 256)
                S.op("dve", lambda e, h=h: e.tensor_copy(
                    ret_sm.ap(768, 256), ret_state.ap(h * 256, 256)),
                    reads=ret_state.k(h * 256, 256), writes=ret_sm.k(768, 256))
                for tkc in range(2):
                    S.op("pe", lambda e, c=c, h=h, tkc=tkc: e.matmul(
                        SC_AP[tkc],
                        arena.ap(AK + h * 512 + c * 256 + tkc * 128, 128),
                        arena.ap(AQ + h * 512 + c * 256, 256),
                        start=True, stop=True),
                        reads=kq + kk, writes=[SC_K[tkc]])
                    S.op("dve", lambda e, h=h, tkc=tkc: e.tensor_tensor(
                        ret_sm.ap(tkc * 256, 256),
                        SC_AP[tkc],
                        masks.ap(h * 512 + tkc * 256, 256), OP.mult),
                        reads=[SC_K[tkc]] + masks.k(h * 512 + tkc * 256, 256),
                        writes=ret_sm.k(tkc * 256, 256))
                    S.op("pe", lambda e, c=c, h=h, tkc=tkc: e.transpose(
                        KT_AP[tkc],
                        arena.ap(AK + h * 512 + c * 256 + tkc * 128, 128),
                        ident_b.ap(0, 128)),
                        reads=kk + ident_b.k(), writes=[KT_K[tkc]])
                    S.op("dve", lambda e, tkc=tkc: e.tensor_copy(
                        ret_sm.ap(512 + tkc * 128, 128),
                        KT_AP[tkc]),
                        reads=[KT_K[tkc]], writes=ret_sm.k(512 + tkc * 128, 128))
                    tc = c * 2 + tkc
                    S.op("dve", lambda e, tc=tc, h=h, tkc=tkc: e.tensor_scalar(
                        ret_sm.ap(1024 + tkc * 256, 256),
                        arena.ap(AV + tc * 2048 + h * 256, 256),
                        prm.t[:, PRM_WK + 2 * h + tkc:PRM_WK + 2 * h + tkc + 1],
                        None, OP.mult),
                        reads=arena.k(AV + tc * 2048 + h * 256, 256) +
                        prm.k(PRM_WK + 2 * h, 2),
                        writes=ret_sm.k(1024 + tkc * 256, 256))
                for tqc in range(2):
                    for tkc in range(tqc + 1):
                        S.op("pe", lambda e, c=c, h=h, tqc=tqc, tkc=tkc: e.matmul(
                            O_AP[tqc],
                            ret_sm.ap(tkc * 256 + tqc * 128, 128),
                            arena.ap(AV + (c * 2 + tkc) * 2048 + h * 256, 256),
                            start=(tkc == 0), stop=(tkc == tqc)),
                            reads=ret_sm.k(tkc * 256, 256) +
                            arena.k(AV + (c * 2 + tkc) * 2048 + h * 256, 256),
                            writes=[O_K[tqc]])
                    S.op("pe", lambda e, c=c, h=h, tqc=tqc: e.matmul(
                        X_AP,
                        arena.ap(AQ + h * 512 + c * 256 + tqc * 128, 128),
                        ret_sm.ap(768, 256), start=True, stop=True),
                        reads=kq + ret_sm.k(768, 256), writes=[X_K])
                    xs = (h % 2) * 256
                    S.op("sc", lambda e, h=h, tqc=tqc, xs=xs: e.activation(
                        scf.ap(xs, 256), X_AP, AF.Identity,
                        scale=prm.t[:, PRM_WR + 2 * h + tqc:PRM_WR + 2 * h + tqc + 1]),
                        reads=[X_K] + prm.k(PRM_WR + 2 * h, 2) + CSTK,
                        writes=scf.k(xs, 256))
                    olo = 3584 + tqc * 2048 + h * 256
                    S.op("dve", lambda e, olo=olo, tqc=tqc, xs=xs: e.tensor_tensor(
                        scf.ap(olo, 256), O_AP[tqc],
                        scf.ap(xs, 256), OP.add),
                        reads=[O_K[tqc]] + scf.k(xs, 256),
                        writes=scf.k(olo, 256))
                for tkc in range(2):
                    S.op("pe", lambda e, tkc=tkc: e.matmul(
                        U_AP, ret_sm.ap(512 + tkc * 128, 128),
                        ret_sm.ap(1024 + tkc * 256, 256),
                        start=(tkc == 0), stop=(tkc == 1)),
                        reads=ret_sm.k(512 + tkc * 128, 128) +
                        ret_sm.k(1024 + tkc * 256, 256),
                        writes=[U_K])
                S.op("dve", lambda e, h=h: e.tensor_scalar(
                    scf.ap(256, 256), ret_state.ap(h * 256, 256),
                    prm.t[:, PRM_D256 + h:PRM_D256 + h + 1], None, OP.mult),
                    reads=ret_state.k(h * 256, 256) + prm.k(PRM_D256, 8),
                    writes=scf.k(256, 256))
                S.op("dve", lambda e, h=h: e.tensor_tensor(
                    ret_state.ap(h * 256, 256), scf.ap(256, 256),
                    U_AP, OP.add),
                    reads=scf.k(256, 256) + [U_K],
                    writes=ret_state.k(h * 256, 256))

            if dbg:
                dmain(tap(f"obuf{c}", [128, 4096]), scf.ap(3584, 4096),
                      reads=scf.k(3584, 4096))
            for half in range(2):
                tc = c * 2 + half
                olo = 3584 + half * 2048
                okeys = scf.k(olo, 2048)
                S.op("dve", lambda e, olo=olo: e.reduce_sum(
                    stat.ap(8, 8).unsqueeze(2),
                    scf.ap(olo, 2048).rearrange("p (h v) -> p h v", h=8),
                    axis=AX.X),
                    reads=okeys, writes=stat.k(8, 8))
                S.op("dve", lambda e, olo=olo: e.tensor_tensor(
                    scf.ap(1024, 2048), scf.ap(olo, 2048), scf.ap(olo, 2048),
                    OP.mult),
                    reads=okeys, writes=scf.k(1024, 2048))
                S.op("dve", lambda e: e.reduce_sum(
                    stat.ap(16, 8).unsqueeze(2),
                    scf.ap(1024, 2048).rearrange("p (h v) -> p h v", h=8),
                    axis=AX.X),
                    reads=scf.k(1024, 2048), writes=stat.k(16, 8))
                S.op("dve", lambda e: e.tensor_scalar(stat.ap(8, 8), stat.ap(8, 8),
                                                      1.0 / 256.0, None, OP.mult),
                     reads=stat.k(8, 8), writes=stat.k(8, 8))
                S.op("dve", lambda e: e.tensor_tensor(stat.ap(24, 8), stat.ap(8, 8),
                                                      stat.ap(8, 8), OP.mult),
                     reads=stat.k(8, 8), writes=stat.k(24, 8))
                S.op("dve", lambda e: e.scalar_tensor_tensor(
                    stat.ap(0, 8), stat.ap(16, 8), 1.0 / 256.0, stat.ap(24, 8),
                    OP.mult, OP.subtract),
                    reads=stat.k(16, 8) + stat.k(24, 8), writes=stat.k(0, 8))
                S.op("dve", lambda e: e.tensor_scalar(stat.ap(24, 8), stat.ap(0, 8),
                                                      1e-5, None, OP.add),
                     reads=stat.k(0, 8), writes=stat.k(24, 8))
                S.op("sc", lambda e: e.activation(stat.ap(0, 8), stat.ap(24, 8),
                                                  AF.Ln),
                     reads=stat.k(24, 8), writes=stat.k(0, 8))
                S.op("sc", lambda e: e.activation(stat.ap(16, 8), stat.ap(0, 8),
                                                  AF.Exp, bias=0.0, scale=cAP(CN05)),
                     reads=stat.k(0, 8) + CSTK, writes=stat.k(16, 8))
                S.op("dve", lambda e: e.scalar_tensor_tensor(
                    stat.ap(24, 8), stat.ap(8, 8), -1.0, stat.ap(16, 8),
                    OP.mult, OP.mult),
                    reads=stat.k(8, 16), writes=stat.k(24, 8))
                for h in range(H):
                    nb = scf.ap((h % 2) * 128, 128).bitcast(BF16)
                    nbk = scf.k((h % 2) * 128, 128)
                    S.op("sc", lambda e, olo=olo, h=h, nb=nb: e.activation(
                        nb, scf.ap(olo + h * 256, 256), AF.Identity,
                        bias=stat.t[:, 24 + h:25 + h],
                        scale=stat.t[:, 16 + h:17 + h]),
                        reads=scf.k(olo + h * 256, 256) + stat.k(16, 8) +
                        stat.k(24, 8),
                        writes=nbk)
                    S.op("dve", lambda e, tc=tc, h=h, nb=nb: e.tensor_tensor(
                        wst.ap(h * 256, 256), nb,
                        arena.ap(AG0 + tc * 2048 + h * 256, 256), OP.mult),
                        reads=nbk + arena.k(AG0 + tc * 2048 + h * 256, 256),
                        writes=wst.k(0, 2048))
                if dbg and tc == 0:
                    dmain(tap("gnt", [128, 2048], BF16), wst.ap(0, 2048),
                          reads=wst.k(0, 2048))
                    dmain(tap("gstat", [128, 32]), stat.ap(0, 32),
                          reads=stat.k(0, 32))
                for vc in range(16):
                    pap, pk = tp_bf(vc)
                    S.op("pe", lambda e, vc=vc, pap=pap: e.transpose(
                        pap, wst.ap(vc * 128, 128), ident_b.ap(0, 128)),
                        reads=wst.k(0, 2048) + ident_b.k(),
                        writes=[pk])
                    S.op("dve", lambda e, vc=vc, tc=tc, pap=pap: e.tensor_copy(
                        arena.ap(AGN + vc * 512 + tc * 128, 128), pap),
                        reads=[pk],
                        writes=arena.k(AGN + vc * 512 + tc * 128, 128))
        if dbg:
            dmain(tap("gn_fm", [128, 8192], BF16), arena.ap(AGN, 8192),
                  reads=arena.k(AGN, 8192))
            dmain(tap("masks", [128, 4096], BF16), masks.ap(0, 4096),
                  reads=masks.k(0, 4096))

        if SUBPHASE < 4:
            dst = out_d[seq, t0:t0 + TT, :].rearrange("(tc p) d -> p tc d", p=128)
            S.dma("sync", lambda e, dst=dst: e.dma_start(
                out=dst, in_=x_tm.ap(0, 4096).rearrange("p (tc d) -> p tc d",
                                                        d=1024)),
                reads=x_tm.k(), writes=[f"outp:{tile_idx}"])
            return
        prb0, prk0 = load_wblock(vP[:, 0].rearrange("p kc n -> p (kc n)"),
                                 8192, "sP")
        prb1, prk1 = load_wblock(vP[:, 1].rearrange("p kc n -> p (kc n)"),
                                 8192, "sP")
        for tc in range(NTC):
            for ncd in range(2):
                base = prb0 if ncd == 0 else prb1
                wkeys = prk0 if ncd == 0 else prk1
                pslot = (tc * 2 + ncd) % 3
                po = psB[:, pslot * 512:pslot * 512 + 512]
                for kc in range(16):
                    S.op("pe", lambda e, tc=tc, kc=kc, po=po, base=base: e.matmul(
                        po, arena.ap(AGN + kc * 512 + tc * 128, 128),
                        wst.ap(base + kc * 512, 512),
                        start=(kc == 0), stop=(kc == 15)),
                        reads=arena.k(AGN + kc * 512 + tc * 128, 128) + wkeys,
                        writes=[f"psB:{pslot}"])
                sb0 = (tc % 2) * 2048
                S.op("dve", lambda e, tc=tc, ncd=ncd, po=po, pslot=pslot, sb0=sb0:
                     e.tensor_tensor(
                         scf.ap(sb0 + ncd * 512, 512), po,
                         x_tm.ap(tc * 1024 + ncd * 512, 512), OP.add),
                     reads=[f"psB:{pslot}"] + x_tm.k(tc * 1024 + ncd * 512, 512),
                     writes=scf.k(sb0 + ncd * 512, 512))
            layer_norm((tc % 2) * 2048, tc)
        if dbg:
            dmain(tap("x1", [128, 4096]), x_tm.ap(0, 4096), reads=x_tm.k())

        if PHASE_LIMIT < 2:
            dst = out_d[seq, t0:t0 + TT, :].rearrange("(tc p) d -> p tc d", p=128)
            S.dma("sync", lambda e, dst=dst: e.dma_start(
                out=dst, in_=x_tm.ap(0, 4096).rearrange("p (tc d) -> p tc d",
                                                        d=1024)),
                reads=x_tm.k(), writes=[f"outp:{tile_idx}"])
            return
        tbase0, tkeys0 = load_wblock(vTn[:, 0].rearrange("p kc n -> p (kc n)"),
                                     4096, "sTn")
        tbase1, tkeys1 = load_wblock(vTn[:, 1].rearrange("p kc n -> p (kc n)"),
                                     4096, "sTn")
        for tc in range(NTC):
            xk = x_tm.k(tc * 1024, 1024)
            xs = x_tm.ap(tc * 1024, 1024)
            S.op("dve", lambda e, xs=xs: e.reduce_max(
                stat.ap(0, 1), xs, axis=AX.X, apply_absolute_value=True),
                reads=xk, writes=stat.k(0, 8))
            S.op("dve", lambda e: e.tensor_single_scalar(stat.ap(3, 1),
                                                         stat.ap(0, 1), 1e-5,
                                                         OP.max),
                 reads=stat.k(0, 8), writes=stat.k(0, 8))
            S.op("dve", lambda e: e.reciprocal(stat.ap(1, 1), stat.ap(3, 1)),
                 reads=stat.k(0, 8), writes=stat.k(0, 8))
            S.op("dve", lambda e: e.tensor_scalar(stat.ap(4, 1), stat.ap(1, 1),
                                                  127.0, None, OP.mult),
                 reads=stat.k(0, 8), writes=stat.k(0, 8))
            S.op("dve", lambda e: e.tensor_scalar(
                stat.ap(2, 1), stat.ap(3, 1), prm.t[:, PRM_WS:PRM_WS + 1],
                1.0 / 127.0, OP.mult, OP.mult),
                reads=stat.k(0, 8) + prm.k(PRM_WS, 1), writes=stat.k(0, 8))
            S.op("dve", lambda e, xs=xs: e.tensor_scalar(
                scf.ap(0, 1024), xs, stat.t[:, 4:5], 0.5, OP.mult, OP.add),
                reads=xk + stat.k(0, 8), writes=scf.k(0, 1024))
            floor_fixup(scf.ap(0, 1024), scf.k(0, 1024),
                        sci.ap(0, 1024), sci.k(0, 1024),
                        sci.ap(1024, 1024), sci.k(1024, 1024),
                        sci.ap(0, 1024), sci.k(0, 1024))
            S.op("dve", lambda e: e.tensor_copy(arena.ap(AQ, 1024), sci.ap(0, 1024)),
                 reads=sci.k(0, 1024), writes=arena.k(AQ, 1024))
            for dc in range(8):
                pap, pk = tp_bf(dc)
                S.op("pe", lambda e, dc=dc, pap=pap: e.transpose(
                    pap, arena.ap(AQ + dc * 128, 128), ident_b.ap(0, 128)),
                    reads=arena.k(AQ, 1024) + ident_b.k(),
                    writes=[pk])
                S.op("dve", lambda e, dc=dc, tc=tc, pap=pap: e.tensor_copy(
                    fmbuf.ap(dc * 512 + tc * 128, 128), pap),
                    reads=[pk],
                    writes=fmbuf.k(dc * 512 + tc * 128, 128))
            for ncd in range(2):
                base = tbase0 if ncd == 0 else tbase1
                wkeys = tkeys0 if ncd == 0 else tkeys1
                pslot = ncd % 3
                po = psB[:, pslot * 512:pslot * 512 + 512]
                for kc in range(8):
                    S.op("pe", lambda e, kc=kc, tc=tc, po=po, base=base: e.matmul(
                        po, fmbuf.ap(kc * 512 + tc * 128, 128),
                        wst.ap(base + kc * 512, 512),
                        start=(kc == 0), stop=(kc == 7)),
                        reads=fmbuf.k(kc * 512 + tc * 128, 128) + wkeys,
                        writes=[f"psB:{pslot}"])
                sb0 = (tc % 2) * 2048
                S.op("dve", lambda e, ncd=ncd, po=po: e.scalar_tensor_tensor(
                    scf.ap(3072 + ncd * 512, 512), po, stat.t[:, 2:3],
                    brep.ap(ncd * 512, 512), OP.mult, OP.add),
                    reads=[f"psB:{pslot}"] + stat.k(0, 8) + brep.k(ncd * 512, 512),
                    writes=scf.k(3072 + ncd * 512, 512))
                S.op("dve", lambda e, tc=tc, ncd=ncd, sb0=sb0: e.tensor_tensor(
                    scf.ap(sb0 + ncd * 512, 512), scf.ap(3072 + ncd * 512, 512),
                    x_tm.ap(tc * 1024 + ncd * 512, 512), OP.add),
                    reads=scf.k(3072 + ncd * 512, 512) +
                    x_tm.k(tc * 1024 + ncd * 512, 512),
                    writes=scf.k(sb0 + ncd * 512, 512))
            layer_norm((tc % 2) * 2048, tc)
        if dbg:
            dmain(tap("x2", [128, 4096]), x_tm.ap(0, 4096), reads=x_tm.k())

        if PHASE_LIMIT < 3:
            dst = out_d[seq, t0:t0 + TT, :].rearrange("(tc p) d -> p tc d", p=128)
            S.dma("sync", lambda e, dst=dst: e.dma_start(
                out=dst, in_=x_tm.ap(0, 4096).rearrange("p (tc d) -> p tc d",
                                                        d=1024)),
                reads=x_tm.k(), writes=[f"outp:{tile_idx}"])
            return
        for tc in range(NTC):
            xk = x_tm.k(tc * 1024, 1024)
            xs = x_tm.ap(tc * 1024, 1024)
            S.op("dve", lambda e, xs=xs: e.reduce_sum(stat.ap(0, 1), xs, axis=AX.X),
                 reads=xk, writes=stat.k(0, 8))
            S.op("sc", lambda e, xs=xs: e.activation(scf.ap(1024, 1024), xs,
                                                     AF.Square,
                                                     accum_out=stat.ap(1, 1)),
                 reads=xk, writes=scf.k(1024, 1024) + stat.k(0, 8))
            S.op("dve", lambda e: e.tensor_scalar(stat.ap(2, 1), stat.ap(0, 1),
                                                  cAP(CI1024), None, OP.mult),
                 reads=stat.k(0, 8) + CSTK, writes=stat.k(0, 8))
            S.op("dve", lambda e: e.tensor_tensor(stat.ap(3, 1), stat.ap(2, 1),
                                                  stat.ap(2, 1), OP.mult),
                 reads=stat.k(0, 8), writes=stat.k(0, 8))
            S.op("dve", lambda e: e.scalar_tensor_tensor(
                stat.ap(4, 1), stat.ap(1, 1), 1.0 / 1024.0, stat.ap(3, 1),
                OP.mult, OP.subtract),
                reads=stat.k(0, 8), writes=stat.k(0, 8))
            S.op("dve", lambda e: e.tensor_scalar(stat.ap(0, 1), stat.ap(4, 1),
                                                  1e-5, None, OP.add),
                 reads=stat.k(0, 8), writes=stat.k(0, 8))
            S.op("sc", lambda e: e.activation(stat.ap(4, 1), stat.ap(0, 1), AF.Ln),
                 reads=stat.k(0, 8), writes=stat.k(0, 8))
            S.op("sc", lambda e: e.activation(stat.ap(5, 1), stat.ap(4, 1), AF.Exp,
                                              bias=0.0, scale=cAP(CN05)),
                 reads=stat.k(0, 8) + CSTK, writes=stat.k(0, 8))
            S.op("dve", lambda e, xs=xs: e.tensor_scalar(
                scf.ap(3584, 1024), xs, stat.t[:, 2:3], stat.t[:, 5:6],
                OP.subtract, OP.mult),
                reads=xk + stat.k(0, 8), writes=scf.k(3584, 1024))
            for dc in range(8):
                pap, pk = tp_f32(dc)
                S.op("pe", lambda e, dc=dc, pap=pap: e.transpose(
                    pap, scf.ap(3584 + dc * 128, 128), ident_f.ap(0, 128)),
                    reads=scf.k(3584 + dc * 128, 128) + ident_f.k(),
                    writes=[pk])
                S.op("dve", lambda e, dc=dc, tc=tc, pap=pap: e.tensor_scalar(
                    fmbuf.ap(dc * 512 + tc * 128, 128), pap,
                    prm.t[:, PRM_GAM + dc:PRM_GAM + dc + 1],
                    prm.t[:, PRM_BET + dc:PRM_BET + dc + 1], OP.mult, OP.add),
                    reads=[pk] + prm.k(PRM_GAM, 16),
                    writes=fmbuf.k(dc * 512 + tc * 128, 128))
        if dbg:
            dmain(tap("xn_fm", [128, 4096], BF16), fmbuf.ap(0, 4096),
                  reads=fmbuf.k())

        ipw = {}
        def inproj_mm(oc, pslot):
            g = oc // 4
            if g not in ipw:
                ipw[g] = load_wblock(
                    vI[:, g * 4:(g + 1) * 4].rearrange("p oc kc m -> p (oc kc m)"),
                    4096, "sI")
            base, wkeys = ipw[g]
            base = base + (oc % 4) * 1024
            po = psB[:, pslot * 512:pslot * 512 + 512]
            for kc in range(8):
                S.op("pe", lambda e, kc=kc, po=po, base=base: e.matmul(
                    po, wst.ap(base + kc * 128, 128), fmbuf.ap(kc * 512, 512),
                    start=(kc == 0), stop=(kc == 7)),
                    reads=wkeys + fmbuf.k(kc * 512, 512), writes=[f"psB:{pslot}"])
            return po

        def conv_ops(cc, po, pkey):
            par = cc % 2
            oc = 16 + cc
            tail_lo = VTAILS + (seq * 16 + cc) * 6
            if par == 0:
                wlo, alo, blo, slo = 0, 512, 0, 2048
                wbase = vcs
            else:
                wlo, alo, blo, slo = 4608, 3072, 3584, 5632
                wbase = scf
            def vwin(off, n):
                return wbase.ap(wlo + off, n), wbase.k(wlo + off, n)
            ca, cak = scf.ap(alo, 512), scf.k(alo, 512)
            cb, cbk = scf.ap(blo, 512), scf.k(blo, 512)
            sc_, sck = scf.ap(slo, 512), scf.k(slo, 512)
            dwb_b = prm.t[:, PRM_DWB + cc:PRM_DWB + cc + 1].broadcast_to([128, 512])
            ops = []
            ops.append(lambda: S.op("dve", lambda e: e.tensor_copy(
                vwin(0, 6)[0], vcs.ap(tail_lo, 6)),
                reads=vcs.k(tail_lo, 6), writes=vwin(0, 6)[1]))
            ops.append(lambda: S.op("sc", lambda e: e.activation(
                vwin(6, 512)[0], po, AF.Identity,
                bias=prm.t[:, PRM_INB + oc:PRM_INB + oc + 1]),
                reads=[pkey] + prm.k(PRM_INB, 32) + CSTK,
                writes=vwin(6, 512)[1]))
            ops.append(lambda: S.op("dve", lambda e: e.scalar_tensor_tensor(
                ca, vwin(0, 512)[0], dwwt.t[:, cc * 7:cc * 7 + 1], dwb_b,
                OP.mult, OP.add),
                reads=vwin(0, 518)[1] + dwwt.k() + prm.k(PRM_DWB, 16),
                writes=cak))
            bufs = [(ca, cak), (cb, cbk)]
            for tp_ in range(1, 7):
                dsta, dkey = bufs[tp_ % 2]
                prev, pvkey = bufs[(tp_ + 1) % 2]
                ops.append(lambda tp_=tp_, dsta=dsta, dkey=dkey, prev=prev,
                           pvkey=pvkey: S.op(
                    "dve", lambda e: e.scalar_tensor_tensor(
                        dsta, vwin(tp_, 512)[0],
                        dwwt.t[:, cc * 7 + tp_:cc * 7 + tp_ + 1], prev,
                        OP.mult, OP.add),
                    reads=vwin(0, 518)[1] + dwwt.k() + pvkey, writes=dkey))
            ops.append(lambda: S.op("dve", lambda e: e.tensor_copy(
                arena.ap(AL1C + cc * 512, 512), ca),
                reads=cak, writes=arena.k(AL1C + cc * 512, 512)))
            ops.append(lambda: S.op("dve", lambda e: e.tensor_tensor_scan(
                sc_, prm.t[:, PRM_A + cc:PRM_A + cc + 1].broadcast_to([128, 512]),
                vwin(6, 512)[0],
                prm.t[:, PRM_CARRY + cc * 2 + seq:PRM_CARRY + cc * 2 + seq + 1],
                OP.mult, OP.add),
                reads=vwin(6, 512)[1] + prm.k(PRM_A, 16) +
                prm.k(PRM_CARRY + cc * 2 + seq, 1),
                writes=sck))
            ops.append(lambda: S.op("dve", lambda e: e.tensor_copy(
                prm.ap(PRM_CARRY + cc * 2 + seq, 1), scf.ap(slo + 511, 1)),
                reads=sck, writes=prm.k(PRM_CARRY + cc * 2 + seq, 1)))
            ops.append(lambda: S.op("dve", lambda e: e.tensor_copy(
                arena.ap(AL1S + cc * 512, 512), sc_),
                reads=sck, writes=arena.k(AL1S + cc * 512, 512)))
            ops.append(lambda: S.op("dve", lambda e: e.tensor_copy(
                vcs.ap(tail_lo, 6), vwin(512, 6)[0]),
                reads=vwin(0, 518)[1], writes=vcs.k(tail_lo, 6)))
            return ops

        import itertools as _it
        for gcc in range(8):
            chains = []
            for cc in (2 * gcc, 2 * gcc + 1):
                pslot = cc % 3
                po = inproj_mm(16 + cc, pslot)
                chains.append(conv_ops(cc, po, f"psB:{pslot}"))
            for pair in _it.zip_longest(*chains):
                for fn in pair:
                    if fn is not None:
                        fn()
        for cc in range(16):
            pslot = cc % 3
            po = inproj_mm(cc, pslot)
            S.op("sc", lambda e, po=po, cc=cc: e.activation(
                arena.ap(AL1G + cc * 512, 512), po, AF.Sigmoid,
                bias=prm.t[:, PRM_INB + cc:PRM_INB + cc + 1], scale=1.0),
                reads=[f"psB:{pslot}"] + prm.k(PRM_INB, 32),
                writes=arena.k(AL1G + cc * 512, 512))
        if dbg:
            dmain(tap("l1s", [128, 8192], BF16), arena.ap(AL1S, 8192),
                  reads=arena.k(AL1S, 8192))
            dmain(tap("l1c", [128, 8192], BF16), arena.ap(AL1C, 8192),
                  reads=arena.k(AL1C, 8192))
            dmain(tap("l1g", [128, 8192], BF16), arena.ap(AL1G, 8192),
                  reads=arena.k(AL1G, 8192))

        for oc in range(16):
            if oc % 2 == 0:
                mxgrp = load_wblock(
                    vM[:, oc:oc + 2].rearrange("p oc kc m -> p (oc kc m)"),
                    4096, "sM")
            base, wkeys = mxgrp
            base = base + (oc % 2) * 2048
            pslot = oc % 3
            po = psB[:, pslot * 512:pslot * 512 + 512]
            for kc in range(16):
                S.op("pe", lambda e, oc=oc, kc=kc, po=po, base=base: e.matmul(
                    po, wst.ap(base + kc * 128, 128),
                    arena.ap(AL1S + kc * 512, 512),
                    start=(kc == 0), stop=(kc == 15)),
                    reads=wkeys + arena.k(AL1S + kc * 512, 512),
                    writes=[f"psB:{pslot}"])
            S.op("dve", lambda e, oc=oc, po=po: e.scalar_tensor_tensor(
                scf.ap(0, 512), po, prm.t[:, PRM_MIXB + oc:PRM_MIXB + oc + 1],
                arena.ap(AL1C + oc * 512, 512), OP.add, OP.add),
                reads=[f"psB:{pslot}"] + prm.k(PRM_MIXB, 16) +
                arena.k(AL1C + oc * 512, 512),
                writes=scf.k(0, 512))
            S.op("dve", lambda e, oc=oc: e.tensor_tensor(
                arena.ap(AL1V + oc * 512, 512), scf.ap(0, 512),
                arena.ap(AL1G + oc * 512, 512), OP.mult),
                reads=scf.k(0, 512) + arena.k(AL1G + oc * 512, 512),
                writes=arena.k(AL1V + oc * 512, 512))
        obase0, okeys0 = load_wblock(vO[:, 0].rearrange("p kc n -> p (kc n)"),
                                     8192, "sOw")
        obase1, okeys1 = load_wblock(vO[:, 1].rearrange("p kc n -> p (kc n)"),
                                     8192, "sOw")
        for tc in range(NTC):
            for ncd in range(2):
                base = obase0 if ncd == 0 else obase1
                wkeys = okeys0 if ncd == 0 else okeys1
                pslot = (tc * 2 + ncd) % 3
                po = psB[:, pslot * 512:pslot * 512 + 512]
                for kc in range(16):
                    S.op("pe", lambda e, kc=kc, tc=tc, po=po, base=base: e.matmul(
                        po, arena.ap(AL1V + kc * 512 + tc * 128, 128),
                        wst.ap(base + kc * 512, 512),
                        start=(kc == 0), stop=False),
                        reads=arena.k(AL1V + kc * 512 + tc * 128, 128) + wkeys,
                        writes=[f"psB:{pslot}"])
                for bb in range(2):
                    S.op("pe", lambda e, po=po, ncd=ncd, bb=bb: e.matmul(
                        po, ones_bf.t[0:1, :],
                        brow.t[0:1, bb * 1024 + ncd * 512:
                               bb * 1024 + (ncd + 1) * 512],
                        start=False, stop=(bb == 1)),
                        reads=ones_bf.k() + brow.k(bb * 1024 + ncd * 512, 512),
                        writes=[f"psB:{pslot}"])
                sb0 = (tc % 2) * 2048
                S.op("dve", lambda e, tc=tc, ncd=ncd, po=po, sb0=sb0:
                     e.scalar_tensor_tensor(
                         scf.ap(sb0 + ncd * 512, 512),
                         x_tm.ap(tc * 1024 + ncd * 512, 512), 2.0, po,
                         OP.mult, OP.add),
                     reads=x_tm.k(tc * 1024 + ncd * 512, 512) + [f"psB:{pslot}"],
                     writes=scf.k(sb0 + ncd * 512, 512))
            layer_norm((tc % 2) * 2048, tc)
        if dbg:
            dmain(tap("x3", [128, 4096]), x_tm.ap(0, 4096), reads=x_tm.k())

        if PHASE_LIMIT < 4:
            dst = out_d[seq, t0:t0 + TT, :].rearrange("(tc p) d -> p tc d", p=128)
            S.dma("sync", lambda e, dst=dst: e.dma_start(
                out=dst, in_=x_tm.ap(0, 4096).rearrange("p (tc d) -> p tc d",
                                                        d=1024)),
                reads=x_tm.k(), writes=[f"outp:{tile_idx}"])
            return
        transpose_tm_to_fm(x_tm, lambda tc, dc: x_tm.k(tc * 1024 + dc * 128, 128))
        def pwl_ops(oc, po, pkey):
            fb = (oc % 2) * 3584
            ib = (oc % 2) * 1024
            U = scf.ap(fb + 0, 512); Uk = scf.k(fb + 0, 512)
            U2 = scf.ap(fb + 512, 512); U2k = scf.k(fb + 512, 512)
            FR = scf.ap(fb + 1024, 256).bitcast(BF16); FRk = scf.k(fb + 1024, 256)
            MM = scf.ap(fb + 1280, 256).bitcast(BF16); MMk = scf.k(fb + 1280, 256)
            G0 = scf.ap(fb + 1536, 256).bitcast(BF16); G0k = scf.k(fb + 1536, 256)
            G1 = scf.ap(fb + 1792, 256).bitcast(BF16); G1k = scf.k(fb + 1792, 256)
            YY = scf.ap(fb + 2048, 256).bitcast(BF16); YYk = scf.k(fb + 2048, 256)
            DD = scf.ap(fb + 2304, 256).bitcast(BF16); DDk = scf.k(fb + 2304, 256)
            U3 = scf.ap(fb + 2560, 512); U3k = scf.k(fb + 2560, 512)
            KF = scf.ap(fb + 3072, 512); KFk = scf.k(fb + 3072, 512)
            Kf = U; Kfk = Uk
            Ci = sci.ap(ib + 0, 512); Cik = sci.k(ib + 0, 512)
            return [
                lambda: S.op("sc", lambda e: e.activation(
                    U, po, AF.Identity,
                    bias=prm.t[:, PRM_B1S + oc:PRM_B1S + oc + 1],
                    scale=cAP(C256)),
                    reads=[pkey] + prm.k(PRM_B1S, 32) + CSTK, writes=Uk),
                lambda: S.op("sc", lambda e: e.activation(Ci, U, AF.Identity),
                             reads=Uk, writes=Cik),
                lambda: S.op("dve", lambda e: e.tensor_tensor(MM, Ci, U,
                                                              OP.is_le),
                             reads=Cik + Uk, writes=MMk),
                lambda: S.op("dve", lambda e: e.scalar_tensor_tensor(
                    Kf, Ci, 1, MM, OP.subtract, OP.add),
                    reads=Cik + MMk, writes=Kfk),
                lambda: S.op("sc", lambda e: e.activation(
                    U2, Kf, AF.Identity, bias=0.0, scale=cAP(CI64)),
                    reads=Kfk + CSTK, writes=U2k),
                lambda: S.op("sc", lambda e: e.activation(Ci, U2, AF.Identity),
                             reads=U2k, writes=Cik),
                lambda: S.op("dve", lambda e: e.tensor_tensor(MM, Ci, U2,
                                                              OP.is_le),
                             reads=Cik + U2k, writes=MMk),
                lambda: S.op("dve", lambda e: e.scalar_tensor_tensor(
                    KF, Ci, 1, MM, OP.subtract, OP.add),
                    reads=Cik + MMk, writes=KFk),
                lambda: S.op("dve", lambda e: e.tensor_tensor(FR, U2, KF,
                                                              OP.subtract),
                             reads=U2k + KFk, writes=FRk),
                lambda: S.op("sc", lambda e: e.activation(
                    G0, KF, AF.Gelu, bias=cAP(C0), scale=cAP(C025)),
                    reads=KFk + CSTK, writes=G0k),
                lambda: S.op("sc", lambda e: e.activation(
                    G1, KF, AF.Gelu, bias=cAP(C025), scale=cAP(C025)),
                    reads=KFk + CSTK, writes=G1k),
                lambda: S.op("dve", lambda e: e.tensor_tensor(DD, G1, G0,
                                                              OP.subtract),
                             reads=G1k + G0k, writes=DDk),
                lambda: S.op("dve", lambda e: e.tensor_tensor(MM, FR, DD,
                                                              OP.mult),
                             reads=FRk + DDk, writes=MMk),
                lambda: S.op("dve", lambda e: e.tensor_tensor(YY, MM, G0,
                                                              OP.add),
                             reads=MMk + G0k, writes=YYk),
                lambda: S.op("sc", lambda e: e.activation(
                    U3, YY, AF.Identity, bias=cAP(C05), scale=cAP(C256)),
                    reads=YYk + CSTK, writes=U3k),
                lambda: S.op("sc", lambda e: e.activation(Ci, U3, AF.Identity),
                             reads=U3k, writes=Cik),
                lambda: S.op("dve", lambda e: e.tensor_tensor(MM, Ci, U3,
                                                              OP.is_le),
                             reads=Cik + U3k, writes=MMk),
                lambda: S.op("sc", lambda e: e.activation(
                    KF, U2, AF.Identity, bias=0.0, scale=cAP(C64)),
                    reads=U2k + CSTK, writes=KFk),
                lambda: S.op("dve", lambda e: e.scalar_tensor_tensor(
                    U3, Ci, 1, MM, OP.subtract, OP.add),
                    reads=Cik + MMk, writes=U3k),
                lambda: S.op("dve", lambda e: e.tensor_single_scalar(
                    Ci, U2, 16.0, OP.is_gt),
                    reads=U2k, writes=Cik),
                lambda: S.op("dve", lambda e: e.copy_predicated(U3, Ci, KF),
                             reads=Cik + KFk + U3k, writes=U3k),
                lambda: S.op("sc", lambda e: e.activation(
                    arena.ap(AH + oc * 512, 512), U3, AF.Identity,
                    bias=0.0, scale=cAP(CI256)),
                    reads=U3k + CSTK, writes=arena.k(AH + oc * 512, 512)),
            ]

        w1grp = None
        for g in range(16):
            chains = []
            for oc in (2 * g, 2 * g + 1):
                if oc % 4 == 0:
                    w1grp = load_wblock(
                        vW1[:, oc:oc + 4].rearrange("p oc kc m -> p (oc kc m)"),
                        4096, "sW1")
                base, wkeys = w1grp
                base = base + (oc % 4) * 1024
                pslot = oc % 3
                po = psB[:, pslot * 512:pslot * 512 + 512]
                pkey = f"psB:{pslot}"
                for kc in range(8):
                    S.op("pe", lambda e, kc=kc, po=po, base=base: e.matmul(
                        po, wst.ap(base + kc * 128, 128),
                        fmbuf.ap(kc * 512, 512),
                        start=(kc == 0), stop=(kc == 7)),
                        reads=wkeys + fmbuf.k(kc * 512, 512), writes=[pkey])
                chains.append(pwl_ops(oc, po, pkey))
            for sa, sb in zip(*chains):
                sa()
                sb()
        if dbg:
            dmain(tap("hbf", [128, 16384], BF16), arena.ap(AH, 16384),
                  reads=arena.k(AH, 16384))
        for ncd in range(2):
            bA, kA = load_wblock(vW2[:, ncd, 0:16, :].rearrange("p kc n -> p (kc n)"),
                                 8192, "sW2")
            bB, kB = load_wblock(vW2[:, ncd, 16:32, :].rearrange("p kc n -> p (kc n)"),
                                 8192, "sW2")
            for tc in range(NTC):
                pslot = tc % 3
                po = psB[:, pslot * 512:pslot * 512 + 512]
                pkey = f"psB:{pslot}"
                for kc in range(32):
                    base = bA if kc < 16 else bB
                    wk_ = kA if kc < 16 else kB
                    off = base + (kc % 16) * 512
                    S.op("pe", lambda e, kc=kc, tc=tc, po=po, off=off: e.matmul(
                        po, arena.ap(AH + kc * 512 + tc * 128, 128),
                        wst.ap(off, 512),
                        start=(kc == 0), stop=False),
                        reads=arena.k(AH + kc * 512 + tc * 128, 128) + wk_,
                        writes=[pkey])
                for bb in range(2):
                    S.op("pe", lambda e, po=po, ncd=ncd, bb=bb: e.matmul(
                        po, ones_bf.t[0:1, :],
                        brow.t[0:1, 2048 + bb * 1024 + ncd * 512:
                               2048 + bb * 1024 + (ncd + 1) * 512],
                        start=False, stop=(bb == 1)),
                        reads=ones_bf.k() + brow.k(2048 + bb * 1024 + ncd * 512,
                                                   512),
                        writes=[pkey])
                S.op("dve", lambda e, po=po: e.tensor_scalar(
                    scf.ap(1024, 512), po, 256.0, 0.5, OP.mult, OP.add),
                    reads=[pkey], writes=scf.k(1024, 512))
                floor_fixup(scf.ap(1024, 512), scf.k(1024, 512),
                            sci.ap(0, 512), sci.k(0, 512),
                            sci.ap(512, 512), sci.k(512, 512),
                            sci.ap(0, 512), sci.k(0, 512))
                S.op("dve", lambda e, tc=tc, ncd=ncd: e.tensor_scalar(
                    scf.ap(3584 + tc * 1024 + ncd * 512, 512), sci.ap(0, 512),
                    cAP(CI256), None, OP.mult),
                    reads=sci.k(0, 512) + CSTK,
                    writes=scf.k(3584 + tc * 1024 + ncd * 512, 512))
        for tc in range(NTC):
            for ncd in range(2):
                elo = 3584 + tc * 1024 + ncd * 512
                sb0 = (tc % 2) * 2048
                S.op("dve", lambda e, elo=elo, tc=tc, ncd=ncd, sb0=sb0:
                     e.tensor_tensor(
                         scf.ap(sb0 + ncd * 512, 512), scf.ap(elo, 512),
                         x_tm.ap(tc * 1024 + ncd * 512, 512), OP.add),
                     reads=scf.k(elo, 512) + x_tm.k(tc * 1024 + ncd * 512, 512),
                     writes=scf.k(sb0 + ncd * 512, 512))
            layer_norm((tc % 2) * 2048, tc)
        if dbg:
            dmain(tap("x4", [128, 4096]), x_tm.ap(0, 4096), reads=x_tm.k())

        dst = out_d[seq, t0:t0 + TT, :].rearrange("(tc p) d -> p tc d", p=128)
        S.dma("sync", lambda e, dst=dst: e.dma_start(
            out=dst, in_=x_tm.ap(0, 4096).rearrange("p (tc d) -> p tc d", d=1024)),
            reads=x_tm.k(), writes=[f"outp:{tile_idx}"])

    n_seq_emit, n_tt_emit = EMIT_TILES or (SEQ_PER_CORE, TILES_PER_SEQ)
    tile_idx = 0
    for seq in range(n_seq_emit):
        if seq > 0:
            S.op("dve", lambda e: e.memset(ret_state.ap(0, 4096), 0.0),
                 reads=ret_state.k(), writes=ret_state.k())
        for tt in range(n_tt_emit):
            emit_tile(seq, tt, tile_idx)
            tile_idx += 1

    S.op("sync", None, reads=[f"outp:{i}" for i in range(tile_idx)])

    S.finalize(nc, es)
    with es:
        blk = es.enter_context(nc.Block())
        blk.sync(lambda eng: S.emit("sync", eng))
        blk.vector(lambda eng: S.emit("dve", eng))
        blk.scalar(lambda eng: S.emit("sc", eng))
        blk.tensor(lambda eng: S.emit("pe", eng))
        blk.gpsimd(lambda eng: S.emit("gp", eng))
    return nc


def _xpos_tables():
    d = np.arange(0, 1024, 2, dtype=np.float64)
    base_scale = (d + 0.4 * 1024.0) / (1.4 * 1024.0)
    pos = np.arange(T, dtype=np.float64)
    scale = base_scale[None, :] ** (pos[:, None] / 512.0)
    inv_freq = 1.0 / 10000.0 ** (np.arange(512, dtype=np.float64) / 512.0)
    sinu = pos[:, None] * inv_freq[None, :]
    sin, cos = np.sin(sinu), np.cos(sinu)
    def expand(tbl):
        out = np.empty((1024, T), np.float64)
        out[0::2] = tbl.T
        out[1::2] = tbl.T
        return out
    sgn = np.empty((1024, 1), np.float64)
    sgn[0::2] = -1.0
    sgn[1::2] = 1.0
    cq = expand(cos * scale)
    sq = expand(sin * scale) * sgn
    ck = expand(cos / scale)
    sk = expand(sin / scale) * sgn
    def tile(tbl):
        return np.ascontiguousarray(
            tbl.reshape(8, 128, TILES_PER_SEQ, TT).transpose(2, 0, 1, 3)
        ).astype(np.float32)
    return tile(cq), tile(sq), tile(ck), tile(sk)


_PROG_CACHE = {}


def kernel(L0_attn_hidden_states, L0_attn_W_qkv, L0_attn_W_g, L0_attn_W_proj,
           L0_attn_decay, L0_ffn_packed_weight, L0_ffn_weight_scale, L0_ffn_bias,
           L1_attn_norm_gamma, L1_attn_norm_beta, L1_attn_in_w, L1_attn_in_b,
           L1_attn_dw_w, L1_attn_dw_b, L1_attn_log_decay, L1_attn_mix_w,
           L1_attn_mix_b, L1_attn_out_w, L1_attn_out_b, L1_ffn_w1, L1_ffn_b1,
           L1_ffn_w2, L1_ffn_b2):
    x = np.ascontiguousarray(L0_attn_hidden_states, dtype=np.float32)
    cq, sq, ck, sk = _xpos_tables()
    consts = np.tile(np.array(CONSTS, np.float32)[None, :], (128, 1))
    idm = np.eye(128, dtype=np.float32)
    key = ("prog", DEBUG_TAPS, EMIT_TILES, PHASE_LIMIT, SUBPHASE)
    if key not in _PROG_CACHE:
        _PROG_CACHE[key] = build_program(DEBUG_TAPS)
    nc = _PROG_CACHE[key]
    common = {
        "wqkv": np.ascontiguousarray(L0_attn_W_qkv, np.float32),
        "wg": np.ascontiguousarray(L0_attn_W_g, np.float32),
        "wproj": np.ascontiguousarray(L0_attn_W_proj, np.float32),
        "decay": np.ascontiguousarray(L0_attn_decay, np.float32).reshape(1, H),
        "packed": np.ascontiguousarray(L0_ffn_packed_weight,
                                       np.int32).reshape(256, 1024),
        "wscale": np.ascontiguousarray(L0_ffn_weight_scale,
                                       np.float32).reshape(1, 1),
        "bias0": np.ascontiguousarray(L0_ffn_bias, np.float32).reshape(1, D),
        "gamma": np.ascontiguousarray(L1_attn_norm_gamma, np.float32),
        "beta": np.ascontiguousarray(L1_attn_norm_beta, np.float32),
        "in_w": np.ascontiguousarray(L1_attn_in_w, np.float32),
        "in_b": np.ascontiguousarray(L1_attn_in_b, np.float32),
        "dw_w": np.ascontiguousarray(L1_attn_dw_w, np.float32),
        "dw_b": np.ascontiguousarray(L1_attn_dw_b, np.float32),
        "log_decay": np.ascontiguousarray(L1_attn_log_decay, np.float32),
        "mix_w": np.ascontiguousarray(L1_attn_mix_w, np.float32),
        "mix_b": np.ascontiguousarray(L1_attn_mix_b, np.float32),
        "out_w": np.ascontiguousarray(L1_attn_out_w, np.float32),
        "out_b": np.ascontiguousarray(L1_attn_out_b, np.float32).reshape(1, D),
        "w1": np.ascontiguousarray(L1_ffn_w1, np.float32),
        "b1": np.ascontiguousarray(L1_ffn_b1, np.float32),
        "w2": np.ascontiguousarray(L1_ffn_w2, np.float32),
        "b2": np.ascontiguousarray(L1_ffn_b2, np.float32).reshape(1, D),
        "consts": consts, "cq_t": cq, "sq_t": sq, "ck_t": ck, "sk_t": sk,
        "idm": idm,
    }
    in_maps = []
    for c in range(N_CORES):
        m = dict(common)
        m["xin"] = np.ascontiguousarray(x[c * SEQ_PER_CORE:(c + 1) * SEQ_PER_CORE])
        in_maps.append(m)
    res = run_bass_kernel_spmd(nc, in_maps, core_ids=list(range(N_CORES)))
    if DEBUG_TAPS:
        kernel._last_results = res.results
    out = np.concatenate([r["out"] for r in res.results], axis=0)
    return out.reshape(B, T, D)


# version 10
# speedup vs baseline: 1.0555x; 1.0555x over previous
import numpy as np
from contextlib import ExitStack

import concourse.bass as bass
import concourse.mybir as mybir
from concourse.bass_utils import run_bass_kernel_spmd

F32 = mybir.dt.float32
BF16 = mybir.dt.bfloat16
I32 = mybir.dt.int32
AF = mybir.ActivationFunctionType
OP = mybir.AluOpType
AX = mybir.AxisListType

DEBUG_TAPS = False
PHASE_LIMIT = 4
SUBPHASE = 9
EMIT_TILES = None
N_CORES = 8
B, T, D = 16, 4096, 1024
SEQ_PER_CORE = 2
TT = 512
NTC = TT // 128
TILES_PER_SEQ = T // TT
H = 8
HC = 2048
FF = 4096
INV_SQRT_DK = float(1.0 / np.sqrt(128.0))

CONSTS = [256.0, 1.0 / 256.0, 1.0 / 64.0, 0.25, -64.0, -63.75, 0.5, 1e-5, 127.0,
          1.0 / 1024.0, 64.0, 2.0, 16.0, -0.5, 1.0, 0.0]
C256, CI256, CI64, C025, CN64, CN6375, C05, CEPS, C127, CI1024, C64, C2, C16, CN05, C1, C0 = range(16)


class Sched:
    ENGS = ("pe", "dve", "sc", "gp", "sync")

    def __init__(self):
        self.ops = {e: [] for e in self.ENGS}
        self.W = {}
        self.R = {}

    def _record(self, eng, fn, reads, writes, extra, is_dma):
        deps = []
        for b in reads:
            t = self.W.get(b)
            if t is not None:
                deps.append(t)
        for b in writes:
            deps.extend(self.R.get(b, ()))
            t = self.W.get(b)
            if t is not None:
                deps.append(t)
        deps.extend(extra)
        tok = (eng, len(self.ops[eng]))
        self.ops[eng].append(dict(fn=fn, deps=set(deps), is_dma=is_dma,
                                  awaited=False))
        for b in reads:
            self.R.setdefault(b, []).append(tok)
        for b in writes:
            self.W[b] = tok
            self.R[b] = []
        return tok

    def op(self, eng, fn, reads=(), writes=(), extra=()):
        return self._record(eng, fn, reads, writes, extra, False)

    def dma(self, eng, fn, reads=(), writes=(), extra=()):
        return self._record(eng, fn, reads, writes, extra, True)

    NPOOL = 24

    def finalize(self, nc, es):
        self.sems = {}
        for e in self.ENGS:
            if any(not o["is_dma"] for o in self.ops[e]):
                self.sems[e] = es.enter_context(nc.semaphore(f"s_{e}"))
        for p in range(self.NPOOL):
            self.sems[f"d{p}"] = es.enter_context(nc.semaphore(f"d{p}"))
        SELF_SYNC = ("dve", "sc", "gp")
        for e in self.ENGS:
            for o in self.ops[e]:
                for (de, di) in o["deps"]:
                    dep = self.ops[de][di]
                    if dep["is_dma"] or de != e or de in SELF_SYNC:
                        dep["awaited"] = True
        self.val = {}
        pool_use = [0] * self.NPOOL
        dma_ord = 0
        for e in self.ENGS:
            cc_ = 0
            for i, o in enumerate(self.ops[e]):
                if o["is_dma"]:
                    p = dma_ord % self.NPOOL
                    dma_ord += 1
                    pool_use[p] += 1
                    sem_name = f"d{p}"
                    o["pre_wait"] = (sem_name, 16 * (pool_use[p] - 1))
                    o["inc"] = (sem_name, 16)
                    self.val[(e, i)] = (sem_name, 16 * pool_use[p])
                elif o["awaited"]:
                    cc_ += 1
                    o["inc"] = (e, 1)
                    self.val[(e, i)] = (e, cc_)
                else:
                    o["inc"] = None

    def emit(self, ename, eng):
        clock = {}
        for i, o in enumerate(self.ops[ename]):
            want = {}
            for (de, di) in sorted(o["deps"]):
                dep = self.ops[de][di]
                if (not dep["is_dma"] and de == ename
                        and de not in ("dve", "sc", "gp")):
                    continue
                sem_name, v = self.val[(de, di)]
                if v > want.get(sem_name, 0):
                    want[sem_name] = v
            pw = o.get("pre_wait")
            if pw is not None and pw[1] > want.get(pw[0], 0):
                want[pw[0]] = pw[1]
            for sem_name, v in sorted(want.items()):
                if v > clock.get(sem_name, 0):
                    eng.wait_ge(self.sems[sem_name], v)
                    clock[sem_name] = v
            inst = o["fn"](eng) if o["fn"] is not None else None
            if o["inc"] is not None:
                assert inst is not None, f"awaited op with no inst on {ename}:{i}"
                sem_name, amt = o["inc"]
                inst.then_inc(self.sems[sem_name], amt)


class TBuf:
    def __init__(self, t, nm, blk=256):
        self.t = t
        self.nm = nm
        self.blk = blk

    def ap(self, lo, n):
        return self.t[:, lo:lo + n]

    def k(self, lo=0, n=None):
        if n is None:
            n = self.t.shape[1] - lo
        b = self.blk
        return [f"{self.nm}:{i}" for i in range(lo // b, (lo + n + b - 1) // b)]


def build_program(debug_taps=False):
    nc = bass.Bass("TRN2", target_bir_lowering=False, detect_race_conditions=False)
    S = Sched()
    es = ExitStack()

    din = {}
    def dram_in(name, shape, dt=F32):
        din[name] = nc.dram_tensor(name, list(shape), dt, kind="ExternalInput").ap()
        return din[name]

    xin = dram_in("xin", [SEQ_PER_CORE, T, D])
    wqkv = dram_in("wqkv", [D, 4096])
    wg_in = dram_in("wg", [D, 2048])
    wproj_in = dram_in("wproj", [2048, D])
    decay_in = dram_in("decay", [1, H])
    packed_in = dram_in("packed", [256, 1024], I32)
    ws_in = dram_in("wscale", [1, 1])
    bias0_in = dram_in("bias0", [1, D])
    gamma_in = dram_in("gamma", [D])
    beta_in = dram_in("beta", [D])
    inw_in = dram_in("in_w", [D, 2 * HC])
    inb_in = dram_in("in_b", [2 * HC])
    dww_in = dram_in("dw_w", [HC, 7])
    dwb_in = dram_in("dw_b", [HC])
    logd_in = dram_in("log_decay", [HC])
    mixw_in = dram_in("mix_w", [HC, HC])
    mixb_in = dram_in("mix_b", [HC])
    outw_in = dram_in("out_w", [HC, D])
    outb_in = dram_in("out_b", [1, D])
    w1_in = dram_in("w1", [D, FF])
    b1_in = dram_in("b1", [FF])
    w2_in = dram_in("w2", [FF, D])
    b2_in = dram_in("b2", [1, D])
    consts_in = dram_in("consts", [128, len(CONSTS)])
    cq_in = dram_in("cq_t", [TILES_PER_SEQ, 8, 128, TT])
    sq_in = dram_in("sq_t", [TILES_PER_SEQ, 8, 128, TT])
    ck_in = dram_in("ck_t", [TILES_PER_SEQ, 8, 128, TT])
    sk_in = dram_in("sk_t", [TILES_PER_SEQ, 8, 128, TT])
    idm_in = dram_in("idm", [128, 128])

    out_d = nc.dram_tensor("out", [SEQ_PER_CORE, T, D], F32, kind="ExternalOutput").ap()

    taps = {}
    def tap(name, shape, dt=F32):
        if name not in taps:
            taps[name] = nc.dram_tensor("tap_" + name, list(shape), dt,
                                        kind="ExternalOutput").ap()
        return taps[name]

    def scr(name, n_el, dt=BF16):
        return nc.dram_tensor(name, [n_el], dt, kind="Internal").ap()

    sQ = scr("sQ", 128 * 8 * 8 * 128)
    sK = scr("sK", 128 * 8 * 8 * 128)
    sV = scr("sV", 128 * 4 * 8 * 512)
    sGw = scr("sGw", 128 * 4 * 8 * 512)
    sP = scr("sP", 128 * 2 * 16 * 512)
    sTn = scr("sTn", 128 * 2 * 8 * 512)
    sI = scr("sI", 128 * 32 * 8 * 128)
    sM = scr("sM", 128 * 16 * 16 * 128)
    sOw = scr("sOw", 128 * 2 * 16 * 512)
    sW1 = scr("sW1", 128 * 32 * 8 * 128)
    sW2 = scr("sW2", 128 * 2 * 32 * 512)

    def sb(nm, shape, dt=F32, blk=256):
        t = es.enter_context(nc.sbuf_tensor(nm, list(shape), dt))
        return TBuf(t, nm, blk)

    x_tm = sb("x_tm", [128, NTC * 1024])
    arena = sb("arena", [128, 32768], BF16, blk=512)
    fmbuf = sb("fmbuf", [128, 8 * 512], BF16, blk=512)
    scf = sb("scf", [128, 7680])
    sci = sb("sci", [128, 2048], I32, blk=512)
    ret_state = sb("ret_state", [128, 16 * 256])
    ret_sm = sb("ret_sm", [128, 1536], BF16, blk=128)
    masks = sb("masks", [128, H * 512], BF16, blk=512)
    rotb = sb("rotb", [128, 1024], F32, blk=512)
    wst = sb("wst", [128, 16384], BF16, blk=8192)
    brep = sb("brep", [128, 1024], F32, blk=1024)
    ones_bf = sb("ones_bf", [1, 128], BF16, blk=128)
    brow = sb("brow", [1, 4096], BF16, blk=1024)
    prm = sb("prm", [128, 232], F32, blk=8)
    cst = sb("cst", [128, len(CONSTS)], F32, blk=16)
    stat = sb("stat", [128, 64], F32, blk=8)
    ident_f = sb("ident_f", [128, 128], F32, blk=128)
    ident_b = sb("ident_b", [128, 128], BF16, blk=128)
    vcs = sb("vcs", [128, 712], F32, blk=64)
    dwwt = sb("dwwt", [128, 112], F32, blk=112)

    PRM_LND, PRM_D256, PRM_WR, PRM_WK, PRM_A, PRM_CARRY = 0, 8, 16, 32, 48, 64
    PRM_INB, PRM_MIXB, PRM_DWB = 96, 128, 144
    PRM_GAM, PRM_BET, PRM_B1S, PRM_WS = 160, 168, 176, 224
    VTAILS = 520

    psA = es.enter_context(nc.psum_tensor("psA", [128, 1536], F32))
    psB = es.enter_context(nc.psum_tensor("psB", [128, 1536], F32))
    psT = es.enter_context(nc.psum_tensor("psT", [128, 512], F32))
    psTb = es.enter_context(nc.psum_tensor("psTb", [128, 1024], BF16))
    TPF32 = [(psT, 0, "psT"), (psA, 0, "psA:0"), (psA, 512, "psA:1"),
             (psA, 1024, "psA:2")]
    TPBF = [(psTb, 0, "psTb"), (psB, 1024, "psB:2"), (psB, 0, "psB:0"),
            (psB, 512, "psB:1")]
    def tp_f32(i):
        t, off, k = TPF32[i % 4]
        return t[:, off:off + 128], k
    def tp_bf(i):
        t, off, k = TPBF[i % 4]
        if k == "psTb":
            return t[:, off:off + 128], k
        return t[:, off:off + 64].bitcast(BF16), k

    cAP = lambda i: cst.t[:, i:i + 1]
    CSTK = cst.k(0)

    def dmain(dst_ap, src_ap, reads=(), writes=(), slow=False):
        return S.dma("sync", lambda e, d=dst_ap, s=src_ap, sl=slow: e.dma_start(
            out=d, in_=s, allow_slow_non_contiguous=sl),
            reads=reads, writes=writes)

    def repl_row(buf, lo, hi):
        n = 1
        while n < 128:
            m = min(n, 128 - n)
            S.dma("sync", lambda e, n=n, m=m: e.dma_start(
                out=buf.t[n:n + m, lo:hi], in_=buf.t[0:m, lo:hi]),
                reads=buf.k(lo, hi - lo), writes=buf.k(lo, hi - lo))
            n += m

    wslot = [0]
    def load_wblock(src3d_ap, n_el, src_key):
        slot = wslot[0]
        wslot[0] ^= 1
        base = slot * 8192
        assert n_el <= 8192
        dmain(wst.ap(base, n_el), src3d_ap, reads=[src_key],
              writes=wst.k(base, n_el))
        return base, wst.k(base, n_el)

    vQ = sQ.rearrange("(p oc kc m) -> p oc kc m", p=128, oc=8, kc=8)
    vK = sK.rearrange("(p oc kc m) -> p oc kc m", p=128, oc=8, kc=8)
    vV = sV.rearrange("(p ncd kc n) -> p ncd kc n", p=128, ncd=4, kc=8)
    vG = sGw.rearrange("(p ncd kc n) -> p ncd kc n", p=128, ncd=4, kc=8)
    vP = sP.rearrange("(p ncd kc n) -> p ncd kc n", p=128, ncd=2, kc=16)
    vTn = sTn.rearrange("(p ncd kc n) -> p ncd kc n", p=128, ncd=2, kc=8)
    vI = sI.rearrange("(p oc kc m) -> p oc kc m", p=128, oc=32, kc=8)
    vM = sM.rearrange("(p oc kc m) -> p oc kc m", p=128, oc=16, kc=16)
    vO = sOw.rearrange("(p ncd kc n) -> p ncd kc n", p=128, ncd=2, kc=16)
    vW1 = sW1.rearrange("(p oc kc m) -> p oc kc m", p=128, oc=32, kc=8)
    vW2 = sW2.rearrange("(p ncd kc n) -> p ncd kc n", p=128, ncd=2, kc=32)

    dmain(cst.ap(0, len(CONSTS)), consts_in, writes=CSTK)
    dmain(ident_f.ap(0, 128), idm_in, writes=ident_f.k())
    S.op("dve", lambda e: e.tensor_copy(ident_b.ap(0, 128), ident_f.ap(0, 128)),
         reads=ident_f.k(), writes=ident_b.k())

    def prep_weight(src_ap, n_rows, n_cols, view, kind, scr_key):
        n_kc = n_rows // 128
        for kc in range(n_kc):
            dmain(scf.ap(0, n_cols), src_ap[kc * 128:(kc + 1) * 128, :],
                  writes=scf.k(0, n_cols))
            S.op("dve", lambda e, c=n_cols: e.tensor_copy(arena.ap(0, c),
                                                          scf.ap(0, c)),
                 reads=scf.k(0, n_cols), writes=arena.k(0, n_cols))
            if kind == "lhsT":
                dst = view[:, :, kc, :]
                src = arena.ap(0, n_cols).rearrange("p (oc m) -> p oc m", m=128)
            else:
                dst = view[:, :, kc, :]
                src = arena.ap(0, n_cols).rearrange("p (ncd n) -> p ncd n", n=512)
            S.dma("sync", lambda e, dst=dst, src=src: e.dma_start(out=dst, in_=src),
                  reads=arena.k(0, n_cols), writes=[scr_key])

    prep_weight(wqkv[:, 0:1024], 1024, 1024, vQ, "lhsT", "sQ")
    prep_weight(wqkv[:, 1024:2048], 1024, 1024, vK, "lhsT", "sK")
    prep_weight(wqkv[:, 2048:4096], 1024, 2048, vV, "rhs", "sV")
    prep_weight(wg_in, 1024, 2048, vG, "rhs", "sGw")
    prep_weight(wproj_in, 2048, 1024, vP, "rhs", "sP")
    prep_weight(inw_in, 1024, 4096, vI, "lhsT", "sI")
    prep_weight(mixw_in, 2048, 2048, vM, "lhsT", "sM")
    prep_weight(outw_in, 2048, 1024, vO, "rhs", "sOw")
    prep_weight(w1_in, 1024, 4096, vW1, "lhsT", "sW1")
    prep_weight(w2_in, 4096, 1024, vW2, "rhs", "sW2")

    def floor_fixup(u_ap, u_keys, c_ap, c_keys, m_ap, m_keys, f_ap, f_keys):
        S.op("sc", lambda e: e.activation(c_ap, u_ap, AF.Identity),
             reads=u_keys, writes=c_keys)
        S.op("dve", lambda e: e.tensor_tensor(m_ap, c_ap, u_ap, OP.is_le),
             reads=c_keys + u_keys, writes=m_keys)
        S.op("dve", lambda e: e.scalar_tensor_tensor(f_ap, c_ap, 1, m_ap,
                                                     OP.subtract, OP.add),
             reads=c_keys + m_keys, writes=f_keys)

    for oc2 in range(8):
        j = oc2 // 2
        r0 = (oc2 % 2) * 128
        dmain(sci.ap(0, 1024), packed_in[r0:r0 + 128, :], writes=sci.k(0, 1024))
        S.op("dve", lambda e: e.tensor_copy(scf.ap(0, 1024), sci.ap(0, 1024)),
             reads=sci.k(0, 1024), writes=scf.k(0, 1024))
        def floor_scaled(dst_off, scale_val):
            S.op("dve", lambda e, s=scale_val, d=dst_off: e.tensor_scalar(
                scf.ap(d, 1024), scf.ap(0, 1024), s, None, OP.mult),
                reads=scf.k(0, 1024), writes=scf.k(dst_off, 1024))
            floor_fixup(scf.ap(dst_off, 1024), scf.k(dst_off, 1024),
                        sci.ap(1024, 1024), sci.k(1024, 1024),
                        scf.ap(4096, 1024), scf.k(4096, 1024),
                        sci.ap(1024, 1024), sci.k(1024, 1024))
            S.op("dve", lambda e, d=dst_off: e.tensor_copy(scf.ap(d, 1024),
                                                           sci.ap(1024, 1024)),
                 reads=sci.k(1024, 1024), writes=scf.k(dst_off, 1024))
        if j == 0:
            floor_scaled(1024, 0.25)
            S.op("dve", lambda e: e.scalar_tensor_tensor(
                scf.ap(2048, 1024), scf.ap(1024, 1024), -4.0, scf.ap(0, 1024),
                OP.mult, OP.add),
                reads=scf.k(0, 2048), writes=scf.k(2048, 1024))
        else:
            floor_scaled(1024, 0.25 ** j)
            floor_scaled(3072, 0.25 ** (j + 1))
            S.op("dve", lambda e: e.scalar_tensor_tensor(
                scf.ap(2048, 1024), scf.ap(3072, 1024), -4.0, scf.ap(1024, 1024),
                OP.mult, OP.add),
                reads=scf.k(1024, 1024) + scf.k(3072, 1024),
                writes=scf.k(2048, 1024))
        S.op("dve", lambda e: e.tensor_scalar(arena.ap(0, 1024), scf.ap(2048, 1024),
                                              1.0, None, OP.subtract),
             reads=scf.k(2048, 1024), writes=arena.k(0, 1024))
        for ic in range(8):
            S.op("pe", lambda e, ic=ic: e.transpose(
                psTb[:, 0:128], arena.ap(ic * 128, 128), ident_b.ap(0, 128)),
                reads=arena.k(0, 1024) + ident_b.k(), writes=["psTb"])
            S.op("dve", lambda e: e.tensor_copy(arena.ap(4096, 128), psTb[:, 0:128]),
                 reads=["psTb"], writes=arena.k(4096, 128))
            dstv = vTn[:, oc2 // 4, ic, (oc2 % 4) * 128:(oc2 % 4) * 128 + 128]
            S.dma("sync", lambda e, dstv=dstv: e.dma_start(out=dstv,
                                                           in_=arena.ap(4096, 128)),
                  reads=arena.k(4096, 128), writes=["sTn"])

    dmain(scf.t[0:1, 2048:2056], decay_in, writes=scf.k(2048, 256))
    S.op("sc", lambda e: e.activation(prm.t[0:1, PRM_LND:PRM_LND + 8],
                                      scf.t[0:1, 2048:2056], AF.Ln),
         reads=scf.k(2048, 256), writes=prm.k(PRM_LND, 8))
    repl_row(prm, PRM_LND, PRM_LND + 8)
    S.op("dve", lambda e: e.tensor_scalar(scf.ap(2048, 8), prm.ap(PRM_LND, 8),
                                          256.0, None, OP.mult),
         reads=prm.k(PRM_LND, 8), writes=scf.k(2048, 256))
    S.op("sc", lambda e: e.activation(prm.ap(PRM_D256, 8), scf.ap(2048, 8),
                                      AF.Exp),
         reads=scf.k(2048, 256), writes=prm.k(PRM_D256, 8))
    S.op("gp", lambda e: e.iota(sci.ap(0, 512), [[-128, 2], [1, 256]], base=0,
                                channel_multiplier=-1),
         writes=sci.k(0, 512))
    S.op("dve", lambda e: e.tensor_copy(scf.ap(0, 512), sci.ap(0, 512)),
         reads=sci.k(0, 512), writes=scf.k(0, 512))
    S.op("dve", lambda e: e.tensor_single_scalar(scf.ap(512, 512), scf.ap(0, 512),
                                                 0.0, OP.is_ge),
         reads=scf.k(0, 512), writes=scf.k(512, 512))
    S.op("dve", lambda e: e.tensor_scalar(scf.ap(1536, 512), scf.ap(512, 512),
                                          INV_SQRT_DK, None, OP.mult),
         reads=scf.k(512, 512), writes=scf.k(1536, 512))
    for h in range(H):
        S.op("dve", lambda e, h=h: e.tensor_scalar(
            scf.ap(1024, 512), scf.ap(0, 512),
            prm.t[:, PRM_LND + h:PRM_LND + h + 1], None, OP.mult),
            reads=scf.k(0, 512) + prm.k(PRM_LND, 8), writes=scf.k(1024, 512))
        S.op("sc", lambda e: e.activation(scf.ap(1024, 512), scf.ap(1024, 512),
                                          AF.Exp),
             reads=scf.k(1024, 512), writes=scf.k(1024, 512))
        S.op("dve", lambda e, h=h: e.tensor_tensor(
            masks.ap(h * 512, 512), scf.ap(1024, 512), scf.ap(1536, 512), OP.mult),
            reads=scf.k(1024, 512) + scf.k(1536, 512),
            writes=masks.k(h * 512, 512))
    S.op("gp", lambda e: e.iota(sci.ap(0, 2), [[128, 2]], base=1,
                                channel_multiplier=1),
         writes=sci.k(0, 512))
    S.op("dve", lambda e: e.tensor_copy(scf.ap(0, 2), sci.ap(0, 2)),
         reads=sci.k(0, 512), writes=scf.k(0, 256))
    S.op("gp", lambda e: e.iota(sci.ap(512, 2), [[-128, 2]], base=255,
                                channel_multiplier=-1),
         writes=sci.k(512, 512))
    S.op("dve", lambda e: e.tensor_copy(scf.ap(256, 2), sci.ap(512, 2)),
         reads=sci.k(512, 512), writes=scf.k(256, 256))
    for h in range(H):
        for (so, dst) in ((0, PRM_WR), (256, PRM_WK)):
            S.op("dve", lambda e, h=h, so=so: e.tensor_scalar(
                scf.ap(2048, 2), scf.ap(so, 2),
                prm.t[:, PRM_LND + h:PRM_LND + h + 1], None, OP.mult),
                reads=scf.k(so, 256) + prm.k(PRM_LND, 8),
                writes=scf.k(2048, 256))
            S.op("sc", lambda e: e.activation(
                scf.ap(2304, 2), scf.ap(2048, 2), AF.Exp),
                reads=scf.k(2048, 256), writes=scf.k(2304, 256))
            if dst == PRM_WR:
                S.op("dve", lambda e, h=h, dst=dst: e.tensor_scalar(
                    prm.ap(dst + 2 * h, 2), scf.ap(2304, 2),
                    INV_SQRT_DK, None, OP.mult),
                    reads=scf.k(2304, 256), writes=prm.k(dst + 2 * h, 2))
            else:
                S.op("dve", lambda e, h=h, dst=dst: e.tensor_copy(
                    prm.ap(dst + 2 * h, 2), scf.ap(2304, 2)),
                    reads=scf.k(2304, 256), writes=prm.k(dst + 2 * h, 2))
    dmain(scf.ap(2048, 16), logd_in.rearrange("(c p) -> p c", p=128),
          writes=scf.k(2048, 256), slow=True)
    S.op("sc", lambda e: e.activation(prm.ap(PRM_A, 16), scf.ap(2048, 16),
                                      AF.Sigmoid),
         reads=scf.k(2048, 256), writes=prm.k(PRM_A, 16))
    S.op("dve", lambda e: e.memset(prm.ap(PRM_CARRY, 32), 0.0),
         writes=prm.k(PRM_CARRY, 32))
    dmain(prm.ap(PRM_INB, 32), inb_in.rearrange("(c p) -> p c", p=128),
          writes=prm.k(PRM_INB, 32), slow=True)
    dmain(prm.ap(PRM_MIXB, 16), mixb_in.rearrange("(c p) -> p c", p=128),
          writes=prm.k(PRM_MIXB, 16), slow=True)
    dmain(prm.ap(PRM_DWB, 16), dwb_in.rearrange("(c p) -> p c", p=128),
          writes=prm.k(PRM_DWB, 16), slow=True)
    dmain(prm.ap(PRM_GAM, 8), gamma_in.rearrange("(c p) -> p c", p=128),
          writes=prm.k(PRM_GAM, 8), slow=True)
    dmain(prm.ap(PRM_BET, 8), beta_in.rearrange("(c p) -> p c", p=128),
          writes=prm.k(PRM_BET, 8), slow=True)
    dmain(prm.ap(PRM_B1S, 32), b1_in.rearrange("(c p) -> p c", p=128),
          writes=prm.k(PRM_B1S, 32), slow=True)
    S.op("dve", lambda e: e.tensor_scalar(scf.ap(2048, 32), prm.ap(PRM_B1S, 32),
                                          256.0, 0.5, OP.mult, OP.add),
         reads=prm.k(PRM_B1S, 32), writes=scf.k(2048, 256))
    S.op("dve", lambda e: e.tensor_copy(prm.ap(PRM_B1S, 32), scf.ap(2048, 32)),
         reads=scf.k(2048, 256), writes=prm.k(PRM_B1S, 32))
    dmain(dwwt.ap(0, 112).rearrange("p (c j) -> p c j", j=7),
          dww_in.rearrange("(c p) j -> p c j", p=128), writes=dwwt.k(), slow=True)
    dmain(prm.t[0:1, PRM_WS:PRM_WS + 1], ws_in, writes=prm.k(PRM_WS, 1))
    repl_row(prm, PRM_WS, PRM_WS + 1)
    dmain(brep.t[0:1, 0:1024], bias0_in, writes=brep.k(0, 1024))
    repl_row(brep, 0, 1024)
    S.op("dve", lambda e: e.memset(ones_bf.ap(0, 128), 1.0), writes=ones_bf.k())
    for bi, (bsrc, blo) in enumerate(((outb_in, 0), (b2_in, 2048))):
        dmain(scf.t[0:1, 0:1024], bsrc, writes=scf.k(0, 1024))
        S.op("dve", lambda e, blo=blo: e.tensor_copy(brow.t[0:1, blo:blo + 1024],
                                                     scf.t[0:1, 0:1024]),
             reads=scf.k(0, 1024), writes=brow.k(blo, 1024))
        S.op("dve", lambda e, blo=blo: e.tensor_copy(scf.t[0:1, 1024:2048],
                                                     brow.t[0:1, blo:blo + 1024]),
             reads=brow.k(blo, 1024), writes=scf.k(1024, 1024))
        S.op("dve", lambda e: e.tensor_tensor(scf.t[0:1, 0:1024],
                                              scf.t[0:1, 0:1024],
                                              scf.t[0:1, 1024:2048], OP.subtract),
             reads=scf.k(0, 2048), writes=scf.k(0, 1024))
        S.op("dve", lambda e, blo=blo: e.tensor_copy(
            brow.t[0:1, blo + 1024:blo + 2048], scf.t[0:1, 0:1024]),
            reads=scf.k(0, 1024), writes=brow.k(blo + 1024, 1024))
    S.op("dve", lambda e: e.memset(ret_state.ap(0, 4096), 0.0),
         writes=ret_state.k())
    S.op("dve", lambda e: e.memset(vcs.ap(VTAILS, 192), 0.0),
         writes=vcs.k(VTAILS, 192))

    AQ, AK, AV, AG0, AGN = 0, 4096, 8192, 16384, 24576
    AL1G, AL1S, AL1C, AL1V = 0, 8192, 16384, 24576
    AH = 0

    def layer_norm(s_lo, tc):
        sb_ = 32 + (tc % 2) * 8
        sk = scf.k(s_lo, 1024)
        sap = scf.ap(s_lo, 1024)
        ST = lambda i: stat.t[:, sb_ + i:sb_ + i + 1]
        STA = lambda i: stat.ap(sb_ + i, 1)
        STK = stat.k(sb_, 8)
        S.op("dve", lambda e: e.reduce_sum(STA(0), sap, axis=AX.X),
             reads=sk, writes=STK)
        S.op("sc", lambda e: e.activation(scf.ap(1024, 1024), sap, AF.Square,
                                          accum_out=STA(1)),
             reads=sk, writes=scf.k(1024, 1024) + STK)
        S.op("dve", lambda e: e.tensor_scalar(STA(2), STA(0), cAP(CI1024), None,
                                              OP.mult),
             reads=STK + CSTK, writes=STK)
        S.op("dve", lambda e: e.tensor_tensor(STA(3), STA(2), STA(2), OP.mult),
             reads=STK, writes=STK)
        S.op("dve", lambda e: e.scalar_tensor_tensor(
            STA(4), STA(1), 1.0 / 1024.0, STA(3), OP.mult, OP.subtract),
            reads=STK, writes=STK)
        S.op("sc", lambda e: e.activation(STA(0), STA(4), AF.Ln, bias=cAP(CEPS),
                                          scale=1.0),
             reads=STK + CSTK, writes=STK)
        S.op("sc", lambda e: e.activation(STA(5), STA(0), AF.Exp,
                                          bias=0.0, scale=cAP(CN05)),
             reads=STK + CSTK, writes=STK)
        S.op("dve", lambda e: e.scalar_tensor_tensor(
            STA(6), STA(2), -1.0, STA(5), OP.mult, OP.mult),
            reads=STK, writes=STK)
        S.op("sc", lambda e: e.activation(
            x_tm.ap(tc * 1024, 1024), sap, AF.Identity,
            bias=ST(6), scale=ST(5)),
            reads=sk + STK, writes=x_tm.k(tc * 1024, 1024))

    def transpose_tm_to_fm(src, src_keys_fn, cast_gb=False):
        for tc in range(NTC):
            for dc in range(8):
                pap, pk = tp_f32(tc * 8 + dc)
                S.op("pe", lambda e, tc=tc, dc=dc, pap=pap: e.transpose(
                    pap, src.ap(tc * 1024 + dc * 128, 128), ident_f.ap(0, 128)),
                    reads=src_keys_fn(tc, dc) + ident_f.k(),
                    writes=[pk])
                dsta = fmbuf.ap(dc * 512 + tc * 128, 128)
                dkey = fmbuf.k(dc * 512 + tc * 128, 128)
                if cast_gb:
                    S.op("dve", lambda e, dsta=dsta, pap=pap, dc=dc:
                         e.tensor_scalar(dsta, pap,
                                         prm.t[:, PRM_GAM + dc:PRM_GAM + dc + 1],
                                         prm.t[:, PRM_BET + dc:PRM_BET + dc + 1],
                                         OP.mult, OP.add),
                         reads=[pk] + prm.k(PRM_GAM, 16), writes=dkey)
                else:
                    S.op("dve", lambda e, dsta=dsta, pap=pap: e.tensor_copy(
                        dsta, pap),
                        reads=[pk], writes=dkey)

    def emit_tile(seq, tt, tile_idx):
        t0 = tt * TT
        dbg = debug_taps and tile_idx == 0

        src = xin[seq, t0:t0 + TT, :].rearrange("(tc p) d -> p tc d", p=128)
        dmain(x_tm.ap(0, 4096).rearrange("p (tc d) -> p tc d", d=1024), src,
              writes=x_tm.k())

        if SUBPHASE < 0.7:
            dst = out_d[seq, t0:t0 + TT, :].rearrange("(tc p) d -> p tc d", p=128)
            S.dma("sync", lambda e, dst=dst: e.dma_start(
                out=dst, in_=x_tm.ap(0, 4096).rearrange("p (tc d) -> p tc d",
                                                        d=1024)),
                reads=x_tm.k(), writes=[f"outp:{tile_idx}"])
            return
        transpose_tm_to_fm(x_tm, lambda tc, dc: x_tm.k(tc * 1024 + dc * 128, 128))
        if SUBPHASE < 1:
            dst = out_d[seq, t0:t0 + TT, :].rearrange("(tc p) d -> p tc d", p=128)
            S.dma("sync", lambda e, dst=dst: e.dma_start(
                out=dst, in_=x_tm.ap(0, 4096).rearrange("p (tc d) -> p tc d",
                                                        d=1024)),
                reads=x_tm.k(), writes=[f"outp:{tile_idx}"])
            return

        for which, aoff, view, skey, ctab, stab in (
                ("q", AQ, vQ, "sQ", cq_in, sq_in),
                ("k", AK, vK, "sK", ck_in, sk_in)):
            base, wkeys = load_wblock(
                (sQ if which == "q" else sK).rearrange("(p x) -> p x", p=128),
                8192, skey)
            def rot_ops(oc, po, pkey, rtb, rlo, aoff=aoff):
                sf = 2048 * (oc % 2)
                ops = []
                ops.append(lambda: S.op("dve", lambda e: e.tensor_tensor(
                    scf.ap(sf + 512, 512), po, rtb.ap(rlo, 512), OP.mult),
                    reads=[pkey] + rtb.k(rlo, 512),
                    writes=scf.k(sf + 512, 512)))
                ops.append(lambda: S.op("dve", lambda e: e.stream_shuffle(
                    scf.ap(sf + 1024, 512), po, [i ^ 1 for i in range(32)]),
                    reads=[pkey], writes=scf.k(sf + 1024, 512)))
                ops.append(lambda: S.op("dve", lambda e: e.tensor_tensor(
                    scf.ap(sf + 1536, 512), scf.ap(sf + 1024, 512),
                    rtb.ap(rlo + 512, 512), OP.mult),
                    reads=scf.k(sf + 1024, 512) + rtb.k(rlo + 512, 512),
                    writes=scf.k(sf + 1536, 512)))
                ops.append(lambda: S.op("dve", lambda e: e.tensor_tensor(
                    arena.ap(aoff + oc * 512, 512), scf.ap(sf + 512, 512),
                    scf.ap(sf + 1536, 512), OP.add),
                    reads=scf.k(sf + 512, 512) + scf.k(sf + 1536, 512),
                    writes=arena.k(aoff + oc * 512, 512)))
                return ops

            import itertools as _it2
            for og in range(4):
                chains = []
                for oc in (2 * og, 2 * og + 1):
                    if oc % 2 == 0:
                        rtb, rlo = rotb, 0
                    else:
                        rtb, rlo = scf, 4096
                    dmain(rtb.ap(rlo, 512), ctab[tt, oc], writes=rtb.k(rlo, 512))
                    dmain(rtb.ap(rlo + 512, 512), stab[tt, oc],
                          writes=rtb.k(rlo + 512, 512))
                    pslot = oc % 3
                    po = psB[:, pslot * 512:pslot * 512 + 512]
                    for kc in range(8):
                        S.op("pe", lambda e, oc=oc, kc=kc, po=po, base=base:
                             e.matmul(
                                 po, wst.ap(base + oc * 1024 + kc * 128, 128),
                                 fmbuf.ap(kc * 512, 512),
                                 start=(kc == 0), stop=(kc == 7)),
                             reads=wkeys + fmbuf.k(kc * 512, 512),
                             writes=[f"psB:{pslot}"])
                    chains.append(rot_ops(oc, po, f"psB:{pslot}", rtb, rlo))
                for pair in _it2.zip_longest(*chains):
                    for fn in pair:
                        if fn is not None:
                            fn()
        if dbg:
            dmain(tap("q_fm", [128, 4096], BF16), arena.ap(AQ, 4096),
                  reads=arena.k(AQ, 4096))
            dmain(tap("k_fm", [128, 4096], BF16), arena.ap(AK, 4096),
                  reads=arena.k(AK, 4096))

        for which, aoff, view, skey, act in (("v", AV, vV, "sV", False),
                                             ("g0", AG0, vG, "sGw", True)):
            for ncd in range(4):
                base, wkeys = load_wblock(view[:, ncd].rearrange("p kc n -> p (kc n)"),
                                          4096, skey)
                for tc in range(NTC):
                    pslot = tc % 3
                    po = psB[:, pslot * 512:pslot * 512 + 512]
                    for kc in range(8):
                        S.op("pe", lambda e, tc=tc, kc=kc, po=po, base=base:
                             e.matmul(po, fmbuf.ap(kc * 512 + tc * 128, 128),
                                      wst.ap(base + kc * 512, 512),
                                      start=(kc == 0), stop=(kc == 7)),
                             reads=wkeys + fmbuf.k(kc * 512 + tc * 128, 128),
                             writes=[f"psB:{pslot}"])
                    dsta = arena.ap(aoff + tc * 2048 + ncd * 512, 512)
                    dkey = arena.k(aoff + tc * 2048 + ncd * 512, 512)
                    if act:
                        S.op("sc", lambda e, dsta=dsta, po=po: e.activation(
                            dsta, po, AF.Silu),
                            reads=[f"psB:{pslot}"], writes=dkey)
                    else:
                        S.op("sc", lambda e, dsta=dsta, po=po: e.activation(
                            dsta, po, AF.Identity),
                            reads=[f"psB:{pslot}"], writes=dkey)
        if dbg:
            dmain(tap("v_tm", [128, 8192], BF16), arena.ap(AV, 8192),
                  reads=arena.k(AV, 8192))

        if SUBPHASE < 3:
            dst = out_d[seq, t0:t0 + TT, :].rearrange("(tc p) d -> p tc d", p=128)
            S.dma("sync", lambda e, dst=dst: e.dma_start(
                out=dst, in_=x_tm.ap(0, 4096).rearrange("p (tc d) -> p tc d",
                                                        d=1024)),
                reads=x_tm.k(), writes=[f"outp:{tile_idx}"])
            return
        SC_AP = [psA[:, 0:256], psA[:, 512:768]]
        SC_K = ["psA:0", "psA:1"]
        O_AP = [psA[:, 1024:1280], psT[:, 0:256]]
        O_K = ["psA:2", "psT"]
        X_AP = psB[:, 0:256]
        X_K = "psB:0"
        U_AP = psB[:, 512:768]
        U_K = "psB:1"
        KT_AP = [psTb[:, 0:128], psB[:, 1024:1088].bitcast(BF16)]
        KT_K = ["psTb", "psB:2"]
        for c in range(2):
            for h in range(H):
                kq = arena.k(AQ + h * 512 + c * 256, 256)
                kk = arena.k(AK + h * 512 + c * 256, 256)
                S.op("dve", lambda e, h=h: e.tensor_copy(
                    ret_sm.ap(768, 256), ret_state.ap(h * 256, 256)),
                    reads=ret_state.k(h * 256, 256), writes=ret_sm.k(768, 256))
                for tkc in range(2):
                    S.op("pe", lambda e, c=c, h=h, tkc=tkc: e.matmul(
                        SC_AP[tkc],
                        arena.ap(AK + h * 512 + c * 256 + tkc * 128, 128),
                        arena.ap(AQ + h * 512 + c * 256, 256),
                        start=True, stop=True),
                        reads=kq + kk, writes=[SC_K[tkc]])
                    S.op("dve", lambda e, h=h, tkc=tkc: e.tensor_tensor(
                        ret_sm.ap(tkc * 256, 256),
                        SC_AP[tkc],
                        masks.ap(h * 512 + tkc * 256, 256), OP.mult),
                        reads=[SC_K[tkc]] + masks.k(h * 512 + tkc * 256, 256),
                        writes=ret_sm.k(tkc * 256, 256))
                    S.op("pe", lambda e, c=c, h=h, tkc=tkc: e.transpose(
                        KT_AP[tkc],
                        arena.ap(AK + h * 512 + c * 256 + tkc * 128, 128),
                        ident_b.ap(0, 128)),
                        reads=kk + ident_b.k(), writes=[KT_K[tkc]])
                    S.op("dve", lambda e, tkc=tkc: e.tensor_copy(
                        ret_sm.ap(512 + tkc * 128, 128),
                        KT_AP[tkc]),
                        reads=[KT_K[tkc]], writes=ret_sm.k(512 + tkc * 128, 128))
                    tc = c * 2 + tkc
                    S.op("dve", lambda e, tc=tc, h=h, tkc=tkc: e.tensor_scalar(
                        ret_sm.ap(1024 + tkc * 256, 256),
                        arena.ap(AV + tc * 2048 + h * 256, 256),
                        prm.t[:, PRM_WK + 2 * h + tkc:PRM_WK + 2 * h + tkc + 1],
                        None, OP.mult),
                        reads=arena.k(AV + tc * 2048 + h * 256, 256) +
                        prm.k(PRM_WK + 2 * h, 2),
                        writes=ret_sm.k(1024 + tkc * 256, 256))
                for tqc in range(2):
                    for tkc in range(tqc + 1):
                        S.op("pe", lambda e, c=c, h=h, tqc=tqc, tkc=tkc: e.matmul(
                            O_AP[tqc],
                            ret_sm.ap(tkc * 256 + tqc * 128, 128),
                            arena.ap(AV + (c * 2 + tkc) * 2048 + h * 256, 256),
                            start=(tkc == 0), stop=(tkc == tqc)),
                            reads=ret_sm.k(tkc * 256, 256) +
                            arena.k(AV + (c * 2 + tkc) * 2048 + h * 256, 256),
                            writes=[O_K[tqc]])
                    S.op("pe", lambda e, c=c, h=h, tqc=tqc: e.matmul(
                        X_AP,
                        arena.ap(AQ + h * 512 + c * 256 + tqc * 128, 128),
                        ret_sm.ap(768, 256), start=True, stop=True),
                        reads=kq + ret_sm.k(768, 256), writes=[X_K])
                    xs = (h % 2) * 256
                    S.op("sc", lambda e, h=h, tqc=tqc, xs=xs: e.activation(
                        scf.ap(xs, 256), X_AP, AF.Identity,
                        scale=prm.t[:, PRM_WR + 2 * h + tqc:PRM_WR + 2 * h + tqc + 1]),
                        reads=[X_K] + prm.k(PRM_WR + 2 * h, 2) + CSTK,
                        writes=scf.k(xs, 256))
                    olo = 3584 + tqc * 2048 + h * 256
                    S.op("dve", lambda e, olo=olo, tqc=tqc, xs=xs: e.tensor_tensor(
                        scf.ap(olo, 256), O_AP[tqc],
                        scf.ap(xs, 256), OP.add),
                        reads=[O_K[tqc]] + scf.k(xs, 256),
                        writes=scf.k(olo, 256))
                for tkc in range(2):
                    S.op("pe", lambda e, tkc=tkc: e.matmul(
                        U_AP, ret_sm.ap(512 + tkc * 128, 128),
                        ret_sm.ap(1024 + tkc * 256, 256),
                        start=(tkc == 0), stop=(tkc == 1)),
                        reads=ret_sm.k(512 + tkc * 128, 128) +
                        ret_sm.k(1024 + tkc * 256, 256),
                        writes=[U_K])
                S.op("dve", lambda e, h=h: e.tensor_scalar(
                    scf.ap(256, 256), ret_state.ap(h * 256, 256),
                    prm.t[:, PRM_D256 + h:PRM_D256 + h + 1], None, OP.mult),
                    reads=ret_state.k(h * 256, 256) + prm.k(PRM_D256, 8),
                    writes=scf.k(256, 256))
                S.op("dve", lambda e, h=h: e.tensor_tensor(
                    ret_state.ap(h * 256, 256), scf.ap(256, 256),
                    U_AP, OP.add),
                    reads=scf.k(256, 256) + [U_K],
                    writes=ret_state.k(h * 256, 256))

            if dbg:
                dmain(tap(f"obuf{c}", [128, 4096]), scf.ap(3584, 4096),
                      reads=scf.k(3584, 4096))
            for half in range(2):
                tc = c * 2 + half
                olo = 3584 + half * 2048
                okeys = scf.k(olo, 2048)
                S.op("dve", lambda e, olo=olo: e.reduce_sum(
                    stat.ap(8, 8).unsqueeze(2),
                    scf.ap(olo, 2048).rearrange("p (h v) -> p h v", h=8),
                    axis=AX.X),
                    reads=okeys, writes=stat.k(8, 8))
                S.op("dve", lambda e, olo=olo: e.tensor_tensor(
                    scf.ap(1024, 2048), scf.ap(olo, 2048), scf.ap(olo, 2048),
                    OP.mult),
                    reads=okeys, writes=scf.k(1024, 2048))
                S.op("dve", lambda e: e.reduce_sum(
                    stat.ap(16, 8).unsqueeze(2),
                    scf.ap(1024, 2048).rearrange("p (h v) -> p h v", h=8),
                    axis=AX.X),
                    reads=scf.k(1024, 2048), writes=stat.k(16, 8))
                S.op("dve", lambda e: e.tensor_scalar(stat.ap(8, 8), stat.ap(8, 8),
                                                      1.0 / 256.0, None, OP.mult),
                     reads=stat.k(8, 8), writes=stat.k(8, 8))
                S.op("dve", lambda e: e.tensor_tensor(stat.ap(24, 8), stat.ap(8, 8),
                                                      stat.ap(8, 8), OP.mult),
                     reads=stat.k(8, 8), writes=stat.k(24, 8))
                S.op("dve", lambda e: e.scalar_tensor_tensor(
                    stat.ap(0, 8), stat.ap(16, 8), 1.0 / 256.0, stat.ap(24, 8),
                    OP.mult, OP.subtract),
                    reads=stat.k(16, 8) + stat.k(24, 8), writes=stat.k(0, 8))
                S.op("dve", lambda e: e.tensor_scalar(stat.ap(24, 8), stat.ap(0, 8),
                                                      1e-5, None, OP.add),
                     reads=stat.k(0, 8), writes=stat.k(24, 8))
                S.op("sc", lambda e: e.activation(stat.ap(0, 8), stat.ap(24, 8),
                                                  AF.Ln),
                     reads=stat.k(24, 8), writes=stat.k(0, 8))
                S.op("sc", lambda e: e.activation(stat.ap(16, 8), stat.ap(0, 8),
                                                  AF.Exp, bias=0.0, scale=cAP(CN05)),
                     reads=stat.k(0, 8) + CSTK, writes=stat.k(16, 8))
                S.op("dve", lambda e: e.scalar_tensor_tensor(
                    stat.ap(24, 8), stat.ap(8, 8), -1.0, stat.ap(16, 8),
                    OP.mult, OP.mult),
                    reads=stat.k(8, 16), writes=stat.k(24, 8))
                for h in range(H):
                    nb = scf.ap((h % 2) * 128, 128).bitcast(BF16)
                    nbk = scf.k((h % 2) * 128, 128)
                    S.op("sc", lambda e, olo=olo, h=h, nb=nb: e.activation(
                        nb, scf.ap(olo + h * 256, 256), AF.Identity,
                        bias=stat.t[:, 24 + h:25 + h],
                        scale=stat.t[:, 16 + h:17 + h]),
                        reads=scf.k(olo + h * 256, 256) + stat.k(16, 8) +
                        stat.k(24, 8),
                        writes=nbk)
                    S.op("dve", lambda e, tc=tc, h=h, nb=nb: e.tensor_tensor(
                        wst.ap(h * 256, 256), nb,
                        arena.ap(AG0 + tc * 2048 + h * 256, 256), OP.mult),
                        reads=nbk + arena.k(AG0 + tc * 2048 + h * 256, 256),
                        writes=wst.k(0, 2048))
                if dbg and tc == 0:
                    dmain(tap("gnt", [128, 2048], BF16), wst.ap(0, 2048),
                          reads=wst.k(0, 2048))
                    dmain(tap("gstat", [128, 32]), stat.ap(0, 32),
                          reads=stat.k(0, 32))
                for vc in range(16):
                    pap, pk = tp_bf(vc)
                    S.op("pe", lambda e, vc=vc, pap=pap: e.transpose(
                        pap, wst.ap(vc * 128, 128), ident_b.ap(0, 128)),
                        reads=wst.k(0, 2048) + ident_b.k(),
                        writes=[pk])
                    S.op("dve", lambda e, vc=vc, tc=tc, pap=pap: e.tensor_copy(
                        arena.ap(AGN + vc * 512 + tc * 128, 128), pap),
                        reads=[pk],
                        writes=arena.k(AGN + vc * 512 + tc * 128, 128))
        if dbg:
            dmain(tap("gn_fm", [128, 8192], BF16), arena.ap(AGN, 8192),
                  reads=arena.k(AGN, 8192))
            dmain(tap("masks", [128, 4096], BF16), masks.ap(0, 4096),
                  reads=masks.k(0, 4096))

        if SUBPHASE < 4:
            dst = out_d[seq, t0:t0 + TT, :].rearrange("(tc p) d -> p tc d", p=128)
            S.dma("sync", lambda e, dst=dst: e.dma_start(
                out=dst, in_=x_tm.ap(0, 4096).rearrange("p (tc d) -> p tc d",
                                                        d=1024)),
                reads=x_tm.k(), writes=[f"outp:{tile_idx}"])
            return
        prb0, prk0 = load_wblock(vP[:, 0].rearrange("p kc n -> p (kc n)"),
                                 8192, "sP")
        prb1, prk1 = load_wblock(vP[:, 1].rearrange("p kc n -> p (kc n)"),
                                 8192, "sP")
        for tc in range(NTC):
            for ncd in range(2):
                base = prb0 if ncd == 0 else prb1
                wkeys = prk0 if ncd == 0 else prk1
                pslot = (tc * 2 + ncd) % 3
                po = psB[:, pslot * 512:pslot * 512 + 512]
                for kc in range(16):
                    S.op("pe", lambda e, tc=tc, kc=kc, po=po, base=base: e.matmul(
                        po, arena.ap(AGN + kc * 512 + tc * 128, 128),
                        wst.ap(base + kc * 512, 512),
                        start=(kc == 0), stop=(kc == 15)),
                        reads=arena.k(AGN + kc * 512 + tc * 128, 128) + wkeys,
                        writes=[f"psB:{pslot}"])
                sb0 = (tc % 2) * 2048
                S.op("dve", lambda e, tc=tc, ncd=ncd, po=po, pslot=pslot, sb0=sb0:
                     e.tensor_tensor(
                         scf.ap(sb0 + ncd * 512, 512), po,
                         x_tm.ap(tc * 1024 + ncd * 512, 512), OP.add),
                     reads=[f"psB:{pslot}"] + x_tm.k(tc * 1024 + ncd * 512, 512),
                     writes=scf.k(sb0 + ncd * 512, 512))
            layer_norm((tc % 2) * 2048, tc)
        if dbg:
            dmain(tap("x1", [128, 4096]), x_tm.ap(0, 4096), reads=x_tm.k())

        if PHASE_LIMIT < 2:
            dst = out_d[seq, t0:t0 + TT, :].rearrange("(tc p) d -> p tc d", p=128)
            S.dma("sync", lambda e, dst=dst: e.dma_start(
                out=dst, in_=x_tm.ap(0, 4096).rearrange("p (tc d) -> p tc d",
                                                        d=1024)),
                reads=x_tm.k(), writes=[f"outp:{tile_idx}"])
            return
        tbase0, tkeys0 = load_wblock(vTn[:, 0].rearrange("p kc n -> p (kc n)"),
                                     4096, "sTn")
        tbase1, tkeys1 = load_wblock(vTn[:, 1].rearrange("p kc n -> p (kc n)"),
                                     4096, "sTn")
        for tc in range(NTC):
            xk = x_tm.k(tc * 1024, 1024)
            xs = x_tm.ap(tc * 1024, 1024)
            S.op("dve", lambda e, xs=xs: e.reduce_max(
                stat.ap(0, 1), xs, axis=AX.X, apply_absolute_value=True),
                reads=xk, writes=stat.k(0, 8))
            S.op("dve", lambda e: e.tensor_single_scalar(stat.ap(3, 1),
                                                         stat.ap(0, 1), 1e-5,
                                                         OP.max),
                 reads=stat.k(0, 8), writes=stat.k(0, 8))
            S.op("dve", lambda e: e.reciprocal(stat.ap(1, 1), stat.ap(3, 1)),
                 reads=stat.k(0, 8), writes=stat.k(0, 8))
            S.op("dve", lambda e: e.tensor_scalar(stat.ap(4, 1), stat.ap(1, 1),
                                                  127.0, None, OP.mult),
                 reads=stat.k(0, 8), writes=stat.k(0, 8))
            S.op("dve", lambda e: e.tensor_scalar(
                stat.ap(2, 1), stat.ap(3, 1), prm.t[:, PRM_WS:PRM_WS + 1],
                1.0 / 127.0, OP.mult, OP.mult),
                reads=stat.k(0, 8) + prm.k(PRM_WS, 1), writes=stat.k(0, 8))
            S.op("dve", lambda e, xs=xs: e.tensor_scalar(
                scf.ap(0, 1024), xs, stat.t[:, 4:5], 0.5, OP.mult, OP.add),
                reads=xk + stat.k(0, 8), writes=scf.k(0, 1024))
            floor_fixup(scf.ap(0, 1024), scf.k(0, 1024),
                        sci.ap(0, 1024), sci.k(0, 1024),
                        sci.ap(1024, 1024), sci.k(1024, 1024),
                        sci.ap(0, 1024), sci.k(0, 1024))
            S.op("dve", lambda e: e.tensor_copy(arena.ap(AQ, 1024), sci.ap(0, 1024)),
                 reads=sci.k(0, 1024), writes=arena.k(AQ, 1024))
            for dc in range(8):
                pap, pk = tp_bf(dc)
                S.op("pe", lambda e, dc=dc, pap=pap: e.transpose(
                    pap, arena.ap(AQ + dc * 128, 128), ident_b.ap(0, 128)),
                    reads=arena.k(AQ, 1024) + ident_b.k(),
                    writes=[pk])
                S.op("dve", lambda e, dc=dc, tc=tc, pap=pap: e.tensor_copy(
                    fmbuf.ap(dc * 512 + tc * 128, 128), pap),
                    reads=[pk],
                    writes=fmbuf.k(dc * 512 + tc * 128, 128))
            for ncd in range(2):
                base = tbase0 if ncd == 0 else tbase1
                wkeys = tkeys0 if ncd == 0 else tkeys1
                pslot = ncd % 3
                po = psB[:, pslot * 512:pslot * 512 + 512]
                for kc in range(8):
                    S.op("pe", lambda e, kc=kc, tc=tc, po=po, base=base: e.matmul(
                        po, fmbuf.ap(kc * 512 + tc * 128, 128),
                        wst.ap(base + kc * 512, 512),
                        start=(kc == 0), stop=(kc == 7)),
                        reads=fmbuf.k(kc * 512 + tc * 128, 128) + wkeys,
                        writes=[f"psB:{pslot}"])
                sb0 = (tc % 2) * 2048
                S.op("dve", lambda e, ncd=ncd, po=po: e.scalar_tensor_tensor(
                    scf.ap(3072 + ncd * 512, 512), po, stat.t[:, 2:3],
                    brep.ap(ncd * 512, 512), OP.mult, OP.add),
                    reads=[f"psB:{pslot}"] + stat.k(0, 8) + brep.k(ncd * 512, 512),
                    writes=scf.k(3072 + ncd * 512, 512))
                S.op("dve", lambda e, tc=tc, ncd=ncd, sb0=sb0: e.tensor_tensor(
                    scf.ap(sb0 + ncd * 512, 512), scf.ap(3072 + ncd * 512, 512),
                    x_tm.ap(tc * 1024 + ncd * 512, 512), OP.add),
                    reads=scf.k(3072 + ncd * 512, 512) +
                    x_tm.k(tc * 1024 + ncd * 512, 512),
                    writes=scf.k(sb0 + ncd * 512, 512))
            layer_norm((tc % 2) * 2048, tc)
        if dbg:
            dmain(tap("x2", [128, 4096]), x_tm.ap(0, 4096), reads=x_tm.k())

        if PHASE_LIMIT < 3:
            dst = out_d[seq, t0:t0 + TT, :].rearrange("(tc p) d -> p tc d", p=128)
            S.dma("sync", lambda e, dst=dst: e.dma_start(
                out=dst, in_=x_tm.ap(0, 4096).rearrange("p (tc d) -> p tc d",
                                                        d=1024)),
                reads=x_tm.k(), writes=[f"outp:{tile_idx}"])
            return
        for tc in range(NTC):
            xk = x_tm.k(tc * 1024, 1024)
            xs = x_tm.ap(tc * 1024, 1024)
            S.op("dve", lambda e, xs=xs: e.reduce_sum(stat.ap(0, 1), xs, axis=AX.X),
                 reads=xk, writes=stat.k(0, 8))
            S.op("sc", lambda e, xs=xs: e.activation(scf.ap(1024, 1024), xs,
                                                     AF.Square,
                                                     accum_out=stat.ap(1, 1)),
                 reads=xk, writes=scf.k(1024, 1024) + stat.k(0, 8))
            S.op("dve", lambda e: e.tensor_scalar(stat.ap(2, 1), stat.ap(0, 1),
                                                  cAP(CI1024), None, OP.mult),
                 reads=stat.k(0, 8) + CSTK, writes=stat.k(0, 8))
            S.op("dve", lambda e: e.tensor_tensor(stat.ap(3, 1), stat.ap(2, 1),
                                                  stat.ap(2, 1), OP.mult),
                 reads=stat.k(0, 8), writes=stat.k(0, 8))
            S.op("dve", lambda e: e.scalar_tensor_tensor(
                stat.ap(4, 1), stat.ap(1, 1), 1.0 / 1024.0, stat.ap(3, 1),
                OP.mult, OP.subtract),
                reads=stat.k(0, 8), writes=stat.k(0, 8))
            S.op("dve", lambda e: e.tensor_scalar(stat.ap(0, 1), stat.ap(4, 1),
                                                  1e-5, None, OP.add),
                 reads=stat.k(0, 8), writes=stat.k(0, 8))
            S.op("sc", lambda e: e.activation(stat.ap(4, 1), stat.ap(0, 1), AF.Ln),
                 reads=stat.k(0, 8), writes=stat.k(0, 8))
            S.op("sc", lambda e: e.activation(stat.ap(5, 1), stat.ap(4, 1), AF.Exp,
                                              bias=0.0, scale=cAP(CN05)),
                 reads=stat.k(0, 8) + CSTK, writes=stat.k(0, 8))
            S.op("dve", lambda e, xs=xs: e.tensor_scalar(
                scf.ap(3584, 1024), xs, stat.t[:, 2:3], stat.t[:, 5:6],
                OP.subtract, OP.mult),
                reads=xk + stat.k(0, 8), writes=scf.k(3584, 1024))
            for dc in range(8):
                pap, pk = tp_f32(dc)
                S.op("pe", lambda e, dc=dc, pap=pap: e.transpose(
                    pap, scf.ap(3584 + dc * 128, 128), ident_f.ap(0, 128)),
                    reads=scf.k(3584 + dc * 128, 128) + ident_f.k(),
                    writes=[pk])
                S.op("dve", lambda e, dc=dc, tc=tc, pap=pap: e.tensor_scalar(
                    fmbuf.ap(dc * 512 + tc * 128, 128), pap,
                    prm.t[:, PRM_GAM + dc:PRM_GAM + dc + 1],
                    prm.t[:, PRM_BET + dc:PRM_BET + dc + 1], OP.mult, OP.add),
                    reads=[pk] + prm.k(PRM_GAM, 16),
                    writes=fmbuf.k(dc * 512 + tc * 128, 128))
        if dbg:
            dmain(tap("xn_fm", [128, 4096], BF16), fmbuf.ap(0, 4096),
                  reads=fmbuf.k())

        ipw = {}
        def inproj_mm(oc, pslot):
            g = oc // 4
            if g not in ipw:
                ipw[g] = load_wblock(
                    vI[:, g * 4:(g + 1) * 4].rearrange("p oc kc m -> p (oc kc m)"),
                    4096, "sI")
            base, wkeys = ipw[g]
            base = base + (oc % 4) * 1024
            po = psB[:, pslot * 512:pslot * 512 + 512]
            for kc in range(8):
                S.op("pe", lambda e, kc=kc, po=po, base=base: e.matmul(
                    po, wst.ap(base + kc * 128, 128), fmbuf.ap(kc * 512, 512),
                    start=(kc == 0), stop=(kc == 7)),
                    reads=wkeys + fmbuf.k(kc * 512, 512), writes=[f"psB:{pslot}"])
            return po

        def conv_ops(cc, po, pkey):
            par = cc % 2
            oc = 16 + cc
            tail_lo = VTAILS + (seq * 16 + cc) * 6
            if par == 0:
                wlo, alo, blo, slo = 0, 512, 0, 2048
                wbase = vcs
            else:
                wlo, alo, blo, slo = 4608, 3072, 3584, 5632
                wbase = scf
            def vwin(off, n):
                return wbase.ap(wlo + off, n), wbase.k(wlo + off, n)
            ca, cak = scf.ap(alo, 512), scf.k(alo, 512)
            cb, cbk = scf.ap(blo, 512), scf.k(blo, 512)
            sc_, sck = scf.ap(slo, 512), scf.k(slo, 512)
            dwb_b = prm.t[:, PRM_DWB + cc:PRM_DWB + cc + 1].broadcast_to([128, 512])
            ops = []
            ops.append(lambda: S.op("dve", lambda e: e.tensor_copy(
                vwin(0, 6)[0], vcs.ap(tail_lo, 6)),
                reads=vcs.k(tail_lo, 6), writes=vwin(0, 6)[1]))
            ops.append(lambda: S.op("sc", lambda e: e.activation(
                vwin(6, 512)[0], po, AF.Identity,
                bias=prm.t[:, PRM_INB + oc:PRM_INB + oc + 1]),
                reads=[pkey] + prm.k(PRM_INB, 32) + CSTK,
                writes=vwin(6, 512)[1]))
            ops.append(lambda: S.op("dve", lambda e: e.scalar_tensor_tensor(
                ca, vwin(0, 512)[0], dwwt.t[:, cc * 7:cc * 7 + 1], dwb_b,
                OP.mult, OP.add),
                reads=vwin(0, 518)[1] + dwwt.k() + prm.k(PRM_DWB, 16),
                writes=cak))
            bufs = [(ca, cak), (cb, cbk)]
            for tp_ in range(1, 7):
                dsta, dkey = bufs[tp_ % 2]
                prev, pvkey = bufs[(tp_ + 1) % 2]
                ops.append(lambda tp_=tp_, dsta=dsta, dkey=dkey, prev=prev,
                           pvkey=pvkey: S.op(
                    "dve", lambda e: e.scalar_tensor_tensor(
                        dsta, vwin(tp_, 512)[0],
                        dwwt.t[:, cc * 7 + tp_:cc * 7 + tp_ + 1], prev,
                        OP.mult, OP.add),
                    reads=vwin(0, 518)[1] + dwwt.k() + pvkey, writes=dkey))
            ops.append(lambda: S.op("dve", lambda e: e.tensor_copy(
                arena.ap(AL1C + cc * 512, 512), ca),
                reads=cak, writes=arena.k(AL1C + cc * 512, 512)))
            ops.append(lambda: S.op("dve", lambda e: e.tensor_tensor_scan(
                sc_, prm.t[:, PRM_A + cc:PRM_A + cc + 1].broadcast_to([128, 512]),
                vwin(6, 512)[0],
                prm.t[:, PRM_CARRY + cc * 2 + seq:PRM_CARRY + cc * 2 + seq + 1],
                OP.mult, OP.add),
                reads=vwin(6, 512)[1] + prm.k(PRM_A, 16) +
                prm.k(PRM_CARRY + cc * 2 + seq, 1),
                writes=sck))
            ops.append(lambda: S.op("dve", lambda e: e.tensor_copy(
                prm.ap(PRM_CARRY + cc * 2 + seq, 1), scf.ap(slo + 511, 1)),
                reads=sck, writes=prm.k(PRM_CARRY + cc * 2 + seq, 1)))
            ops.append(lambda: S.op("dve", lambda e: e.tensor_copy(
                arena.ap(AL1S + cc * 512, 512), sc_),
                reads=sck, writes=arena.k(AL1S + cc * 512, 512)))
            ops.append(lambda: S.op("dve", lambda e: e.tensor_copy(
                vcs.ap(tail_lo, 6), vwin(512, 6)[0]),
                reads=vwin(0, 518)[1], writes=vcs.k(tail_lo, 6)))
            return ops

        import itertools as _it
        for gcc in range(8):
            chains = []
            for cc in (2 * gcc, 2 * gcc + 1):
                pslot = cc % 3
                po = inproj_mm(16 + cc, pslot)
                chains.append(conv_ops(cc, po, f"psB:{pslot}"))
            for pair in _it.zip_longest(*chains):
                for fn in pair:
                    if fn is not None:
                        fn()
        for cc in range(16):
            pslot = cc % 3
            po = inproj_mm(cc, pslot)
            S.op("sc", lambda e, po=po, cc=cc: e.activation(
                arena.ap(AL1G + cc * 512, 512), po, AF.Sigmoid,
                bias=prm.t[:, PRM_INB + cc:PRM_INB + cc + 1], scale=1.0),
                reads=[f"psB:{pslot}"] + prm.k(PRM_INB, 32),
                writes=arena.k(AL1G + cc * 512, 512))
        if dbg:
            dmain(tap("l1s", [128, 8192], BF16), arena.ap(AL1S, 8192),
                  reads=arena.k(AL1S, 8192))
            dmain(tap("l1c", [128, 8192], BF16), arena.ap(AL1C, 8192),
                  reads=arena.k(AL1C, 8192))
            dmain(tap("l1g", [128, 8192], BF16), arena.ap(AL1G, 8192),
                  reads=arena.k(AL1G, 8192))

        for oc in range(16):
            if oc % 2 == 0:
                mxgrp = load_wblock(
                    vM[:, oc:oc + 2].rearrange("p oc kc m -> p (oc kc m)"),
                    4096, "sM")
            base, wkeys = mxgrp
            base = base + (oc % 2) * 2048
            pslot = oc % 3
            po = psB[:, pslot * 512:pslot * 512 + 512]
            for kc in range(16):
                S.op("pe", lambda e, oc=oc, kc=kc, po=po, base=base: e.matmul(
                    po, wst.ap(base + kc * 128, 128),
                    arena.ap(AL1S + kc * 512, 512),
                    start=(kc == 0), stop=(kc == 15)),
                    reads=wkeys + arena.k(AL1S + kc * 512, 512),
                    writes=[f"psB:{pslot}"])
            S.op("dve", lambda e, oc=oc, po=po: e.scalar_tensor_tensor(
                scf.ap(0, 512), po, prm.t[:, PRM_MIXB + oc:PRM_MIXB + oc + 1],
                arena.ap(AL1C + oc * 512, 512), OP.add, OP.add),
                reads=[f"psB:{pslot}"] + prm.k(PRM_MIXB, 16) +
                arena.k(AL1C + oc * 512, 512),
                writes=scf.k(0, 512))
            S.op("dve", lambda e, oc=oc: e.tensor_tensor(
                arena.ap(AL1V + oc * 512, 512), scf.ap(0, 512),
                arena.ap(AL1G + oc * 512, 512), OP.mult),
                reads=scf.k(0, 512) + arena.k(AL1G + oc * 512, 512),
                writes=arena.k(AL1V + oc * 512, 512))
        obase0, okeys0 = load_wblock(vO[:, 0].rearrange("p kc n -> p (kc n)"),
                                     8192, "sOw")
        obase1, okeys1 = load_wblock(vO[:, 1].rearrange("p kc n -> p (kc n)"),
                                     8192, "sOw")
        for tc in range(NTC):
            for ncd in range(2):
                base = obase0 if ncd == 0 else obase1
                wkeys = okeys0 if ncd == 0 else okeys1
                pslot = (tc * 2 + ncd) % 3
                po = psB[:, pslot * 512:pslot * 512 + 512]
                for kc in range(16):
                    S.op("pe", lambda e, kc=kc, tc=tc, po=po, base=base: e.matmul(
                        po, arena.ap(AL1V + kc * 512 + tc * 128, 128),
                        wst.ap(base + kc * 512, 512),
                        start=(kc == 0), stop=False),
                        reads=arena.k(AL1V + kc * 512 + tc * 128, 128) + wkeys,
                        writes=[f"psB:{pslot}"])
                for bb in range(2):
                    S.op("pe", lambda e, po=po, ncd=ncd, bb=bb: e.matmul(
                        po, ones_bf.t[0:1, :],
                        brow.t[0:1, bb * 1024 + ncd * 512:
                               bb * 1024 + (ncd + 1) * 512],
                        start=False, stop=(bb == 1)),
                        reads=ones_bf.k() + brow.k(bb * 1024 + ncd * 512, 512),
                        writes=[f"psB:{pslot}"])
                sb0 = (tc % 2) * 2048
                S.op("dve", lambda e, tc=tc, ncd=ncd, po=po, sb0=sb0:
                     e.scalar_tensor_tensor(
                         scf.ap(sb0 + ncd * 512, 512),
                         x_tm.ap(tc * 1024 + ncd * 512, 512), 2.0, po,
                         OP.mult, OP.add),
                     reads=x_tm.k(tc * 1024 + ncd * 512, 512) + [f"psB:{pslot}"],
                     writes=scf.k(sb0 + ncd * 512, 512))
            layer_norm((tc % 2) * 2048, tc)
        if dbg:
            dmain(tap("x3", [128, 4096]), x_tm.ap(0, 4096), reads=x_tm.k())

        if PHASE_LIMIT < 4:
            dst = out_d[seq, t0:t0 + TT, :].rearrange("(tc p) d -> p tc d", p=128)
            S.dma("sync", lambda e, dst=dst: e.dma_start(
                out=dst, in_=x_tm.ap(0, 4096).rearrange("p (tc d) -> p tc d",
                                                        d=1024)),
                reads=x_tm.k(), writes=[f"outp:{tile_idx}"])
            return
        transpose_tm_to_fm(x_tm, lambda tc, dc: x_tm.k(tc * 1024 + dc * 128, 128))
        def pwl_ops(oc, po, pkey):
            fb = (oc % 3) * 2560
            ib = (oc % 3) * 512
            A = scf.ap(fb + 0, 512); Ak = scf.k(fb + 0, 512)
            B = scf.ap(fb + 512, 512); Bk = scf.k(fb + 512, 512)
            C = scf.ap(fb + 1024, 512); Ck = scf.k(fb + 1024, 512)
            FR = scf.ap(fb + 1536, 256).bitcast(BF16); FRk = scf.k(fb + 1536, 256)
            MM = scf.ap(fb + 1792, 256).bitcast(BF16); MMk = scf.k(fb + 1792, 256)
            G0 = scf.ap(fb + 2048, 256).bitcast(BF16); G0k = scf.k(fb + 2048, 256)
            G1 = scf.ap(fb + 2304, 256).bitcast(BF16); G1k = scf.k(fb + 2304, 256)
            Ci = sci.ap(ib + 0, 512); Cik = sci.k(ib + 0, 512)
            return [
                lambda: S.op("sc", lambda e: e.activation(
                    A, po, AF.Identity,
                    bias=prm.t[:, PRM_B1S + oc:PRM_B1S + oc + 1],
                    scale=cAP(C256)),
                    reads=[pkey] + prm.k(PRM_B1S, 32) + CSTK, writes=Ak),
                lambda: S.op("sc", lambda e: e.activation(Ci, A, AF.Identity),
                             reads=Ak, writes=Cik),
                lambda: S.op("dve", lambda e: e.tensor_tensor(MM, Ci, A,
                                                              OP.is_le),
                             reads=Cik + Ak, writes=MMk),
                lambda: S.op("dve", lambda e: e.scalar_tensor_tensor(
                    B, Ci, 1, MM, OP.subtract, OP.add),
                    reads=Cik + MMk, writes=Bk),
                lambda: S.op("sc", lambda e: e.activation(
                    C, A, AF.Identity, bias=0.0, scale=cAP(CI64)),
                    reads=Ak + CSTK, writes=Ck),
                lambda: S.op("sc", lambda e: e.activation(Ci, C, AF.Identity),
                             reads=Ck, writes=Cik),
                lambda: S.op("dve", lambda e: e.tensor_tensor(MM, Ci, C,
                                                              OP.is_le),
                             reads=Cik + Ck, writes=MMk),
                lambda: S.op("dve", lambda e: e.scalar_tensor_tensor(
                    A, Ci, 1, MM, OP.subtract, OP.add),
                    reads=Cik + MMk, writes=Ak),
                lambda: S.op("dve", lambda e: e.scalar_tensor_tensor(
                    FR, B, 1.0 / 64.0, A, OP.mult, OP.subtract),
                    reads=Bk + Ak, writes=FRk),
                lambda: S.op("sc", lambda e: e.activation(
                    G0, A, AF.Gelu, bias=cAP(C0), scale=cAP(C025)),
                    reads=Ak + CSTK, writes=G0k),
                lambda: S.op("sc", lambda e: e.activation(
                    G1, A, AF.Gelu, bias=cAP(C025), scale=cAP(C025)),
                    reads=Ak + CSTK, writes=G1k),
                lambda: S.op("dve", lambda e: e.tensor_tensor(MM, G1, G0,
                                                              OP.subtract),
                             reads=G1k + G0k, writes=MMk),
                lambda: S.op("dve", lambda e: e.tensor_tensor(G1, FR, MM,
                                                              OP.mult),
                             reads=FRk + MMk, writes=G1k),
                lambda: S.op("dve", lambda e: e.tensor_tensor(FR, G1, G0,
                                                              OP.add),
                             reads=G1k + G0k, writes=FRk),
                lambda: S.op("sc", lambda e: e.activation(
                    C, FR, AF.Identity, bias=cAP(C05), scale=cAP(C256)),
                    reads=FRk + CSTK, writes=Ck),
                lambda: S.op("sc", lambda e: e.activation(Ci, C, AF.Identity),
                             reads=Ck, writes=Cik),
                lambda: S.op("dve", lambda e: e.tensor_tensor(MM, Ci, C,
                                                              OP.is_le),
                             reads=Cik + Ck, writes=MMk),
                lambda: S.op("dve", lambda e: e.scalar_tensor_tensor(
                    A, Ci, 1, MM, OP.subtract, OP.add),
                    reads=Cik + MMk, writes=Ak),
                lambda: S.op("dve", lambda e: e.tensor_single_scalar(
                    Ci, B, 1024.0, OP.is_gt),
                    reads=Bk, writes=Cik),
                lambda: S.op("dve", lambda e: e.copy_predicated(A, Ci, B),
                             reads=Cik + Bk + Ak, writes=Ak),
                lambda: S.op("sc", lambda e: e.activation(
                    arena.ap(AH + oc * 512, 512), A, AF.Identity,
                    bias=0.0, scale=cAP(CI256)),
                    reads=Ak + CSTK, writes=arena.k(AH + oc * 512, 512)),
            ]

        w1grp = None
        import itertools as _it3
        gi = 0
        while gi < 32:
            grp = list(range(gi, min(gi + 3, 32)))
            gi += len(grp)
            chains = []
            for oc in grp:
                if oc % 4 == 0:
                    w1grp = load_wblock(
                        vW1[:, oc:oc + 4].rearrange("p oc kc m -> p (oc kc m)"),
                        4096, "sW1")
                base, wkeys = w1grp
                base = base + (oc % 4) * 1024
                pslot = oc % 3
                po = psB[:, pslot * 512:pslot * 512 + 512]
                pkey = f"psB:{pslot}"
                for kc in range(8):
                    S.op("pe", lambda e, kc=kc, po=po, base=base: e.matmul(
                        po, wst.ap(base + kc * 128, 128),
                        fmbuf.ap(kc * 512, 512),
                        start=(kc == 0), stop=(kc == 7)),
                        reads=wkeys + fmbuf.k(kc * 512, 512), writes=[pkey])
                chains.append(pwl_ops(oc, po, pkey))
            for stg in _it3.zip_longest(*chains):
                for fn in stg:
                    if fn is not None:
                        fn()
        if dbg:
            dmain(tap("hbf", [128, 16384], BF16), arena.ap(AH, 16384),
                  reads=arena.k(AH, 16384))
        for ncd in range(2):
            bA, kA = load_wblock(vW2[:, ncd, 0:16, :].rearrange("p kc n -> p (kc n)"),
                                 8192, "sW2")
            bB, kB = load_wblock(vW2[:, ncd, 16:32, :].rearrange("p kc n -> p (kc n)"),
                                 8192, "sW2")
            for tc in range(NTC):
                pslot = tc % 3
                po = psB[:, pslot * 512:pslot * 512 + 512]
                pkey = f"psB:{pslot}"
                for kc in range(32):
                    base = bA if kc < 16 else bB
                    wk_ = kA if kc < 16 else kB
                    off = base + (kc % 16) * 512
                    S.op("pe", lambda e, kc=kc, tc=tc, po=po, off=off: e.matmul(
                        po, arena.ap(AH + kc * 512 + tc * 128, 128),
                        wst.ap(off, 512),
                        start=(kc == 0), stop=False),
                        reads=arena.k(AH + kc * 512 + tc * 128, 128) + wk_,
                        writes=[pkey])
                for bb in range(2):
                    S.op("pe", lambda e, po=po, ncd=ncd, bb=bb: e.matmul(
                        po, ones_bf.t[0:1, :],
                        brow.t[0:1, 2048 + bb * 1024 + ncd * 512:
                               2048 + bb * 1024 + (ncd + 1) * 512],
                        start=False, stop=(bb == 1)),
                        reads=ones_bf.k() + brow.k(2048 + bb * 1024 + ncd * 512,
                                                   512),
                        writes=[pkey])
                S.op("dve", lambda e, po=po: e.tensor_scalar(
                    scf.ap(1024, 512), po, 256.0, 0.5, OP.mult, OP.add),
                    reads=[pkey], writes=scf.k(1024, 512))
                floor_fixup(scf.ap(1024, 512), scf.k(1024, 512),
                            sci.ap(0, 512), sci.k(0, 512),
                            sci.ap(512, 512), sci.k(512, 512),
                            sci.ap(0, 512), sci.k(0, 512))
                S.op("dve", lambda e, tc=tc, ncd=ncd: e.tensor_scalar(
                    scf.ap(3584 + tc * 1024 + ncd * 512, 512), sci.ap(0, 512),
                    cAP(CI256), None, OP.mult),
                    reads=sci.k(0, 512) + CSTK,
                    writes=scf.k(3584 + tc * 1024 + ncd * 512, 512))
        for tc in range(NTC):
            for ncd in range(2):
                elo = 3584 + tc * 1024 + ncd * 512
                sb0 = (tc % 2) * 2048
                S.op("dve", lambda e, elo=elo, tc=tc, ncd=ncd, sb0=sb0:
                     e.tensor_tensor(
                         scf.ap(sb0 + ncd * 512, 512), scf.ap(elo, 512),
                         x_tm.ap(tc * 1024 + ncd * 512, 512), OP.add),
                     reads=scf.k(elo, 512) + x_tm.k(tc * 1024 + ncd * 512, 512),
                     writes=scf.k(sb0 + ncd * 512, 512))
            layer_norm((tc % 2) * 2048, tc)
        if dbg:
            dmain(tap("x4", [128, 4096]), x_tm.ap(0, 4096), reads=x_tm.k())

        dst = out_d[seq, t0:t0 + TT, :].rearrange("(tc p) d -> p tc d", p=128)
        S.dma("sync", lambda e, dst=dst: e.dma_start(
            out=dst, in_=x_tm.ap(0, 4096).rearrange("p (tc d) -> p tc d", d=1024)),
            reads=x_tm.k(), writes=[f"outp:{tile_idx}"])

    n_seq_emit, n_tt_emit = EMIT_TILES or (SEQ_PER_CORE, TILES_PER_SEQ)
    tile_idx = 0
    for seq in range(n_seq_emit):
        if seq > 0:
            S.op("dve", lambda e: e.memset(ret_state.ap(0, 4096), 0.0),
                 reads=ret_state.k(), writes=ret_state.k())
        for tt in range(n_tt_emit):
            emit_tile(seq, tt, tile_idx)
            tile_idx += 1

    S.op("sync", None, reads=[f"outp:{i}" for i in range(tile_idx)])

    S.finalize(nc, es)
    with es:
        blk = es.enter_context(nc.Block())
        blk.sync(lambda eng: S.emit("sync", eng))
        blk.vector(lambda eng: S.emit("dve", eng))
        blk.scalar(lambda eng: S.emit("sc", eng))
        blk.tensor(lambda eng: S.emit("pe", eng))
        blk.gpsimd(lambda eng: S.emit("gp", eng))
    return nc


def _xpos_tables():
    d = np.arange(0, 1024, 2, dtype=np.float64)
    base_scale = (d + 0.4 * 1024.0) / (1.4 * 1024.0)
    pos = np.arange(T, dtype=np.float64)
    scale = base_scale[None, :] ** (pos[:, None] / 512.0)
    inv_freq = 1.0 / 10000.0 ** (np.arange(512, dtype=np.float64) / 512.0)
    sinu = pos[:, None] * inv_freq[None, :]
    sin, cos = np.sin(sinu), np.cos(sinu)
    def expand(tbl):
        out = np.empty((1024, T), np.float64)
        out[0::2] = tbl.T
        out[1::2] = tbl.T
        return out
    sgn = np.empty((1024, 1), np.float64)
    sgn[0::2] = -1.0
    sgn[1::2] = 1.0
    cq = expand(cos * scale)
    sq = expand(sin * scale) * sgn
    ck = expand(cos / scale)
    sk = expand(sin / scale) * sgn
    def tile(tbl):
        return np.ascontiguousarray(
            tbl.reshape(8, 128, TILES_PER_SEQ, TT).transpose(2, 0, 1, 3)
        ).astype(np.float32)
    return tile(cq), tile(sq), tile(ck), tile(sk)


_PROG_CACHE = {}


def kernel(L0_attn_hidden_states, L0_attn_W_qkv, L0_attn_W_g, L0_attn_W_proj,
           L0_attn_decay, L0_ffn_packed_weight, L0_ffn_weight_scale, L0_ffn_bias,
           L1_attn_norm_gamma, L1_attn_norm_beta, L1_attn_in_w, L1_attn_in_b,
           L1_attn_dw_w, L1_attn_dw_b, L1_attn_log_decay, L1_attn_mix_w,
           L1_attn_mix_b, L1_attn_out_w, L1_attn_out_b, L1_ffn_w1, L1_ffn_b1,
           L1_ffn_w2, L1_ffn_b2):
    x = np.ascontiguousarray(L0_attn_hidden_states, dtype=np.float32)
    cq, sq, ck, sk = _xpos_tables()
    consts = np.tile(np.array(CONSTS, np.float32)[None, :], (128, 1))
    idm = np.eye(128, dtype=np.float32)
    key = ("prog", DEBUG_TAPS, EMIT_TILES, PHASE_LIMIT, SUBPHASE)
    if key not in _PROG_CACHE:
        _PROG_CACHE[key] = build_program(DEBUG_TAPS)
    nc = _PROG_CACHE[key]
    common = {
        "wqkv": np.ascontiguousarray(L0_attn_W_qkv, np.float32),
        "wg": np.ascontiguousarray(L0_attn_W_g, np.float32),
        "wproj": np.ascontiguousarray(L0_attn_W_proj, np.float32),
        "decay": np.ascontiguousarray(L0_attn_decay, np.float32).reshape(1, H),
        "packed": np.ascontiguousarray(L0_ffn_packed_weight,
                                       np.int32).reshape(256, 1024),
        "wscale": np.ascontiguousarray(L0_ffn_weight_scale,
                                       np.float32).reshape(1, 1),
        "bias0": np.ascontiguousarray(L0_ffn_bias, np.float32).reshape(1, D),
        "gamma": np.ascontiguousarray(L1_attn_norm_gamma, np.float32),
        "beta": np.ascontiguousarray(L1_attn_norm_beta, np.float32),
        "in_w": np.ascontiguousarray(L1_attn_in_w, np.float32),
        "in_b": np.ascontiguousarray(L1_attn_in_b, np.float32),
        "dw_w": np.ascontiguousarray(L1_attn_dw_w, np.float32),
        "dw_b": np.ascontiguousarray(L1_attn_dw_b, np.float32),
        "log_decay": np.ascontiguousarray(L1_attn_log_decay, np.float32),
        "mix_w": np.ascontiguousarray(L1_attn_mix_w, np.float32),
        "mix_b": np.ascontiguousarray(L1_attn_mix_b, np.float32),
        "out_w": np.ascontiguousarray(L1_attn_out_w, np.float32),
        "out_b": np.ascontiguousarray(L1_attn_out_b, np.float32).reshape(1, D),
        "w1": np.ascontiguousarray(L1_ffn_w1, np.float32),
        "b1": np.ascontiguousarray(L1_ffn_b1, np.float32),
        "w2": np.ascontiguousarray(L1_ffn_w2, np.float32),
        "b2": np.ascontiguousarray(L1_ffn_b2, np.float32).reshape(1, D),
        "consts": consts, "cq_t": cq, "sq_t": sq, "ck_t": ck, "sk_t": sk,
        "idm": idm,
    }
    in_maps = []
    for c in range(N_CORES):
        m = dict(common)
        m["xin"] = np.ascontiguousarray(x[c * SEQ_PER_CORE:(c + 1) * SEQ_PER_CORE])
        in_maps.append(m)
    res = run_bass_kernel_spmd(nc, in_maps, core_ids=list(range(N_CORES)))
    if DEBUG_TAPS:
        kernel._last_results = res.results
    out = np.concatenate([r["out"] for r in res.results], axis=0)
    return out.reshape(B, T, D)


# version 11
# speedup vs baseline: 1.0618x; 1.0060x over previous
import numpy as np
from contextlib import ExitStack

import concourse.bass as bass
import concourse.mybir as mybir
from concourse.bass_utils import run_bass_kernel_spmd

F32 = mybir.dt.float32
BF16 = mybir.dt.bfloat16
I32 = mybir.dt.int32
AF = mybir.ActivationFunctionType
OP = mybir.AluOpType
AX = mybir.AxisListType

DEBUG_TAPS = False
PHASE_LIMIT = 4
SUBPHASE = 9
EMIT_TILES = None
N_CORES = 8
B, T, D = 16, 4096, 1024
SEQ_PER_CORE = 2
TT = 512
NTC = TT // 128
TILES_PER_SEQ = T // TT
H = 8
HC = 2048
FF = 4096
INV_SQRT_DK = float(1.0 / np.sqrt(128.0))

CONSTS = [256.0, 1.0 / 256.0, 1.0 / 64.0, 0.25, -64.0, -63.75, 0.5, 1e-5, 127.0,
          1.0 / 1024.0, 64.0, 2.0, 16.0, -0.5, 1.0, 0.0]
C256, CI256, CI64, C025, CN64, CN6375, C05, CEPS, C127, CI1024, C64, C2, C16, CN05, C1, C0 = range(16)


class Sched:
    ENGS = ("pe", "dve", "sc", "gp", "sync")

    def __init__(self):
        self.ops = {e: [] for e in self.ENGS}
        self.W = {}
        self.R = {}

    def _record(self, eng, fn, reads, writes, extra, is_dma):
        deps = []
        for b in reads:
            t = self.W.get(b)
            if t is not None:
                deps.append(t)
        for b in writes:
            deps.extend(self.R.get(b, ()))
            t = self.W.get(b)
            if t is not None:
                deps.append(t)
        deps.extend(extra)
        tok = (eng, len(self.ops[eng]))
        self.ops[eng].append(dict(fn=fn, deps=set(deps), is_dma=is_dma,
                                  awaited=False))
        for b in reads:
            self.R.setdefault(b, []).append(tok)
        for b in writes:
            self.W[b] = tok
            self.R[b] = []
        return tok

    def op(self, eng, fn, reads=(), writes=(), extra=()):
        return self._record(eng, fn, reads, writes, extra, False)

    def dma(self, eng, fn, reads=(), writes=(), extra=()):
        return self._record(eng, fn, reads, writes, extra, True)

    NPOOL = 24

    def finalize(self, nc, es):
        self.sems = {}
        for e in self.ENGS:
            if any(not o["is_dma"] for o in self.ops[e]):
                self.sems[e] = es.enter_context(nc.semaphore(f"s_{e}"))
        for p in range(self.NPOOL):
            self.sems[f"d{p}"] = es.enter_context(nc.semaphore(f"d{p}"))
        SELF_SYNC = ("dve", "sc", "gp")
        for e in self.ENGS:
            for o in self.ops[e]:
                for (de, di) in o["deps"]:
                    dep = self.ops[de][di]
                    if dep["is_dma"] or de != e or de in SELF_SYNC:
                        dep["awaited"] = True
        self.val = {}
        pool_use = [0] * self.NPOOL
        dma_ord = 0
        for e in self.ENGS:
            cc_ = 0
            for i, o in enumerate(self.ops[e]):
                if o["is_dma"]:
                    p = dma_ord % self.NPOOL
                    dma_ord += 1
                    pool_use[p] += 1
                    sem_name = f"d{p}"
                    o["pre_wait"] = (sem_name, 16 * (pool_use[p] - 1))
                    o["inc"] = (sem_name, 16)
                    self.val[(e, i)] = (sem_name, 16 * pool_use[p])
                elif o["awaited"]:
                    cc_ += 1
                    o["inc"] = (e, 1)
                    self.val[(e, i)] = (e, cc_)
                else:
                    o["inc"] = None

    def emit(self, ename, eng):
        clock = {}
        for i, o in enumerate(self.ops[ename]):
            want = {}
            for (de, di) in sorted(o["deps"]):
                dep = self.ops[de][di]
                if (not dep["is_dma"] and de == ename
                        and de not in ("dve", "sc", "gp")):
                    continue
                sem_name, v = self.val[(de, di)]
                if v > want.get(sem_name, 0):
                    want[sem_name] = v
            pw = o.get("pre_wait")
            if pw is not None and pw[1] > want.get(pw[0], 0):
                want[pw[0]] = pw[1]
            for sem_name, v in sorted(want.items()):
                if v > clock.get(sem_name, 0):
                    eng.wait_ge(self.sems[sem_name], v)
                    clock[sem_name] = v
            inst = o["fn"](eng) if o["fn"] is not None else None
            if o["inc"] is not None:
                assert inst is not None, f"awaited op with no inst on {ename}:{i}"
                sem_name, amt = o["inc"]
                inst.then_inc(self.sems[sem_name], amt)


class TBuf:
    def __init__(self, t, nm, blk=256):
        self.t = t
        self.nm = nm
        self.blk = blk

    def ap(self, lo, n):
        return self.t[:, lo:lo + n]

    def k(self, lo=0, n=None):
        if n is None:
            n = self.t.shape[1] - lo
        b = self.blk
        return [f"{self.nm}:{i}" for i in range(lo // b, (lo + n + b - 1) // b)]


def build_program(debug_taps=False):
    nc = bass.Bass("TRN2", target_bir_lowering=False, detect_race_conditions=False)
    S = Sched()
    es = ExitStack()

    din = {}
    def dram_in(name, shape, dt=F32):
        din[name] = nc.dram_tensor(name, list(shape), dt, kind="ExternalInput").ap()
        return din[name]

    xin = dram_in("xin", [SEQ_PER_CORE, T, D])
    wqkv = dram_in("wqkv", [D, 4096])
    wg_in = dram_in("wg", [D, 2048])
    wproj_in = dram_in("wproj", [2048, D])
    decay_in = dram_in("decay", [1, H])
    packed_in = dram_in("packed", [256, 1024], I32)
    ws_in = dram_in("wscale", [1, 1])
    bias0_in = dram_in("bias0", [1, D])
    gamma_in = dram_in("gamma", [D])
    beta_in = dram_in("beta", [D])
    inw_in = dram_in("in_w", [D, 2 * HC])
    inb_in = dram_in("in_b", [2 * HC])
    dww_in = dram_in("dw_w", [HC, 7])
    dwb_in = dram_in("dw_b", [HC])
    logd_in = dram_in("log_decay", [HC])
    mixw_in = dram_in("mix_w", [HC, HC])
    mixb_in = dram_in("mix_b", [HC])
    outw_in = dram_in("out_w", [HC, D])
    outb_in = dram_in("out_b", [1, D])
    w1_in = dram_in("w1", [D, FF])
    b1_in = dram_in("b1", [FF])
    w2_in = dram_in("w2", [FF, D])
    b2_in = dram_in("b2", [1, D])
    consts_in = dram_in("consts", [128, len(CONSTS)])
    cq_in = dram_in("cq_t", [TILES_PER_SEQ, 8, 128, TT])
    sq_in = dram_in("sq_t", [TILES_PER_SEQ, 8, 128, TT])
    ck_in = dram_in("ck_t", [TILES_PER_SEQ, 8, 128, TT])
    sk_in = dram_in("sk_t", [TILES_PER_SEQ, 8, 128, TT])
    idm_in = dram_in("idm", [128, 128])

    out_d = nc.dram_tensor("out", [SEQ_PER_CORE, T, D], F32, kind="ExternalOutput").ap()

    taps = {}
    def tap(name, shape, dt=F32):
        if name not in taps:
            taps[name] = nc.dram_tensor("tap_" + name, list(shape), dt,
                                        kind="ExternalOutput").ap()
        return taps[name]

    def scr(name, n_el, dt=BF16):
        return nc.dram_tensor(name, [n_el], dt, kind="Internal").ap()

    sQ = scr("sQ", 128 * 8 * 8 * 128)
    sK = scr("sK", 128 * 8 * 8 * 128)
    sV = scr("sV", 128 * 4 * 8 * 512)
    sGw = scr("sGw", 128 * 4 * 8 * 512)
    sP = scr("sP", 128 * 2 * 16 * 512)
    sTn = scr("sTn", 128 * 2 * 8 * 512)
    sI = scr("sI", 128 * 32 * 8 * 128)
    sM = scr("sM", 128 * 16 * 16 * 128)
    sOw = scr("sOw", 128 * 2 * 16 * 512)
    sW1 = scr("sW1", 128 * 32 * 8 * 128)
    sW2 = scr("sW2", 128 * 2 * 32 * 512)

    def sb(nm, shape, dt=F32, blk=256):
        t = es.enter_context(nc.sbuf_tensor(nm, list(shape), dt))
        return TBuf(t, nm, blk)

    x_tm = sb("x_tm", [128, NTC * 1024])
    arena = sb("arena", [128, 32768], BF16, blk=512)
    fmbuf = sb("fmbuf", [128, 8 * 512], BF16, blk=512)
    scf = sb("scf", [128, 7680])
    sci = sb("sci", [128, 2048], I32, blk=512)
    ret_state = sb("ret_state", [128, 16 * 256])
    ret_sm = sb("ret_sm", [128, 1536], BF16, blk=128)
    masks = sb("masks", [128, H * 512], BF16, blk=512)
    rotb = sb("rotb", [128, 1024], F32, blk=512)
    wst = sb("wst", [128, 16384], BF16, blk=8192)
    brep = sb("brep", [128, 1024], F32, blk=1024)
    ones_bf = sb("ones_bf", [1, 128], BF16, blk=128)
    brow = sb("brow", [1, 4096], BF16, blk=1024)
    prm = sb("prm", [128, 232], F32, blk=8)
    cst = sb("cst", [128, len(CONSTS)], F32, blk=16)
    stat = sb("stat", [128, 64], F32, blk=8)
    ident_f = sb("ident_f", [128, 128], F32, blk=128)
    ident_b = sb("ident_b", [128, 128], BF16, blk=128)
    vcs = sb("vcs", [128, 712], F32, blk=64)
    dwwt = sb("dwwt", [128, 112], F32, blk=112)

    PRM_LND, PRM_D256, PRM_WR, PRM_WK, PRM_A, PRM_CARRY = 0, 8, 16, 32, 48, 64
    PRM_INB, PRM_MIXB, PRM_DWB = 96, 128, 144
    PRM_GAM, PRM_BET, PRM_B1S, PRM_WS = 160, 168, 176, 224
    VTAILS = 520

    psA = es.enter_context(nc.psum_tensor("psA", [128, 1536], F32))
    psB = es.enter_context(nc.psum_tensor("psB", [128, 1536], F32))
    psT = es.enter_context(nc.psum_tensor("psT", [128, 512], F32))
    psTb = es.enter_context(nc.psum_tensor("psTb", [128, 1024], BF16))
    TPF32 = [(psT, 0, "psT"), (psA, 0, "psA:0"), (psA, 512, "psA:1"),
             (psA, 1024, "psA:2")]
    TPBF = [(psTb, 0, "psTb"), (psB, 1024, "psB:2"), (psB, 0, "psB:0"),
            (psB, 512, "psB:1")]
    def tp_f32(i):
        t, off, k = TPF32[i % 4]
        return t[:, off:off + 128], k
    def tp_bf(i):
        t, off, k = TPBF[i % 4]
        if k == "psTb":
            return t[:, off:off + 128], k
        return t[:, off:off + 64].bitcast(BF16), k

    cAP = lambda i: cst.t[:, i:i + 1]
    CSTK = cst.k(0)

    def dmain(dst_ap, src_ap, reads=(), writes=(), slow=False):
        return S.dma("sync", lambda e, d=dst_ap, s=src_ap, sl=slow: e.dma_start(
            out=d, in_=s, allow_slow_non_contiguous=sl),
            reads=reads, writes=writes)

    def repl_row(buf, lo, hi):
        n = 1
        while n < 128:
            m = min(n, 128 - n)
            S.dma("sync", lambda e, n=n, m=m: e.dma_start(
                out=buf.t[n:n + m, lo:hi], in_=buf.t[0:m, lo:hi]),
                reads=buf.k(lo, hi - lo), writes=buf.k(lo, hi - lo))
            n += m

    wslot = [0]
    def load_wblock(src3d_ap, n_el, src_key):
        slot = wslot[0]
        wslot[0] ^= 1
        base = slot * 8192
        assert n_el <= 8192
        dmain(wst.ap(base, n_el), src3d_ap, reads=[src_key],
              writes=wst.k(base, n_el))
        return base, wst.k(base, n_el)

    vQ = sQ.rearrange("(p oc kc m) -> p oc kc m", p=128, oc=8, kc=8)
    vK = sK.rearrange("(p oc kc m) -> p oc kc m", p=128, oc=8, kc=8)
    vV = sV.rearrange("(p ncd kc n) -> p ncd kc n", p=128, ncd=4, kc=8)
    vG = sGw.rearrange("(p ncd kc n) -> p ncd kc n", p=128, ncd=4, kc=8)
    vP = sP.rearrange("(p ncd kc n) -> p ncd kc n", p=128, ncd=2, kc=16)
    vTn = sTn.rearrange("(p ncd kc n) -> p ncd kc n", p=128, ncd=2, kc=8)
    vI = sI.rearrange("(p oc kc m) -> p oc kc m", p=128, oc=32, kc=8)
    vM = sM.rearrange("(p oc kc m) -> p oc kc m", p=128, oc=16, kc=16)
    vO = sOw.rearrange("(p ncd kc n) -> p ncd kc n", p=128, ncd=2, kc=16)
    vW1 = sW1.rearrange("(p oc kc m) -> p oc kc m", p=128, oc=32, kc=8)
    vW2 = sW2.rearrange("(p ncd kc n) -> p ncd kc n", p=128, ncd=2, kc=32)

    dmain(cst.ap(0, len(CONSTS)), consts_in, writes=CSTK)
    dmain(ident_f.ap(0, 128), idm_in, writes=ident_f.k())
    S.op("dve", lambda e: e.tensor_copy(ident_b.ap(0, 128), ident_f.ap(0, 128)),
         reads=ident_f.k(), writes=ident_b.k())

    def prep_weight(src_ap, n_rows, n_cols, view, kind, scr_key):
        n_kc = n_rows // 128
        for kc in range(n_kc):
            dmain(scf.ap(0, n_cols), src_ap[kc * 128:(kc + 1) * 128, :],
                  writes=scf.k(0, n_cols))
            S.op("dve", lambda e, c=n_cols: e.tensor_copy(arena.ap(0, c),
                                                          scf.ap(0, c)),
                 reads=scf.k(0, n_cols), writes=arena.k(0, n_cols))
            if kind == "lhsT":
                dst = view[:, :, kc, :]
                src = arena.ap(0, n_cols).rearrange("p (oc m) -> p oc m", m=128)
            else:
                dst = view[:, :, kc, :]
                src = arena.ap(0, n_cols).rearrange("p (ncd n) -> p ncd n", n=512)
            S.dma("sync", lambda e, dst=dst, src=src: e.dma_start(out=dst, in_=src),
                  reads=arena.k(0, n_cols), writes=[scr_key])

    prep_weight(wqkv[:, 0:1024], 1024, 1024, vQ, "lhsT", "sQ")
    prep_weight(wqkv[:, 1024:2048], 1024, 1024, vK, "lhsT", "sK")
    prep_weight(wqkv[:, 2048:4096], 1024, 2048, vV, "rhs", "sV")
    prep_weight(wg_in, 1024, 2048, vG, "rhs", "sGw")
    prep_weight(wproj_in, 2048, 1024, vP, "rhs", "sP")
    prep_weight(inw_in, 1024, 4096, vI, "lhsT", "sI")
    prep_weight(mixw_in, 2048, 2048, vM, "lhsT", "sM")
    prep_weight(outw_in, 2048, 1024, vO, "rhs", "sOw")
    prep_weight(w1_in, 1024, 4096, vW1, "lhsT", "sW1")
    prep_weight(w2_in, 4096, 1024, vW2, "rhs", "sW2")

    def floor_fixup(u_ap, u_keys, c_ap, c_keys, m_ap, m_keys, f_ap, f_keys):
        S.op("sc", lambda e: e.activation(c_ap, u_ap, AF.Identity),
             reads=u_keys, writes=c_keys)
        S.op("dve", lambda e: e.tensor_tensor(m_ap, c_ap, u_ap, OP.is_le),
             reads=c_keys + u_keys, writes=m_keys)
        S.op("dve", lambda e: e.scalar_tensor_tensor(f_ap, c_ap, 1, m_ap,
                                                     OP.subtract, OP.add),
             reads=c_keys + m_keys, writes=f_keys)

    for oc2 in range(8):
        j = oc2 // 2
        r0 = (oc2 % 2) * 128
        dmain(sci.ap(0, 1024), packed_in[r0:r0 + 128, :], writes=sci.k(0, 1024))
        S.op("dve", lambda e: e.tensor_copy(scf.ap(0, 1024), sci.ap(0, 1024)),
             reads=sci.k(0, 1024), writes=scf.k(0, 1024))
        def floor_scaled(dst_off, scale_val):
            S.op("dve", lambda e, s=scale_val, d=dst_off: e.tensor_scalar(
                scf.ap(d, 1024), scf.ap(0, 1024), s, None, OP.mult),
                reads=scf.k(0, 1024), writes=scf.k(dst_off, 1024))
            floor_fixup(scf.ap(dst_off, 1024), scf.k(dst_off, 1024),
                        sci.ap(1024, 1024), sci.k(1024, 1024),
                        scf.ap(4096, 1024), scf.k(4096, 1024),
                        sci.ap(1024, 1024), sci.k(1024, 1024))
            S.op("dve", lambda e, d=dst_off: e.tensor_copy(scf.ap(d, 1024),
                                                           sci.ap(1024, 1024)),
                 reads=sci.k(1024, 1024), writes=scf.k(dst_off, 1024))
        if j == 0:
            floor_scaled(1024, 0.25)
            S.op("dve", lambda e: e.scalar_tensor_tensor(
                scf.ap(2048, 1024), scf.ap(1024, 1024), -4.0, scf.ap(0, 1024),
                OP.mult, OP.add),
                reads=scf.k(0, 2048), writes=scf.k(2048, 1024))
        else:
            floor_scaled(1024, 0.25 ** j)
            floor_scaled(3072, 0.25 ** (j + 1))
            S.op("dve", lambda e: e.scalar_tensor_tensor(
                scf.ap(2048, 1024), scf.ap(3072, 1024), -4.0, scf.ap(1024, 1024),
                OP.mult, OP.add),
                reads=scf.k(1024, 1024) + scf.k(3072, 1024),
                writes=scf.k(2048, 1024))
        S.op("dve", lambda e: e.tensor_scalar(arena.ap(0, 1024), scf.ap(2048, 1024),
                                              1.0, None, OP.subtract),
             reads=scf.k(2048, 1024), writes=arena.k(0, 1024))
        for ic in range(8):
            S.op("pe", lambda e, ic=ic: e.transpose(
                psTb[:, 0:128], arena.ap(ic * 128, 128), ident_b.ap(0, 128)),
                reads=arena.k(0, 1024) + ident_b.k(), writes=["psTb"])
            S.op("dve", lambda e: e.tensor_copy(arena.ap(4096, 128), psTb[:, 0:128]),
                 reads=["psTb"], writes=arena.k(4096, 128))
            dstv = vTn[:, oc2 // 4, ic, (oc2 % 4) * 128:(oc2 % 4) * 128 + 128]
            S.dma("sync", lambda e, dstv=dstv: e.dma_start(out=dstv,
                                                           in_=arena.ap(4096, 128)),
                  reads=arena.k(4096, 128), writes=["sTn"])

    dmain(scf.t[0:1, 2048:2056], decay_in, writes=scf.k(2048, 256))
    S.op("sc", lambda e: e.activation(prm.t[0:1, PRM_LND:PRM_LND + 8],
                                      scf.t[0:1, 2048:2056], AF.Ln),
         reads=scf.k(2048, 256), writes=prm.k(PRM_LND, 8))
    repl_row(prm, PRM_LND, PRM_LND + 8)
    S.op("dve", lambda e: e.tensor_scalar(scf.ap(2048, 8), prm.ap(PRM_LND, 8),
                                          256.0, None, OP.mult),
         reads=prm.k(PRM_LND, 8), writes=scf.k(2048, 256))
    S.op("sc", lambda e: e.activation(prm.ap(PRM_D256, 8), scf.ap(2048, 8),
                                      AF.Exp),
         reads=scf.k(2048, 256), writes=prm.k(PRM_D256, 8))
    S.op("gp", lambda e: e.iota(sci.ap(0, 512), [[-128, 2], [1, 256]], base=0,
                                channel_multiplier=-1),
         writes=sci.k(0, 512))
    S.op("dve", lambda e: e.tensor_copy(scf.ap(0, 512), sci.ap(0, 512)),
         reads=sci.k(0, 512), writes=scf.k(0, 512))
    S.op("dve", lambda e: e.tensor_single_scalar(scf.ap(512, 512), scf.ap(0, 512),
                                                 0.0, OP.is_ge),
         reads=scf.k(0, 512), writes=scf.k(512, 512))
    S.op("dve", lambda e: e.tensor_scalar(scf.ap(1536, 512), scf.ap(512, 512),
                                          INV_SQRT_DK, None, OP.mult),
         reads=scf.k(512, 512), writes=scf.k(1536, 512))
    for h in range(H):
        S.op("dve", lambda e, h=h: e.tensor_scalar(
            scf.ap(1024, 512), scf.ap(0, 512),
            prm.t[:, PRM_LND + h:PRM_LND + h + 1], None, OP.mult),
            reads=scf.k(0, 512) + prm.k(PRM_LND, 8), writes=scf.k(1024, 512))
        S.op("sc", lambda e: e.activation(scf.ap(1024, 512), scf.ap(1024, 512),
                                          AF.Exp),
             reads=scf.k(1024, 512), writes=scf.k(1024, 512))
        S.op("dve", lambda e, h=h: e.tensor_tensor(
            masks.ap(h * 512, 512), scf.ap(1024, 512), scf.ap(1536, 512), OP.mult),
            reads=scf.k(1024, 512) + scf.k(1536, 512),
            writes=masks.k(h * 512, 512))
    S.op("gp", lambda e: e.iota(sci.ap(0, 2), [[128, 2]], base=1,
                                channel_multiplier=1),
         writes=sci.k(0, 512))
    S.op("dve", lambda e: e.tensor_copy(scf.ap(0, 2), sci.ap(0, 2)),
         reads=sci.k(0, 512), writes=scf.k(0, 256))
    S.op("gp", lambda e: e.iota(sci.ap(512, 2), [[-128, 2]], base=255,
                                channel_multiplier=-1),
         writes=sci.k(512, 512))
    S.op("dve", lambda e: e.tensor_copy(scf.ap(256, 2), sci.ap(512, 2)),
         reads=sci.k(512, 512), writes=scf.k(256, 256))
    for h in range(H):
        for (so, dst) in ((0, PRM_WR), (256, PRM_WK)):
            S.op("dve", lambda e, h=h, so=so: e.tensor_scalar(
                scf.ap(2048, 2), scf.ap(so, 2),
                prm.t[:, PRM_LND + h:PRM_LND + h + 1], None, OP.mult),
                reads=scf.k(so, 256) + prm.k(PRM_LND, 8),
                writes=scf.k(2048, 256))
            S.op("sc", lambda e: e.activation(
                scf.ap(2304, 2), scf.ap(2048, 2), AF.Exp),
                reads=scf.k(2048, 256), writes=scf.k(2304, 256))
            if dst == PRM_WR:
                S.op("dve", lambda e, h=h, dst=dst: e.tensor_scalar(
                    prm.ap(dst + 2 * h, 2), scf.ap(2304, 2),
                    INV_SQRT_DK, None, OP.mult),
                    reads=scf.k(2304, 256), writes=prm.k(dst + 2 * h, 2))
            else:
                S.op("dve", lambda e, h=h, dst=dst: e.tensor_copy(
                    prm.ap(dst + 2 * h, 2), scf.ap(2304, 2)),
                    reads=scf.k(2304, 256), writes=prm.k(dst + 2 * h, 2))
    dmain(scf.ap(2048, 16), logd_in.rearrange("(c p) -> p c", p=128),
          writes=scf.k(2048, 256), slow=True)
    S.op("sc", lambda e: e.activation(prm.ap(PRM_A, 16), scf.ap(2048, 16),
                                      AF.Sigmoid),
         reads=scf.k(2048, 256), writes=prm.k(PRM_A, 16))
    S.op("dve", lambda e: e.memset(prm.ap(PRM_CARRY, 32), 0.0),
         writes=prm.k(PRM_CARRY, 32))
    dmain(prm.ap(PRM_INB, 32), inb_in.rearrange("(c p) -> p c", p=128),
          writes=prm.k(PRM_INB, 32), slow=True)
    dmain(prm.ap(PRM_MIXB, 16), mixb_in.rearrange("(c p) -> p c", p=128),
          writes=prm.k(PRM_MIXB, 16), slow=True)
    dmain(prm.ap(PRM_DWB, 16), dwb_in.rearrange("(c p) -> p c", p=128),
          writes=prm.k(PRM_DWB, 16), slow=True)
    dmain(prm.ap(PRM_GAM, 8), gamma_in.rearrange("(c p) -> p c", p=128),
          writes=prm.k(PRM_GAM, 8), slow=True)
    dmain(prm.ap(PRM_BET, 8), beta_in.rearrange("(c p) -> p c", p=128),
          writes=prm.k(PRM_BET, 8), slow=True)
    dmain(prm.ap(PRM_B1S, 32), b1_in.rearrange("(c p) -> p c", p=128),
          writes=prm.k(PRM_B1S, 32), slow=True)
    S.op("dve", lambda e: e.tensor_scalar(scf.ap(2048, 32), prm.ap(PRM_B1S, 32),
                                          256.0, 0.5, OP.mult, OP.add),
         reads=prm.k(PRM_B1S, 32), writes=scf.k(2048, 256))
    S.op("dve", lambda e: e.tensor_copy(prm.ap(PRM_B1S, 32), scf.ap(2048, 32)),
         reads=scf.k(2048, 256), writes=prm.k(PRM_B1S, 32))
    dmain(dwwt.ap(0, 112).rearrange("p (c j) -> p c j", j=7),
          dww_in.rearrange("(c p) j -> p c j", p=128), writes=dwwt.k(), slow=True)
    dmain(prm.t[0:1, PRM_WS:PRM_WS + 1], ws_in, writes=prm.k(PRM_WS, 1))
    repl_row(prm, PRM_WS, PRM_WS + 1)
    dmain(brep.t[0:1, 0:1024], bias0_in, writes=brep.k(0, 1024))
    repl_row(brep, 0, 1024)
    S.op("dve", lambda e: e.memset(ones_bf.ap(0, 128), 1.0), writes=ones_bf.k())
    for bi, (bsrc, blo) in enumerate(((outb_in, 0), (b2_in, 2048))):
        dmain(scf.t[0:1, 0:1024], bsrc, writes=scf.k(0, 1024))
        S.op("dve", lambda e, blo=blo: e.tensor_copy(brow.t[0:1, blo:blo + 1024],
                                                     scf.t[0:1, 0:1024]),
             reads=scf.k(0, 1024), writes=brow.k(blo, 1024))
        S.op("dve", lambda e, blo=blo: e.tensor_copy(scf.t[0:1, 1024:2048],
                                                     brow.t[0:1, blo:blo + 1024]),
             reads=brow.k(blo, 1024), writes=scf.k(1024, 1024))
        S.op("dve", lambda e: e.tensor_tensor(scf.t[0:1, 0:1024],
                                              scf.t[0:1, 0:1024],
                                              scf.t[0:1, 1024:2048], OP.subtract),
             reads=scf.k(0, 2048), writes=scf.k(0, 1024))
        S.op("dve", lambda e, blo=blo: e.tensor_copy(
            brow.t[0:1, blo + 1024:blo + 2048], scf.t[0:1, 0:1024]),
            reads=scf.k(0, 1024), writes=brow.k(blo + 1024, 1024))
    S.op("dve", lambda e: e.memset(ret_state.ap(0, 4096), 0.0),
         writes=ret_state.k())
    S.op("dve", lambda e: e.memset(vcs.ap(VTAILS, 192), 0.0),
         writes=vcs.k(VTAILS, 192))

    AQ, AK, AV, AG0, AGN = 0, 4096, 8192, 16384, 24576
    AL1G, AL1S, AL1C, AL1V = 0, 8192, 16384, 24576
    AH = 0

    def layer_norm(s_lo, tc):
        sb_ = 32 + (tc % 2) * 8
        sk = scf.k(s_lo, 1024)
        sap = scf.ap(s_lo, 1024)
        ST = lambda i: stat.t[:, sb_ + i:sb_ + i + 1]
        STA = lambda i: stat.ap(sb_ + i, 1)
        STK = stat.k(sb_, 8)
        S.op("dve", lambda e: e.reduce_sum(STA(0), sap, axis=AX.X),
             reads=sk, writes=STK)
        S.op("sc", lambda e: e.activation(scf.ap(1024, 1024), sap, AF.Square,
                                          accum_out=STA(1)),
             reads=sk, writes=scf.k(1024, 1024) + STK)
        S.op("dve", lambda e: e.tensor_scalar(STA(2), STA(0), cAP(CI1024), None,
                                              OP.mult),
             reads=STK + CSTK, writes=STK)
        S.op("dve", lambda e: e.tensor_tensor(STA(3), STA(2), STA(2), OP.mult),
             reads=STK, writes=STK)
        S.op("dve", lambda e: e.scalar_tensor_tensor(
            STA(4), STA(1), 1.0 / 1024.0, STA(3), OP.mult, OP.subtract),
            reads=STK, writes=STK)
        S.op("sc", lambda e: e.activation(STA(0), STA(4), AF.Ln, bias=cAP(CEPS),
                                          scale=1.0),
             reads=STK + CSTK, writes=STK)
        S.op("sc", lambda e: e.activation(STA(5), STA(0), AF.Exp,
                                          bias=0.0, scale=cAP(CN05)),
             reads=STK + CSTK, writes=STK)
        S.op("dve", lambda e: e.scalar_tensor_tensor(
            STA(6), STA(2), -1.0, STA(5), OP.mult, OP.mult),
            reads=STK, writes=STK)
        S.op("sc", lambda e: e.activation(
            x_tm.ap(tc * 1024, 1024), sap, AF.Identity,
            bias=ST(6), scale=ST(5)),
            reads=sk + STK, writes=x_tm.k(tc * 1024, 1024))

    def transpose_tm_to_fm(src, src_keys_fn, cast_gb=False):
        for tc in range(NTC):
            for dc in range(8):
                pap, pk = tp_f32(tc * 8 + dc)
                S.op("pe", lambda e, tc=tc, dc=dc, pap=pap: e.transpose(
                    pap, src.ap(tc * 1024 + dc * 128, 128), ident_f.ap(0, 128)),
                    reads=src_keys_fn(tc, dc) + ident_f.k(),
                    writes=[pk])
                dsta = fmbuf.ap(dc * 512 + tc * 128, 128)
                dkey = fmbuf.k(dc * 512 + tc * 128, 128)
                if cast_gb:
                    S.op("dve", lambda e, dsta=dsta, pap=pap, dc=dc:
                         e.tensor_scalar(dsta, pap,
                                         prm.t[:, PRM_GAM + dc:PRM_GAM + dc + 1],
                                         prm.t[:, PRM_BET + dc:PRM_BET + dc + 1],
                                         OP.mult, OP.add),
                         reads=[pk] + prm.k(PRM_GAM, 16), writes=dkey)
                else:
                    S.op("dve", lambda e, dsta=dsta, pap=pap: e.tensor_copy(
                        dsta, pap),
                        reads=[pk], writes=dkey)

    def emit_tile(seq, tt, tile_idx):
        t0 = tt * TT
        dbg = debug_taps and tile_idx == 0

        src = xin[seq, t0:t0 + TT, :].rearrange("(tc p) d -> p tc d", p=128)
        dmain(x_tm.ap(0, 4096).rearrange("p (tc d) -> p tc d", d=1024), src,
              writes=x_tm.k())

        if SUBPHASE < 0.7:
            dst = out_d[seq, t0:t0 + TT, :].rearrange("(tc p) d -> p tc d", p=128)
            S.dma("sync", lambda e, dst=dst: e.dma_start(
                out=dst, in_=x_tm.ap(0, 4096).rearrange("p (tc d) -> p tc d",
                                                        d=1024)),
                reads=x_tm.k(), writes=[f"outp:{tile_idx}"])
            return
        transpose_tm_to_fm(x_tm, lambda tc, dc: x_tm.k(tc * 1024 + dc * 128, 128))
        if SUBPHASE < 1:
            dst = out_d[seq, t0:t0 + TT, :].rearrange("(tc p) d -> p tc d", p=128)
            S.dma("sync", lambda e, dst=dst: e.dma_start(
                out=dst, in_=x_tm.ap(0, 4096).rearrange("p (tc d) -> p tc d",
                                                        d=1024)),
                reads=x_tm.k(), writes=[f"outp:{tile_idx}"])
            return

        for which, aoff, view, skey, ctab, stab in (
                ("q", AQ, vQ, "sQ", cq_in, sq_in),
                ("k", AK, vK, "sK", ck_in, sk_in)):
            base, wkeys = load_wblock(
                (sQ if which == "q" else sK).rearrange("(p x) -> p x", p=128),
                8192, skey)
            def rot_ops(oc, po, pkey, rtb, rlo, aoff=aoff):
                sf = 2048 * (oc % 2)
                ops = []
                ops.append(lambda: S.op("dve", lambda e: e.tensor_tensor(
                    scf.ap(sf + 512, 512), po, rtb.ap(rlo, 512), OP.mult),
                    reads=[pkey] + rtb.k(rlo, 512),
                    writes=scf.k(sf + 512, 512)))
                ops.append(lambda: S.op("dve", lambda e: e.stream_shuffle(
                    scf.ap(sf + 1024, 512), po, [i ^ 1 for i in range(32)]),
                    reads=[pkey], writes=scf.k(sf + 1024, 512)))
                ops.append(lambda: S.op("dve", lambda e: e.tensor_tensor(
                    scf.ap(sf + 1536, 512), scf.ap(sf + 1024, 512),
                    rtb.ap(rlo + 512, 512), OP.mult),
                    reads=scf.k(sf + 1024, 512) + rtb.k(rlo + 512, 512),
                    writes=scf.k(sf + 1536, 512)))
                ops.append(lambda: S.op("dve", lambda e: e.tensor_tensor(
                    arena.ap(aoff + oc * 512, 512), scf.ap(sf + 512, 512),
                    scf.ap(sf + 1536, 512), OP.add),
                    reads=scf.k(sf + 512, 512) + scf.k(sf + 1536, 512),
                    writes=arena.k(aoff + oc * 512, 512)))
                return ops

            import itertools as _it2
            for og in range(4):
                chains = []
                for oc in (2 * og, 2 * og + 1):
                    if oc % 2 == 0:
                        rtb, rlo = rotb, 0
                    else:
                        rtb, rlo = scf, 4096
                    dmain(rtb.ap(rlo, 512), ctab[tt, oc], writes=rtb.k(rlo, 512))
                    dmain(rtb.ap(rlo + 512, 512), stab[tt, oc],
                          writes=rtb.k(rlo + 512, 512))
                    pslot = oc % 3
                    po = psB[:, pslot * 512:pslot * 512 + 512]
                    for kc in range(8):
                        S.op("pe", lambda e, oc=oc, kc=kc, po=po, base=base:
                             e.matmul(
                                 po, wst.ap(base + oc * 1024 + kc * 128, 128),
                                 fmbuf.ap(kc * 512, 512),
                                 start=(kc == 0), stop=(kc == 7)),
                             reads=wkeys + fmbuf.k(kc * 512, 512),
                             writes=[f"psB:{pslot}"])
                    chains.append(rot_ops(oc, po, f"psB:{pslot}", rtb, rlo))
                for pair in _it2.zip_longest(*chains):
                    for fn in pair:
                        if fn is not None:
                            fn()
        if dbg:
            dmain(tap("q_fm", [128, 4096], BF16), arena.ap(AQ, 4096),
                  reads=arena.k(AQ, 4096))
            dmain(tap("k_fm", [128, 4096], BF16), arena.ap(AK, 4096),
                  reads=arena.k(AK, 4096))

        for which, aoff, view, skey, act in (("v", AV, vV, "sV", False),
                                             ("g0", AG0, vG, "sGw", True)):
            for ncd in range(4):
                base, wkeys = load_wblock(view[:, ncd].rearrange("p kc n -> p (kc n)"),
                                          4096, skey)
                for tc in range(NTC):
                    pslot = tc % 3
                    po = psB[:, pslot * 512:pslot * 512 + 512]
                    for kc in range(8):
                        S.op("pe", lambda e, tc=tc, kc=kc, po=po, base=base:
                             e.matmul(po, fmbuf.ap(kc * 512 + tc * 128, 128),
                                      wst.ap(base + kc * 512, 512),
                                      start=(kc == 0), stop=(kc == 7)),
                             reads=wkeys + fmbuf.k(kc * 512 + tc * 128, 128),
                             writes=[f"psB:{pslot}"])
                    dsta = arena.ap(aoff + tc * 2048 + ncd * 512, 512)
                    dkey = arena.k(aoff + tc * 2048 + ncd * 512, 512)
                    if act:
                        S.op("sc", lambda e, dsta=dsta, po=po: e.activation(
                            dsta, po, AF.Silu),
                            reads=[f"psB:{pslot}"], writes=dkey)
                    else:
                        S.op("sc", lambda e, dsta=dsta, po=po: e.activation(
                            dsta, po, AF.Identity),
                            reads=[f"psB:{pslot}"], writes=dkey)
        if dbg:
            dmain(tap("v_tm", [128, 8192], BF16), arena.ap(AV, 8192),
                  reads=arena.k(AV, 8192))

        if SUBPHASE < 3:
            dst = out_d[seq, t0:t0 + TT, :].rearrange("(tc p) d -> p tc d", p=128)
            S.dma("sync", lambda e, dst=dst: e.dma_start(
                out=dst, in_=x_tm.ap(0, 4096).rearrange("p (tc d) -> p tc d",
                                                        d=1024)),
                reads=x_tm.k(), writes=[f"outp:{tile_idx}"])
            return
        SC_AP = [psA[:, 0:256], psA[:, 512:768]]
        SC_K = ["psA:0", "psA:1"]
        O_AP = [psA[:, 1024:1280], psT[:, 0:256]]
        O_K = ["psA:2", "psT"]
        X_AP = psB[:, 0:256]
        X_K = "psB:0"
        U_AP = psB[:, 512:768]
        U_K = "psB:1"
        KT_AP = [psTb[:, 0:128], psB[:, 1024:1088].bitcast(BF16)]
        KT_K = ["psTb", "psB:2"]
        for c in range(2):
            for h in range(H):
                kq = arena.k(AQ + h * 512 + c * 256, 256)
                kk = arena.k(AK + h * 512 + c * 256, 256)
                S.op("dve", lambda e, h=h: e.tensor_copy(
                    ret_sm.ap(768, 256), ret_state.ap(h * 256, 256)),
                    reads=ret_state.k(h * 256, 256), writes=ret_sm.k(768, 256))
                for tkc in range(2):
                    S.op("pe", lambda e, c=c, h=h, tkc=tkc: e.matmul(
                        SC_AP[tkc],
                        arena.ap(AK + h * 512 + c * 256 + tkc * 128, 128),
                        arena.ap(AQ + h * 512 + c * 256, 256),
                        start=True, stop=True),
                        reads=kq + kk, writes=[SC_K[tkc]])
                    S.op("dve", lambda e, h=h, tkc=tkc: e.tensor_tensor(
                        ret_sm.ap(tkc * 256, 256),
                        SC_AP[tkc],
                        masks.ap(h * 512 + tkc * 256, 256), OP.mult),
                        reads=[SC_K[tkc]] + masks.k(h * 512 + tkc * 256, 256),
                        writes=ret_sm.k(tkc * 256, 256))
                    S.op("pe", lambda e, c=c, h=h, tkc=tkc: e.transpose(
                        KT_AP[tkc],
                        arena.ap(AK + h * 512 + c * 256 + tkc * 128, 128),
                        ident_b.ap(0, 128)),
                        reads=kk + ident_b.k(), writes=[KT_K[tkc]])
                    S.op("dve", lambda e, tkc=tkc: e.tensor_copy(
                        ret_sm.ap(512 + tkc * 128, 128),
                        KT_AP[tkc]),
                        reads=[KT_K[tkc]], writes=ret_sm.k(512 + tkc * 128, 128))
                    tc = c * 2 + tkc
                    S.op("dve", lambda e, tc=tc, h=h, tkc=tkc: e.tensor_scalar(
                        ret_sm.ap(1024 + tkc * 256, 256),
                        arena.ap(AV + tc * 2048 + h * 256, 256),
                        prm.t[:, PRM_WK + 2 * h + tkc:PRM_WK + 2 * h + tkc + 1],
                        None, OP.mult),
                        reads=arena.k(AV + tc * 2048 + h * 256, 256) +
                        prm.k(PRM_WK + 2 * h, 2),
                        writes=ret_sm.k(1024 + tkc * 256, 256))
                for tqc in range(2):
                    for tkc in range(tqc + 1):
                        S.op("pe", lambda e, c=c, h=h, tqc=tqc, tkc=tkc: e.matmul(
                            O_AP[tqc],
                            ret_sm.ap(tkc * 256 + tqc * 128, 128),
                            arena.ap(AV + (c * 2 + tkc) * 2048 + h * 256, 256),
                            start=(tkc == 0), stop=(tkc == tqc)),
                            reads=ret_sm.k(tkc * 256, 256) +
                            arena.k(AV + (c * 2 + tkc) * 2048 + h * 256, 256),
                            writes=[O_K[tqc]])
                    S.op("pe", lambda e, c=c, h=h, tqc=tqc: e.matmul(
                        X_AP,
                        arena.ap(AQ + h * 512 + c * 256 + tqc * 128, 128),
                        ret_sm.ap(768, 256), start=True, stop=True),
                        reads=kq + ret_sm.k(768, 256), writes=[X_K])
                    xs = (h % 2) * 256
                    S.op("sc", lambda e, h=h, tqc=tqc, xs=xs: e.activation(
                        scf.ap(xs, 256), X_AP, AF.Identity,
                        scale=prm.t[:, PRM_WR + 2 * h + tqc:PRM_WR + 2 * h + tqc + 1]),
                        reads=[X_K] + prm.k(PRM_WR + 2 * h, 2) + CSTK,
                        writes=scf.k(xs, 256))
                    olo = 3584 + tqc * 2048 + h * 256
                    S.op("dve", lambda e, olo=olo, tqc=tqc, xs=xs: e.tensor_tensor(
                        scf.ap(olo, 256), O_AP[tqc],
                        scf.ap(xs, 256), OP.add),
                        reads=[O_K[tqc]] + scf.k(xs, 256),
                        writes=scf.k(olo, 256))
                for tkc in range(2):
                    S.op("pe", lambda e, tkc=tkc: e.matmul(
                        U_AP, ret_sm.ap(512 + tkc * 128, 128),
                        ret_sm.ap(1024 + tkc * 256, 256),
                        start=(tkc == 0), stop=(tkc == 1)),
                        reads=ret_sm.k(512 + tkc * 128, 128) +
                        ret_sm.k(1024 + tkc * 256, 256),
                        writes=[U_K])
                S.op("dve", lambda e, h=h: e.tensor_scalar(
                    scf.ap(256, 256), ret_state.ap(h * 256, 256),
                    prm.t[:, PRM_D256 + h:PRM_D256 + h + 1], None, OP.mult),
                    reads=ret_state.k(h * 256, 256) + prm.k(PRM_D256, 8),
                    writes=scf.k(256, 256))
                S.op("dve", lambda e, h=h: e.tensor_tensor(
                    ret_state.ap(h * 256, 256), scf.ap(256, 256),
                    U_AP, OP.add),
                    reads=scf.k(256, 256) + [U_K],
                    writes=ret_state.k(h * 256, 256))

            if dbg:
                dmain(tap(f"obuf{c}", [128, 4096]), scf.ap(3584, 4096),
                      reads=scf.k(3584, 4096))
            for half in range(2):
                tc = c * 2 + half
                olo = 3584 + half * 2048
                okeys = scf.k(olo, 2048)
                S.op("dve", lambda e, olo=olo: e.reduce_sum(
                    stat.ap(8, 8).unsqueeze(2),
                    scf.ap(olo, 2048).rearrange("p (h v) -> p h v", h=8),
                    axis=AX.X),
                    reads=okeys, writes=stat.k(8, 8))
                S.op("dve", lambda e, olo=olo: e.tensor_tensor(
                    scf.ap(1024, 2048), scf.ap(olo, 2048), scf.ap(olo, 2048),
                    OP.mult),
                    reads=okeys, writes=scf.k(1024, 2048))
                S.op("dve", lambda e: e.reduce_sum(
                    stat.ap(16, 8).unsqueeze(2),
                    scf.ap(1024, 2048).rearrange("p (h v) -> p h v", h=8),
                    axis=AX.X),
                    reads=scf.k(1024, 2048), writes=stat.k(16, 8))
                S.op("dve", lambda e: e.tensor_scalar(stat.ap(8, 8), stat.ap(8, 8),
                                                      1.0 / 256.0, None, OP.mult),
                     reads=stat.k(8, 8), writes=stat.k(8, 8))
                S.op("dve", lambda e: e.tensor_tensor(stat.ap(24, 8), stat.ap(8, 8),
                                                      stat.ap(8, 8), OP.mult),
                     reads=stat.k(8, 8), writes=stat.k(24, 8))
                S.op("dve", lambda e: e.scalar_tensor_tensor(
                    stat.ap(0, 8), stat.ap(16, 8), 1.0 / 256.0, stat.ap(24, 8),
                    OP.mult, OP.subtract),
                    reads=stat.k(16, 8) + stat.k(24, 8), writes=stat.k(0, 8))
                S.op("dve", lambda e: e.tensor_scalar(stat.ap(24, 8), stat.ap(0, 8),
                                                      1e-5, None, OP.add),
                     reads=stat.k(0, 8), writes=stat.k(24, 8))
                S.op("sc", lambda e: e.activation(stat.ap(0, 8), stat.ap(24, 8),
                                                  AF.Ln),
                     reads=stat.k(24, 8), writes=stat.k(0, 8))
                S.op("sc", lambda e: e.activation(stat.ap(16, 8), stat.ap(0, 8),
                                                  AF.Exp, bias=0.0, scale=cAP(CN05)),
                     reads=stat.k(0, 8) + CSTK, writes=stat.k(16, 8))
                S.op("dve", lambda e: e.scalar_tensor_tensor(
                    stat.ap(24, 8), stat.ap(8, 8), -1.0, stat.ap(16, 8),
                    OP.mult, OP.mult),
                    reads=stat.k(8, 16), writes=stat.k(24, 8))
                for h in range(H):
                    nb = scf.ap((h % 2) * 128, 128).bitcast(BF16)
                    nbk = scf.k((h % 2) * 128, 128)
                    S.op("sc", lambda e, olo=olo, h=h, nb=nb: e.activation(
                        nb, scf.ap(olo + h * 256, 256), AF.Identity,
                        bias=stat.t[:, 24 + h:25 + h],
                        scale=stat.t[:, 16 + h:17 + h]),
                        reads=scf.k(olo + h * 256, 256) + stat.k(16, 8) +
                        stat.k(24, 8),
                        writes=nbk)
                    S.op("dve", lambda e, tc=tc, h=h, nb=nb: e.tensor_tensor(
                        wst.ap(h * 256, 256), nb,
                        arena.ap(AG0 + tc * 2048 + h * 256, 256), OP.mult),
                        reads=nbk + arena.k(AG0 + tc * 2048 + h * 256, 256),
                        writes=wst.k(0, 2048))
                if dbg and tc == 0:
                    dmain(tap("gnt", [128, 2048], BF16), wst.ap(0, 2048),
                          reads=wst.k(0, 2048))
                    dmain(tap("gstat", [128, 32]), stat.ap(0, 32),
                          reads=stat.k(0, 32))
                for vc in range(16):
                    pap, pk = tp_bf(vc)
                    S.op("pe", lambda e, vc=vc, pap=pap: e.transpose(
                        pap, wst.ap(vc * 128, 128), ident_b.ap(0, 128)),
                        reads=wst.k(0, 2048) + ident_b.k(),
                        writes=[pk])
                    S.op("dve", lambda e, vc=vc, tc=tc, pap=pap: e.tensor_copy(
                        arena.ap(AGN + vc * 512 + tc * 128, 128), pap),
                        reads=[pk],
                        writes=arena.k(AGN + vc * 512 + tc * 128, 128))
        if dbg:
            dmain(tap("gn_fm", [128, 8192], BF16), arena.ap(AGN, 8192),
                  reads=arena.k(AGN, 8192))
            dmain(tap("masks", [128, 4096], BF16), masks.ap(0, 4096),
                  reads=masks.k(0, 4096))

        if SUBPHASE < 4:
            dst = out_d[seq, t0:t0 + TT, :].rearrange("(tc p) d -> p tc d", p=128)
            S.dma("sync", lambda e, dst=dst: e.dma_start(
                out=dst, in_=x_tm.ap(0, 4096).rearrange("p (tc d) -> p tc d",
                                                        d=1024)),
                reads=x_tm.k(), writes=[f"outp:{tile_idx}"])
            return
        prb0, prk0 = load_wblock(vP[:, 0].rearrange("p kc n -> p (kc n)"),
                                 8192, "sP")
        prb1, prk1 = load_wblock(vP[:, 1].rearrange("p kc n -> p (kc n)"),
                                 8192, "sP")
        for tc in range(NTC):
            for ncd in range(2):
                base = prb0 if ncd == 0 else prb1
                wkeys = prk0 if ncd == 0 else prk1
                pslot = (tc * 2 + ncd) % 3
                po = psB[:, pslot * 512:pslot * 512 + 512]
                for kc in range(16):
                    S.op("pe", lambda e, tc=tc, kc=kc, po=po, base=base: e.matmul(
                        po, arena.ap(AGN + kc * 512 + tc * 128, 128),
                        wst.ap(base + kc * 512, 512),
                        start=(kc == 0), stop=(kc == 15)),
                        reads=arena.k(AGN + kc * 512 + tc * 128, 128) + wkeys,
                        writes=[f"psB:{pslot}"])
                sb0 = (tc % 2) * 2048
                S.op("dve", lambda e, tc=tc, ncd=ncd, po=po, pslot=pslot, sb0=sb0:
                     e.tensor_tensor(
                         scf.ap(sb0 + ncd * 512, 512), po,
                         x_tm.ap(tc * 1024 + ncd * 512, 512), OP.add),
                     reads=[f"psB:{pslot}"] + x_tm.k(tc * 1024 + ncd * 512, 512),
                     writes=scf.k(sb0 + ncd * 512, 512))
            layer_norm((tc % 2) * 2048, tc)
        if dbg:
            dmain(tap("x1", [128, 4096]), x_tm.ap(0, 4096), reads=x_tm.k())

        if PHASE_LIMIT < 2:
            dst = out_d[seq, t0:t0 + TT, :].rearrange("(tc p) d -> p tc d", p=128)
            S.dma("sync", lambda e, dst=dst: e.dma_start(
                out=dst, in_=x_tm.ap(0, 4096).rearrange("p (tc d) -> p tc d",
                                                        d=1024)),
                reads=x_tm.k(), writes=[f"outp:{tile_idx}"])
            return
        tbase0, tkeys0 = load_wblock(vTn[:, 0].rearrange("p kc n -> p (kc n)"),
                                     4096, "sTn")
        tbase1, tkeys1 = load_wblock(vTn[:, 1].rearrange("p kc n -> p (kc n)"),
                                     4096, "sTn")
        for tc in range(NTC):
            xk = x_tm.k(tc * 1024, 1024)
            xs = x_tm.ap(tc * 1024, 1024)
            S.op("dve", lambda e, xs=xs: e.reduce_max(
                stat.ap(0, 1), xs, axis=AX.X, apply_absolute_value=True),
                reads=xk, writes=stat.k(0, 8))
            S.op("dve", lambda e: e.tensor_single_scalar(stat.ap(3, 1),
                                                         stat.ap(0, 1), 1e-5,
                                                         OP.max),
                 reads=stat.k(0, 8), writes=stat.k(0, 8))
            S.op("dve", lambda e: e.reciprocal(stat.ap(1, 1), stat.ap(3, 1)),
                 reads=stat.k(0, 8), writes=stat.k(0, 8))
            S.op("dve", lambda e: e.tensor_scalar(stat.ap(4, 1), stat.ap(1, 1),
                                                  127.0, None, OP.mult),
                 reads=stat.k(0, 8), writes=stat.k(0, 8))
            S.op("dve", lambda e: e.tensor_scalar(
                stat.ap(2, 1), stat.ap(3, 1), prm.t[:, PRM_WS:PRM_WS + 1],
                1.0 / 127.0, OP.mult, OP.mult),
                reads=stat.k(0, 8) + prm.k(PRM_WS, 1), writes=stat.k(0, 8))
            S.op("dve", lambda e, xs=xs: e.tensor_scalar(
                scf.ap(0, 1024), xs, stat.t[:, 4:5], 0.5, OP.mult, OP.add),
                reads=xk + stat.k(0, 8), writes=scf.k(0, 1024))
            floor_fixup(scf.ap(0, 1024), scf.k(0, 1024),
                        sci.ap(0, 1024), sci.k(0, 1024),
                        sci.ap(1024, 1024), sci.k(1024, 1024),
                        sci.ap(0, 1024), sci.k(0, 1024))
            S.op("dve", lambda e: e.tensor_copy(arena.ap(AQ, 1024), sci.ap(0, 1024)),
                 reads=sci.k(0, 1024), writes=arena.k(AQ, 1024))
            for dc in range(8):
                pap, pk = tp_bf(dc)
                S.op("pe", lambda e, dc=dc, pap=pap: e.transpose(
                    pap, arena.ap(AQ + dc * 128, 128), ident_b.ap(0, 128)),
                    reads=arena.k(AQ, 1024) + ident_b.k(),
                    writes=[pk])
                S.op("dve", lambda e, dc=dc, tc=tc, pap=pap: e.tensor_copy(
                    fmbuf.ap(dc * 512 + tc * 128, 128), pap),
                    reads=[pk],
                    writes=fmbuf.k(dc * 512 + tc * 128, 128))
            for ncd in range(2):
                base = tbase0 if ncd == 0 else tbase1
                wkeys = tkeys0 if ncd == 0 else tkeys1
                pslot = ncd % 3
                po = psB[:, pslot * 512:pslot * 512 + 512]
                for kc in range(8):
                    S.op("pe", lambda e, kc=kc, tc=tc, po=po, base=base: e.matmul(
                        po, fmbuf.ap(kc * 512 + tc * 128, 128),
                        wst.ap(base + kc * 512, 512),
                        start=(kc == 0), stop=(kc == 7)),
                        reads=fmbuf.k(kc * 512 + tc * 128, 128) + wkeys,
                        writes=[f"psB:{pslot}"])
                sb0 = (tc % 2) * 2048
                S.op("dve", lambda e, ncd=ncd, po=po: e.scalar_tensor_tensor(
                    scf.ap(3072 + ncd * 512, 512), po, stat.t[:, 2:3],
                    brep.ap(ncd * 512, 512), OP.mult, OP.add),
                    reads=[f"psB:{pslot}"] + stat.k(0, 8) + brep.k(ncd * 512, 512),
                    writes=scf.k(3072 + ncd * 512, 512))
                S.op("dve", lambda e, tc=tc, ncd=ncd, sb0=sb0: e.tensor_tensor(
                    scf.ap(sb0 + ncd * 512, 512), scf.ap(3072 + ncd * 512, 512),
                    x_tm.ap(tc * 1024 + ncd * 512, 512), OP.add),
                    reads=scf.k(3072 + ncd * 512, 512) +
                    x_tm.k(tc * 1024 + ncd * 512, 512),
                    writes=scf.k(sb0 + ncd * 512, 512))
            layer_norm((tc % 2) * 2048, tc)
        if dbg:
            dmain(tap("x2", [128, 4096]), x_tm.ap(0, 4096), reads=x_tm.k())

        if PHASE_LIMIT < 3:
            dst = out_d[seq, t0:t0 + TT, :].rearrange("(tc p) d -> p tc d", p=128)
            S.dma("sync", lambda e, dst=dst: e.dma_start(
                out=dst, in_=x_tm.ap(0, 4096).rearrange("p (tc d) -> p tc d",
                                                        d=1024)),
                reads=x_tm.k(), writes=[f"outp:{tile_idx}"])
            return
        for tc in range(NTC):
            xk = x_tm.k(tc * 1024, 1024)
            xs = x_tm.ap(tc * 1024, 1024)
            S.op("dve", lambda e, xs=xs: e.reduce_sum(stat.ap(0, 1), xs, axis=AX.X),
                 reads=xk, writes=stat.k(0, 8))
            S.op("sc", lambda e, xs=xs: e.activation(scf.ap(1024, 1024), xs,
                                                     AF.Square,
                                                     accum_out=stat.ap(1, 1)),
                 reads=xk, writes=scf.k(1024, 1024) + stat.k(0, 8))
            S.op("dve", lambda e: e.tensor_scalar(stat.ap(2, 1), stat.ap(0, 1),
                                                  cAP(CI1024), None, OP.mult),
                 reads=stat.k(0, 8) + CSTK, writes=stat.k(0, 8))
            S.op("dve", lambda e: e.tensor_tensor(stat.ap(3, 1), stat.ap(2, 1),
                                                  stat.ap(2, 1), OP.mult),
                 reads=stat.k(0, 8), writes=stat.k(0, 8))
            S.op("dve", lambda e: e.scalar_tensor_tensor(
                stat.ap(4, 1), stat.ap(1, 1), 1.0 / 1024.0, stat.ap(3, 1),
                OP.mult, OP.subtract),
                reads=stat.k(0, 8), writes=stat.k(0, 8))
            S.op("dve", lambda e: e.tensor_scalar(stat.ap(0, 1), stat.ap(4, 1),
                                                  1e-5, None, OP.add),
                 reads=stat.k(0, 8), writes=stat.k(0, 8))
            S.op("sc", lambda e: e.activation(stat.ap(4, 1), stat.ap(0, 1), AF.Ln),
                 reads=stat.k(0, 8), writes=stat.k(0, 8))
            S.op("sc", lambda e: e.activation(stat.ap(5, 1), stat.ap(4, 1), AF.Exp,
                                              bias=0.0, scale=cAP(CN05)),
                 reads=stat.k(0, 8) + CSTK, writes=stat.k(0, 8))
            S.op("dve", lambda e, xs=xs: e.tensor_scalar(
                scf.ap(3584, 1024), xs, stat.t[:, 2:3], stat.t[:, 5:6],
                OP.subtract, OP.mult),
                reads=xk + stat.k(0, 8), writes=scf.k(3584, 1024))
            for dc in range(8):
                pap, pk = tp_f32(dc)
                S.op("pe", lambda e, dc=dc, pap=pap: e.transpose(
                    pap, scf.ap(3584 + dc * 128, 128), ident_f.ap(0, 128)),
                    reads=scf.k(3584 + dc * 128, 128) + ident_f.k(),
                    writes=[pk])
                S.op("dve", lambda e, dc=dc, tc=tc, pap=pap: e.tensor_scalar(
                    fmbuf.ap(dc * 512 + tc * 128, 128), pap,
                    prm.t[:, PRM_GAM + dc:PRM_GAM + dc + 1],
                    prm.t[:, PRM_BET + dc:PRM_BET + dc + 1], OP.mult, OP.add),
                    reads=[pk] + prm.k(PRM_GAM, 16),
                    writes=fmbuf.k(dc * 512 + tc * 128, 128))
        if dbg:
            dmain(tap("xn_fm", [128, 4096], BF16), fmbuf.ap(0, 4096),
                  reads=fmbuf.k())

        ipw = {}
        def inproj_mm(oc, pslot):
            g = oc // 4
            if g not in ipw:
                ipw[g] = load_wblock(
                    vI[:, g * 4:(g + 1) * 4].rearrange("p oc kc m -> p (oc kc m)"),
                    4096, "sI")
            base, wkeys = ipw[g]
            base = base + (oc % 4) * 1024
            po = psB[:, pslot * 512:pslot * 512 + 512]
            for kc in range(8):
                S.op("pe", lambda e, kc=kc, po=po, base=base: e.matmul(
                    po, wst.ap(base + kc * 128, 128), fmbuf.ap(kc * 512, 512),
                    start=(kc == 0), stop=(kc == 7)),
                    reads=wkeys + fmbuf.k(kc * 512, 512), writes=[f"psB:{pslot}"])
            return po

        def conv_ops(cc, po, pkey):
            par = cc % 2
            oc = 16 + cc
            tail_lo = VTAILS + (seq * 16 + cc) * 6
            if par == 0:
                wlo, alo, blo, slo = 0, 512, 0, 2048
                wbase = vcs
            else:
                wlo, alo, blo, slo = 4608, 3072, 3584, 5632
                wbase = scf
            def vwin(off, n):
                return wbase.ap(wlo + off, n), wbase.k(wlo + off, n)
            ca, cak = scf.ap(alo, 512), scf.k(alo, 512)
            cb, cbk = scf.ap(blo, 512), scf.k(blo, 512)
            sc_, sck = scf.ap(slo, 512), scf.k(slo, 512)
            dwb_b = prm.t[:, PRM_DWB + cc:PRM_DWB + cc + 1].broadcast_to([128, 512])
            ops = []
            ops.append(lambda: S.op("dve", lambda e: e.tensor_copy(
                vwin(0, 6)[0], vcs.ap(tail_lo, 6)),
                reads=vcs.k(tail_lo, 6), writes=vwin(0, 6)[1]))
            ops.append(lambda: S.op("sc", lambda e: e.activation(
                vwin(6, 512)[0], po, AF.Identity,
                bias=prm.t[:, PRM_INB + oc:PRM_INB + oc + 1]),
                reads=[pkey] + prm.k(PRM_INB, 32) + CSTK,
                writes=vwin(6, 512)[1]))
            ops.append(lambda: S.op("dve", lambda e: e.scalar_tensor_tensor(
                ca, vwin(0, 512)[0], dwwt.t[:, cc * 7:cc * 7 + 1], dwb_b,
                OP.mult, OP.add),
                reads=vwin(0, 518)[1] + dwwt.k() + prm.k(PRM_DWB, 16),
                writes=cak))
            bufs = [(ca, cak), (cb, cbk)]
            for tp_ in range(1, 7):
                dsta, dkey = bufs[tp_ % 2]
                prev, pvkey = bufs[(tp_ + 1) % 2]
                ops.append(lambda tp_=tp_, dsta=dsta, dkey=dkey, prev=prev,
                           pvkey=pvkey: S.op(
                    "dve", lambda e: e.scalar_tensor_tensor(
                        dsta, vwin(tp_, 512)[0],
                        dwwt.t[:, cc * 7 + tp_:cc * 7 + tp_ + 1], prev,
                        OP.mult, OP.add),
                    reads=vwin(0, 518)[1] + dwwt.k() + pvkey, writes=dkey))
            ops.append(lambda: S.op("dve", lambda e: e.tensor_copy(
                arena.ap(AL1C + cc * 512, 512), ca),
                reads=cak, writes=arena.k(AL1C + cc * 512, 512)))
            ops.append(lambda: S.op("dve", lambda e: e.tensor_tensor_scan(
                sc_, prm.t[:, PRM_A + cc:PRM_A + cc + 1].broadcast_to([128, 512]),
                vwin(6, 512)[0],
                prm.t[:, PRM_CARRY + cc * 2 + seq:PRM_CARRY + cc * 2 + seq + 1],
                OP.mult, OP.add),
                reads=vwin(6, 512)[1] + prm.k(PRM_A, 16) +
                prm.k(PRM_CARRY + cc * 2 + seq, 1),
                writes=sck))
            ops.append(lambda: S.op("dve", lambda e: e.tensor_copy(
                prm.ap(PRM_CARRY + cc * 2 + seq, 1), scf.ap(slo + 511, 1)),
                reads=sck, writes=prm.k(PRM_CARRY + cc * 2 + seq, 1)))
            ops.append(lambda: S.op("dve", lambda e: e.tensor_copy(
                arena.ap(AL1S + cc * 512, 512), sc_),
                reads=sck, writes=arena.k(AL1S + cc * 512, 512)))
            ops.append(lambda: S.op("dve", lambda e: e.tensor_copy(
                vcs.ap(tail_lo, 6), vwin(512, 6)[0]),
                reads=vwin(0, 518)[1], writes=vcs.k(tail_lo, 6)))
            return ops

        import itertools as _it
        for gcc in range(8):
            chains = []
            for cc in (2 * gcc, 2 * gcc + 1):
                pslot = cc % 3
                po = inproj_mm(16 + cc, pslot)
                chains.append(conv_ops(cc, po, f"psB:{pslot}"))
            for pair in _it.zip_longest(*chains):
                for fn in pair:
                    if fn is not None:
                        fn()
        for cc in range(16):
            pslot = cc % 3
            po = inproj_mm(cc, pslot)
            S.op("sc", lambda e, po=po, cc=cc: e.activation(
                arena.ap(AL1G + cc * 512, 512), po, AF.Sigmoid,
                bias=prm.t[:, PRM_INB + cc:PRM_INB + cc + 1], scale=1.0),
                reads=[f"psB:{pslot}"] + prm.k(PRM_INB, 32),
                writes=arena.k(AL1G + cc * 512, 512))
        if dbg:
            dmain(tap("l1s", [128, 8192], BF16), arena.ap(AL1S, 8192),
                  reads=arena.k(AL1S, 8192))
            dmain(tap("l1c", [128, 8192], BF16), arena.ap(AL1C, 8192),
                  reads=arena.k(AL1C, 8192))
            dmain(tap("l1g", [128, 8192], BF16), arena.ap(AL1G, 8192),
                  reads=arena.k(AL1G, 8192))

        for oc in range(16):
            if oc % 2 == 0:
                mxgrp = load_wblock(
                    vM[:, oc:oc + 2].rearrange("p oc kc m -> p (oc kc m)"),
                    4096, "sM")
            base, wkeys = mxgrp
            base = base + (oc % 2) * 2048
            pslot = oc % 3
            po = psB[:, pslot * 512:pslot * 512 + 512]
            for kc in range(16):
                S.op("pe", lambda e, oc=oc, kc=kc, po=po, base=base: e.matmul(
                    po, wst.ap(base + kc * 128, 128),
                    arena.ap(AL1S + kc * 512, 512),
                    start=(kc == 0), stop=(kc == 15)),
                    reads=wkeys + arena.k(AL1S + kc * 512, 512),
                    writes=[f"psB:{pslot}"])
            S.op("dve", lambda e, oc=oc, po=po: e.scalar_tensor_tensor(
                scf.ap(0, 512), po, prm.t[:, PRM_MIXB + oc:PRM_MIXB + oc + 1],
                arena.ap(AL1C + oc * 512, 512), OP.add, OP.add),
                reads=[f"psB:{pslot}"] + prm.k(PRM_MIXB, 16) +
                arena.k(AL1C + oc * 512, 512),
                writes=scf.k(0, 512))
            S.op("dve", lambda e, oc=oc: e.tensor_tensor(
                arena.ap(AL1V + oc * 512, 512), scf.ap(0, 512),
                arena.ap(AL1G + oc * 512, 512), OP.mult),
                reads=scf.k(0, 512) + arena.k(AL1G + oc * 512, 512),
                writes=arena.k(AL1V + oc * 512, 512))
        obase0, okeys0 = load_wblock(vO[:, 0].rearrange("p kc n -> p (kc n)"),
                                     8192, "sOw")
        obase1, okeys1 = load_wblock(vO[:, 1].rearrange("p kc n -> p (kc n)"),
                                     8192, "sOw")
        for tc in range(NTC):
            for ncd in range(2):
                base = obase0 if ncd == 0 else obase1
                wkeys = okeys0 if ncd == 0 else okeys1
                pslot = (tc * 2 + ncd) % 3
                po = psB[:, pslot * 512:pslot * 512 + 512]
                for kc in range(16):
                    S.op("pe", lambda e, kc=kc, tc=tc, po=po, base=base: e.matmul(
                        po, arena.ap(AL1V + kc * 512 + tc * 128, 128),
                        wst.ap(base + kc * 512, 512),
                        start=(kc == 0), stop=False),
                        reads=arena.k(AL1V + kc * 512 + tc * 128, 128) + wkeys,
                        writes=[f"psB:{pslot}"])
                for bb in range(2):
                    S.op("pe", lambda e, po=po, ncd=ncd, bb=bb: e.matmul(
                        po, ones_bf.t[0:1, :],
                        brow.t[0:1, bb * 1024 + ncd * 512:
                               bb * 1024 + (ncd + 1) * 512],
                        start=False, stop=(bb == 1)),
                        reads=ones_bf.k() + brow.k(bb * 1024 + ncd * 512, 512),
                        writes=[f"psB:{pslot}"])
                sb0 = (tc % 2) * 2048
                S.op("dve", lambda e, tc=tc, ncd=ncd, po=po, sb0=sb0:
                     e.scalar_tensor_tensor(
                         scf.ap(sb0 + ncd * 512, 512),
                         x_tm.ap(tc * 1024 + ncd * 512, 512), 2.0, po,
                         OP.mult, OP.add),
                     reads=x_tm.k(tc * 1024 + ncd * 512, 512) + [f"psB:{pslot}"],
                     writes=scf.k(sb0 + ncd * 512, 512))
            layer_norm((tc % 2) * 2048, tc)
        if dbg:
            dmain(tap("x3", [128, 4096]), x_tm.ap(0, 4096), reads=x_tm.k())

        if PHASE_LIMIT < 4:
            dst = out_d[seq, t0:t0 + TT, :].rearrange("(tc p) d -> p tc d", p=128)
            S.dma("sync", lambda e, dst=dst: e.dma_start(
                out=dst, in_=x_tm.ap(0, 4096).rearrange("p (tc d) -> p tc d",
                                                        d=1024)),
                reads=x_tm.k(), writes=[f"outp:{tile_idx}"])
            return
        transpose_tm_to_fm(x_tm, lambda tc, dc: x_tm.k(tc * 1024 + dc * 128, 128))
        def pwl_ops(oc, po, pkey):
            fb = (oc % 3) * 2560
            ib = (oc % 3) * 512
            A = scf.ap(fb + 0, 512); Ak = scf.k(fb + 0, 512)
            B = scf.ap(fb + 512, 512); Bk = scf.k(fb + 512, 512)
            C = scf.ap(fb + 1024, 512); Ck = scf.k(fb + 1024, 512)
            FR = scf.ap(fb + 1536, 256).bitcast(BF16); FRk = scf.k(fb + 1536, 256)
            MM = scf.ap(fb + 1792, 256).bitcast(BF16); MMk = scf.k(fb + 1792, 256)
            G0 = scf.ap(fb + 2048, 256).bitcast(BF16); G0k = scf.k(fb + 2048, 256)
            G1 = scf.ap(fb + 2304, 256).bitcast(BF16); G1k = scf.k(fb + 2304, 256)
            Ci = sci.ap(ib + 0, 512); Cik = sci.k(ib + 0, 512)
            return [
                lambda: S.op("sc", lambda e: e.activation(
                    A, po, AF.Identity,
                    bias=prm.t[:, PRM_B1S + oc:PRM_B1S + oc + 1],
                    scale=cAP(C256)),
                    reads=[pkey] + prm.k(PRM_B1S, 32) + CSTK, writes=Ak),
                lambda: S.op("sc", lambda e: e.activation(Ci, A, AF.Identity),
                             reads=Ak, writes=Cik),
                lambda: S.op("dve", lambda e: e.tensor_tensor(MM, Ci, A,
                                                              OP.is_le),
                             reads=Cik + Ak, writes=MMk),
                lambda: S.op("dve", lambda e: e.scalar_tensor_tensor(
                    B, Ci, 1, MM, OP.subtract, OP.add),
                    reads=Cik + MMk, writes=Bk),
                lambda: S.op("sc", lambda e: e.activation(
                    C, A, AF.Identity, bias=0.0, scale=cAP(CI64)),
                    reads=Ak + CSTK, writes=Ck),
                lambda: S.op("sc", lambda e: e.activation(Ci, C, AF.Identity),
                             reads=Ck, writes=Cik),
                lambda: S.op("dve", lambda e: e.tensor_tensor(MM, Ci, C,
                                                              OP.is_le),
                             reads=Cik + Ck, writes=MMk),
                lambda: S.op("dve", lambda e: e.scalar_tensor_tensor(
                    A, Ci, 1, MM, OP.subtract, OP.add),
                    reads=Cik + MMk, writes=Ak),
                lambda: S.op("dve", lambda e: e.scalar_tensor_tensor(
                    FR, B, 1.0 / 64.0, A, OP.mult, OP.subtract),
                    reads=Bk + Ak, writes=FRk),
                lambda: S.op("sc", lambda e: e.activation(
                    G0, A, AF.Gelu, bias=cAP(C0), scale=cAP(C025)),
                    reads=Ak + CSTK, writes=G0k),
                lambda: S.op("sc", lambda e: e.activation(
                    G1, A, AF.Gelu, bias=cAP(C025), scale=cAP(C025)),
                    reads=Ak + CSTK, writes=G1k),
                lambda: S.op("dve", lambda e: e.tensor_tensor(MM, G1, G0,
                                                              OP.subtract),
                             reads=G1k + G0k, writes=MMk),
                lambda: S.op("dve", lambda e: e.tensor_tensor(G1, FR, MM,
                                                              OP.mult),
                             reads=FRk + MMk, writes=G1k),
                lambda: S.op("dve", lambda e: e.tensor_tensor(FR, G1, G0,
                                                              OP.add),
                             reads=G1k + G0k, writes=FRk),
                lambda: S.op("sc", lambda e: e.activation(
                    C, FR, AF.Identity, bias=cAP(C05), scale=cAP(C256)),
                    reads=FRk + CSTK, writes=Ck),
                lambda: S.op("sc", lambda e: e.activation(Ci, C, AF.Identity),
                             reads=Ck, writes=Cik),
                lambda: S.op("dve", lambda e: e.tensor_tensor(MM, Ci, C,
                                                              OP.is_le),
                             reads=Cik + Ck, writes=MMk),
                lambda: S.op("dve", lambda e: e.scalar_tensor_tensor(
                    A, Ci, 1, MM, OP.subtract, OP.add),
                    reads=Cik + MMk, writes=Ak),
                lambda: S.op("dve", lambda e: e.tensor_single_scalar(
                    Ci, B, 1024.0, OP.is_gt),
                    reads=Bk, writes=Cik),
                lambda: S.op("dve", lambda e: e.copy_predicated(A, Ci, B),
                             reads=Cik + Bk + Ak, writes=Ak),
                lambda: S.op("sc", lambda e: e.activation(
                    arena.ap(AH + oc * 512, 512), A, AF.Identity,
                    bias=0.0, scale=cAP(CI256)),
                    reads=Ak + CSTK, writes=arena.k(AH + oc * 512, 512)),
            ]

        w1grp = None
        import itertools as _it3
        W2C = 4096
        W2A_BANKS = [(psA[:, 0:512], "psA:0"), (psA[:, 512:1024], "psA:1"),
                     (psA[:, 1024:1536], "psA:2"), (psT[:, 0:512], "psT")]

        def w2a_mms(oc):
            for tc in range(NTC):
                bank, bkey = W2A_BANKS[tc]
                S.op("pe", lambda e, oc=oc, tc=tc, bank=bank: e.matmul(
                    bank, arena.ap(AH + oc * 512 + tc * 128, 128),
                    wst.ap(W2C + (oc % 8) * 512, 512),
                    start=(oc == 0), stop=False),
                    reads=arena.k(AH + oc * 512 + tc * 128, 128) +
                    wst.k(W2C + (oc % 8) * 512, 512),
                    writes=[bkey])

        gi = 0
        prev_grp = []
        nd1_ld = [None, None]
        while gi < 32:
            grp = list(range(gi, min(gi + 3, 32)))
            gi += len(grp)
            for oc in grp:
                dmain(wst.ap(W2C + (oc % 8) * 512, 512), vW2[:, 0, oc, :],
                      reads=["sW2"], writes=wst.k(W2C + (oc % 8) * 512, 512))
            chains = []
            for oc in grp:
                if oc % 4 == 0:
                    w1grp = load_wblock(
                        vW1[:, oc:oc + 4].rearrange("p oc kc m -> p (oc kc m)"),
                        4096, "sW1")
                base, wkeys = w1grp
                base = base + (oc % 4) * 1024
                pslot = oc % 3
                po = psB[:, pslot * 512:pslot * 512 + 512]
                pkey = f"psB:{pslot}"
                for kc in range(8):
                    S.op("pe", lambda e, kc=kc, po=po, base=base: e.matmul(
                        po, wst.ap(base + kc * 128, 128),
                        fmbuf.ap(kc * 512, 512),
                        start=(kc == 0), stop=(kc == 7)),
                        reads=wkeys + fmbuf.k(kc * 512, 512), writes=[pkey])
                chains.append(pwl_ops(oc, po, pkey))
            for oc in prev_grp:
                w2a_mms(oc)
            prev_grp = grp
            for stg in _it3.zip_longest(*chains):
                for fn in stg:
                    if fn is not None:
                        fn()
        for oc in prev_grp:
            w2a_mms(oc)
        nd1_ld[0] = load_wblock(
            vW2[:, 1, 0:16, :].rearrange("p kc n -> p (kc n)"), 8192, "sW2")
        nd1_ld[1] = load_wblock(
            vW2[:, 1, 16:32, :].rearrange("p kc n -> p (kc n)"), 8192, "sW2")
        for tc in range(NTC):
            bank, bkey = W2A_BANKS[tc]
            for bb in range(2):
                S.op("pe", lambda e, bank=bank, bb=bb: e.matmul(
                    bank, ones_bf.t[0:1, :],
                    brow.t[0:1, 2048 + bb * 1024:2048 + bb * 1024 + 512],
                    start=False, stop=(bb == 1)),
                    reads=ones_bf.k() + brow.k(2048 + bb * 1024, 512),
                    writes=[bkey])
        if dbg:
            dmain(tap("hbf", [128, 16384], BF16), arena.ap(AH, 16384),
                  reads=arena.k(AH, 16384))

        def w2_evac(po, pkey, tc, ncd):
            j = (tc + ncd) % 2
            us, uk = scf.ap(1024 + j * 512, 512), scf.k(1024 + j * 512, 512)
            ci, cik = sci.ap(j * 1024, 512), sci.k(j * 1024, 512)
            mi, mik = sci.ap(j * 1024 + 512, 512), sci.k(j * 1024 + 512, 512)
            S.op("sc", lambda e, po=po, us=us: e.activation(
                us, po, AF.Identity, bias=cAP(C05), scale=cAP(C256)),
                reads=[pkey] + CSTK, writes=uk)
            floor_fixup(us, uk, ci, cik, mi, mik, ci, cik)
            S.op("dve", lambda e, tc=tc, ncd=ncd, ci=ci: e.tensor_scalar(
                scf.ap(3584 + tc * 1024 + ncd * 512, 512), ci,
                cAP(CI256), None, OP.mult),
                reads=cik + CSTK,
                writes=scf.k(3584 + tc * 1024 + ncd * 512, 512))

        for tc in range(NTC):
            bank, bkey = W2A_BANKS[tc]
            w2_evac(bank, bkey, tc, 0)
        bA, kA = nd1_ld[0]
        bB, kB = nd1_ld[1]
        for tc in range(NTC):
            pslot = tc % 3
            po = psB[:, pslot * 512:pslot * 512 + 512]
            pkey = f"psB:{pslot}"
            for kc in range(32):
                base = bA if kc < 16 else bB
                wk_ = kA if kc < 16 else kB
                off = base + (kc % 16) * 512
                S.op("pe", lambda e, kc=kc, tc=tc, po=po, off=off: e.matmul(
                    po, arena.ap(AH + kc * 512 + tc * 128, 128),
                    wst.ap(off, 512),
                    start=(kc == 0), stop=False),
                    reads=arena.k(AH + kc * 512 + tc * 128, 128) + wk_,
                    writes=[pkey])
            for bb in range(2):
                S.op("pe", lambda e, po=po, bb=bb: e.matmul(
                    po, ones_bf.t[0:1, :],
                    brow.t[0:1, 2048 + bb * 1024 + 512:
                           2048 + bb * 1024 + 1024],
                    start=False, stop=(bb == 1)),
                    reads=ones_bf.k() + brow.k(2048 + bb * 1024 + 512, 512),
                    writes=[pkey])
            w2_evac(po, pkey, tc, 1)
        for tc in range(NTC):
            for ncd in range(2):
                elo = 3584 + tc * 1024 + ncd * 512
                sb0 = (tc % 2) * 2048
                S.op("dve", lambda e, elo=elo, tc=tc, ncd=ncd, sb0=sb0:
                     e.tensor_tensor(
                         scf.ap(sb0 + ncd * 512, 512), scf.ap(elo, 512),
                         x_tm.ap(tc * 1024 + ncd * 512, 512), OP.add),
                     reads=scf.k(elo, 512) + x_tm.k(tc * 1024 + ncd * 512, 512),
                     writes=scf.k(sb0 + ncd * 512, 512))
            layer_norm((tc % 2) * 2048, tc)
        if dbg:
            dmain(tap("x4", [128, 4096]), x_tm.ap(0, 4096), reads=x_tm.k())

        dst = out_d[seq, t0:t0 + TT, :].rearrange("(tc p) d -> p tc d", p=128)
        S.dma("sync", lambda e, dst=dst: e.dma_start(
            out=dst, in_=x_tm.ap(0, 4096).rearrange("p (tc d) -> p tc d", d=1024)),
            reads=x_tm.k(), writes=[f"outp:{tile_idx}"])

    n_seq_emit, n_tt_emit = EMIT_TILES or (SEQ_PER_CORE, TILES_PER_SEQ)
    tile_idx = 0
    for seq in range(n_seq_emit):
        if seq > 0:
            S.op("dve", lambda e: e.memset(ret_state.ap(0, 4096), 0.0),
                 reads=ret_state.k(), writes=ret_state.k())
        for tt in range(n_tt_emit):
            emit_tile(seq, tt, tile_idx)
            tile_idx += 1

    S.op("sync", None, reads=[f"outp:{i}" for i in range(tile_idx)])

    S.finalize(nc, es)
    with es:
        blk = es.enter_context(nc.Block())
        blk.sync(lambda eng: S.emit("sync", eng))
        blk.vector(lambda eng: S.emit("dve", eng))
        blk.scalar(lambda eng: S.emit("sc", eng))
        blk.tensor(lambda eng: S.emit("pe", eng))
        blk.gpsimd(lambda eng: S.emit("gp", eng))
    return nc


def _xpos_tables():
    d = np.arange(0, 1024, 2, dtype=np.float64)
    base_scale = (d + 0.4 * 1024.0) / (1.4 * 1024.0)
    pos = np.arange(T, dtype=np.float64)
    scale = base_scale[None, :] ** (pos[:, None] / 512.0)
    inv_freq = 1.0 / 10000.0 ** (np.arange(512, dtype=np.float64) / 512.0)
    sinu = pos[:, None] * inv_freq[None, :]
    sin, cos = np.sin(sinu), np.cos(sinu)
    def expand(tbl):
        out = np.empty((1024, T), np.float64)
        out[0::2] = tbl.T
        out[1::2] = tbl.T
        return out
    sgn = np.empty((1024, 1), np.float64)
    sgn[0::2] = -1.0
    sgn[1::2] = 1.0
    cq = expand(cos * scale)
    sq = expand(sin * scale) * sgn
    ck = expand(cos / scale)
    sk = expand(sin / scale) * sgn
    def tile(tbl):
        return np.ascontiguousarray(
            tbl.reshape(8, 128, TILES_PER_SEQ, TT).transpose(2, 0, 1, 3)
        ).astype(np.float32)
    return tile(cq), tile(sq), tile(ck), tile(sk)


_PROG_CACHE = {}


def kernel(L0_attn_hidden_states, L0_attn_W_qkv, L0_attn_W_g, L0_attn_W_proj,
           L0_attn_decay, L0_ffn_packed_weight, L0_ffn_weight_scale, L0_ffn_bias,
           L1_attn_norm_gamma, L1_attn_norm_beta, L1_attn_in_w, L1_attn_in_b,
           L1_attn_dw_w, L1_attn_dw_b, L1_attn_log_decay, L1_attn_mix_w,
           L1_attn_mix_b, L1_attn_out_w, L1_attn_out_b, L1_ffn_w1, L1_ffn_b1,
           L1_ffn_w2, L1_ffn_b2):
    x = np.ascontiguousarray(L0_attn_hidden_states, dtype=np.float32)
    cq, sq, ck, sk = _xpos_tables()
    consts = np.tile(np.array(CONSTS, np.float32)[None, :], (128, 1))
    idm = np.eye(128, dtype=np.float32)
    key = ("prog", DEBUG_TAPS, EMIT_TILES, PHASE_LIMIT, SUBPHASE)
    if key not in _PROG_CACHE:
        _PROG_CACHE[key] = build_program(DEBUG_TAPS)
    nc = _PROG_CACHE[key]
    common = {
        "wqkv": np.ascontiguousarray(L0_attn_W_qkv, np.float32),
        "wg": np.ascontiguousarray(L0_attn_W_g, np.float32),
        "wproj": np.ascontiguousarray(L0_attn_W_proj, np.float32),
        "decay": np.ascontiguousarray(L0_attn_decay, np.float32).reshape(1, H),
        "packed": np.ascontiguousarray(L0_ffn_packed_weight,
                                       np.int32).reshape(256, 1024),
        "wscale": np.ascontiguousarray(L0_ffn_weight_scale,
                                       np.float32).reshape(1, 1),
        "bias0": np.ascontiguousarray(L0_ffn_bias, np.float32).reshape(1, D),
        "gamma": np.ascontiguousarray(L1_attn_norm_gamma, np.float32),
        "beta": np.ascontiguousarray(L1_attn_norm_beta, np.float32),
        "in_w": np.ascontiguousarray(L1_attn_in_w, np.float32),
        "in_b": np.ascontiguousarray(L1_attn_in_b, np.float32),
        "dw_w": np.ascontiguousarray(L1_attn_dw_w, np.float32),
        "dw_b": np.ascontiguousarray(L1_attn_dw_b, np.float32),
        "log_decay": np.ascontiguousarray(L1_attn_log_decay, np.float32),
        "mix_w": np.ascontiguousarray(L1_attn_mix_w, np.float32),
        "mix_b": np.ascontiguousarray(L1_attn_mix_b, np.float32),
        "out_w": np.ascontiguousarray(L1_attn_out_w, np.float32),
        "out_b": np.ascontiguousarray(L1_attn_out_b, np.float32).reshape(1, D),
        "w1": np.ascontiguousarray(L1_ffn_w1, np.float32),
        "b1": np.ascontiguousarray(L1_ffn_b1, np.float32),
        "w2": np.ascontiguousarray(L1_ffn_w2, np.float32),
        "b2": np.ascontiguousarray(L1_ffn_b2, np.float32).reshape(1, D),
        "consts": consts, "cq_t": cq, "sq_t": sq, "ck_t": ck, "sk_t": sk,
        "idm": idm,
    }
    in_maps = []
    for c in range(N_CORES):
        m = dict(common)
        m["xin"] = np.ascontiguousarray(x[c * SEQ_PER_CORE:(c + 1) * SEQ_PER_CORE])
        in_maps.append(m)
    res = run_bass_kernel_spmd(nc, in_maps, core_ids=list(range(N_CORES)))
    if DEBUG_TAPS:
        kernel._last_results = res.results
    out = np.concatenate([r["out"] for r in res.results], axis=0)
    return out.reshape(B, T, D)


# version 12 (final)
# speedup vs baseline: 1.0899x; 1.0265x over previous
import numpy as np
from contextlib import ExitStack

import concourse.bass as bass
import concourse.mybir as mybir
from concourse.bass_utils import run_bass_kernel_spmd

F32 = mybir.dt.float32
BF16 = mybir.dt.bfloat16
I32 = mybir.dt.int32
AF = mybir.ActivationFunctionType
OP = mybir.AluOpType
AX = mybir.AxisListType

DEBUG_TAPS = False
PHASE_LIMIT = 4
SUBPHASE = 9
EMIT_TILES = None
N_CORES = 8
B, T, D = 16, 4096, 1024
SEQ_PER_CORE = 2
TT = 512
NTC = TT // 128
TILES_PER_SEQ = T // TT
H = 8
HC = 2048
FF = 4096
INV_SQRT_DK = float(1.0 / np.sqrt(128.0))

CONSTS = [256.0, 1.0 / 256.0, 1.0 / 64.0, 0.25, -64.0, -63.75, 0.5, 1e-5, 127.0,
          1.0 / 1024.0, 64.0, 2.0, 16.0, -0.5, 1.0, 0.0]
C256, CI256, CI64, C025, CN64, CN6375, C05, CEPS, C127, CI1024, C64, C2, C16, CN05, C1, C0 = range(16)


class Sched:
    ENGS = ("pe", "dve", "sc", "gp", "sync")

    def __init__(self):
        self.ops = {e: [] for e in self.ENGS}
        self.W = {}
        self.R = {}

    def _record(self, eng, fn, reads, writes, extra, is_dma):
        deps = []
        for b in reads:
            t = self.W.get(b)
            if t is not None:
                deps.append(t)
        for b in writes:
            deps.extend(self.R.get(b, ()))
            t = self.W.get(b)
            if t is not None:
                deps.append(t)
        deps.extend(extra)
        tok = (eng, len(self.ops[eng]))
        self.ops[eng].append(dict(fn=fn, deps=set(deps), is_dma=is_dma,
                                  awaited=False))
        for b in reads:
            self.R.setdefault(b, []).append(tok)
        for b in writes:
            self.W[b] = tok
            self.R[b] = []
        return tok

    def op(self, eng, fn, reads=(), writes=(), extra=()):
        return self._record(eng, fn, reads, writes, extra, False)

    def dma(self, eng, fn, reads=(), writes=(), extra=()):
        return self._record(eng, fn, reads, writes, extra, True)

    NPOOL = 24

    def finalize(self, nc, es):
        self.sems = {}
        for e in self.ENGS:
            if any(not o["is_dma"] for o in self.ops[e]):
                self.sems[e] = es.enter_context(nc.semaphore(f"s_{e}"))
        for p in range(self.NPOOL):
            self.sems[f"d{p}"] = es.enter_context(nc.semaphore(f"d{p}"))
        SELF_SYNC = ("dve", "sc", "gp")
        for e in self.ENGS:
            for o in self.ops[e]:
                for (de, di) in o["deps"]:
                    dep = self.ops[de][di]
                    if dep["is_dma"] or de != e or de in SELF_SYNC:
                        dep["awaited"] = True
        self.val = {}
        pool_use = [0] * self.NPOOL
        dma_ord = 0
        for e in self.ENGS:
            cc_ = 0
            for i, o in enumerate(self.ops[e]):
                if o["is_dma"]:
                    p = dma_ord % self.NPOOL
                    dma_ord += 1
                    pool_use[p] += 1
                    sem_name = f"d{p}"
                    o["pre_wait"] = (sem_name, 16 * (pool_use[p] - 1))
                    o["inc"] = (sem_name, 16)
                    self.val[(e, i)] = (sem_name, 16 * pool_use[p])
                elif o["awaited"]:
                    cc_ += 1
                    o["inc"] = (e, 1)
                    self.val[(e, i)] = (e, cc_)
                else:
                    o["inc"] = None

    def emit(self, ename, eng):
        clock = {}
        for i, o in enumerate(self.ops[ename]):
            want = {}
            for (de, di) in sorted(o["deps"]):
                dep = self.ops[de][di]
                if (not dep["is_dma"] and de == ename
                        and de not in ("dve", "sc", "gp")):
                    continue
                sem_name, v = self.val[(de, di)]
                if v > want.get(sem_name, 0):
                    want[sem_name] = v
            pw = o.get("pre_wait")
            if pw is not None and pw[1] > want.get(pw[0], 0):
                want[pw[0]] = pw[1]
            for sem_name, v in sorted(want.items()):
                if v > clock.get(sem_name, 0):
                    eng.wait_ge(self.sems[sem_name], v)
                    clock[sem_name] = v
            inst = o["fn"](eng) if o["fn"] is not None else None
            if o["inc"] is not None:
                assert inst is not None, f"awaited op with no inst on {ename}:{i}"
                sem_name, amt = o["inc"]
                inst.then_inc(self.sems[sem_name], amt)


class TBuf:
    def __init__(self, t, nm, blk=256):
        self.t = t
        self.nm = nm
        self.blk = blk

    def ap(self, lo, n):
        return self.t[:, lo:lo + n]

    def k(self, lo=0, n=None):
        if n is None:
            n = self.t.shape[1] - lo
        b = self.blk
        return [f"{self.nm}:{i}" for i in range(lo // b, (lo + n + b - 1) // b)]


def build_program(debug_taps=False):
    nc = bass.Bass("TRN2", target_bir_lowering=False, detect_race_conditions=False)
    S = Sched()
    es = ExitStack()

    din = {}
    def dram_in(name, shape, dt=F32):
        din[name] = nc.dram_tensor(name, list(shape), dt, kind="ExternalInput").ap()
        return din[name]

    xin = dram_in("xin", [SEQ_PER_CORE, T, D])
    wqkv = dram_in("wqkv", [D, 4096])
    wg_in = dram_in("wg", [D, 2048])
    wproj_in = dram_in("wproj", [2048, D])
    decay_in = dram_in("decay", [1, H])
    packed_in = dram_in("packed", [256, 1024], I32)
    ws_in = dram_in("wscale", [1, 1])
    bias0_in = dram_in("bias0", [1, D])
    gamma_in = dram_in("gamma", [D])
    beta_in = dram_in("beta", [D])
    inw_in = dram_in("in_w", [D, 2 * HC])
    inb_in = dram_in("in_b", [2 * HC])
    dww_in = dram_in("dw_w", [HC, 7])
    dwb_in = dram_in("dw_b", [HC])
    logd_in = dram_in("log_decay", [HC])
    mixw_in = dram_in("mix_w", [HC, HC])
    mixb_in = dram_in("mix_b", [HC])
    outw_in = dram_in("out_w", [HC, D])
    outb_in = dram_in("out_b", [1, D])
    w1_in = dram_in("w1", [D, FF])
    b1_in = dram_in("b1", [FF])
    w2_in = dram_in("w2", [FF, D])
    b2_in = dram_in("b2", [1, D])
    consts_in = dram_in("consts", [128, len(CONSTS)])
    cq_in = dram_in("cq_t", [TILES_PER_SEQ, 8, 128, TT])
    sq_in = dram_in("sq_t", [TILES_PER_SEQ, 8, 128, TT])
    ck_in = dram_in("ck_t", [TILES_PER_SEQ, 8, 128, TT])
    sk_in = dram_in("sk_t", [TILES_PER_SEQ, 8, 128, TT])
    idm_in = dram_in("idm", [128, 128])

    out_d = nc.dram_tensor("out", [SEQ_PER_CORE, T, D], F32, kind="ExternalOutput").ap()

    taps = {}
    def tap(name, shape, dt=F32):
        if name not in taps:
            taps[name] = nc.dram_tensor("tap_" + name, list(shape), dt,
                                        kind="ExternalOutput").ap()
        return taps[name]

    def scr(name, n_el, dt=BF16):
        return nc.dram_tensor(name, [n_el], dt, kind="Internal").ap()

    sQ = scr("sQ", 128 * 8 * 8 * 128)
    sK = scr("sK", 128 * 8 * 8 * 128)
    sV = scr("sV", 128 * 4 * 8 * 512)
    sGw = scr("sGw", 128 * 4 * 8 * 512)
    sP = scr("sP", 128 * 2 * 16 * 512)
    sTn = scr("sTn", 128 * 2 * 8 * 512)
    sI = scr("sI", 128 * 32 * 8 * 128)
    sM = scr("sM", 128 * 16 * 16 * 128)
    sOw = scr("sOw", 128 * 2 * 16 * 512)
    sW1 = scr("sW1", 128 * 32 * 8 * 128)
    sW2 = scr("sW2", 128 * 2 * 32 * 512)

    def sb(nm, shape, dt=F32, blk=256):
        t = es.enter_context(nc.sbuf_tensor(nm, list(shape), dt))
        return TBuf(t, nm, blk)

    x_tm = sb("x_tm", [128, NTC * 1024])
    arena = sb("arena", [128, 32768], BF16, blk=512)
    fmbuf = sb("fmbuf", [128, 8 * 512], BF16, blk=512)
    scf = sb("scf", [128, 7680])
    sci = sb("sci", [128, 2048], I32, blk=512)
    ret_state = sb("ret_state", [128, 16 * 256])
    ret_sm = sb("ret_sm", [128, 1536], BF16, blk=128)
    masks = sb("masks", [128, H * 512], BF16, blk=512)
    rotb = sb("rotb", [128, 1024], F32, blk=512)
    wst = sb("wst", [128, 16384], BF16, blk=8192)
    brep = sb("brep", [128, 1024], F32, blk=1024)
    ones_bf = sb("ones_bf", [1, 128], BF16, blk=128)
    brow = sb("brow", [1, 4096], BF16, blk=1024)
    prm = sb("prm", [128, 232], F32, blk=8)
    cst = sb("cst", [128, len(CONSTS)], F32, blk=16)
    stat = sb("stat", [128, 64], F32, blk=8)
    ident_f = sb("ident_f", [128, 128], F32, blk=128)
    ident_b = sb("ident_b", [128, 128], BF16, blk=128)
    vcs = sb("vcs", [128, 712], F32, blk=64)
    dwwt = sb("dwwt", [128, 112], F32, blk=112)

    PRM_LND, PRM_D256, PRM_WR, PRM_WK, PRM_A, PRM_CARRY = 0, 8, 16, 32, 48, 64
    PRM_INB, PRM_MIXB, PRM_DWB = 96, 128, 144
    PRM_GAM, PRM_BET, PRM_B1S, PRM_WS = 160, 168, 176, 224
    VTAILS = 520

    psA = es.enter_context(nc.psum_tensor("psA", [128, 1536], F32))
    psB = es.enter_context(nc.psum_tensor("psB", [128, 1536], F32))
    psT = es.enter_context(nc.psum_tensor("psT", [128, 512], F32))
    psTb = es.enter_context(nc.psum_tensor("psTb", [128, 1024], BF16))
    TPF32 = [(psT, 0, "psT"), (psA, 0, "psA:0"), (psA, 512, "psA:1"),
             (psA, 1024, "psA:2")]
    TPBF = [(psTb, 0, "psTb"), (psB, 1024, "psB:2"), (psB, 0, "psB:0"),
            (psB, 512, "psB:1")]
    def tp_f32(i):
        t, off, k = TPF32[i % 4]
        return t[:, off:off + 128], k
    def tp_bf(i):
        t, off, k = TPBF[i % 4]
        if k == "psTb":
            return t[:, off:off + 128], k
        return t[:, off:off + 64].bitcast(BF16), k

    cAP = lambda i: cst.t[:, i:i + 1]
    CSTK = cst.k(0)

    def dmain(dst_ap, src_ap, reads=(), writes=(), slow=False):
        return S.dma("sync", lambda e, d=dst_ap, s=src_ap, sl=slow: e.dma_start(
            out=d, in_=s, allow_slow_non_contiguous=sl),
            reads=reads, writes=writes)

    def repl_row(buf, lo, hi):
        n = 1
        while n < 128:
            m = min(n, 128 - n)
            S.dma("sync", lambda e, n=n, m=m: e.dma_start(
                out=buf.t[n:n + m, lo:hi], in_=buf.t[0:m, lo:hi]),
                reads=buf.k(lo, hi - lo), writes=buf.k(lo, hi - lo))
            n += m

    wslot = [0]
    def load_wblock(src3d_ap, n_el, src_key):
        slot = wslot[0]
        wslot[0] ^= 1
        base = slot * 8192
        assert n_el <= 8192
        dmain(wst.ap(base, n_el), src3d_ap, reads=[src_key],
              writes=wst.k(base, n_el))
        return base, wst.k(base, n_el)

    vQ = sQ.rearrange("(p oc kc m) -> p oc kc m", p=128, oc=8, kc=8)
    vK = sK.rearrange("(p oc kc m) -> p oc kc m", p=128, oc=8, kc=8)
    vV = sV.rearrange("(p ncd kc n) -> p ncd kc n", p=128, ncd=4, kc=8)
    vG = sGw.rearrange("(p ncd kc n) -> p ncd kc n", p=128, ncd=4, kc=8)
    vP = sP.rearrange("(p ncd kc n) -> p ncd kc n", p=128, ncd=2, kc=16)
    vTn = sTn.rearrange("(p ncd kc n) -> p ncd kc n", p=128, ncd=2, kc=8)
    vI = sI.rearrange("(p oc kc m) -> p oc kc m", p=128, oc=32, kc=8)
    vM = sM.rearrange("(p oc kc m) -> p oc kc m", p=128, oc=16, kc=16)
    vO = sOw.rearrange("(p ncd kc n) -> p ncd kc n", p=128, ncd=2, kc=16)
    vW1 = sW1.rearrange("(p oc kc m) -> p oc kc m", p=128, oc=32, kc=8)
    vW2 = sW2.rearrange("(p ncd kc n) -> p ncd kc n", p=128, ncd=2, kc=32)

    dmain(cst.ap(0, len(CONSTS)), consts_in, writes=CSTK)
    dmain(ident_f.ap(0, 128), idm_in, writes=ident_f.k())
    S.op("dve", lambda e: e.tensor_copy(ident_b.ap(0, 128), ident_f.ap(0, 128)),
         reads=ident_f.k(), writes=ident_b.k())

    def prep_weight(src_ap, n_rows, n_cols, view, kind, scr_key):
        n_kc = n_rows // 128
        for kc in range(n_kc):
            dmain(scf.ap(0, n_cols), src_ap[kc * 128:(kc + 1) * 128, :],
                  writes=scf.k(0, n_cols))
            S.op("dve", lambda e, c=n_cols: e.tensor_copy(arena.ap(0, c),
                                                          scf.ap(0, c)),
                 reads=scf.k(0, n_cols), writes=arena.k(0, n_cols))
            if kind == "lhsT":
                dst = view[:, :, kc, :]
                src = arena.ap(0, n_cols).rearrange("p (oc m) -> p oc m", m=128)
            else:
                dst = view[:, :, kc, :]
                src = arena.ap(0, n_cols).rearrange("p (ncd n) -> p ncd n", n=512)
            S.dma("sync", lambda e, dst=dst, src=src: e.dma_start(out=dst, in_=src),
                  reads=arena.k(0, n_cols), writes=[scr_key])

    prep_weight(wqkv[:, 0:1024], 1024, 1024, vQ, "lhsT", "sQ")
    prep_weight(wqkv[:, 1024:2048], 1024, 1024, vK, "lhsT", "sK")
    prep_weight(wqkv[:, 2048:4096], 1024, 2048, vV, "rhs", "sV")
    prep_weight(wg_in, 1024, 2048, vG, "rhs", "sGw")
    prep_weight(wproj_in, 2048, 1024, vP, "rhs", "sP")
    prep_weight(inw_in, 1024, 4096, vI, "lhsT", "sI")
    prep_weight(mixw_in, 2048, 2048, vM, "lhsT", "sM")
    prep_weight(outw_in, 2048, 1024, vO, "rhs", "sOw")
    prep_weight(w1_in, 1024, 4096, vW1, "lhsT", "sW1")
    prep_weight(w2_in, 4096, 1024, vW2, "rhs", "sW2")

    def floor_fixup(u_ap, u_keys, c_ap, c_keys, m_ap, m_keys, f_ap, f_keys):
        S.op("sc", lambda e: e.activation(c_ap, u_ap, AF.Identity),
             reads=u_keys, writes=c_keys)
        S.op("dve", lambda e: e.tensor_tensor(m_ap, c_ap, u_ap, OP.is_le),
             reads=c_keys + u_keys, writes=m_keys)
        S.op("dve", lambda e: e.scalar_tensor_tensor(f_ap, c_ap, 1, m_ap,
                                                     OP.subtract, OP.add),
             reads=c_keys + m_keys, writes=f_keys)

    for oc2 in range(8):
        j = oc2 // 2
        r0 = (oc2 % 2) * 128
        dmain(sci.ap(0, 1024), packed_in[r0:r0 + 128, :], writes=sci.k(0, 1024))
        S.op("dve", lambda e: e.tensor_copy(scf.ap(0, 1024), sci.ap(0, 1024)),
             reads=sci.k(0, 1024), writes=scf.k(0, 1024))
        def floor_scaled(dst_off, scale_val):
            S.op("dve", lambda e, s=scale_val, d=dst_off: e.tensor_scalar(
                scf.ap(d, 1024), scf.ap(0, 1024), s, None, OP.mult),
                reads=scf.k(0, 1024), writes=scf.k(dst_off, 1024))
            floor_fixup(scf.ap(dst_off, 1024), scf.k(dst_off, 1024),
                        sci.ap(1024, 1024), sci.k(1024, 1024),
                        scf.ap(4096, 1024), scf.k(4096, 1024),
                        sci.ap(1024, 1024), sci.k(1024, 1024))
            S.op("dve", lambda e, d=dst_off: e.tensor_copy(scf.ap(d, 1024),
                                                           sci.ap(1024, 1024)),
                 reads=sci.k(1024, 1024), writes=scf.k(dst_off, 1024))
        if j == 0:
            floor_scaled(1024, 0.25)
            S.op("dve", lambda e: e.scalar_tensor_tensor(
                scf.ap(2048, 1024), scf.ap(1024, 1024), -4.0, scf.ap(0, 1024),
                OP.mult, OP.add),
                reads=scf.k(0, 2048), writes=scf.k(2048, 1024))
        else:
            floor_scaled(1024, 0.25 ** j)
            floor_scaled(3072, 0.25 ** (j + 1))
            S.op("dve", lambda e: e.scalar_tensor_tensor(
                scf.ap(2048, 1024), scf.ap(3072, 1024), -4.0, scf.ap(1024, 1024),
                OP.mult, OP.add),
                reads=scf.k(1024, 1024) + scf.k(3072, 1024),
                writes=scf.k(2048, 1024))
        S.op("dve", lambda e: e.tensor_scalar(arena.ap(0, 1024), scf.ap(2048, 1024),
                                              1.0, None, OP.subtract),
             reads=scf.k(2048, 1024), writes=arena.k(0, 1024))
        for ic in range(8):
            S.op("pe", lambda e, ic=ic: e.transpose(
                psTb[:, 0:128], arena.ap(ic * 128, 128), ident_b.ap(0, 128)),
                reads=arena.k(0, 1024) + ident_b.k(), writes=["psTb"])
            S.op("dve", lambda e: e.tensor_copy(arena.ap(4096, 128), psTb[:, 0:128]),
                 reads=["psTb"], writes=arena.k(4096, 128))
            dstv = vTn[:, oc2 // 4, ic, (oc2 % 4) * 128:(oc2 % 4) * 128 + 128]
            S.dma("sync", lambda e, dstv=dstv: e.dma_start(out=dstv,
                                                           in_=arena.ap(4096, 128)),
                  reads=arena.k(4096, 128), writes=["sTn"])

    dmain(scf.t[0:1, 2048:2056], decay_in, writes=scf.k(2048, 256))
    S.op("sc", lambda e: e.activation(prm.t[0:1, PRM_LND:PRM_LND + 8],
                                      scf.t[0:1, 2048:2056], AF.Ln),
         reads=scf.k(2048, 256), writes=prm.k(PRM_LND, 8))
    repl_row(prm, PRM_LND, PRM_LND + 8)
    S.op("dve", lambda e: e.tensor_scalar(scf.ap(2048, 8), prm.ap(PRM_LND, 8),
                                          256.0, None, OP.mult),
         reads=prm.k(PRM_LND, 8), writes=scf.k(2048, 256))
    S.op("sc", lambda e: e.activation(prm.ap(PRM_D256, 8), scf.ap(2048, 8),
                                      AF.Exp),
         reads=scf.k(2048, 256), writes=prm.k(PRM_D256, 8))
    S.op("gp", lambda e: e.iota(sci.ap(0, 512), [[-128, 2], [1, 256]], base=0,
                                channel_multiplier=-1),
         writes=sci.k(0, 512))
    S.op("dve", lambda e: e.tensor_copy(scf.ap(0, 512), sci.ap(0, 512)),
         reads=sci.k(0, 512), writes=scf.k(0, 512))
    S.op("dve", lambda e: e.tensor_single_scalar(scf.ap(512, 512), scf.ap(0, 512),
                                                 0.0, OP.is_ge),
         reads=scf.k(0, 512), writes=scf.k(512, 512))
    S.op("dve", lambda e: e.tensor_scalar(scf.ap(1536, 512), scf.ap(512, 512),
                                          INV_SQRT_DK, None, OP.mult),
         reads=scf.k(512, 512), writes=scf.k(1536, 512))
    for h in range(H):
        S.op("dve", lambda e, h=h: e.tensor_scalar(
            scf.ap(1024, 512), scf.ap(0, 512),
            prm.t[:, PRM_LND + h:PRM_LND + h + 1], None, OP.mult),
            reads=scf.k(0, 512) + prm.k(PRM_LND, 8), writes=scf.k(1024, 512))
        S.op("sc", lambda e: e.activation(scf.ap(1024, 512), scf.ap(1024, 512),
                                          AF.Exp),
             reads=scf.k(1024, 512), writes=scf.k(1024, 512))
        S.op("dve", lambda e, h=h: e.tensor_tensor(
            masks.ap(h * 512, 512), scf.ap(1024, 512), scf.ap(1536, 512), OP.mult),
            reads=scf.k(1024, 512) + scf.k(1536, 512),
            writes=masks.k(h * 512, 512))
    S.op("gp", lambda e: e.iota(sci.ap(0, 2), [[128, 2]], base=1,
                                channel_multiplier=1),
         writes=sci.k(0, 512))
    S.op("dve", lambda e: e.tensor_copy(scf.ap(0, 2), sci.ap(0, 2)),
         reads=sci.k(0, 512), writes=scf.k(0, 256))
    S.op("gp", lambda e: e.iota(sci.ap(512, 2), [[-128, 2]], base=255,
                                channel_multiplier=-1),
         writes=sci.k(512, 512))
    S.op("dve", lambda e: e.tensor_copy(scf.ap(256, 2), sci.ap(512, 2)),
         reads=sci.k(512, 512), writes=scf.k(256, 256))
    for h in range(H):
        for (so, dst) in ((0, PRM_WR), (256, PRM_WK)):
            S.op("dve", lambda e, h=h, so=so: e.tensor_scalar(
                scf.ap(2048, 2), scf.ap(so, 2),
                prm.t[:, PRM_LND + h:PRM_LND + h + 1], None, OP.mult),
                reads=scf.k(so, 256) + prm.k(PRM_LND, 8),
                writes=scf.k(2048, 256))
            S.op("sc", lambda e: e.activation(
                scf.ap(2304, 2), scf.ap(2048, 2), AF.Exp),
                reads=scf.k(2048, 256), writes=scf.k(2304, 256))
            if dst == PRM_WR:
                S.op("dve", lambda e, h=h, dst=dst: e.tensor_scalar(
                    prm.ap(dst + 2 * h, 2), scf.ap(2304, 2),
                    INV_SQRT_DK, None, OP.mult),
                    reads=scf.k(2304, 256), writes=prm.k(dst + 2 * h, 2))
            else:
                S.op("dve", lambda e, h=h, dst=dst: e.tensor_copy(
                    prm.ap(dst + 2 * h, 2), scf.ap(2304, 2)),
                    reads=scf.k(2304, 256), writes=prm.k(dst + 2 * h, 2))
    dmain(scf.ap(2048, 16), logd_in.rearrange("(c p) -> p c", p=128),
          writes=scf.k(2048, 256), slow=True)
    S.op("sc", lambda e: e.activation(prm.ap(PRM_A, 16), scf.ap(2048, 16),
                                      AF.Sigmoid),
         reads=scf.k(2048, 256), writes=prm.k(PRM_A, 16))
    S.op("dve", lambda e: e.memset(prm.ap(PRM_CARRY, 32), 0.0),
         writes=prm.k(PRM_CARRY, 32))
    dmain(prm.ap(PRM_INB, 32), inb_in.rearrange("(c p) -> p c", p=128),
          writes=prm.k(PRM_INB, 32), slow=True)
    dmain(prm.ap(PRM_MIXB, 16), mixb_in.rearrange("(c p) -> p c", p=128),
          writes=prm.k(PRM_MIXB, 16), slow=True)
    dmain(prm.ap(PRM_DWB, 16), dwb_in.rearrange("(c p) -> p c", p=128),
          writes=prm.k(PRM_DWB, 16), slow=True)
    dmain(prm.ap(PRM_GAM, 8), gamma_in.rearrange("(c p) -> p c", p=128),
          writes=prm.k(PRM_GAM, 8), slow=True)
    dmain(prm.ap(PRM_BET, 8), beta_in.rearrange("(c p) -> p c", p=128),
          writes=prm.k(PRM_BET, 8), slow=True)
    dmain(prm.ap(PRM_B1S, 32), b1_in.rearrange("(c p) -> p c", p=128),
          writes=prm.k(PRM_B1S, 32), slow=True)
    S.op("dve", lambda e: e.tensor_scalar(scf.ap(2048, 32), prm.ap(PRM_B1S, 32),
                                          256.0, 0.5, OP.mult, OP.add),
         reads=prm.k(PRM_B1S, 32), writes=scf.k(2048, 256))
    S.op("dve", lambda e: e.tensor_copy(prm.ap(PRM_B1S, 32), scf.ap(2048, 32)),
         reads=scf.k(2048, 256), writes=prm.k(PRM_B1S, 32))
    dmain(dwwt.ap(0, 112).rearrange("p (c j) -> p c j", j=7),
          dww_in.rearrange("(c p) j -> p c j", p=128), writes=dwwt.k(), slow=True)
    dmain(prm.t[0:1, PRM_WS:PRM_WS + 1], ws_in, writes=prm.k(PRM_WS, 1))
    repl_row(prm, PRM_WS, PRM_WS + 1)
    dmain(brep.t[0:1, 0:1024], bias0_in, writes=brep.k(0, 1024))
    repl_row(brep, 0, 1024)
    S.op("dve", lambda e: e.memset(ones_bf.ap(0, 128), 1.0), writes=ones_bf.k())
    for bi, (bsrc, blo) in enumerate(((outb_in, 0), (b2_in, 2048))):
        dmain(scf.t[0:1, 0:1024], bsrc, writes=scf.k(0, 1024))
        S.op("dve", lambda e, blo=blo: e.tensor_copy(brow.t[0:1, blo:blo + 1024],
                                                     scf.t[0:1, 0:1024]),
             reads=scf.k(0, 1024), writes=brow.k(blo, 1024))
        S.op("dve", lambda e, blo=blo: e.tensor_copy(scf.t[0:1, 1024:2048],
                                                     brow.t[0:1, blo:blo + 1024]),
             reads=brow.k(blo, 1024), writes=scf.k(1024, 1024))
        S.op("dve", lambda e: e.tensor_tensor(scf.t[0:1, 0:1024],
                                              scf.t[0:1, 0:1024],
                                              scf.t[0:1, 1024:2048], OP.subtract),
             reads=scf.k(0, 2048), writes=scf.k(0, 1024))
        S.op("dve", lambda e, blo=blo: e.tensor_copy(
            brow.t[0:1, blo + 1024:blo + 2048], scf.t[0:1, 0:1024]),
            reads=scf.k(0, 1024), writes=brow.k(blo + 1024, 1024))
    S.op("dve", lambda e: e.memset(ret_state.ap(0, 4096), 0.0),
         writes=ret_state.k())
    S.op("dve", lambda e: e.memset(vcs.ap(VTAILS, 192), 0.0),
         writes=vcs.k(VTAILS, 192))

    AQ, AK, AV, AG0, AGN = 0, 4096, 8192, 16384, 24576
    AL1G, AL1S, AL1C, AL1V = 0, 8192, 16384, 24576
    AH = 0

    def layer_norm(s_lo, tc):
        sb_ = 32 + (tc % 2) * 8
        sk = scf.k(s_lo, 1024)
        sap = scf.ap(s_lo, 1024)
        ST = lambda i: stat.t[:, sb_ + i:sb_ + i + 1]
        STA = lambda i: stat.ap(sb_ + i, 1)
        STK = stat.k(sb_, 8)
        S.op("dve", lambda e: e.reduce_sum(STA(0), sap, axis=AX.X),
             reads=sk, writes=STK)
        S.op("sc", lambda e: e.activation(scf.ap(1024, 1024), sap, AF.Square,
                                          accum_out=STA(1)),
             reads=sk, writes=scf.k(1024, 1024) + STK)
        S.op("dve", lambda e: e.tensor_scalar(STA(2), STA(0), cAP(CI1024), None,
                                              OP.mult),
             reads=STK + CSTK, writes=STK)
        S.op("dve", lambda e: e.tensor_tensor(STA(3), STA(2), STA(2), OP.mult),
             reads=STK, writes=STK)
        S.op("dve", lambda e: e.scalar_tensor_tensor(
            STA(4), STA(1), 1.0 / 1024.0, STA(3), OP.mult, OP.subtract),
            reads=STK, writes=STK)
        S.op("sc", lambda e: e.activation(STA(0), STA(4), AF.Ln, bias=cAP(CEPS),
                                          scale=1.0),
             reads=STK + CSTK, writes=STK)
        S.op("sc", lambda e: e.activation(STA(5), STA(0), AF.Exp,
                                          bias=0.0, scale=cAP(CN05)),
             reads=STK + CSTK, writes=STK)
        S.op("dve", lambda e: e.scalar_tensor_tensor(
            STA(6), STA(2), -1.0, STA(5), OP.mult, OP.mult),
            reads=STK, writes=STK)
        S.op("sc", lambda e: e.activation(
            x_tm.ap(tc * 1024, 1024), sap, AF.Identity,
            bias=ST(6), scale=ST(5)),
            reads=sk + STK, writes=x_tm.k(tc * 1024, 1024))

    def transpose_tm_to_fm(src, src_keys_fn, cast_gb=False):
        for tc in range(NTC):
            for dc in range(8):
                pap, pk = tp_f32(tc * 8 + dc)
                S.op("pe", lambda e, tc=tc, dc=dc, pap=pap: e.transpose(
                    pap, src.ap(tc * 1024 + dc * 128, 128), ident_f.ap(0, 128)),
                    reads=src_keys_fn(tc, dc) + ident_f.k(),
                    writes=[pk])
                dsta = fmbuf.ap(dc * 512 + tc * 128, 128)
                dkey = fmbuf.k(dc * 512 + tc * 128, 128)
                if cast_gb:
                    S.op("sc", lambda e, dsta=dsta, pap=pap, dc=dc:
                         e.activation(dsta, pap, AF.Identity,
                                      bias=prm.t[:, PRM_BET + dc:PRM_BET + dc + 1],
                                      scale=prm.t[:, PRM_GAM + dc:PRM_GAM + dc + 1]),
                         reads=[pk] + prm.k(PRM_GAM, 16), writes=dkey)
                else:
                    S.op("dve", lambda e, dsta=dsta, pap=pap: e.tensor_copy(
                        dsta, pap),
                        reads=[pk], writes=dkey)

    def emit_tile(seq, tt, tile_idx):
        t0 = tt * TT
        dbg = debug_taps and tile_idx == 0

        src = xin[seq, t0:t0 + TT, :].rearrange("(tc p) d -> p tc d", p=128)
        dmain(x_tm.ap(0, 4096).rearrange("p (tc d) -> p tc d", d=1024), src,
              writes=x_tm.k())

        if SUBPHASE < 0.7:
            dst = out_d[seq, t0:t0 + TT, :].rearrange("(tc p) d -> p tc d", p=128)
            S.dma("sync", lambda e, dst=dst: e.dma_start(
                out=dst, in_=x_tm.ap(0, 4096).rearrange("p (tc d) -> p tc d",
                                                        d=1024)),
                reads=x_tm.k(), writes=[f"outp:{tile_idx}"])
            return
        transpose_tm_to_fm(x_tm, lambda tc, dc: x_tm.k(tc * 1024 + dc * 128, 128))
        if SUBPHASE < 1:
            dst = out_d[seq, t0:t0 + TT, :].rearrange("(tc p) d -> p tc d", p=128)
            S.dma("sync", lambda e, dst=dst: e.dma_start(
                out=dst, in_=x_tm.ap(0, 4096).rearrange("p (tc d) -> p tc d",
                                                        d=1024)),
                reads=x_tm.k(), writes=[f"outp:{tile_idx}"])
            return

        for which, aoff, view, skey, ctab, stab in (
                ("q", AQ, vQ, "sQ", cq_in, sq_in),
                ("k", AK, vK, "sK", ck_in, sk_in)):
            base, wkeys = load_wblock(
                (sQ if which == "q" else sK).rearrange("(p x) -> p x", p=128),
                8192, skey)
            def rot_ops(oc, po, pkey, rtb, rlo, aoff=aoff):
                sf = 2048 * (oc % 2)
                ops = []
                ops.append(lambda: S.op("dve", lambda e: e.tensor_tensor(
                    scf.ap(sf + 512, 512), po, rtb.ap(rlo, 512), OP.mult),
                    reads=[pkey] + rtb.k(rlo, 512),
                    writes=scf.k(sf + 512, 512)))
                ops.append(lambda: S.op("dve", lambda e: e.stream_shuffle(
                    scf.ap(sf + 1024, 512), po, [i ^ 1 for i in range(32)]),
                    reads=[pkey], writes=scf.k(sf + 1024, 512)))
                ops.append(lambda: S.op("dve", lambda e: e.tensor_tensor(
                    scf.ap(sf + 1536, 512), scf.ap(sf + 1024, 512),
                    rtb.ap(rlo + 512, 512), OP.mult),
                    reads=scf.k(sf + 1024, 512) + rtb.k(rlo + 512, 512),
                    writes=scf.k(sf + 1536, 512)))
                ops.append(lambda: S.op("dve", lambda e: e.tensor_tensor(
                    arena.ap(aoff + oc * 512, 512), scf.ap(sf + 512, 512),
                    scf.ap(sf + 1536, 512), OP.add),
                    reads=scf.k(sf + 512, 512) + scf.k(sf + 1536, 512),
                    writes=arena.k(aoff + oc * 512, 512)))
                return ops

            import itertools as _it2
            for og in range(4):
                chains = []
                for oc in (2 * og, 2 * og + 1):
                    if oc % 2 == 0:
                        rtb, rlo = rotb, 0
                    else:
                        rtb, rlo = scf, 4096
                    dmain(rtb.ap(rlo, 512), ctab[tt, oc], writes=rtb.k(rlo, 512))
                    dmain(rtb.ap(rlo + 512, 512), stab[tt, oc],
                          writes=rtb.k(rlo + 512, 512))
                    pslot = oc % 3
                    po = psB[:, pslot * 512:pslot * 512 + 512]
                    for kc in range(8):
                        S.op("pe", lambda e, oc=oc, kc=kc, po=po, base=base:
                             e.matmul(
                                 po, wst.ap(base + oc * 1024 + kc * 128, 128),
                                 fmbuf.ap(kc * 512, 512),
                                 start=(kc == 0), stop=(kc == 7)),
                             reads=wkeys + fmbuf.k(kc * 512, 512),
                             writes=[f"psB:{pslot}"])
                    chains.append(rot_ops(oc, po, f"psB:{pslot}", rtb, rlo))
                for pair in _it2.zip_longest(*chains):
                    for fn in pair:
                        if fn is not None:
                            fn()
        if dbg:
            dmain(tap("q_fm", [128, 4096], BF16), arena.ap(AQ, 4096),
                  reads=arena.k(AQ, 4096))
            dmain(tap("k_fm", [128, 4096], BF16), arena.ap(AK, 4096),
                  reads=arena.k(AK, 4096))

        for which, aoff, view, skey, act in (("v", AV, vV, "sV", False),
                                             ("g0", AG0, vG, "sGw", True)):
            for ncd in range(4):
                base, wkeys = load_wblock(view[:, ncd].rearrange("p kc n -> p (kc n)"),
                                          4096, skey)
                for tc in range(NTC):
                    pslot = tc % 3
                    po = psB[:, pslot * 512:pslot * 512 + 512]
                    for kc in range(8):
                        S.op("pe", lambda e, tc=tc, kc=kc, po=po, base=base:
                             e.matmul(po, fmbuf.ap(kc * 512 + tc * 128, 128),
                                      wst.ap(base + kc * 512, 512),
                                      start=(kc == 0), stop=(kc == 7)),
                             reads=wkeys + fmbuf.k(kc * 512 + tc * 128, 128),
                             writes=[f"psB:{pslot}"])
                    dsta = arena.ap(aoff + tc * 2048 + ncd * 512, 512)
                    dkey = arena.k(aoff + tc * 2048 + ncd * 512, 512)
                    if act:
                        S.op("sc", lambda e, dsta=dsta, po=po: e.activation(
                            dsta, po, AF.Silu),
                            reads=[f"psB:{pslot}"], writes=dkey)
                    else:
                        S.op("sc", lambda e, dsta=dsta, po=po: e.activation(
                            dsta, po, AF.Identity),
                            reads=[f"psB:{pslot}"], writes=dkey)
        if dbg:
            dmain(tap("v_tm", [128, 8192], BF16), arena.ap(AV, 8192),
                  reads=arena.k(AV, 8192))

        if SUBPHASE < 3:
            dst = out_d[seq, t0:t0 + TT, :].rearrange("(tc p) d -> p tc d", p=128)
            S.dma("sync", lambda e, dst=dst: e.dma_start(
                out=dst, in_=x_tm.ap(0, 4096).rearrange("p (tc d) -> p tc d",
                                                        d=1024)),
                reads=x_tm.k(), writes=[f"outp:{tile_idx}"])
            return
        SC_AP = [psA[:, 0:256], psA[:, 512:768]]
        SC_K = ["psA:0", "psA:1"]
        O_AP = [psA[:, 1024:1280], psT[:, 0:256]]
        O_K = ["psA:2", "psT"]
        X_AP = psB[:, 0:256]
        X_K = "psB:0"
        U_AP = psB[:, 512:768]
        U_K = "psB:1"
        KT_AP = [psTb[:, 0:128], psB[:, 1024:1088].bitcast(BF16)]
        KT_K = ["psTb", "psB:2"]
        for c in range(2):
            for h in range(H):
                kq = arena.k(AQ + h * 512 + c * 256, 256)
                kk = arena.k(AK + h * 512 + c * 256, 256)
                S.op("dve", lambda e, h=h: e.tensor_copy(
                    ret_sm.ap(768, 256), ret_state.ap(h * 256, 256)),
                    reads=ret_state.k(h * 256, 256), writes=ret_sm.k(768, 256))
                for tkc in range(2):
                    S.op("pe", lambda e, c=c, h=h, tkc=tkc: e.matmul(
                        SC_AP[tkc],
                        arena.ap(AK + h * 512 + c * 256 + tkc * 128, 128),
                        arena.ap(AQ + h * 512 + c * 256, 256),
                        start=True, stop=True),
                        reads=kq + kk, writes=[SC_K[tkc]])
                    S.op("dve", lambda e, h=h, tkc=tkc: e.tensor_tensor(
                        ret_sm.ap(tkc * 256, 256),
                        SC_AP[tkc],
                        masks.ap(h * 512 + tkc * 256, 256), OP.mult),
                        reads=[SC_K[tkc]] + masks.k(h * 512 + tkc * 256, 256),
                        writes=ret_sm.k(tkc * 256, 256))
                    S.op("pe", lambda e, c=c, h=h, tkc=tkc: e.transpose(
                        KT_AP[tkc],
                        arena.ap(AK + h * 512 + c * 256 + tkc * 128, 128),
                        ident_b.ap(0, 128)),
                        reads=kk + ident_b.k(), writes=[KT_K[tkc]])
                    S.op("dve", lambda e, tkc=tkc: e.tensor_copy(
                        ret_sm.ap(512 + tkc * 128, 128),
                        KT_AP[tkc]),
                        reads=[KT_K[tkc]], writes=ret_sm.k(512 + tkc * 128, 128))
                    tc = c * 2 + tkc
                    S.op("dve", lambda e, tc=tc, h=h, tkc=tkc: e.tensor_scalar(
                        ret_sm.ap(1024 + tkc * 256, 256),
                        arena.ap(AV + tc * 2048 + h * 256, 256),
                        prm.t[:, PRM_WK + 2 * h + tkc:PRM_WK + 2 * h + tkc + 1],
                        None, OP.mult),
                        reads=arena.k(AV + tc * 2048 + h * 256, 256) +
                        prm.k(PRM_WK + 2 * h, 2),
                        writes=ret_sm.k(1024 + tkc * 256, 256))
                for tqc in range(2):
                    for tkc in range(tqc + 1):
                        S.op("pe", lambda e, c=c, h=h, tqc=tqc, tkc=tkc: e.matmul(
                            O_AP[tqc],
                            ret_sm.ap(tkc * 256 + tqc * 128, 128),
                            arena.ap(AV + (c * 2 + tkc) * 2048 + h * 256, 256),
                            start=(tkc == 0), stop=(tkc == tqc)),
                            reads=ret_sm.k(tkc * 256, 256) +
                            arena.k(AV + (c * 2 + tkc) * 2048 + h * 256, 256),
                            writes=[O_K[tqc]])
                    S.op("pe", lambda e, c=c, h=h, tqc=tqc: e.matmul(
                        X_AP,
                        arena.ap(AQ + h * 512 + c * 256 + tqc * 128, 128),
                        ret_sm.ap(768, 256), start=True, stop=True),
                        reads=kq + ret_sm.k(768, 256), writes=[X_K])
                    xs = (h % 2) * 256
                    S.op("sc", lambda e, h=h, tqc=tqc, xs=xs: e.activation(
                        scf.ap(xs, 256), X_AP, AF.Identity,
                        scale=prm.t[:, PRM_WR + 2 * h + tqc:PRM_WR + 2 * h + tqc + 1]),
                        reads=[X_K] + prm.k(PRM_WR + 2 * h, 2) + CSTK,
                        writes=scf.k(xs, 256))
                    olo = 3584 + tqc * 2048 + h * 256
                    S.op("dve", lambda e, olo=olo, tqc=tqc, xs=xs: e.tensor_tensor(
                        scf.ap(olo, 256), O_AP[tqc],
                        scf.ap(xs, 256), OP.add),
                        reads=[O_K[tqc]] + scf.k(xs, 256),
                        writes=scf.k(olo, 256))
                for tkc in range(2):
                    S.op("pe", lambda e, tkc=tkc: e.matmul(
                        U_AP, ret_sm.ap(512 + tkc * 128, 128),
                        ret_sm.ap(1024 + tkc * 256, 256),
                        start=(tkc == 0), stop=(tkc == 1)),
                        reads=ret_sm.k(512 + tkc * 128, 128) +
                        ret_sm.k(1024 + tkc * 256, 256),
                        writes=[U_K])
                S.op("dve", lambda e, h=h: e.tensor_scalar(
                    scf.ap(256, 256), ret_state.ap(h * 256, 256),
                    prm.t[:, PRM_D256 + h:PRM_D256 + h + 1], None, OP.mult),
                    reads=ret_state.k(h * 256, 256) + prm.k(PRM_D256, 8),
                    writes=scf.k(256, 256))
                S.op("dve", lambda e, h=h: e.tensor_tensor(
                    ret_state.ap(h * 256, 256), scf.ap(256, 256),
                    U_AP, OP.add),
                    reads=scf.k(256, 256) + [U_K],
                    writes=ret_state.k(h * 256, 256))

            if dbg:
                dmain(tap(f"obuf{c}", [128, 4096]), scf.ap(3584, 4096),
                      reads=scf.k(3584, 4096))
            for half in range(2):
                tc = c * 2 + half
                olo = 3584 + half * 2048
                okeys = scf.k(olo, 2048)
                S.op("dve", lambda e, olo=olo: e.reduce_sum(
                    stat.ap(8, 8).unsqueeze(2),
                    scf.ap(olo, 2048).rearrange("p (h v) -> p h v", h=8),
                    axis=AX.X),
                    reads=okeys, writes=stat.k(8, 8))
                S.op("sc", lambda e, olo=olo: e.activation(
                    scf.ap(1024, 2048), scf.ap(olo, 2048), AF.Square),
                    reads=okeys, writes=scf.k(1024, 2048))
                S.op("dve", lambda e: e.reduce_sum(
                    stat.ap(16, 8).unsqueeze(2),
                    scf.ap(1024, 2048).rearrange("p (h v) -> p h v", h=8),
                    axis=AX.X),
                    reads=scf.k(1024, 2048), writes=stat.k(16, 8))
                S.op("dve", lambda e: e.tensor_scalar(stat.ap(8, 8), stat.ap(8, 8),
                                                      1.0 / 256.0, None, OP.mult),
                     reads=stat.k(8, 8), writes=stat.k(8, 8))
                S.op("dve", lambda e: e.tensor_tensor(stat.ap(24, 8), stat.ap(8, 8),
                                                      stat.ap(8, 8), OP.mult),
                     reads=stat.k(8, 8), writes=stat.k(24, 8))
                S.op("dve", lambda e: e.scalar_tensor_tensor(
                    stat.ap(0, 8), stat.ap(16, 8), 1.0 / 256.0, stat.ap(24, 8),
                    OP.mult, OP.subtract),
                    reads=stat.k(16, 8) + stat.k(24, 8), writes=stat.k(0, 8))
                S.op("dve", lambda e: e.tensor_scalar(stat.ap(24, 8), stat.ap(0, 8),
                                                      1e-5, None, OP.add),
                     reads=stat.k(0, 8), writes=stat.k(24, 8))
                S.op("sc", lambda e: e.activation(stat.ap(0, 8), stat.ap(24, 8),
                                                  AF.Ln),
                     reads=stat.k(24, 8), writes=stat.k(0, 8))
                S.op("sc", lambda e: e.activation(stat.ap(16, 8), stat.ap(0, 8),
                                                  AF.Exp, bias=0.0, scale=cAP(CN05)),
                     reads=stat.k(0, 8) + CSTK, writes=stat.k(16, 8))
                S.op("dve", lambda e: e.scalar_tensor_tensor(
                    stat.ap(24, 8), stat.ap(8, 8), -1.0, stat.ap(16, 8),
                    OP.mult, OP.mult),
                    reads=stat.k(8, 16), writes=stat.k(24, 8))
                for h in range(H):
                    nb = scf.ap((h % 2) * 128, 128).bitcast(BF16)
                    nbk = scf.k((h % 2) * 128, 128)
                    S.op("sc", lambda e, olo=olo, h=h, nb=nb: e.activation(
                        nb, scf.ap(olo + h * 256, 256), AF.Identity,
                        bias=stat.t[:, 24 + h:25 + h],
                        scale=stat.t[:, 16 + h:17 + h]),
                        reads=scf.k(olo + h * 256, 256) + stat.k(16, 8) +
                        stat.k(24, 8),
                        writes=nbk)
                    S.op("dve", lambda e, tc=tc, h=h, nb=nb: e.tensor_tensor(
                        wst.ap(h * 256, 256), nb,
                        arena.ap(AG0 + tc * 2048 + h * 256, 256), OP.mult),
                        reads=nbk + arena.k(AG0 + tc * 2048 + h * 256, 256),
                        writes=wst.k(0, 2048))
                if dbg and tc == 0:
                    dmain(tap("gnt", [128, 2048], BF16), wst.ap(0, 2048),
                          reads=wst.k(0, 2048))
                    dmain(tap("gstat", [128, 32]), stat.ap(0, 32),
                          reads=stat.k(0, 32))
                for vc in range(16):
                    pap, pk = tp_bf(vc)
                    S.op("pe", lambda e, vc=vc, pap=pap: e.transpose(
                        pap, wst.ap(vc * 128, 128), ident_b.ap(0, 128)),
                        reads=wst.k(0, 2048) + ident_b.k(),
                        writes=[pk])
                    S.op("dve", lambda e, vc=vc, tc=tc, pap=pap: e.tensor_copy(
                        arena.ap(AGN + vc * 512 + tc * 128, 128), pap),
                        reads=[pk],
                        writes=arena.k(AGN + vc * 512 + tc * 128, 128))
        if dbg:
            dmain(tap("gn_fm", [128, 8192], BF16), arena.ap(AGN, 8192),
                  reads=arena.k(AGN, 8192))
            dmain(tap("masks", [128, 4096], BF16), masks.ap(0, 4096),
                  reads=masks.k(0, 4096))

        if SUBPHASE < 4:
            dst = out_d[seq, t0:t0 + TT, :].rearrange("(tc p) d -> p tc d", p=128)
            S.dma("sync", lambda e, dst=dst: e.dma_start(
                out=dst, in_=x_tm.ap(0, 4096).rearrange("p (tc d) -> p tc d",
                                                        d=1024)),
                reads=x_tm.k(), writes=[f"outp:{tile_idx}"])
            return
        prb0, prk0 = load_wblock(vP[:, 0].rearrange("p kc n -> p (kc n)"),
                                 8192, "sP")
        prb1, prk1 = load_wblock(vP[:, 1].rearrange("p kc n -> p (kc n)"),
                                 8192, "sP")
        for tc in range(NTC):
            for ncd in range(2):
                base = prb0 if ncd == 0 else prb1
                wkeys = prk0 if ncd == 0 else prk1
                pslot = (tc * 2 + ncd) % 3
                po = psB[:, pslot * 512:pslot * 512 + 512]
                for kc in range(16):
                    S.op("pe", lambda e, tc=tc, kc=kc, po=po, base=base: e.matmul(
                        po, arena.ap(AGN + kc * 512 + tc * 128, 128),
                        wst.ap(base + kc * 512, 512),
                        start=(kc == 0), stop=(kc == 15)),
                        reads=arena.k(AGN + kc * 512 + tc * 128, 128) + wkeys,
                        writes=[f"psB:{pslot}"])
                sb0 = (tc % 2) * 2048
                S.op("dve", lambda e, tc=tc, ncd=ncd, po=po, pslot=pslot, sb0=sb0:
                     e.tensor_tensor(
                         scf.ap(sb0 + ncd * 512, 512), po,
                         x_tm.ap(tc * 1024 + ncd * 512, 512), OP.add),
                     reads=[f"psB:{pslot}"] + x_tm.k(tc * 1024 + ncd * 512, 512),
                     writes=scf.k(sb0 + ncd * 512, 512))
            layer_norm((tc % 2) * 2048, tc)
        if dbg:
            dmain(tap("x1", [128, 4096]), x_tm.ap(0, 4096), reads=x_tm.k())

        if PHASE_LIMIT < 2:
            dst = out_d[seq, t0:t0 + TT, :].rearrange("(tc p) d -> p tc d", p=128)
            S.dma("sync", lambda e, dst=dst: e.dma_start(
                out=dst, in_=x_tm.ap(0, 4096).rearrange("p (tc d) -> p tc d",
                                                        d=1024)),
                reads=x_tm.k(), writes=[f"outp:{tile_idx}"])
            return
        tbase0, tkeys0 = load_wblock(vTn[:, 0].rearrange("p kc n -> p (kc n)"),
                                     4096, "sTn")
        tbase1, tkeys1 = load_wblock(vTn[:, 1].rearrange("p kc n -> p (kc n)"),
                                     4096, "sTn")
        for tc in range(NTC):
            xk = x_tm.k(tc * 1024, 1024)
            xs = x_tm.ap(tc * 1024, 1024)
            S.op("dve", lambda e, xs=xs: e.reduce_max(
                stat.ap(0, 1), xs, axis=AX.X, apply_absolute_value=True),
                reads=xk, writes=stat.k(0, 8))
            S.op("dve", lambda e: e.tensor_single_scalar(stat.ap(3, 1),
                                                         stat.ap(0, 1), 1e-5,
                                                         OP.max),
                 reads=stat.k(0, 8), writes=stat.k(0, 8))
            S.op("dve", lambda e: e.reciprocal(stat.ap(1, 1), stat.ap(3, 1)),
                 reads=stat.k(0, 8), writes=stat.k(0, 8))
            S.op("dve", lambda e: e.tensor_scalar(stat.ap(4, 1), stat.ap(1, 1),
                                                  127.0, None, OP.mult),
                 reads=stat.k(0, 8), writes=stat.k(0, 8))
            S.op("dve", lambda e: e.tensor_scalar(
                stat.ap(2, 1), stat.ap(3, 1), prm.t[:, PRM_WS:PRM_WS + 1],
                1.0 / 127.0, OP.mult, OP.mult),
                reads=stat.k(0, 8) + prm.k(PRM_WS, 1), writes=stat.k(0, 8))
            S.op("dve", lambda e, xs=xs: e.tensor_scalar(
                scf.ap(0, 1024), xs, stat.t[:, 4:5], 0.5, OP.mult, OP.add),
                reads=xk + stat.k(0, 8), writes=scf.k(0, 1024))
            floor_fixup(scf.ap(0, 1024), scf.k(0, 1024),
                        sci.ap(0, 1024), sci.k(0, 1024),
                        sci.ap(1024, 1024), sci.k(1024, 1024),
                        sci.ap(0, 1024), sci.k(0, 1024))
            S.op("dve", lambda e: e.tensor_copy(arena.ap(AQ, 1024), sci.ap(0, 1024)),
                 reads=sci.k(0, 1024), writes=arena.k(AQ, 1024))
            for dc in range(8):
                pap, pk = tp_bf(dc)
                S.op("pe", lambda e, dc=dc, pap=pap: e.transpose(
                    pap, arena.ap(AQ + dc * 128, 128), ident_b.ap(0, 128)),
                    reads=arena.k(AQ, 1024) + ident_b.k(),
                    writes=[pk])
                S.op("dve", lambda e, dc=dc, tc=tc, pap=pap: e.tensor_copy(
                    fmbuf.ap(dc * 512 + tc * 128, 128), pap),
                    reads=[pk],
                    writes=fmbuf.k(dc * 512 + tc * 128, 128))
            for ncd in range(2):
                base = tbase0 if ncd == 0 else tbase1
                wkeys = tkeys0 if ncd == 0 else tkeys1
                pslot = ncd % 3
                po = psB[:, pslot * 512:pslot * 512 + 512]
                for kc in range(8):
                    S.op("pe", lambda e, kc=kc, tc=tc, po=po, base=base: e.matmul(
                        po, fmbuf.ap(kc * 512 + tc * 128, 128),
                        wst.ap(base + kc * 512, 512),
                        start=(kc == 0), stop=(kc == 7)),
                        reads=fmbuf.k(kc * 512 + tc * 128, 128) + wkeys,
                        writes=[f"psB:{pslot}"])
                sb0 = (tc % 2) * 2048
                S.op("dve", lambda e, ncd=ncd, po=po: e.scalar_tensor_tensor(
                    scf.ap(3072 + ncd * 512, 512), po, stat.t[:, 2:3],
                    brep.ap(ncd * 512, 512), OP.mult, OP.add),
                    reads=[f"psB:{pslot}"] + stat.k(0, 8) + brep.k(ncd * 512, 512),
                    writes=scf.k(3072 + ncd * 512, 512))
                S.op("dve", lambda e, tc=tc, ncd=ncd, sb0=sb0: e.tensor_tensor(
                    scf.ap(sb0 + ncd * 512, 512), scf.ap(3072 + ncd * 512, 512),
                    x_tm.ap(tc * 1024 + ncd * 512, 512), OP.add),
                    reads=scf.k(3072 + ncd * 512, 512) +
                    x_tm.k(tc * 1024 + ncd * 512, 512),
                    writes=scf.k(sb0 + ncd * 512, 512))
            layer_norm((tc % 2) * 2048, tc)
        if dbg:
            dmain(tap("x2", [128, 4096]), x_tm.ap(0, 4096), reads=x_tm.k())

        if PHASE_LIMIT < 3:
            dst = out_d[seq, t0:t0 + TT, :].rearrange("(tc p) d -> p tc d", p=128)
            S.dma("sync", lambda e, dst=dst: e.dma_start(
                out=dst, in_=x_tm.ap(0, 4096).rearrange("p (tc d) -> p tc d",
                                                        d=1024)),
                reads=x_tm.k(), writes=[f"outp:{tile_idx}"])
            return
        for tc in range(NTC):
            xk = x_tm.k(tc * 1024, 1024)
            xs = x_tm.ap(tc * 1024, 1024)
            S.op("dve", lambda e, xs=xs: e.reduce_sum(stat.ap(0, 1), xs, axis=AX.X),
                 reads=xk, writes=stat.k(0, 8))
            S.op("sc", lambda e, xs=xs: e.activation(scf.ap(1024, 1024), xs,
                                                     AF.Square,
                                                     accum_out=stat.ap(1, 1)),
                 reads=xk, writes=scf.k(1024, 1024) + stat.k(0, 8))
            S.op("dve", lambda e: e.tensor_scalar(stat.ap(2, 1), stat.ap(0, 1),
                                                  cAP(CI1024), None, OP.mult),
                 reads=stat.k(0, 8) + CSTK, writes=stat.k(0, 8))
            S.op("dve", lambda e: e.tensor_tensor(stat.ap(3, 1), stat.ap(2, 1),
                                                  stat.ap(2, 1), OP.mult),
                 reads=stat.k(0, 8), writes=stat.k(0, 8))
            S.op("dve", lambda e: e.scalar_tensor_tensor(
                stat.ap(4, 1), stat.ap(1, 1), 1.0 / 1024.0, stat.ap(3, 1),
                OP.mult, OP.subtract),
                reads=stat.k(0, 8), writes=stat.k(0, 8))
            S.op("dve", lambda e: e.tensor_scalar(stat.ap(0, 1), stat.ap(4, 1),
                                                  1e-5, None, OP.add),
                 reads=stat.k(0, 8), writes=stat.k(0, 8))
            S.op("sc", lambda e: e.activation(stat.ap(4, 1), stat.ap(0, 1), AF.Ln),
                 reads=stat.k(0, 8), writes=stat.k(0, 8))
            S.op("sc", lambda e: e.activation(stat.ap(5, 1), stat.ap(4, 1), AF.Exp,
                                              bias=0.0, scale=cAP(CN05)),
                 reads=stat.k(0, 8) + CSTK, writes=stat.k(0, 8))
            S.op("dve", lambda e, xs=xs: e.tensor_scalar(
                scf.ap(3584, 1024), xs, stat.t[:, 2:3], stat.t[:, 5:6],
                OP.subtract, OP.mult),
                reads=xk + stat.k(0, 8), writes=scf.k(3584, 1024))
            for dc in range(8):
                pap, pk = tp_f32(dc)
                S.op("pe", lambda e, dc=dc, pap=pap: e.transpose(
                    pap, scf.ap(3584 + dc * 128, 128), ident_f.ap(0, 128)),
                    reads=scf.k(3584 + dc * 128, 128) + ident_f.k(),
                    writes=[pk])
                S.op("dve", lambda e, dc=dc, tc=tc, pap=pap: e.tensor_scalar(
                    fmbuf.ap(dc * 512 + tc * 128, 128), pap,
                    prm.t[:, PRM_GAM + dc:PRM_GAM + dc + 1],
                    prm.t[:, PRM_BET + dc:PRM_BET + dc + 1], OP.mult, OP.add),
                    reads=[pk] + prm.k(PRM_GAM, 16),
                    writes=fmbuf.k(dc * 512 + tc * 128, 128))
        if dbg:
            dmain(tap("xn_fm", [128, 4096], BF16), fmbuf.ap(0, 4096),
                  reads=fmbuf.k())

        ipw = {}
        def inproj_mm(oc, pslot):
            g = oc // 4
            if g not in ipw:
                ipw[g] = load_wblock(
                    vI[:, g * 4:(g + 1) * 4].rearrange("p oc kc m -> p (oc kc m)"),
                    4096, "sI")
            base, wkeys = ipw[g]
            base = base + (oc % 4) * 1024
            po = psB[:, pslot * 512:pslot * 512 + 512]
            for kc in range(8):
                S.op("pe", lambda e, kc=kc, po=po, base=base: e.matmul(
                    po, wst.ap(base + kc * 128, 128), fmbuf.ap(kc * 512, 512),
                    start=(kc == 0), stop=(kc == 7)),
                    reads=wkeys + fmbuf.k(kc * 512, 512), writes=[f"psB:{pslot}"])
            return po

        def conv_ops(cc, po, pkey):
            par = cc % 2
            oc = 16 + cc
            tail_lo = VTAILS + (seq * 16 + cc) * 6
            if par == 0:
                wlo, alo, blo, slo = 0, 512, 0, 2048
                wbase = vcs
            else:
                wlo, alo, blo, slo = 4608, 3072, 3584, 5632
                wbase = scf
            def vwin(off, n):
                return wbase.ap(wlo + off, n), wbase.k(wlo + off, n)
            ca, cak = scf.ap(alo, 512), scf.k(alo, 512)
            cb, cbk = scf.ap(blo, 512), scf.k(blo, 512)
            sc_, sck = scf.ap(slo, 512), scf.k(slo, 512)
            dwb_b = prm.t[:, PRM_DWB + cc:PRM_DWB + cc + 1].broadcast_to([128, 512])
            ops = []
            ops.append(lambda: S.op("dve", lambda e: e.tensor_copy(
                vwin(0, 6)[0], vcs.ap(tail_lo, 6)),
                reads=vcs.k(tail_lo, 6), writes=vwin(0, 6)[1]))
            ops.append(lambda: S.op("sc", lambda e: e.activation(
                vwin(6, 512)[0], po, AF.Identity,
                bias=prm.t[:, PRM_INB + oc:PRM_INB + oc + 1]),
                reads=[pkey] + prm.k(PRM_INB, 32) + CSTK,
                writes=vwin(6, 512)[1]))
            ops.append(lambda: S.op("dve", lambda e: e.scalar_tensor_tensor(
                ca, vwin(0, 512)[0], dwwt.t[:, cc * 7:cc * 7 + 1], dwb_b,
                OP.mult, OP.add),
                reads=vwin(0, 518)[1] + dwwt.k() + prm.k(PRM_DWB, 16),
                writes=cak))
            bufs = [(ca, cak), (cb, cbk)]
            for tp_ in range(1, 7):
                dsta, dkey = bufs[tp_ % 2]
                prev, pvkey = bufs[(tp_ + 1) % 2]
                ops.append(lambda tp_=tp_, dsta=dsta, dkey=dkey, prev=prev,
                           pvkey=pvkey: S.op(
                    "dve", lambda e: e.scalar_tensor_tensor(
                        dsta, vwin(tp_, 512)[0],
                        dwwt.t[:, cc * 7 + tp_:cc * 7 + tp_ + 1], prev,
                        OP.mult, OP.add),
                    reads=vwin(0, 518)[1] + dwwt.k() + pvkey, writes=dkey))
            ops.append(lambda: S.op("dve", lambda e: e.tensor_copy(
                arena.ap(AL1C + cc * 512, 512), ca),
                reads=cak, writes=arena.k(AL1C + cc * 512, 512)))
            ops.append(lambda: S.op("dve", lambda e: e.tensor_tensor_scan(
                sc_, prm.t[:, PRM_A + cc:PRM_A + cc + 1].broadcast_to([128, 512]),
                vwin(6, 512)[0],
                prm.t[:, PRM_CARRY + cc * 2 + seq:PRM_CARRY + cc * 2 + seq + 1],
                OP.mult, OP.add),
                reads=vwin(6, 512)[1] + prm.k(PRM_A, 16) +
                prm.k(PRM_CARRY + cc * 2 + seq, 1),
                writes=sck))
            ops.append(lambda: S.op("dve", lambda e: e.tensor_copy(
                prm.ap(PRM_CARRY + cc * 2 + seq, 1), scf.ap(slo + 511, 1)),
                reads=sck, writes=prm.k(PRM_CARRY + cc * 2 + seq, 1)))
            ops.append(lambda: S.op("dve", lambda e: e.tensor_copy(
                arena.ap(AL1S + cc * 512, 512), sc_),
                reads=sck, writes=arena.k(AL1S + cc * 512, 512)))
            ops.append(lambda: S.op("dve", lambda e: e.tensor_copy(
                vcs.ap(tail_lo, 6), vwin(512, 6)[0]),
                reads=vwin(0, 518)[1], writes=vcs.k(tail_lo, 6)))
            return ops

        import itertools as _it
        for gcc in range(8):
            chains = []
            for cc in (2 * gcc, 2 * gcc + 1):
                pslot = cc % 3
                po = inproj_mm(16 + cc, pslot)
                chains.append(conv_ops(cc, po, f"psB:{pslot}"))
            for pair in _it.zip_longest(*chains):
                for fn in pair:
                    if fn is not None:
                        fn()
        for cc in range(16):
            pslot = cc % 3
            po = inproj_mm(cc, pslot)
            S.op("sc", lambda e, po=po, cc=cc: e.activation(
                arena.ap(AL1G + cc * 512, 512), po, AF.Sigmoid,
                bias=prm.t[:, PRM_INB + cc:PRM_INB + cc + 1], scale=1.0),
                reads=[f"psB:{pslot}"] + prm.k(PRM_INB, 32),
                writes=arena.k(AL1G + cc * 512, 512))
        if dbg:
            dmain(tap("l1s", [128, 8192], BF16), arena.ap(AL1S, 8192),
                  reads=arena.k(AL1S, 8192))
            dmain(tap("l1c", [128, 8192], BF16), arena.ap(AL1C, 8192),
                  reads=arena.k(AL1C, 8192))
            dmain(tap("l1g", [128, 8192], BF16), arena.ap(AL1G, 8192),
                  reads=arena.k(AL1G, 8192))

        for oc in range(16):
            if oc % 2 == 0:
                mxgrp = load_wblock(
                    vM[:, oc:oc + 2].rearrange("p oc kc m -> p (oc kc m)"),
                    4096, "sM")
            base, wkeys = mxgrp
            base = base + (oc % 2) * 2048
            pslot = oc % 3
            po = psB[:, pslot * 512:pslot * 512 + 512]
            for kc in range(16):
                S.op("pe", lambda e, oc=oc, kc=kc, po=po, base=base: e.matmul(
                    po, wst.ap(base + kc * 128, 128),
                    arena.ap(AL1S + kc * 512, 512),
                    start=(kc == 0), stop=(kc == 15)),
                    reads=wkeys + arena.k(AL1S + kc * 512, 512),
                    writes=[f"psB:{pslot}"])
            S.op("dve", lambda e, oc=oc, po=po: e.scalar_tensor_tensor(
                scf.ap(0, 512), po, prm.t[:, PRM_MIXB + oc:PRM_MIXB + oc + 1],
                arena.ap(AL1C + oc * 512, 512), OP.add, OP.add),
                reads=[f"psB:{pslot}"] + prm.k(PRM_MIXB, 16) +
                arena.k(AL1C + oc * 512, 512),
                writes=scf.k(0, 512))
            S.op("dve", lambda e, oc=oc: e.tensor_tensor(
                arena.ap(AL1V + oc * 512, 512), scf.ap(0, 512),
                arena.ap(AL1G + oc * 512, 512), OP.mult),
                reads=scf.k(0, 512) + arena.k(AL1G + oc * 512, 512),
                writes=arena.k(AL1V + oc * 512, 512))
        obase0, okeys0 = load_wblock(vO[:, 0].rearrange("p kc n -> p (kc n)"),
                                     8192, "sOw")
        obase1, okeys1 = load_wblock(vO[:, 1].rearrange("p kc n -> p (kc n)"),
                                     8192, "sOw")
        for tc in range(NTC):
            for ncd in range(2):
                base = obase0 if ncd == 0 else obase1
                wkeys = okeys0 if ncd == 0 else okeys1
                pslot = (tc * 2 + ncd) % 3
                po = psB[:, pslot * 512:pslot * 512 + 512]
                for kc in range(16):
                    S.op("pe", lambda e, kc=kc, tc=tc, po=po, base=base: e.matmul(
                        po, arena.ap(AL1V + kc * 512 + tc * 128, 128),
                        wst.ap(base + kc * 512, 512),
                        start=(kc == 0), stop=False),
                        reads=arena.k(AL1V + kc * 512 + tc * 128, 128) + wkeys,
                        writes=[f"psB:{pslot}"])
                for bb in range(2):
                    S.op("pe", lambda e, po=po, ncd=ncd, bb=bb: e.matmul(
                        po, ones_bf.t[0:1, :],
                        brow.t[0:1, bb * 1024 + ncd * 512:
                               bb * 1024 + (ncd + 1) * 512],
                        start=False, stop=(bb == 1)),
                        reads=ones_bf.k() + brow.k(bb * 1024 + ncd * 512, 512),
                        writes=[f"psB:{pslot}"])
                sb0 = (tc % 2) * 2048
                S.op("dve", lambda e, tc=tc, ncd=ncd, po=po, sb0=sb0:
                     e.scalar_tensor_tensor(
                         scf.ap(sb0 + ncd * 512, 512),
                         x_tm.ap(tc * 1024 + ncd * 512, 512), 2.0, po,
                         OP.mult, OP.add),
                     reads=x_tm.k(tc * 1024 + ncd * 512, 512) + [f"psB:{pslot}"],
                     writes=scf.k(sb0 + ncd * 512, 512))
            layer_norm((tc % 2) * 2048, tc)
        if dbg:
            dmain(tap("x3", [128, 4096]), x_tm.ap(0, 4096), reads=x_tm.k())

        if PHASE_LIMIT < 4:
            dst = out_d[seq, t0:t0 + TT, :].rearrange("(tc p) d -> p tc d", p=128)
            S.dma("sync", lambda e, dst=dst: e.dma_start(
                out=dst, in_=x_tm.ap(0, 4096).rearrange("p (tc d) -> p tc d",
                                                        d=1024)),
                reads=x_tm.k(), writes=[f"outp:{tile_idx}"])
            return
        transpose_tm_to_fm(x_tm, lambda tc, dc: x_tm.k(tc * 1024 + dc * 128, 128))
        def pwl_ops(oc, po, pkey):
            fb = (oc % 3) * 2560
            ib = (oc % 3) * 512
            A = scf.ap(fb + 0, 512); Ak = scf.k(fb + 0, 512)
            B = scf.ap(fb + 512, 512); Bk = scf.k(fb + 512, 512)
            C = scf.ap(fb + 1024, 512); Ck = scf.k(fb + 1024, 512)
            FR = scf.ap(fb + 1536, 256).bitcast(BF16); FRk = scf.k(fb + 1536, 256)
            MM = scf.ap(fb + 1792, 256).bitcast(BF16); MMk = scf.k(fb + 1792, 256)
            G0 = scf.ap(fb + 2048, 256).bitcast(BF16); G0k = scf.k(fb + 2048, 256)
            G1 = scf.ap(fb + 2304, 256).bitcast(BF16); G1k = scf.k(fb + 2304, 256)
            Ci = sci.ap(ib + 0, 512); Cik = sci.k(ib + 0, 512)
            return [
                lambda: S.op("sc", lambda e: e.activation(
                    A, po, AF.Identity,
                    bias=prm.t[:, PRM_B1S + oc:PRM_B1S + oc + 1],
                    scale=cAP(C256)),
                    reads=[pkey] + prm.k(PRM_B1S, 32) + CSTK, writes=Ak),
                lambda: S.op("sc", lambda e: e.activation(Ci, A, AF.Identity),
                             reads=Ak, writes=Cik),
                lambda: S.op("dve", lambda e: e.tensor_tensor(MM, Ci, A,
                                                              OP.is_le),
                             reads=Cik + Ak, writes=MMk),
                lambda: S.op("dve", lambda e: e.scalar_tensor_tensor(
                    B, Ci, 1, MM, OP.subtract, OP.add),
                    reads=Cik + MMk, writes=Bk),
                lambda: S.op("sc", lambda e: e.activation(
                    C, A, AF.Identity, bias=0.0, scale=cAP(CI64)),
                    reads=Ak + CSTK, writes=Ck),
                lambda: S.op("sc", lambda e: e.activation(Ci, C, AF.Identity),
                             reads=Ck, writes=Cik),
                lambda: S.op("dve", lambda e: e.tensor_tensor(MM, Ci, C,
                                                              OP.is_le),
                             reads=Cik + Ck, writes=MMk),
                lambda: S.op("dve", lambda e: e.scalar_tensor_tensor(
                    A, Ci, 1, MM, OP.subtract, OP.add),
                    reads=Cik + MMk, writes=Ak),
                lambda: S.op("dve", lambda e: e.scalar_tensor_tensor(
                    FR, B, 1.0 / 64.0, A, OP.mult, OP.subtract),
                    reads=Bk + Ak, writes=FRk),
                lambda: S.op("sc", lambda e: e.activation(
                    G0, A, AF.Gelu, bias=cAP(C0), scale=cAP(C025)),
                    reads=Ak + CSTK, writes=G0k),
                lambda: S.op("sc", lambda e: e.activation(
                    G1, A, AF.Gelu, bias=cAP(C025), scale=cAP(C025)),
                    reads=Ak + CSTK, writes=G1k),
                lambda: S.op("dve", lambda e: e.tensor_tensor(MM, G1, G0,
                                                              OP.subtract),
                             reads=G1k + G0k, writes=MMk),
                lambda: S.op("dve", lambda e: e.tensor_tensor(G1, FR, MM,
                                                              OP.mult),
                             reads=FRk + MMk, writes=G1k),
                lambda: S.op("dve", lambda e: e.tensor_tensor(FR, G1, G0,
                                                              OP.add),
                             reads=G1k + G0k, writes=FRk),
                lambda: S.op("sc", lambda e: e.activation(
                    C, FR, AF.Identity, bias=cAP(C05), scale=cAP(C256)),
                    reads=FRk + CSTK, writes=Ck),
                lambda: S.op("sc", lambda e: e.activation(Ci, C, AF.Identity),
                             reads=Ck, writes=Cik),
                lambda: S.op("dve", lambda e: e.tensor_tensor(MM, Ci, C,
                                                              OP.is_le),
                             reads=Cik + Ck, writes=MMk),
                lambda: S.op("dve", lambda e: e.scalar_tensor_tensor(
                    A, Ci, 1, MM, OP.subtract, OP.add),
                    reads=Cik + MMk, writes=Ak),
                lambda: S.op("dve", lambda e: e.tensor_single_scalar(
                    Ci, B, 1024.0, OP.is_gt),
                    reads=Bk, writes=Cik),
                lambda: S.op("dve", lambda e: e.copy_predicated(A, Ci, B),
                             reads=Cik + Bk + Ak, writes=Ak),
                lambda: S.op("sc", lambda e: e.activation(
                    arena.ap(AH + oc * 512, 512), A, AF.Identity,
                    bias=0.0, scale=cAP(CI256)),
                    reads=Ak + CSTK, writes=arena.k(AH + oc * 512, 512)),
            ]

        w1grp = None
        import itertools as _it3
        W2C = 4096
        W2A_BANKS = [(psA[:, 0:512], "psA:0"), (psA[:, 512:1024], "psA:1"),
                     (psA[:, 1024:1536], "psA:2"), (psT[:, 0:512], "psT")]

        def w2a_mms(oc):
            for tc in range(NTC):
                bank, bkey = W2A_BANKS[tc]
                S.op("pe", lambda e, oc=oc, tc=tc, bank=bank: e.matmul(
                    bank, arena.ap(AH + oc * 512 + tc * 128, 128),
                    wst.ap(W2C + (oc % 8) * 512, 512),
                    start=(oc == 0), stop=False),
                    reads=arena.k(AH + oc * 512 + tc * 128, 128) +
                    wst.k(W2C + (oc % 8) * 512, 512),
                    writes=[bkey])

        gi = 0
        prev_grp = []
        nd1_ld = [None, None]
        while gi < 32:
            grp = list(range(gi, min(gi + 3, 32)))
            gi += len(grp)
            for oc in grp:
                dmain(wst.ap(W2C + (oc % 8) * 512, 512), vW2[:, 0, oc, :],
                      reads=["sW2"], writes=wst.k(W2C + (oc % 8) * 512, 512))
            chains = []
            for oc in grp:
                if oc % 4 == 0:
                    w1grp = load_wblock(
                        vW1[:, oc:oc + 4].rearrange("p oc kc m -> p (oc kc m)"),
                        4096, "sW1")
                base, wkeys = w1grp
                base = base + (oc % 4) * 1024
                pslot = oc % 3
                po = psB[:, pslot * 512:pslot * 512 + 512]
                pkey = f"psB:{pslot}"
                for kc in range(8):
                    S.op("pe", lambda e, kc=kc, po=po, base=base: e.matmul(
                        po, wst.ap(base + kc * 128, 128),
                        fmbuf.ap(kc * 512, 512),
                        start=(kc == 0), stop=(kc == 7)),
                        reads=wkeys + fmbuf.k(kc * 512, 512), writes=[pkey])
                chains.append(pwl_ops(oc, po, pkey))
            for oc in prev_grp:
                w2a_mms(oc)
            prev_grp = grp
            for stg in _it3.zip_longest(*chains):
                for fn in stg:
                    if fn is not None:
                        fn()
        for oc in prev_grp:
            w2a_mms(oc)
        nd1_ld[0] = load_wblock(
            vW2[:, 1, 0:16, :].rearrange("p kc n -> p (kc n)"), 8192, "sW2")
        nd1_ld[1] = load_wblock(
            vW2[:, 1, 16:32, :].rearrange("p kc n -> p (kc n)"), 8192, "sW2")
        for tc in range(NTC):
            bank, bkey = W2A_BANKS[tc]
            for bb in range(2):
                S.op("pe", lambda e, bank=bank, bb=bb: e.matmul(
                    bank, ones_bf.t[0:1, :],
                    brow.t[0:1, 2048 + bb * 1024:2048 + bb * 1024 + 512],
                    start=False, stop=(bb == 1)),
                    reads=ones_bf.k() + brow.k(2048 + bb * 1024, 512),
                    writes=[bkey])
        if dbg:
            dmain(tap("hbf", [128, 16384], BF16), arena.ap(AH, 16384),
                  reads=arena.k(AH, 16384))

        def w2_evac(po, pkey, tc, ncd):
            j = (tc + ncd) % 2
            us, uk = scf.ap(1024 + j * 512, 512), scf.k(1024 + j * 512, 512)
            ci, cik = sci.ap(j * 1024, 512), sci.k(j * 1024, 512)
            mi, mik = sci.ap(j * 1024 + 512, 512), sci.k(j * 1024 + 512, 512)
            S.op("sc", lambda e, po=po, us=us: e.activation(
                us, po, AF.Identity, bias=cAP(C05), scale=cAP(C256)),
                reads=[pkey] + CSTK, writes=uk)
            floor_fixup(us, uk, ci, cik, mi, mik, ci, cik)
            S.op("dve", lambda e, tc=tc, ncd=ncd, ci=ci: e.tensor_scalar(
                scf.ap(3584 + tc * 1024 + ncd * 512, 512), ci,
                cAP(CI256), None, OP.mult),
                reads=cik + CSTK,
                writes=scf.k(3584 + tc * 1024 + ncd * 512, 512))

        for tc in range(NTC):
            bank, bkey = W2A_BANKS[tc]
            w2_evac(bank, bkey, tc, 0)
        bA, kA = nd1_ld[0]
        bB, kB = nd1_ld[1]
        for tc in range(NTC):
            pslot = tc % 3
            po = psB[:, pslot * 512:pslot * 512 + 512]
            pkey = f"psB:{pslot}"
            for kc in range(32):
                base = bA if kc < 16 else bB
                wk_ = kA if kc < 16 else kB
                off = base + (kc % 16) * 512
                S.op("pe", lambda e, kc=kc, tc=tc, po=po, off=off: e.matmul(
                    po, arena.ap(AH + kc * 512 + tc * 128, 128),
                    wst.ap(off, 512),
                    start=(kc == 0), stop=False),
                    reads=arena.k(AH + kc * 512 + tc * 128, 128) + wk_,
                    writes=[pkey])
            for bb in range(2):
                S.op("pe", lambda e, po=po, bb=bb: e.matmul(
                    po, ones_bf.t[0:1, :],
                    brow.t[0:1, 2048 + bb * 1024 + 512:
                           2048 + bb * 1024 + 1024],
                    start=False, stop=(bb == 1)),
                    reads=ones_bf.k() + brow.k(2048 + bb * 1024 + 512, 512),
                    writes=[pkey])
            w2_evac(po, pkey, tc, 1)
        for tc in range(NTC):
            for ncd in range(2):
                elo = 3584 + tc * 1024 + ncd * 512
                sb0 = (tc % 2) * 2048
                S.op("dve", lambda e, elo=elo, tc=tc, ncd=ncd, sb0=sb0:
                     e.tensor_tensor(
                         scf.ap(sb0 + ncd * 512, 512), scf.ap(elo, 512),
                         x_tm.ap(tc * 1024 + ncd * 512, 512), OP.add),
                     reads=scf.k(elo, 512) + x_tm.k(tc * 1024 + ncd * 512, 512),
                     writes=scf.k(sb0 + ncd * 512, 512))
            layer_norm((tc % 2) * 2048, tc)
        if dbg:
            dmain(tap("x4", [128, 4096]), x_tm.ap(0, 4096), reads=x_tm.k())

        dst = out_d[seq, t0:t0 + TT, :].rearrange("(tc p) d -> p tc d", p=128)
        S.dma("sync", lambda e, dst=dst: e.dma_start(
            out=dst, in_=x_tm.ap(0, 4096).rearrange("p (tc d) -> p tc d", d=1024)),
            reads=x_tm.k(), writes=[f"outp:{tile_idx}"])

    n_seq_emit, n_tt_emit = EMIT_TILES or (SEQ_PER_CORE, TILES_PER_SEQ)
    tile_idx = 0
    for seq in range(n_seq_emit):
        if seq > 0:
            S.op("dve", lambda e: e.memset(ret_state.ap(0, 4096), 0.0),
                 reads=ret_state.k(), writes=ret_state.k())
        for tt in range(n_tt_emit):
            emit_tile(seq, tt, tile_idx)
            tile_idx += 1

    S.op("sync", None, reads=[f"outp:{i}" for i in range(tile_idx)])

    S.finalize(nc, es)
    with es:
        blk = es.enter_context(nc.Block())
        blk.sync(lambda eng: S.emit("sync", eng))
        blk.vector(lambda eng: S.emit("dve", eng))
        blk.scalar(lambda eng: S.emit("sc", eng))
        blk.tensor(lambda eng: S.emit("pe", eng))
        blk.gpsimd(lambda eng: S.emit("gp", eng))
    return nc


def _xpos_tables():
    d = np.arange(0, 1024, 2, dtype=np.float64)
    base_scale = (d + 0.4 * 1024.0) / (1.4 * 1024.0)
    pos = np.arange(T, dtype=np.float64)
    scale = base_scale[None, :] ** (pos[:, None] / 512.0)
    inv_freq = 1.0 / 10000.0 ** (np.arange(512, dtype=np.float64) / 512.0)
    sinu = pos[:, None] * inv_freq[None, :]
    sin, cos = np.sin(sinu), np.cos(sinu)
    def expand(tbl):
        out = np.empty((1024, T), np.float64)
        out[0::2] = tbl.T
        out[1::2] = tbl.T
        return out
    sgn = np.empty((1024, 1), np.float64)
    sgn[0::2] = -1.0
    sgn[1::2] = 1.0
    cq = expand(cos * scale)
    sq = expand(sin * scale) * sgn
    ck = expand(cos / scale)
    sk = expand(sin / scale) * sgn
    def tile(tbl):
        return np.ascontiguousarray(
            tbl.reshape(8, 128, TILES_PER_SEQ, TT).transpose(2, 0, 1, 3)
        ).astype(np.float32)
    return tile(cq), tile(sq), tile(ck), tile(sk)


_PROG_CACHE = {}


def kernel(L0_attn_hidden_states, L0_attn_W_qkv, L0_attn_W_g, L0_attn_W_proj,
           L0_attn_decay, L0_ffn_packed_weight, L0_ffn_weight_scale, L0_ffn_bias,
           L1_attn_norm_gamma, L1_attn_norm_beta, L1_attn_in_w, L1_attn_in_b,
           L1_attn_dw_w, L1_attn_dw_b, L1_attn_log_decay, L1_attn_mix_w,
           L1_attn_mix_b, L1_attn_out_w, L1_attn_out_b, L1_ffn_w1, L1_ffn_b1,
           L1_ffn_w2, L1_ffn_b2):
    x = np.ascontiguousarray(L0_attn_hidden_states, dtype=np.float32)
    cq, sq, ck, sk = _xpos_tables()
    consts = np.tile(np.array(CONSTS, np.float32)[None, :], (128, 1))
    idm = np.eye(128, dtype=np.float32)
    key = ("prog", DEBUG_TAPS, EMIT_TILES, PHASE_LIMIT, SUBPHASE)
    if key not in _PROG_CACHE:
        _PROG_CACHE[key] = build_program(DEBUG_TAPS)
    nc = _PROG_CACHE[key]
    common = {
        "wqkv": np.ascontiguousarray(L0_attn_W_qkv, np.float32),
        "wg": np.ascontiguousarray(L0_attn_W_g, np.float32),
        "wproj": np.ascontiguousarray(L0_attn_W_proj, np.float32),
        "decay": np.ascontiguousarray(L0_attn_decay, np.float32).reshape(1, H),
        "packed": np.ascontiguousarray(L0_ffn_packed_weight,
                                       np.int32).reshape(256, 1024),
        "wscale": np.ascontiguousarray(L0_ffn_weight_scale,
                                       np.float32).reshape(1, 1),
        "bias0": np.ascontiguousarray(L0_ffn_bias, np.float32).reshape(1, D),
        "gamma": np.ascontiguousarray(L1_attn_norm_gamma, np.float32),
        "beta": np.ascontiguousarray(L1_attn_norm_beta, np.float32),
        "in_w": np.ascontiguousarray(L1_attn_in_w, np.float32),
        "in_b": np.ascontiguousarray(L1_attn_in_b, np.float32),
        "dw_w": np.ascontiguousarray(L1_attn_dw_w, np.float32),
        "dw_b": np.ascontiguousarray(L1_attn_dw_b, np.float32),
        "log_decay": np.ascontiguousarray(L1_attn_log_decay, np.float32),
        "mix_w": np.ascontiguousarray(L1_attn_mix_w, np.float32),
        "mix_b": np.ascontiguousarray(L1_attn_mix_b, np.float32),
        "out_w": np.ascontiguousarray(L1_attn_out_w, np.float32),
        "out_b": np.ascontiguousarray(L1_attn_out_b, np.float32).reshape(1, D),
        "w1": np.ascontiguousarray(L1_ffn_w1, np.float32),
        "b1": np.ascontiguousarray(L1_ffn_b1, np.float32),
        "w2": np.ascontiguousarray(L1_ffn_w2, np.float32),
        "b2": np.ascontiguousarray(L1_ffn_b2, np.float32).reshape(1, D),
        "consts": consts, "cq_t": cq, "sq_t": sq, "ck_t": ck, "sk_t": sk,
        "idm": idm,
    }
    in_maps = []
    for c in range(N_CORES):
        m = dict(common)
        m["xin"] = np.ascontiguousarray(x[c * SEQ_PER_CORE:(c + 1) * SEQ_PER_CORE])
        in_maps.append(m)
    res = run_bass_kernel_spmd(nc, in_maps, core_ids=list(range(N_CORES)))
    if DEBUG_TAPS:
        kernel._last_results = res.results
    out = np.concatenate([r["out"] for r in res.results], axis=0)
    return out.reshape(B, T, D)


# version 13
# speedup vs baseline: 1.0921x; 1.0020x over previous
import numpy as np
from contextlib import ExitStack

import concourse.bass as bass
import concourse.mybir as mybir
from concourse.bass_utils import run_bass_kernel_spmd

F32 = mybir.dt.float32
BF16 = mybir.dt.bfloat16
I32 = mybir.dt.int32
AF = mybir.ActivationFunctionType
OP = mybir.AluOpType
AX = mybir.AxisListType

DEBUG_TAPS = False
PHASE_LIMIT = 4
SUBPHASE = 9
EMIT_TILES = None
N_CORES = 8
B, T, D = 16, 4096, 1024
SEQ_PER_CORE = 2
TT = 512
NTC = TT // 128
TILES_PER_SEQ = T // TT
H = 8
HC = 2048
FF = 4096
INV_SQRT_DK = float(1.0 / np.sqrt(128.0))

CONSTS = [256.0, 1.0 / 256.0, 1.0 / 64.0, 0.25, -64.0, -63.75, 0.5, 1e-5, 127.0,
          1.0 / 1024.0, 64.0, 2.0, 16.0, -0.5, 1.0, 0.0]
C256, CI256, CI64, C025, CN64, CN6375, C05, CEPS, C127, CI1024, C64, C2, C16, CN05, C1, C0 = range(16)


class Sched:
    ENGS = ("pe", "dve", "sc", "gp", "sync")

    def __init__(self):
        self.ops = {e: [] for e in self.ENGS}
        self.W = {}
        self.R = {}

    def _record(self, eng, fn, reads, writes, extra, is_dma):
        deps = []
        for b in reads:
            t = self.W.get(b)
            if t is not None:
                deps.append(t)
        for b in writes:
            deps.extend(self.R.get(b, ()))
            t = self.W.get(b)
            if t is not None:
                deps.append(t)
        deps.extend(extra)
        tok = (eng, len(self.ops[eng]))
        self.ops[eng].append(dict(fn=fn, deps=set(deps), is_dma=is_dma,
                                  awaited=False))
        for b in reads:
            self.R.setdefault(b, []).append(tok)
        for b in writes:
            self.W[b] = tok
            self.R[b] = []
        return tok

    def op(self, eng, fn, reads=(), writes=(), extra=()):
        return self._record(eng, fn, reads, writes, extra, False)

    def dma(self, eng, fn, reads=(), writes=(), extra=()):
        return self._record(eng, fn, reads, writes, extra, True)

    NPOOL = 24

    def finalize(self, nc, es):
        self.sems = {}
        for e in self.ENGS:
            if any(not o["is_dma"] for o in self.ops[e]):
                self.sems[e] = es.enter_context(nc.semaphore(f"s_{e}"))
        for p in range(self.NPOOL):
            self.sems[f"d{p}"] = es.enter_context(nc.semaphore(f"d{p}"))
        SELF_SYNC = ("dve", "sc", "gp")
        for e in self.ENGS:
            for o in self.ops[e]:
                for (de, di) in o["deps"]:
                    dep = self.ops[de][di]
                    if dep["is_dma"] or de != e or de in SELF_SYNC:
                        dep["awaited"] = True
        self.val = {}
        pool_use = [0] * self.NPOOL
        dma_ord = 0
        for e in self.ENGS:
            cc_ = 0
            for i, o in enumerate(self.ops[e]):
                if o["is_dma"]:
                    p = dma_ord % self.NPOOL
                    dma_ord += 1
                    pool_use[p] += 1
                    sem_name = f"d{p}"
                    o["pre_wait"] = (sem_name, 16 * (pool_use[p] - 1))
                    o["inc"] = (sem_name, 16)
                    self.val[(e, i)] = (sem_name, 16 * pool_use[p])
                elif o["awaited"]:
                    cc_ += 1
                    o["inc"] = (e, 1)
                    self.val[(e, i)] = (e, cc_)
                else:
                    o["inc"] = None

    def emit(self, ename, eng):
        clock = {}
        for i, o in enumerate(self.ops[ename]):
            want = {}
            for (de, di) in sorted(o["deps"]):
                dep = self.ops[de][di]
                if (not dep["is_dma"] and de == ename
                        and de not in ("dve", "sc", "gp")):
                    continue
                sem_name, v = self.val[(de, di)]
                if v > want.get(sem_name, 0):
                    want[sem_name] = v
            pw = o.get("pre_wait")
            if pw is not None and pw[1] > want.get(pw[0], 0):
                want[pw[0]] = pw[1]
            for sem_name, v in sorted(want.items()):
                if v > clock.get(sem_name, 0):
                    eng.wait_ge(self.sems[sem_name], v)
                    clock[sem_name] = v
            inst = o["fn"](eng) if o["fn"] is not None else None
            if o["inc"] is not None:
                assert inst is not None, f"awaited op with no inst on {ename}:{i}"
                sem_name, amt = o["inc"]
                inst.then_inc(self.sems[sem_name], amt)


class TBuf:
    def __init__(self, t, nm, blk=256):
        self.t = t
        self.nm = nm
        self.blk = blk

    def ap(self, lo, n):
        return self.t[:, lo:lo + n]

    def k(self, lo=0, n=None):
        if n is None:
            n = self.t.shape[1] - lo
        b = self.blk
        return [f"{self.nm}:{i}" for i in range(lo // b, (lo + n + b - 1) // b)]


def build_program(debug_taps=False):
    nc = bass.Bass("TRN2", target_bir_lowering=False, detect_race_conditions=False)
    S = Sched()
    es = ExitStack()

    din = {}
    def dram_in(name, shape, dt=F32):
        din[name] = nc.dram_tensor(name, list(shape), dt, kind="ExternalInput").ap()
        return din[name]

    xin = dram_in("xin", [SEQ_PER_CORE, T, D])
    wqkv = dram_in("wqkv", [D, 4096])
    wg_in = dram_in("wg", [D, 2048])
    wproj_in = dram_in("wproj", [2048, D])
    decay_in = dram_in("decay", [1, H])
    packed_in = dram_in("packed", [256, 1024], I32)
    ws_in = dram_in("wscale", [1, 1])
    bias0_in = dram_in("bias0", [1, D])
    gamma_in = dram_in("gamma", [D])
    beta_in = dram_in("beta", [D])
    inw_in = dram_in("in_w", [D, 2 * HC])
    inb_in = dram_in("in_b", [2 * HC])
    dww_in = dram_in("dw_w", [HC, 7])
    dwb_in = dram_in("dw_b", [HC])
    logd_in = dram_in("log_decay", [HC])
    mixw_in = dram_in("mix_w", [HC, HC])
    mixb_in = dram_in("mix_b", [HC])
    outw_in = dram_in("out_w", [HC, D])
    outb_in = dram_in("out_b", [1, D])
    w1_in = dram_in("w1", [D, FF])
    b1_in = dram_in("b1", [FF])
    w2_in = dram_in("w2", [FF, D])
    b2_in = dram_in("b2", [1, D])
    consts_in = dram_in("consts", [128, len(CONSTS)])
    cq_in = dram_in("cq_t", [TILES_PER_SEQ, 8, 128, TT])
    sq_in = dram_in("sq_t", [TILES_PER_SEQ, 8, 128, TT])
    ck_in = dram_in("ck_t", [TILES_PER_SEQ, 8, 128, TT])
    sk_in = dram_in("sk_t", [TILES_PER_SEQ, 8, 128, TT])
    idm_in = dram_in("idm", [128, 128])

    out_d = nc.dram_tensor("out", [SEQ_PER_CORE, T, D], F32, kind="ExternalOutput").ap()

    taps = {}
    def tap(name, shape, dt=F32):
        if name not in taps:
            taps[name] = nc.dram_tensor("tap_" + name, list(shape), dt,
                                        kind="ExternalOutput").ap()
        return taps[name]

    def scr(name, n_el, dt=BF16):
        return nc.dram_tensor(name, [n_el], dt, kind="Internal").ap()

    sQ = scr("sQ", 128 * 8 * 8 * 128)
    sK = scr("sK", 128 * 8 * 8 * 128)
    sV = scr("sV", 128 * 4 * 8 * 512)
    sGw = scr("sGw", 128 * 4 * 8 * 512)
    sP = scr("sP", 128 * 2 * 16 * 512)
    sTn = scr("sTn", 128 * 2 * 8 * 512)
    sI = scr("sI", 128 * 32 * 8 * 128)
    sM = scr("sM", 128 * 16 * 16 * 128)
    sOw = scr("sOw", 128 * 2 * 16 * 512)
    sW1 = scr("sW1", 128 * 32 * 8 * 128)
    sW2 = scr("sW2", 128 * 2 * 32 * 512)

    def sb(nm, shape, dt=F32, blk=256):
        t = es.enter_context(nc.sbuf_tensor(nm, list(shape), dt))
        return TBuf(t, nm, blk)

    x_tm = sb("x_tm", [128, NTC * 1024])
    arena = sb("arena", [128, 32768], BF16, blk=512)
    fmbuf = sb("fmbuf", [128, 8 * 512], BF16, blk=512)
    scf = sb("scf", [128, 7680])
    sci = sb("sci", [128, 2048], I32, blk=512)
    ret_state = sb("ret_state", [128, 16 * 256])
    ret_sm = sb("ret_sm", [128, 1536], BF16, blk=128)
    masks = sb("masks", [128, H * 512], BF16, blk=512)
    rotb = sb("rotb", [128, 1024], F32, blk=512)
    wst = sb("wst", [128, 16384], BF16, blk=8192)
    brep = sb("brep", [128, 1024], F32, blk=1024)
    ones_bf = sb("ones_bf", [1, 128], BF16, blk=128)
    brow = sb("brow", [1, 4096], BF16, blk=1024)
    prm = sb("prm", [128, 232], F32, blk=8)
    cst = sb("cst", [128, len(CONSTS)], F32, blk=16)
    stat = sb("stat", [128, 64], F32, blk=8)
    ident_f = sb("ident_f", [128, 128], F32, blk=128)
    ident_b = sb("ident_b", [128, 128], BF16, blk=128)
    vcs = sb("vcs", [128, 712], F32, blk=64)
    dwwt = sb("dwwt", [128, 112], F32, blk=112)

    PRM_LND, PRM_D256, PRM_WR, PRM_WK, PRM_A, PRM_CARRY = 0, 8, 16, 32, 48, 64
    PRM_INB, PRM_MIXB, PRM_DWB = 96, 128, 144
    PRM_GAM, PRM_BET, PRM_B1S, PRM_WS = 160, 168, 176, 224
    VTAILS = 520

    psA = es.enter_context(nc.psum_tensor("psA", [128, 1536], F32))
    psB = es.enter_context(nc.psum_tensor("psB", [128, 1536], F32))
    psT = es.enter_context(nc.psum_tensor("psT", [128, 512], F32))
    psTb = es.enter_context(nc.psum_tensor("psTb", [128, 1024], BF16))
    TPF32 = [(psT, 0, "psT"), (psA, 0, "psA:0"), (psA, 512, "psA:1"),
             (psA, 1024, "psA:2")]
    TPBF = [(psTb, 0, "psTb"), (psB, 1024, "psB:2"), (psB, 0, "psB:0"),
            (psB, 512, "psB:1")]
    def tp_f32(i):
        t, off, k = TPF32[i % 4]
        return t[:, off:off + 128], k
    def tp_bf(i):
        t, off, k = TPBF[i % 4]
        if k == "psTb":
            return t[:, off:off + 128], k
        return t[:, off:off + 64].bitcast(BF16), k

    cAP = lambda i: cst.t[:, i:i + 1]
    CSTK = cst.k(0)

    def dmain(dst_ap, src_ap, reads=(), writes=(), slow=False):
        return S.dma("sync", lambda e, d=dst_ap, s=src_ap, sl=slow: e.dma_start(
            out=d, in_=s, allow_slow_non_contiguous=sl),
            reads=reads, writes=writes)

    def repl_row(buf, lo, hi):
        n = 1
        while n < 128:
            m = min(n, 128 - n)
            S.dma("sync", lambda e, n=n, m=m: e.dma_start(
                out=buf.t[n:n + m, lo:hi], in_=buf.t[0:m, lo:hi]),
                reads=buf.k(lo, hi - lo), writes=buf.k(lo, hi - lo))
            n += m

    wslot = [0]
    def load_wblock(src3d_ap, n_el, src_key):
        slot = wslot[0]
        wslot[0] ^= 1
        base = slot * 8192
        assert n_el <= 8192
        dmain(wst.ap(base, n_el), src3d_ap, reads=[src_key],
              writes=wst.k(base, n_el))
        return base, wst.k(base, n_el)

    vQ = sQ.rearrange("(p oc kc m) -> p oc kc m", p=128, oc=8, kc=8)
    vK = sK.rearrange("(p oc kc m) -> p oc kc m", p=128, oc=8, kc=8)
    vV = sV.rearrange("(p ncd kc n) -> p ncd kc n", p=128, ncd=4, kc=8)
    vG = sGw.rearrange("(p ncd kc n) -> p ncd kc n", p=128, ncd=4, kc=8)
    vP = sP.rearrange("(p ncd kc n) -> p ncd kc n", p=128, ncd=2, kc=16)
    vTn = sTn.rearrange("(p ncd kc n) -> p ncd kc n", p=128, ncd=2, kc=8)
    vI = sI.rearrange("(p oc kc m) -> p oc kc m", p=128, oc=32, kc=8)
    vM = sM.rearrange("(p oc kc m) -> p oc kc m", p=128, oc=16, kc=16)
    vO = sOw.rearrange("(p ncd kc n) -> p ncd kc n", p=128, ncd=2, kc=16)
    vW1 = sW1.rearrange("(p oc kc m) -> p oc kc m", p=128, oc=32, kc=8)
    vW2 = sW2.rearrange("(p ncd kc n) -> p ncd kc n", p=128, ncd=2, kc=32)

    dmain(cst.ap(0, len(CONSTS)), consts_in, writes=CSTK)
    dmain(ident_f.ap(0, 128), idm_in, writes=ident_f.k())
    S.op("dve", lambda e: e.tensor_copy(ident_b.ap(0, 128), ident_f.ap(0, 128)),
         reads=ident_f.k(), writes=ident_b.k())

    def prep_weight(src_ap, n_rows, n_cols, view, kind, scr_key):
        n_kc = n_rows // 128
        for kc in range(n_kc):
            dmain(scf.ap(0, n_cols), src_ap[kc * 128:(kc + 1) * 128, :],
                  writes=scf.k(0, n_cols))
            S.op("dve", lambda e, c=n_cols: e.tensor_copy(arena.ap(0, c),
                                                          scf.ap(0, c)),
                 reads=scf.k(0, n_cols), writes=arena.k(0, n_cols))
            if kind == "lhsT":
                dst = view[:, :, kc, :]
                src = arena.ap(0, n_cols).rearrange("p (oc m) -> p oc m", m=128)
            else:
                dst = view[:, :, kc, :]
                src = arena.ap(0, n_cols).rearrange("p (ncd n) -> p ncd n", n=512)
            S.dma("sync", lambda e, dst=dst, src=src: e.dma_start(out=dst, in_=src),
                  reads=arena.k(0, n_cols), writes=[scr_key])

    prep_weight(wqkv[:, 0:1024], 1024, 1024, vQ, "lhsT", "sQ")
    prep_weight(wqkv[:, 1024:2048], 1024, 1024, vK, "lhsT", "sK")
    prep_weight(wqkv[:, 2048:4096], 1024, 2048, vV, "rhs", "sV")
    prep_weight(wg_in, 1024, 2048, vG, "rhs", "sGw")
    prep_weight(wproj_in, 2048, 1024, vP, "rhs", "sP")
    prep_weight(inw_in, 1024, 4096, vI, "lhsT", "sI")
    prep_weight(mixw_in, 2048, 2048, vM, "lhsT", "sM")
    prep_weight(outw_in, 2048, 1024, vO, "rhs", "sOw")
    prep_weight(w1_in, 1024, 4096, vW1, "lhsT", "sW1")
    prep_weight(w2_in, 4096, 1024, vW2, "rhs", "sW2")

    def floor_fixup(u_ap, u_keys, c_ap, c_keys, m_ap, m_keys, f_ap, f_keys):
        S.op("sc", lambda e: e.activation(c_ap, u_ap, AF.Identity),
             reads=u_keys, writes=c_keys)
        S.op("dve", lambda e: e.tensor_tensor(m_ap, c_ap, u_ap, OP.is_le),
             reads=c_keys + u_keys, writes=m_keys)
        S.op("dve", lambda e: e.scalar_tensor_tensor(f_ap, c_ap, 1, m_ap,
                                                     OP.subtract, OP.add),
             reads=c_keys + m_keys, writes=f_keys)

    for oc2 in range(8):
        j = oc2 // 2
        r0 = (oc2 % 2) * 128
        dmain(sci.ap(0, 1024), packed_in[r0:r0 + 128, :], writes=sci.k(0, 1024))
        S.op("dve", lambda e: e.tensor_copy(scf.ap(0, 1024), sci.ap(0, 1024)),
             reads=sci.k(0, 1024), writes=scf.k(0, 1024))
        def floor_scaled(dst_off, scale_val):
            S.op("dve", lambda e, s=scale_val, d=dst_off: e.tensor_scalar(
                scf.ap(d, 1024), scf.ap(0, 1024), s, None, OP.mult),
                reads=scf.k(0, 1024), writes=scf.k(dst_off, 1024))
            floor_fixup(scf.ap(dst_off, 1024), scf.k(dst_off, 1024),
                        sci.ap(1024, 1024), sci.k(1024, 1024),
                        scf.ap(4096, 1024), scf.k(4096, 1024),
                        sci.ap(1024, 1024), sci.k(1024, 1024))
            S.op("dve", lambda e, d=dst_off: e.tensor_copy(scf.ap(d, 1024),
                                                           sci.ap(1024, 1024)),
                 reads=sci.k(1024, 1024), writes=scf.k(dst_off, 1024))
        if j == 0:
            floor_scaled(1024, 0.25)
            S.op("dve", lambda e: e.scalar_tensor_tensor(
                scf.ap(2048, 1024), scf.ap(1024, 1024), -4.0, scf.ap(0, 1024),
                OP.mult, OP.add),
                reads=scf.k(0, 2048), writes=scf.k(2048, 1024))
        else:
            floor_scaled(1024, 0.25 ** j)
            floor_scaled(3072, 0.25 ** (j + 1))
            S.op("dve", lambda e: e.scalar_tensor_tensor(
                scf.ap(2048, 1024), scf.ap(3072, 1024), -4.0, scf.ap(1024, 1024),
                OP.mult, OP.add),
                reads=scf.k(1024, 1024) + scf.k(3072, 1024),
                writes=scf.k(2048, 1024))
        S.op("dve", lambda e: e.tensor_scalar(arena.ap(0, 1024), scf.ap(2048, 1024),
                                              1.0, None, OP.subtract),
             reads=scf.k(2048, 1024), writes=arena.k(0, 1024))
        for ic in range(8):
            S.op("pe", lambda e, ic=ic: e.transpose(
                psTb[:, 0:128], arena.ap(ic * 128, 128), ident_b.ap(0, 128)),
                reads=arena.k(0, 1024) + ident_b.k(), writes=["psTb"])
            S.op("dve", lambda e: e.tensor_copy(arena.ap(4096, 128), psTb[:, 0:128]),
                 reads=["psTb"], writes=arena.k(4096, 128))
            dstv = vTn[:, oc2 // 4, ic, (oc2 % 4) * 128:(oc2 % 4) * 128 + 128]
            S.dma("sync", lambda e, dstv=dstv: e.dma_start(out=dstv,
                                                           in_=arena.ap(4096, 128)),
                  reads=arena.k(4096, 128), writes=["sTn"])

    dmain(scf.t[0:1, 2048:2056], decay_in, writes=scf.k(2048, 256))
    S.op("sc", lambda e: e.activation(prm.t[0:1, PRM_LND:PRM_LND + 8],
                                      scf.t[0:1, 2048:2056], AF.Ln),
         reads=scf.k(2048, 256), writes=prm.k(PRM_LND, 8))
    repl_row(prm, PRM_LND, PRM_LND + 8)
    S.op("dve", lambda e: e.tensor_scalar(scf.ap(2048, 8), prm.ap(PRM_LND, 8),
                                          256.0, None, OP.mult),
         reads=prm.k(PRM_LND, 8), writes=scf.k(2048, 256))
    S.op("sc", lambda e: e.activation(prm.ap(PRM_D256, 8), scf.ap(2048, 8),
                                      AF.Exp),
         reads=scf.k(2048, 256), writes=prm.k(PRM_D256, 8))
    S.op("gp", lambda e: e.iota(sci.ap(0, 512), [[-128, 2], [1, 256]], base=0,
                                channel_multiplier=-1),
         writes=sci.k(0, 512))
    S.op("dve", lambda e: e.tensor_copy(scf.ap(0, 512), sci.ap(0, 512)),
         reads=sci.k(0, 512), writes=scf.k(0, 512))
    S.op("dve", lambda e: e.tensor_single_scalar(scf.ap(512, 512), scf.ap(0, 512),
                                                 0.0, OP.is_ge),
         reads=scf.k(0, 512), writes=scf.k(512, 512))
    S.op("dve", lambda e: e.tensor_scalar(scf.ap(1536, 512), scf.ap(512, 512),
                                          INV_SQRT_DK, None, OP.mult),
         reads=scf.k(512, 512), writes=scf.k(1536, 512))
    for h in range(H):
        S.op("dve", lambda e, h=h: e.tensor_scalar(
            scf.ap(1024, 512), scf.ap(0, 512),
            prm.t[:, PRM_LND + h:PRM_LND + h + 1], None, OP.mult),
            reads=scf.k(0, 512) + prm.k(PRM_LND, 8), writes=scf.k(1024, 512))
        S.op("sc", lambda e: e.activation(scf.ap(1024, 512), scf.ap(1024, 512),
                                          AF.Exp),
             reads=scf.k(1024, 512), writes=scf.k(1024, 512))
        S.op("dve", lambda e, h=h: e.tensor_tensor(
            masks.ap(h * 512, 512), scf.ap(1024, 512), scf.ap(1536, 512), OP.mult),
            reads=scf.k(1024, 512) + scf.k(1536, 512),
            writes=masks.k(h * 512, 512))
    S.op("gp", lambda e: e.iota(sci.ap(0, 2), [[128, 2]], base=1,
                                channel_multiplier=1),
         writes=sci.k(0, 512))
    S.op("dve", lambda e: e.tensor_copy(scf.ap(0, 2), sci.ap(0, 2)),
         reads=sci.k(0, 512), writes=scf.k(0, 256))
    S.op("gp", lambda e: e.iota(sci.ap(512, 2), [[-128, 2]], base=255,
                                channel_multiplier=-1),
         writes=sci.k(512, 512))
    S.op("dve", lambda e: e.tensor_copy(scf.ap(256, 2), sci.ap(512, 2)),
         reads=sci.k(512, 512), writes=scf.k(256, 256))
    for h in range(H):
        for (so, dst) in ((0, PRM_WR), (256, PRM_WK)):
            S.op("dve", lambda e, h=h, so=so: e.tensor_scalar(
                scf.ap(2048, 2), scf.ap(so, 2),
                prm.t[:, PRM_LND + h:PRM_LND + h + 1], None, OP.mult),
                reads=scf.k(so, 256) + prm.k(PRM_LND, 8),
                writes=scf.k(2048, 256))
            S.op("sc", lambda e: e.activation(
                scf.ap(2304, 2), scf.ap(2048, 2), AF.Exp),
                reads=scf.k(2048, 256), writes=scf.k(2304, 256))
            if dst == PRM_WR:
                S.op("dve", lambda e, h=h, dst=dst: e.tensor_scalar(
                    prm.ap(dst + 2 * h, 2), scf.ap(2304, 2),
                    INV_SQRT_DK, None, OP.mult),
                    reads=scf.k(2304, 256), writes=prm.k(dst + 2 * h, 2))
            else:
                S.op("dve", lambda e, h=h, dst=dst: e.tensor_copy(
                    prm.ap(dst + 2 * h, 2), scf.ap(2304, 2)),
                    reads=scf.k(2304, 256), writes=prm.k(dst + 2 * h, 2))
    dmain(scf.ap(2048, 16), logd_in.rearrange("(c p) -> p c", p=128),
          writes=scf.k(2048, 256), slow=True)
    S.op("sc", lambda e: e.activation(prm.ap(PRM_A, 16), scf.ap(2048, 16),
                                      AF.Sigmoid),
         reads=scf.k(2048, 256), writes=prm.k(PRM_A, 16))
    S.op("dve", lambda e: e.memset(prm.ap(PRM_CARRY, 32), 0.0),
         writes=prm.k(PRM_CARRY, 32))
    dmain(prm.ap(PRM_INB, 32), inb_in.rearrange("(c p) -> p c", p=128),
          writes=prm.k(PRM_INB, 32), slow=True)
    dmain(prm.ap(PRM_MIXB, 16), mixb_in.rearrange("(c p) -> p c", p=128),
          writes=prm.k(PRM_MIXB, 16), slow=True)
    dmain(prm.ap(PRM_DWB, 16), dwb_in.rearrange("(c p) -> p c", p=128),
          writes=prm.k(PRM_DWB, 16), slow=True)
    dmain(prm.ap(PRM_GAM, 8), gamma_in.rearrange("(c p) -> p c", p=128),
          writes=prm.k(PRM_GAM, 8), slow=True)
    dmain(prm.ap(PRM_BET, 8), beta_in.rearrange("(c p) -> p c", p=128),
          writes=prm.k(PRM_BET, 8), slow=True)
    dmain(prm.ap(PRM_B1S, 32), b1_in.rearrange("(c p) -> p c", p=128),
          writes=prm.k(PRM_B1S, 32), slow=True)
    S.op("dve", lambda e: e.tensor_scalar(scf.ap(2048, 32), prm.ap(PRM_B1S, 32),
                                          256.0, 0.5, OP.mult, OP.add),
         reads=prm.k(PRM_B1S, 32), writes=scf.k(2048, 256))
    S.op("dve", lambda e: e.tensor_copy(prm.ap(PRM_B1S, 32), scf.ap(2048, 32)),
         reads=scf.k(2048, 256), writes=prm.k(PRM_B1S, 32))
    dmain(dwwt.ap(0, 112).rearrange("p (c j) -> p c j", j=7),
          dww_in.rearrange("(c p) j -> p c j", p=128), writes=dwwt.k(), slow=True)
    dmain(prm.t[0:1, PRM_WS:PRM_WS + 1], ws_in, writes=prm.k(PRM_WS, 1))
    repl_row(prm, PRM_WS, PRM_WS + 1)
    dmain(brep.t[0:1, 0:1024], bias0_in, writes=brep.k(0, 1024))
    repl_row(brep, 0, 1024)
    S.op("dve", lambda e: e.memset(ones_bf.ap(0, 128), 1.0), writes=ones_bf.k())
    for bi, (bsrc, blo) in enumerate(((outb_in, 0), (b2_in, 2048))):
        dmain(scf.t[0:1, 0:1024], bsrc, writes=scf.k(0, 1024))
        S.op("dve", lambda e, blo=blo: e.tensor_copy(brow.t[0:1, blo:blo + 1024],
                                                     scf.t[0:1, 0:1024]),
             reads=scf.k(0, 1024), writes=brow.k(blo, 1024))
        S.op("dve", lambda e, blo=blo: e.tensor_copy(scf.t[0:1, 1024:2048],
                                                     brow.t[0:1, blo:blo + 1024]),
             reads=brow.k(blo, 1024), writes=scf.k(1024, 1024))
        S.op("dve", lambda e: e.tensor_tensor(scf.t[0:1, 0:1024],
                                              scf.t[0:1, 0:1024],
                                              scf.t[0:1, 1024:2048], OP.subtract),
             reads=scf.k(0, 2048), writes=scf.k(0, 1024))
        S.op("dve", lambda e, blo=blo: e.tensor_copy(
            brow.t[0:1, blo + 1024:blo + 2048], scf.t[0:1, 0:1024]),
            reads=scf.k(0, 1024), writes=brow.k(blo + 1024, 1024))
    S.op("dve", lambda e: e.memset(ret_state.ap(0, 4096), 0.0),
         writes=ret_state.k())
    S.op("dve", lambda e: e.memset(vcs.ap(VTAILS, 192), 0.0),
         writes=vcs.k(VTAILS, 192))

    AQ, AK, AV, AG0, AGN = 0, 4096, 8192, 16384, 24576
    AL1G, AL1S, AL1C, AL1V = 0, 8192, 16384, 24576
    AH = 0

    def layer_norm(s_lo, tc):
        sb_ = 32 + (tc % 2) * 8
        sk = scf.k(s_lo, 1024)
        sap = scf.ap(s_lo, 1024)
        ST = lambda i: stat.t[:, sb_ + i:sb_ + i + 1]
        STA = lambda i: stat.ap(sb_ + i, 1)
        STK = stat.k(sb_, 8)
        S.op("dve", lambda e: e.reduce_sum(STA(0), sap, axis=AX.X),
             reads=sk, writes=STK)
        S.op("sc", lambda e: e.activation(scf.ap(1024, 1024), sap, AF.Square,
                                          accum_out=STA(1)),
             reads=sk, writes=scf.k(1024, 1024) + STK)
        S.op("dve", lambda e: e.tensor_scalar(STA(2), STA(0), cAP(CI1024), None,
                                              OP.mult),
             reads=STK + CSTK, writes=STK)
        S.op("dve", lambda e: e.tensor_tensor(STA(3), STA(2), STA(2), OP.mult),
             reads=STK, writes=STK)
        S.op("dve", lambda e: e.scalar_tensor_tensor(
            STA(4), STA(1), 1.0 / 1024.0, STA(3), OP.mult, OP.subtract),
            reads=STK, writes=STK)
        S.op("sc", lambda e: e.activation(STA(0), STA(4), AF.Ln, bias=cAP(CEPS),
                                          scale=1.0),
             reads=STK + CSTK, writes=STK)
        S.op("sc", lambda e: e.activation(STA(5), STA(0), AF.Exp,
                                          bias=0.0, scale=cAP(CN05)),
             reads=STK + CSTK, writes=STK)
        S.op("dve", lambda e: e.scalar_tensor_tensor(
            STA(6), STA(2), -1.0, STA(5), OP.mult, OP.mult),
            reads=STK, writes=STK)
        S.op("sc", lambda e: e.activation(
            x_tm.ap(tc * 1024, 1024), sap, AF.Identity,
            bias=ST(6), scale=ST(5)),
            reads=sk + STK, writes=x_tm.k(tc * 1024, 1024))

    def transpose_tm_to_fm(src, src_keys_fn, cast_gb=False):
        for tc in range(NTC):
            for dc in range(8):
                pap, pk = tp_f32(tc * 8 + dc)
                S.op("pe", lambda e, tc=tc, dc=dc, pap=pap: e.transpose(
                    pap, src.ap(tc * 1024 + dc * 128, 128), ident_f.ap(0, 128)),
                    reads=src_keys_fn(tc, dc) + ident_f.k(),
                    writes=[pk])
                dsta = fmbuf.ap(dc * 512 + tc * 128, 128)
                dkey = fmbuf.k(dc * 512 + tc * 128, 128)
                if cast_gb:
                    S.op("sc", lambda e, dsta=dsta, pap=pap, dc=dc:
                         e.activation(dsta, pap, AF.Identity,
                                      bias=prm.t[:, PRM_BET + dc:PRM_BET + dc + 1],
                                      scale=prm.t[:, PRM_GAM + dc:PRM_GAM + dc + 1]),
                         reads=[pk] + prm.k(PRM_GAM, 16), writes=dkey)
                else:
                    S.op("dve", lambda e, dsta=dsta, pap=pap: e.tensor_copy(
                        dsta, pap),
                        reads=[pk], writes=dkey)

    def emit_tile(seq, tt, tile_idx):
        t0 = tt * TT
        dbg = debug_taps and tile_idx == 0

        for tc in range(NTC):
            src = xin[seq, t0 + tc * 128:t0 + (tc + 1) * 128, :]
            dmain(x_tm.ap(tc * 1024, 1024), src,
                  writes=x_tm.k(tc * 1024, 1024))

        if SUBPHASE < 0.7:
            dst = out_d[seq, t0:t0 + TT, :].rearrange("(tc p) d -> p tc d", p=128)
            S.dma("sync", lambda e, dst=dst: e.dma_start(
                out=dst, in_=x_tm.ap(0, 4096).rearrange("p (tc d) -> p tc d",
                                                        d=1024)),
                reads=x_tm.k(), writes=[f"outp:{tile_idx}"])
            return
        transpose_tm_to_fm(x_tm, lambda tc, dc: x_tm.k(tc * 1024 + dc * 128, 128))
        if SUBPHASE < 1:
            dst = out_d[seq, t0:t0 + TT, :].rearrange("(tc p) d -> p tc d", p=128)
            S.dma("sync", lambda e, dst=dst: e.dma_start(
                out=dst, in_=x_tm.ap(0, 4096).rearrange("p (tc d) -> p tc d",
                                                        d=1024)),
                reads=x_tm.k(), writes=[f"outp:{tile_idx}"])
            return

        for which, aoff, view, skey, ctab, stab in (
                ("q", AQ, vQ, "sQ", cq_in, sq_in),
                ("k", AK, vK, "sK", ck_in, sk_in)):
            base, wkeys = load_wblock(
                (sQ if which == "q" else sK).rearrange("(p x) -> p x", p=128),
                8192, skey)
            def rot_ops(oc, po, pkey, rtb, rlo, aoff=aoff):
                sf = 2048 * (oc % 2)
                ops = []
                ops.append(lambda: S.op("dve", lambda e: e.tensor_tensor(
                    scf.ap(sf + 512, 512), po, rtb.ap(rlo, 512), OP.mult),
                    reads=[pkey] + rtb.k(rlo, 512),
                    writes=scf.k(sf + 512, 512)))
                ops.append(lambda: S.op("dve", lambda e: e.stream_shuffle(
                    scf.ap(sf + 1024, 512), po, [i ^ 1 for i in range(32)]),
                    reads=[pkey], writes=scf.k(sf + 1024, 512)))
                ops.append(lambda: S.op("dve", lambda e: e.tensor_tensor(
                    scf.ap(sf + 1536, 512), scf.ap(sf + 1024, 512),
                    rtb.ap(rlo + 512, 512), OP.mult),
                    reads=scf.k(sf + 1024, 512) + rtb.k(rlo + 512, 512),
                    writes=scf.k(sf + 1536, 512)))
                ops.append(lambda: S.op("dve", lambda e: e.tensor_tensor(
                    arena.ap(aoff + oc * 512, 512), scf.ap(sf + 512, 512),
                    scf.ap(sf + 1536, 512), OP.add),
                    reads=scf.k(sf + 512, 512) + scf.k(sf + 1536, 512),
                    writes=arena.k(aoff + oc * 512, 512)))
                return ops

            import itertools as _it2
            for og in range(4):
                chains = []
                for oc in (2 * og, 2 * og + 1):
                    if oc % 2 == 0:
                        rtb, rlo = rotb, 0
                    else:
                        rtb, rlo = scf, 4096
                    dmain(rtb.ap(rlo, 512), ctab[tt, oc], writes=rtb.k(rlo, 512))
                    dmain(rtb.ap(rlo + 512, 512), stab[tt, oc],
                          writes=rtb.k(rlo + 512, 512))
                    pslot = oc % 3
                    po = psB[:, pslot * 512:pslot * 512 + 512]
                    for kc in range(8):
                        S.op("pe", lambda e, oc=oc, kc=kc, po=po, base=base:
                             e.matmul(
                                 po, wst.ap(base + oc * 1024 + kc * 128, 128),
                                 fmbuf.ap(kc * 512, 512),
                                 start=(kc == 0), stop=(kc == 7)),
                             reads=wkeys + fmbuf.k(kc * 512, 512),
                             writes=[f"psB:{pslot}"])
                    chains.append(rot_ops(oc, po, f"psB:{pslot}", rtb, rlo))
                for pair in _it2.zip_longest(*chains):
                    for fn in pair:
                        if fn is not None:
                            fn()
        if dbg:
            dmain(tap("q_fm", [128, 4096], BF16), arena.ap(AQ, 4096),
                  reads=arena.k(AQ, 4096))
            dmain(tap("k_fm", [128, 4096], BF16), arena.ap(AK, 4096),
                  reads=arena.k(AK, 4096))

        for which, aoff, view, skey, act in (("v", AV, vV, "sV", False),
                                             ("g0", AG0, vG, "sGw", True)):
            for ncd in range(4):
                base, wkeys = load_wblock(view[:, ncd].rearrange("p kc n -> p (kc n)"),
                                          4096, skey)
                for tc in range(NTC):
                    pslot = tc % 3
                    po = psB[:, pslot * 512:pslot * 512 + 512]
                    for kc in range(8):
                        S.op("pe", lambda e, tc=tc, kc=kc, po=po, base=base:
                             e.matmul(po, fmbuf.ap(kc * 512 + tc * 128, 128),
                                      wst.ap(base + kc * 512, 512),
                                      start=(kc == 0), stop=(kc == 7)),
                             reads=wkeys + fmbuf.k(kc * 512 + tc * 128, 128),
                             writes=[f"psB:{pslot}"])
                    dsta = arena.ap(aoff + tc * 2048 + ncd * 512, 512)
                    dkey = arena.k(aoff + tc * 2048 + ncd * 512, 512)
                    if act:
                        S.op("sc", lambda e, dsta=dsta, po=po: e.activation(
                            dsta, po, AF.Silu),
                            reads=[f"psB:{pslot}"], writes=dkey)
                    else:
                        S.op("sc", lambda e, dsta=dsta, po=po: e.activation(
                            dsta, po, AF.Identity),
                            reads=[f"psB:{pslot}"], writes=dkey)
        if dbg:
            dmain(tap("v_tm", [128, 8192], BF16), arena.ap(AV, 8192),
                  reads=arena.k(AV, 8192))

        if SUBPHASE < 3:
            dst = out_d[seq, t0:t0 + TT, :].rearrange("(tc p) d -> p tc d", p=128)
            S.dma("sync", lambda e, dst=dst: e.dma_start(
                out=dst, in_=x_tm.ap(0, 4096).rearrange("p (tc d) -> p tc d",
                                                        d=1024)),
                reads=x_tm.k(), writes=[f"outp:{tile_idx}"])
            return
        SC_AP = [psA[:, 0:256], psA[:, 512:768]]
        SC_K = ["psA:0", "psA:1"]
        O_AP = [psA[:, 1024:1280], psT[:, 0:256]]
        O_K = ["psA:2", "psT"]
        X_AP = psB[:, 0:256]
        X_K = "psB:0"
        U_AP = psB[:, 512:768]
        U_K = "psB:1"
        KT_AP = [psTb[:, 0:128], psB[:, 1024:1088].bitcast(BF16)]
        KT_K = ["psTb", "psB:2"]
        for c in range(2):
            for h in range(H):
                kq = arena.k(AQ + h * 512 + c * 256, 256)
                kk = arena.k(AK + h * 512 + c * 256, 256)
                S.op("dve", lambda e, h=h: e.tensor_copy(
                    ret_sm.ap(768, 256), ret_state.ap(h * 256, 256)),
                    reads=ret_state.k(h * 256, 256), writes=ret_sm.k(768, 256))
                for tkc in range(2):
                    S.op("pe", lambda e, c=c, h=h, tkc=tkc: e.matmul(
                        SC_AP[tkc],
                        arena.ap(AK + h * 512 + c * 256 + tkc * 128, 128),
                        arena.ap(AQ + h * 512 + c * 256, 256),
                        start=True, stop=True),
                        reads=kq + kk, writes=[SC_K[tkc]])
                    S.op("dve", lambda e, h=h, tkc=tkc: e.tensor_tensor(
                        ret_sm.ap(tkc * 256, 256),
                        SC_AP[tkc],
                        masks.ap(h * 512 + tkc * 256, 256), OP.mult),
                        reads=[SC_K[tkc]] + masks.k(h * 512 + tkc * 256, 256),
                        writes=ret_sm.k(tkc * 256, 256))
                    S.op("pe", lambda e, c=c, h=h, tkc=tkc: e.transpose(
                        KT_AP[tkc],
                        arena.ap(AK + h * 512 + c * 256 + tkc * 128, 128),
                        ident_b.ap(0, 128)),
                        reads=kk + ident_b.k(), writes=[KT_K[tkc]])
                    S.op("dve", lambda e, tkc=tkc: e.tensor_copy(
                        ret_sm.ap(512 + tkc * 128, 128),
                        KT_AP[tkc]),
                        reads=[KT_K[tkc]], writes=ret_sm.k(512 + tkc * 128, 128))
                    tc = c * 2 + tkc
                    S.op("dve", lambda e, tc=tc, h=h, tkc=tkc: e.tensor_scalar(
                        ret_sm.ap(1024 + tkc * 256, 256),
                        arena.ap(AV + tc * 2048 + h * 256, 256),
                        prm.t[:, PRM_WK + 2 * h + tkc:PRM_WK + 2 * h + tkc + 1],
                        None, OP.mult),
                        reads=arena.k(AV + tc * 2048 + h * 256, 256) +
                        prm.k(PRM_WK + 2 * h, 2),
                        writes=ret_sm.k(1024 + tkc * 256, 256))
                for tqc in range(2):
                    for tkc in range(tqc + 1):
                        S.op("pe", lambda e, c=c, h=h, tqc=tqc, tkc=tkc: e.matmul(
                            O_AP[tqc],
                            ret_sm.ap(tkc * 256 + tqc * 128, 128),
                            arena.ap(AV + (c * 2 + tkc) * 2048 + h * 256, 256),
                            start=(tkc == 0), stop=(tkc == tqc)),
                            reads=ret_sm.k(tkc * 256, 256) +
                            arena.k(AV + (c * 2 + tkc) * 2048 + h * 256, 256),
                            writes=[O_K[tqc]])
                    S.op("pe", lambda e, c=c, h=h, tqc=tqc: e.matmul(
                        X_AP,
                        arena.ap(AQ + h * 512 + c * 256 + tqc * 128, 128),
                        ret_sm.ap(768, 256), start=True, stop=True),
                        reads=kq + ret_sm.k(768, 256), writes=[X_K])
                    xs = (h % 2) * 256
                    S.op("sc", lambda e, h=h, tqc=tqc, xs=xs: e.activation(
                        scf.ap(xs, 256), X_AP, AF.Identity,
                        scale=prm.t[:, PRM_WR + 2 * h + tqc:PRM_WR + 2 * h + tqc + 1]),
                        reads=[X_K] + prm.k(PRM_WR + 2 * h, 2) + CSTK,
                        writes=scf.k(xs, 256))
                    olo = 3584 + tqc * 2048 + h * 256
                    S.op("dve", lambda e, olo=olo, tqc=tqc, xs=xs: e.tensor_tensor(
                        scf.ap(olo, 256), O_AP[tqc],
                        scf.ap(xs, 256), OP.add),
                        reads=[O_K[tqc]] + scf.k(xs, 256),
                        writes=scf.k(olo, 256))
                for tkc in range(2):
                    S.op("pe", lambda e, tkc=tkc: e.matmul(
                        U_AP, ret_sm.ap(512 + tkc * 128, 128),
                        ret_sm.ap(1024 + tkc * 256, 256),
                        start=(tkc == 0), stop=(tkc == 1)),
                        reads=ret_sm.k(512 + tkc * 128, 128) +
                        ret_sm.k(1024 + tkc * 256, 256),
                        writes=[U_K])
                S.op("dve", lambda e, h=h: e.tensor_scalar(
                    scf.ap(256, 256), ret_state.ap(h * 256, 256),
                    prm.t[:, PRM_D256 + h:PRM_D256 + h + 1], None, OP.mult),
                    reads=ret_state.k(h * 256, 256) + prm.k(PRM_D256, 8),
                    writes=scf.k(256, 256))
                S.op("dve", lambda e, h=h: e.tensor_tensor(
                    ret_state.ap(h * 256, 256), scf.ap(256, 256),
                    U_AP, OP.add),
                    reads=scf.k(256, 256) + [U_K],
                    writes=ret_state.k(h * 256, 256))

            if dbg:
                dmain(tap(f"obuf{c}", [128, 4096]), scf.ap(3584, 4096),
                      reads=scf.k(3584, 4096))
            for half in range(2):
                tc = c * 2 + half
                olo = 3584 + half * 2048
                okeys = scf.k(olo, 2048)
                S.op("dve", lambda e, olo=olo: e.reduce_sum(
                    stat.ap(8, 8).unsqueeze(2),
                    scf.ap(olo, 2048).rearrange("p (h v) -> p h v", h=8),
                    axis=AX.X),
                    reads=okeys, writes=stat.k(8, 8))
                S.op("sc", lambda e, olo=olo: e.activation(
                    scf.ap(1024, 2048), scf.ap(olo, 2048), AF.Square),
                    reads=okeys, writes=scf.k(1024, 2048))
                S.op("dve", lambda e: e.reduce_sum(
                    stat.ap(16, 8).unsqueeze(2),
                    scf.ap(1024, 2048).rearrange("p (h v) -> p h v", h=8),
                    axis=AX.X),
                    reads=scf.k(1024, 2048), writes=stat.k(16, 8))
                S.op("dve", lambda e: e.tensor_scalar(stat.ap(8, 8), stat.ap(8, 8),
                                                      1.0 / 256.0, None, OP.mult),
                     reads=stat.k(8, 8), writes=stat.k(8, 8))
                S.op("dve", lambda e: e.tensor_tensor(stat.ap(24, 8), stat.ap(8, 8),
                                                      stat.ap(8, 8), OP.mult),
                     reads=stat.k(8, 8), writes=stat.k(24, 8))
                S.op("dve", lambda e: e.scalar_tensor_tensor(
                    stat.ap(0, 8), stat.ap(16, 8), 1.0 / 256.0, stat.ap(24, 8),
                    OP.mult, OP.subtract),
                    reads=stat.k(16, 8) + stat.k(24, 8), writes=stat.k(0, 8))
                S.op("dve", lambda e: e.tensor_scalar(stat.ap(24, 8), stat.ap(0, 8),
                                                      1e-5, None, OP.add),
                     reads=stat.k(0, 8), writes=stat.k(24, 8))
                S.op("sc", lambda e: e.activation(stat.ap(0, 8), stat.ap(24, 8),
                                                  AF.Ln),
                     reads=stat.k(24, 8), writes=stat.k(0, 8))
                S.op("sc", lambda e: e.activation(stat.ap(16, 8), stat.ap(0, 8),
                                                  AF.Exp, bias=0.0, scale=cAP(CN05)),
                     reads=stat.k(0, 8) + CSTK, writes=stat.k(16, 8))
                S.op("dve", lambda e: e.scalar_tensor_tensor(
                    stat.ap(24, 8), stat.ap(8, 8), -1.0, stat.ap(16, 8),
                    OP.mult, OP.mult),
                    reads=stat.k(8, 16), writes=stat.k(24, 8))
                for h in range(H):
                    nb = scf.ap((h % 2) * 128, 128).bitcast(BF16)
                    nbk = scf.k((h % 2) * 128, 128)
                    S.op("sc", lambda e, olo=olo, h=h, nb=nb: e.activation(
                        nb, scf.ap(olo + h * 256, 256), AF.Identity,
                        bias=stat.t[:, 24 + h:25 + h],
                        scale=stat.t[:, 16 + h:17 + h]),
                        reads=scf.k(olo + h * 256, 256) + stat.k(16, 8) +
                        stat.k(24, 8),
                        writes=nbk)
                    S.op("dve", lambda e, tc=tc, h=h, nb=nb: e.tensor_tensor(
                        wst.ap(h * 256, 256), nb,
                        arena.ap(AG0 + tc * 2048 + h * 256, 256), OP.mult),
                        reads=nbk + arena.k(AG0 + tc * 2048 + h * 256, 256),
                        writes=wst.k(0, 2048))
                if dbg and tc == 0:
                    dmain(tap("gnt", [128, 2048], BF16), wst.ap(0, 2048),
                          reads=wst.k(0, 2048))
                    dmain(tap("gstat", [128, 32]), stat.ap(0, 32),
                          reads=stat.k(0, 32))
                for vc in range(16):
                    pap, pk = tp_bf(vc)
                    S.op("pe", lambda e, vc=vc, pap=pap: e.transpose(
                        pap, wst.ap(vc * 128, 128), ident_b.ap(0, 128)),
                        reads=wst.k(0, 2048) + ident_b.k(),
                        writes=[pk])
                    S.op("dve", lambda e, vc=vc, tc=tc, pap=pap: e.tensor_copy(
                        arena.ap(AGN + vc * 512 + tc * 128, 128), pap),
                        reads=[pk],
                        writes=arena.k(AGN + vc * 512 + tc * 128, 128))
        if dbg:
            dmain(tap("gn_fm", [128, 8192], BF16), arena.ap(AGN, 8192),
                  reads=arena.k(AGN, 8192))
            dmain(tap("masks", [128, 4096], BF16), masks.ap(0, 4096),
                  reads=masks.k(0, 4096))

        if SUBPHASE < 4:
            dst = out_d[seq, t0:t0 + TT, :].rearrange("(tc p) d -> p tc d", p=128)
            S.dma("sync", lambda e, dst=dst: e.dma_start(
                out=dst, in_=x_tm.ap(0, 4096).rearrange("p (tc d) -> p tc d",
                                                        d=1024)),
                reads=x_tm.k(), writes=[f"outp:{tile_idx}"])
            return
        prb0, prk0 = load_wblock(vP[:, 0].rearrange("p kc n -> p (kc n)"),
                                 8192, "sP")
        prb1, prk1 = load_wblock(vP[:, 1].rearrange("p kc n -> p (kc n)"),
                                 8192, "sP")
        for tc in range(NTC):
            for ncd in range(2):
                base = prb0 if ncd == 0 else prb1
                wkeys = prk0 if ncd == 0 else prk1
                pslot = (tc * 2 + ncd) % 3
                po = psB[:, pslot * 512:pslot * 512 + 512]
                for kc in range(16):
                    S.op("pe", lambda e, tc=tc, kc=kc, po=po, base=base: e.matmul(
                        po, arena.ap(AGN + kc * 512 + tc * 128, 128),
                        wst.ap(base + kc * 512, 512),
                        start=(kc == 0), stop=(kc == 15)),
                        reads=arena.k(AGN + kc * 512 + tc * 128, 128) + wkeys,
                        writes=[f"psB:{pslot}"])
                sb0 = (tc % 2) * 2048
                S.op("dve", lambda e, tc=tc, ncd=ncd, po=po, pslot=pslot, sb0=sb0:
                     e.tensor_tensor(
                         scf.ap(sb0 + ncd * 512, 512), po,
                         x_tm.ap(tc * 1024 + ncd * 512, 512), OP.add),
                     reads=[f"psB:{pslot}"] + x_tm.k(tc * 1024 + ncd * 512, 512),
                     writes=scf.k(sb0 + ncd * 512, 512))
            layer_norm((tc % 2) * 2048, tc)
        if dbg:
            dmain(tap("x1", [128, 4096]), x_tm.ap(0, 4096), reads=x_tm.k())

        if PHASE_LIMIT < 2:
            dst = out_d[seq, t0:t0 + TT, :].rearrange("(tc p) d -> p tc d", p=128)
            S.dma("sync", lambda e, dst=dst: e.dma_start(
                out=dst, in_=x_tm.ap(0, 4096).rearrange("p (tc d) -> p tc d",
                                                        d=1024)),
                reads=x_tm.k(), writes=[f"outp:{tile_idx}"])
            return
        tbase0, tkeys0 = load_wblock(vTn[:, 0].rearrange("p kc n -> p (kc n)"),
                                     4096, "sTn")
        tbase1, tkeys1 = load_wblock(vTn[:, 1].rearrange("p kc n -> p (kc n)"),
                                     4096, "sTn")
        for tc in range(NTC):
            xk = x_tm.k(tc * 1024, 1024)
            xs = x_tm.ap(tc * 1024, 1024)
            S.op("dve", lambda e, xs=xs: e.reduce_max(
                stat.ap(0, 1), xs, axis=AX.X, apply_absolute_value=True),
                reads=xk, writes=stat.k(0, 8))
            S.op("dve", lambda e: e.tensor_single_scalar(stat.ap(3, 1),
                                                         stat.ap(0, 1), 1e-5,
                                                         OP.max),
                 reads=stat.k(0, 8), writes=stat.k(0, 8))
            S.op("dve", lambda e: e.reciprocal(stat.ap(1, 1), stat.ap(3, 1)),
                 reads=stat.k(0, 8), writes=stat.k(0, 8))
            S.op("dve", lambda e: e.tensor_scalar(stat.ap(4, 1), stat.ap(1, 1),
                                                  127.0, None, OP.mult),
                 reads=stat.k(0, 8), writes=stat.k(0, 8))
            S.op("dve", lambda e: e.tensor_scalar(
                stat.ap(2, 1), stat.ap(3, 1), prm.t[:, PRM_WS:PRM_WS + 1],
                1.0 / 127.0, OP.mult, OP.mult),
                reads=stat.k(0, 8) + prm.k(PRM_WS, 1), writes=stat.k(0, 8))
            S.op("dve", lambda e, xs=xs: e.tensor_scalar(
                scf.ap(0, 1024), xs, stat.t[:, 4:5], 0.5, OP.mult, OP.add),
                reads=xk + stat.k(0, 8), writes=scf.k(0, 1024))
            floor_fixup(scf.ap(0, 1024), scf.k(0, 1024),
                        sci.ap(0, 1024), sci.k(0, 1024),
                        sci.ap(1024, 1024), sci.k(1024, 1024),
                        sci.ap(0, 1024), sci.k(0, 1024))
            S.op("dve", lambda e: e.tensor_copy(arena.ap(AQ, 1024), sci.ap(0, 1024)),
                 reads=sci.k(0, 1024), writes=arena.k(AQ, 1024))
            for dc in range(8):
                pap, pk = tp_bf(dc)
                S.op("pe", lambda e, dc=dc, pap=pap: e.transpose(
                    pap, arena.ap(AQ + dc * 128, 128), ident_b.ap(0, 128)),
                    reads=arena.k(AQ, 1024) + ident_b.k(),
                    writes=[pk])
                S.op("dve", lambda e, dc=dc, tc=tc, pap=pap: e.tensor_copy(
                    fmbuf.ap(dc * 512 + tc * 128, 128), pap),
                    reads=[pk],
                    writes=fmbuf.k(dc * 512 + tc * 128, 128))
            for ncd in range(2):
                base = tbase0 if ncd == 0 else tbase1
                wkeys = tkeys0 if ncd == 0 else tkeys1
                pslot = ncd % 3
                po = psB[:, pslot * 512:pslot * 512 + 512]
                for kc in range(8):
                    S.op("pe", lambda e, kc=kc, tc=tc, po=po, base=base: e.matmul(
                        po, fmbuf.ap(kc * 512 + tc * 128, 128),
                        wst.ap(base + kc * 512, 512),
                        start=(kc == 0), stop=(kc == 7)),
                        reads=fmbuf.k(kc * 512 + tc * 128, 128) + wkeys,
                        writes=[f"psB:{pslot}"])
                sb0 = (tc % 2) * 2048
                S.op("dve", lambda e, ncd=ncd, po=po: e.scalar_tensor_tensor(
                    scf.ap(3072 + ncd * 512, 512), po, stat.t[:, 2:3],
                    brep.ap(ncd * 512, 512), OP.mult, OP.add),
                    reads=[f"psB:{pslot}"] + stat.k(0, 8) + brep.k(ncd * 512, 512),
                    writes=scf.k(3072 + ncd * 512, 512))
                S.op("dve", lambda e, tc=tc, ncd=ncd, sb0=sb0: e.tensor_tensor(
                    scf.ap(sb0 + ncd * 512, 512), scf.ap(3072 + ncd * 512, 512),
                    x_tm.ap(tc * 1024 + ncd * 512, 512), OP.add),
                    reads=scf.k(3072 + ncd * 512, 512) +
                    x_tm.k(tc * 1024 + ncd * 512, 512),
                    writes=scf.k(sb0 + ncd * 512, 512))
            layer_norm((tc % 2) * 2048, tc)
        if dbg:
            dmain(tap("x2", [128, 4096]), x_tm.ap(0, 4096), reads=x_tm.k())

        if PHASE_LIMIT < 3:
            dst = out_d[seq, t0:t0 + TT, :].rearrange("(tc p) d -> p tc d", p=128)
            S.dma("sync", lambda e, dst=dst: e.dma_start(
                out=dst, in_=x_tm.ap(0, 4096).rearrange("p (tc d) -> p tc d",
                                                        d=1024)),
                reads=x_tm.k(), writes=[f"outp:{tile_idx}"])
            return
        for tc in range(NTC):
            xk = x_tm.k(tc * 1024, 1024)
            xs = x_tm.ap(tc * 1024, 1024)
            S.op("dve", lambda e, xs=xs: e.reduce_sum(stat.ap(0, 1), xs, axis=AX.X),
                 reads=xk, writes=stat.k(0, 8))
            S.op("sc", lambda e, xs=xs: e.activation(scf.ap(1024, 1024), xs,
                                                     AF.Square,
                                                     accum_out=stat.ap(1, 1)),
                 reads=xk, writes=scf.k(1024, 1024) + stat.k(0, 8))
            S.op("dve", lambda e: e.tensor_scalar(stat.ap(2, 1), stat.ap(0, 1),
                                                  cAP(CI1024), None, OP.mult),
                 reads=stat.k(0, 8) + CSTK, writes=stat.k(0, 8))
            S.op("dve", lambda e: e.tensor_tensor(stat.ap(3, 1), stat.ap(2, 1),
                                                  stat.ap(2, 1), OP.mult),
                 reads=stat.k(0, 8), writes=stat.k(0, 8))
            S.op("dve", lambda e: e.scalar_tensor_tensor(
                stat.ap(4, 1), stat.ap(1, 1), 1.0 / 1024.0, stat.ap(3, 1),
                OP.mult, OP.subtract),
                reads=stat.k(0, 8), writes=stat.k(0, 8))
            S.op("dve", lambda e: e.tensor_scalar(stat.ap(0, 1), stat.ap(4, 1),
                                                  1e-5, None, OP.add),
                 reads=stat.k(0, 8), writes=stat.k(0, 8))
            S.op("sc", lambda e: e.activation(stat.ap(4, 1), stat.ap(0, 1), AF.Ln),
                 reads=stat.k(0, 8), writes=stat.k(0, 8))
            S.op("sc", lambda e: e.activation(stat.ap(5, 1), stat.ap(4, 1), AF.Exp,
                                              bias=0.0, scale=cAP(CN05)),
                 reads=stat.k(0, 8) + CSTK, writes=stat.k(0, 8))
            S.op("dve", lambda e, xs=xs: e.tensor_scalar(
                scf.ap(3584, 1024), xs, stat.t[:, 2:3], stat.t[:, 5:6],
                OP.subtract, OP.mult),
                reads=xk + stat.k(0, 8), writes=scf.k(3584, 1024))
            for dc in range(8):
                pap, pk = tp_f32(dc)
                S.op("pe", lambda e, dc=dc, pap=pap: e.transpose(
                    pap, scf.ap(3584 + dc * 128, 128), ident_f.ap(0, 128)),
                    reads=scf.k(3584 + dc * 128, 128) + ident_f.k(),
                    writes=[pk])
                S.op("dve", lambda e, dc=dc, tc=tc, pap=pap: e.tensor_scalar(
                    fmbuf.ap(dc * 512 + tc * 128, 128), pap,
                    prm.t[:, PRM_GAM + dc:PRM_GAM + dc + 1],
                    prm.t[:, PRM_BET + dc:PRM_BET + dc + 1], OP.mult, OP.add),
                    reads=[pk] + prm.k(PRM_GAM, 16),
                    writes=fmbuf.k(dc * 512 + tc * 128, 128))
        if dbg:
            dmain(tap("xn_fm", [128, 4096], BF16), fmbuf.ap(0, 4096),
                  reads=fmbuf.k())

        ipw = {}
        def inproj_mm(oc, pslot):
            g = oc // 4
            if g not in ipw:
                ipw[g] = load_wblock(
                    vI[:, g * 4:(g + 1) * 4].rearrange("p oc kc m -> p (oc kc m)"),
                    4096, "sI")
            base, wkeys = ipw[g]
            base = base + (oc % 4) * 1024
            po = psB[:, pslot * 512:pslot * 512 + 512]
            for kc in range(8):
                S.op("pe", lambda e, kc=kc, po=po, base=base: e.matmul(
                    po, wst.ap(base + kc * 128, 128), fmbuf.ap(kc * 512, 512),
                    start=(kc == 0), stop=(kc == 7)),
                    reads=wkeys + fmbuf.k(kc * 512, 512), writes=[f"psB:{pslot}"])
            return po

        def conv_ops(cc, po, pkey):
            par = cc % 2
            oc = 16 + cc
            tail_lo = VTAILS + (seq * 16 + cc) * 6
            if par == 0:
                wlo, alo, blo, slo = 0, 512, 0, 2048
                wbase = vcs
            else:
                wlo, alo, blo, slo = 4608, 3072, 3584, 5632
                wbase = scf
            def vwin(off, n):
                return wbase.ap(wlo + off, n), wbase.k(wlo + off, n)
            ca, cak = scf.ap(alo, 512), scf.k(alo, 512)
            cb, cbk = scf.ap(blo, 512), scf.k(blo, 512)
            sc_, sck = scf.ap(slo, 512), scf.k(slo, 512)
            dwb_b = prm.t[:, PRM_DWB + cc:PRM_DWB + cc + 1].broadcast_to([128, 512])
            ops = []
            ops.append(lambda: S.op("dve", lambda e: e.tensor_copy(
                vwin(0, 6)[0], vcs.ap(tail_lo, 6)),
                reads=vcs.k(tail_lo, 6), writes=vwin(0, 6)[1]))
            ops.append(lambda: S.op("sc", lambda e: e.activation(
                vwin(6, 512)[0], po, AF.Identity,
                bias=prm.t[:, PRM_INB + oc:PRM_INB + oc + 1]),
                reads=[pkey] + prm.k(PRM_INB, 32) + CSTK,
                writes=vwin(6, 512)[1]))
            ops.append(lambda: S.op("dve", lambda e: e.scalar_tensor_tensor(
                ca, vwin(0, 512)[0], dwwt.t[:, cc * 7:cc * 7 + 1], dwb_b,
                OP.mult, OP.add),
                reads=vwin(0, 518)[1] + dwwt.k() + prm.k(PRM_DWB, 16),
                writes=cak))
            bufs = [(ca, cak), (cb, cbk)]
            for tp_ in range(1, 7):
                dsta, dkey = bufs[tp_ % 2]
                prev, pvkey = bufs[(tp_ + 1) % 2]
                ops.append(lambda tp_=tp_, dsta=dsta, dkey=dkey, prev=prev,
                           pvkey=pvkey: S.op(
                    "dve", lambda e: e.scalar_tensor_tensor(
                        dsta, vwin(tp_, 512)[0],
                        dwwt.t[:, cc * 7 + tp_:cc * 7 + tp_ + 1], prev,
                        OP.mult, OP.add),
                    reads=vwin(0, 518)[1] + dwwt.k() + pvkey, writes=dkey))
            ops.append(lambda: S.op("dve", lambda e: e.tensor_copy(
                arena.ap(AL1C + cc * 512, 512), ca),
                reads=cak, writes=arena.k(AL1C + cc * 512, 512)))
            ops.append(lambda: S.op("dve", lambda e: e.tensor_tensor_scan(
                sc_, prm.t[:, PRM_A + cc:PRM_A + cc + 1].broadcast_to([128, 512]),
                vwin(6, 512)[0],
                prm.t[:, PRM_CARRY + cc * 2 + seq:PRM_CARRY + cc * 2 + seq + 1],
                OP.mult, OP.add),
                reads=vwin(6, 512)[1] + prm.k(PRM_A, 16) +
                prm.k(PRM_CARRY + cc * 2 + seq, 1),
                writes=sck))
            ops.append(lambda: S.op("dve", lambda e: e.tensor_copy(
                prm.ap(PRM_CARRY + cc * 2 + seq, 1), scf.ap(slo + 511, 1)),
                reads=sck, writes=prm.k(PRM_CARRY + cc * 2 + seq, 1)))
            ops.append(lambda: S.op("dve", lambda e: e.tensor_copy(
                arena.ap(AL1S + cc * 512, 512), sc_),
                reads=sck, writes=arena.k(AL1S + cc * 512, 512)))
            ops.append(lambda: S.op("dve", lambda e: e.tensor_copy(
                vcs.ap(tail_lo, 6), vwin(512, 6)[0]),
                reads=vwin(0, 518)[1], writes=vcs.k(tail_lo, 6)))
            return ops

        import itertools as _it
        for gcc in range(8):
            chains = []
            for cc in (2 * gcc, 2 * gcc + 1):
                pslot = cc % 3
                po = inproj_mm(16 + cc, pslot)
                chains.append(conv_ops(cc, po, f"psB:{pslot}"))
            for pair in _it.zip_longest(*chains):
                for fn in pair:
                    if fn is not None:
                        fn()
        for cc in range(16):
            pslot = cc % 3
            po = inproj_mm(cc, pslot)
            S.op("sc", lambda e, po=po, cc=cc: e.activation(
                arena.ap(AL1G + cc * 512, 512), po, AF.Sigmoid,
                bias=prm.t[:, PRM_INB + cc:PRM_INB + cc + 1], scale=1.0),
                reads=[f"psB:{pslot}"] + prm.k(PRM_INB, 32),
                writes=arena.k(AL1G + cc * 512, 512))
        if dbg:
            dmain(tap("l1s", [128, 8192], BF16), arena.ap(AL1S, 8192),
                  reads=arena.k(AL1S, 8192))
            dmain(tap("l1c", [128, 8192], BF16), arena.ap(AL1C, 8192),
                  reads=arena.k(AL1C, 8192))
            dmain(tap("l1g", [128, 8192], BF16), arena.ap(AL1G, 8192),
                  reads=arena.k(AL1G, 8192))

        for oc in range(16):
            if oc % 2 == 0:
                mxgrp = load_wblock(
                    vM[:, oc:oc + 2].rearrange("p oc kc m -> p (oc kc m)"),
                    4096, "sM")
            base, wkeys = mxgrp
            base = base + (oc % 2) * 2048
            pslot = oc % 3
            po = psB[:, pslot * 512:pslot * 512 + 512]
            for kc in range(16):
                S.op("pe", lambda e, oc=oc, kc=kc, po=po, base=base: e.matmul(
                    po, wst.ap(base + kc * 128, 128),
                    arena.ap(AL1S + kc * 512, 512),
                    start=(kc == 0), stop=(kc == 15)),
                    reads=wkeys + arena.k(AL1S + kc * 512, 512),
                    writes=[f"psB:{pslot}"])
            S.op("dve", lambda e, oc=oc, po=po: e.scalar_tensor_tensor(
                scf.ap(0, 512), po, prm.t[:, PRM_MIXB + oc:PRM_MIXB + oc + 1],
                arena.ap(AL1C + oc * 512, 512), OP.add, OP.add),
                reads=[f"psB:{pslot}"] + prm.k(PRM_MIXB, 16) +
                arena.k(AL1C + oc * 512, 512),
                writes=scf.k(0, 512))
            S.op("dve", lambda e, oc=oc: e.tensor_tensor(
                arena.ap(AL1V + oc * 512, 512), scf.ap(0, 512),
                arena.ap(AL1G + oc * 512, 512), OP.mult),
                reads=scf.k(0, 512) + arena.k(AL1G + oc * 512, 512),
                writes=arena.k(AL1V + oc * 512, 512))
        obase0, okeys0 = load_wblock(vO[:, 0].rearrange("p kc n -> p (kc n)"),
                                     8192, "sOw")
        obase1, okeys1 = load_wblock(vO[:, 1].rearrange("p kc n -> p (kc n)"),
                                     8192, "sOw")
        for tc in range(NTC):
            for ncd in range(2):
                base = obase0 if ncd == 0 else obase1
                wkeys = okeys0 if ncd == 0 else okeys1
                pslot = (tc * 2 + ncd) % 3
                po = psB[:, pslot * 512:pslot * 512 + 512]
                for kc in range(16):
                    S.op("pe", lambda e, kc=kc, tc=tc, po=po, base=base: e.matmul(
                        po, arena.ap(AL1V + kc * 512 + tc * 128, 128),
                        wst.ap(base + kc * 512, 512),
                        start=(kc == 0), stop=False),
                        reads=arena.k(AL1V + kc * 512 + tc * 128, 128) + wkeys,
                        writes=[f"psB:{pslot}"])
                for bb in range(2):
                    S.op("pe", lambda e, po=po, ncd=ncd, bb=bb: e.matmul(
                        po, ones_bf.t[0:1, :],
                        brow.t[0:1, bb * 1024 + ncd * 512:
                               bb * 1024 + (ncd + 1) * 512],
                        start=False, stop=(bb == 1)),
                        reads=ones_bf.k() + brow.k(bb * 1024 + ncd * 512, 512),
                        writes=[f"psB:{pslot}"])
                sb0 = (tc % 2) * 2048
                S.op("dve", lambda e, tc=tc, ncd=ncd, po=po, sb0=sb0:
                     e.scalar_tensor_tensor(
                         scf.ap(sb0 + ncd * 512, 512),
                         x_tm.ap(tc * 1024 + ncd * 512, 512), 2.0, po,
                         OP.mult, OP.add),
                     reads=x_tm.k(tc * 1024 + ncd * 512, 512) + [f"psB:{pslot}"],
                     writes=scf.k(sb0 + ncd * 512, 512))
            layer_norm((tc % 2) * 2048, tc)
        if dbg:
            dmain(tap("x3", [128, 4096]), x_tm.ap(0, 4096), reads=x_tm.k())

        if PHASE_LIMIT < 4:
            dst = out_d[seq, t0:t0 + TT, :].rearrange("(tc p) d -> p tc d", p=128)
            S.dma("sync", lambda e, dst=dst: e.dma_start(
                out=dst, in_=x_tm.ap(0, 4096).rearrange("p (tc d) -> p tc d",
                                                        d=1024)),
                reads=x_tm.k(), writes=[f"outp:{tile_idx}"])
            return
        transpose_tm_to_fm(x_tm, lambda tc, dc: x_tm.k(tc * 1024 + dc * 128, 128))
        def pwl_ops(oc, po, pkey):
            fb = (oc % 3) * 2560
            ib = (oc % 3) * 512
            A = scf.ap(fb + 0, 512); Ak = scf.k(fb + 0, 512)
            B = scf.ap(fb + 512, 512); Bk = scf.k(fb + 512, 512)
            C = scf.ap(fb + 1024, 512); Ck = scf.k(fb + 1024, 512)
            FR = scf.ap(fb + 1536, 256).bitcast(BF16); FRk = scf.k(fb + 1536, 256)
            MM = scf.ap(fb + 1792, 256).bitcast(BF16); MMk = scf.k(fb + 1792, 256)
            G0 = scf.ap(fb + 2048, 256).bitcast(BF16); G0k = scf.k(fb + 2048, 256)
            G1 = scf.ap(fb + 2304, 256).bitcast(BF16); G1k = scf.k(fb + 2304, 256)
            Ci = sci.ap(ib + 0, 512); Cik = sci.k(ib + 0, 512)
            return [
                lambda: S.op("sc", lambda e: e.activation(
                    A, po, AF.Identity,
                    bias=prm.t[:, PRM_B1S + oc:PRM_B1S + oc + 1],
                    scale=cAP(C256)),
                    reads=[pkey] + prm.k(PRM_B1S, 32) + CSTK, writes=Ak),
                lambda: S.op("sc", lambda e: e.activation(Ci, A, AF.Identity),
                             reads=Ak, writes=Cik),
                lambda: S.op("dve", lambda e: e.tensor_tensor(MM, Ci, A,
                                                              OP.is_le),
                             reads=Cik + Ak, writes=MMk),
                lambda: S.op("dve", lambda e: e.scalar_tensor_tensor(
                    B, Ci, 1, MM, OP.subtract, OP.add),
                    reads=Cik + MMk, writes=Bk),
                lambda: S.op("sc", lambda e: e.activation(
                    C, A, AF.Identity, bias=0.0, scale=cAP(CI64)),
                    reads=Ak + CSTK, writes=Ck),
                lambda: S.op("sc", lambda e: e.activation(Ci, C, AF.Identity),
                             reads=Ck, writes=Cik),
                lambda: S.op("dve", lambda e: e.tensor_tensor(MM, Ci, C,
                                                              OP.is_le),
                             reads=Cik + Ck, writes=MMk),
                lambda: S.op("dve", lambda e: e.scalar_tensor_tensor(
                    A, Ci, 1, MM, OP.subtract, OP.add),
                    reads=Cik + MMk, writes=Ak),
                lambda: S.op("dve", lambda e: e.scalar_tensor_tensor(
                    FR, B, 1.0 / 64.0, A, OP.mult, OP.subtract),
                    reads=Bk + Ak, writes=FRk),
                lambda: S.op("sc", lambda e: e.activation(
                    G0, A, AF.Gelu, bias=cAP(C0), scale=cAP(C025)),
                    reads=Ak + CSTK, writes=G0k),
                lambda: S.op("sc", lambda e: e.activation(
                    G1, A, AF.Gelu, bias=cAP(C025), scale=cAP(C025)),
                    reads=Ak + CSTK, writes=G1k),
                lambda: S.op("dve", lambda e: e.tensor_tensor(MM, G1, G0,
                                                              OP.subtract),
                             reads=G1k + G0k, writes=MMk),
                lambda: S.op("dve", lambda e: e.tensor_tensor(G1, FR, MM,
                                                              OP.mult),
                             reads=FRk + MMk, writes=G1k),
                lambda: S.op("dve", lambda e: e.tensor_tensor(FR, G1, G0,
                                                              OP.add),
                             reads=G1k + G0k, writes=FRk),
                lambda: S.op("sc", lambda e: e.activation(
                    C, FR, AF.Identity, bias=cAP(C05), scale=cAP(C256)),
                    reads=FRk + CSTK, writes=Ck),
                lambda: S.op("sc", lambda e: e.activation(Ci, C, AF.Identity),
                             reads=Ck, writes=Cik),
                lambda: S.op("dve", lambda e: e.tensor_tensor(MM, Ci, C,
                                                              OP.is_le),
                             reads=Cik + Ck, writes=MMk),
                lambda: S.op("dve", lambda e: e.scalar_tensor_tensor(
                    A, Ci, 1, MM, OP.subtract, OP.add),
                    reads=Cik + MMk, writes=Ak),
                lambda: S.op("dve", lambda e: e.tensor_single_scalar(
                    Ci, B, 1024.0, OP.is_gt),
                    reads=Bk, writes=Cik),
                lambda: S.op("dve", lambda e: e.copy_predicated(A, Ci, B),
                             reads=Cik + Bk + Ak, writes=Ak),
                lambda: S.op("sc", lambda e: e.activation(
                    arena.ap(AH + oc * 512, 512), A, AF.Identity,
                    bias=0.0, scale=cAP(CI256)),
                    reads=Ak + CSTK, writes=arena.k(AH + oc * 512, 512)),
            ]

        w1grp = None
        import itertools as _it3
        W2C = 4096
        W2A_BANKS = [(psA[:, 0:512], "psA:0"), (psA[:, 512:1024], "psA:1"),
                     (psA[:, 1024:1536], "psA:2"), (psT[:, 0:512], "psT")]

        def w2a_mms(oc):
            for tc in range(NTC):
                bank, bkey = W2A_BANKS[tc]
                S.op("pe", lambda e, oc=oc, tc=tc, bank=bank: e.matmul(
                    bank, arena.ap(AH + oc * 512 + tc * 128, 128),
                    wst.ap(W2C + (oc % 8) * 512, 512),
                    start=(oc == 0), stop=False),
                    reads=arena.k(AH + oc * 512 + tc * 128, 128) +
                    wst.k(W2C + (oc % 8) * 512, 512),
                    writes=[bkey])

        gi = 0
        prev_grp = []
        nd1_ld = [None, None]
        while gi < 32:
            grp = list(range(gi, min(gi + 3, 32)))
            gi += len(grp)
            for oc in grp:
                dmain(wst.ap(W2C + (oc % 8) * 512, 512), vW2[:, 0, oc, :],
                      reads=["sW2"], writes=wst.k(W2C + (oc % 8) * 512, 512))
            chains = []
            for oc in grp:
                if oc % 4 == 0:
                    w1grp = load_wblock(
                        vW1[:, oc:oc + 4].rearrange("p oc kc m -> p (oc kc m)"),
                        4096, "sW1")
                base, wkeys = w1grp
                base = base + (oc % 4) * 1024
                pslot = oc % 3
                po = psB[:, pslot * 512:pslot * 512 + 512]
                pkey = f"psB:{pslot}"
                for kc in range(8):
                    S.op("pe", lambda e, kc=kc, po=po, base=base: e.matmul(
                        po, wst.ap(base + kc * 128, 128),
                        fmbuf.ap(kc * 512, 512),
                        start=(kc == 0), stop=(kc == 7)),
                        reads=wkeys + fmbuf.k(kc * 512, 512), writes=[pkey])
                chains.append(pwl_ops(oc, po, pkey))
            for oc in prev_grp:
                w2a_mms(oc)
            prev_grp = grp
            for stg in _it3.zip_longest(*chains):
                for fn in stg:
                    if fn is not None:
                        fn()
        for oc in prev_grp:
            w2a_mms(oc)
        nd1_ld[0] = load_wblock(
            vW2[:, 1, 0:16, :].rearrange("p kc n -> p (kc n)"), 8192, "sW2")
        nd1_ld[1] = load_wblock(
            vW2[:, 1, 16:32, :].rearrange("p kc n -> p (kc n)"), 8192, "sW2")
        for tc in range(NTC):
            bank, bkey = W2A_BANKS[tc]
            for bb in range(2):
                S.op("pe", lambda e, bank=bank, bb=bb: e.matmul(
                    bank, ones_bf.t[0:1, :],
                    brow.t[0:1, 2048 + bb * 1024:2048 + bb * 1024 + 512],
                    start=False, stop=(bb == 1)),
                    reads=ones_bf.k() + brow.k(2048 + bb * 1024, 512),
                    writes=[bkey])
        if dbg:
            dmain(tap("hbf", [128, 16384], BF16), arena.ap(AH, 16384),
                  reads=arena.k(AH, 16384))

        def w2_evac(po, pkey, tc, ncd):
            j = (tc + ncd) % 2
            us, uk = scf.ap(1024 + j * 512, 512), scf.k(1024 + j * 512, 512)
            ci, cik = sci.ap(j * 1024, 512), sci.k(j * 1024, 512)
            mi, mik = sci.ap(j * 1024 + 512, 512), sci.k(j * 1024 + 512, 512)
            S.op("sc", lambda e, po=po, us=us: e.activation(
                us, po, AF.Identity, bias=cAP(C05), scale=cAP(C256)),
                reads=[pkey] + CSTK, writes=uk)
            floor_fixup(us, uk, ci, cik, mi, mik, ci, cik)
            S.op("dve", lambda e, tc=tc, ncd=ncd, ci=ci: e.tensor_scalar(
                scf.ap(3584 + tc * 1024 + ncd * 512, 512), ci,
                cAP(CI256), None, OP.mult),
                reads=cik + CSTK,
                writes=scf.k(3584 + tc * 1024 + ncd * 512, 512))

        for tc in range(NTC):
            bank, bkey = W2A_BANKS[tc]
            w2_evac(bank, bkey, tc, 0)
        bA, kA = nd1_ld[0]
        bB, kB = nd1_ld[1]
        for tc in range(NTC):
            pslot = tc % 3
            po = psB[:, pslot * 512:pslot * 512 + 512]
            pkey = f"psB:{pslot}"
            for kc in range(32):
                base = bA if kc < 16 else bB
                wk_ = kA if kc < 16 else kB
                off = base + (kc % 16) * 512
                S.op("pe", lambda e, kc=kc, tc=tc, po=po, off=off: e.matmul(
                    po, arena.ap(AH + kc * 512 + tc * 128, 128),
                    wst.ap(off, 512),
                    start=(kc == 0), stop=False),
                    reads=arena.k(AH + kc * 512 + tc * 128, 128) + wk_,
                    writes=[pkey])
            for bb in range(2):
                S.op("pe", lambda e, po=po, bb=bb: e.matmul(
                    po, ones_bf.t[0:1, :],
                    brow.t[0:1, 2048 + bb * 1024 + 512:
                           2048 + bb * 1024 + 1024],
                    start=False, stop=(bb == 1)),
                    reads=ones_bf.k() + brow.k(2048 + bb * 1024 + 512, 512),
                    writes=[pkey])
            w2_evac(po, pkey, tc, 1)
        for tc in range(NTC):
            for ncd in range(2):
                elo = 3584 + tc * 1024 + ncd * 512
                sb0 = (tc % 2) * 2048
                S.op("dve", lambda e, elo=elo, tc=tc, ncd=ncd, sb0=sb0:
                     e.tensor_tensor(
                         scf.ap(sb0 + ncd * 512, 512), scf.ap(elo, 512),
                         x_tm.ap(tc * 1024 + ncd * 512, 512), OP.add),
                     reads=scf.k(elo, 512) + x_tm.k(tc * 1024 + ncd * 512, 512),
                     writes=scf.k(sb0 + ncd * 512, 512))
            layer_norm((tc % 2) * 2048, tc)
        if dbg:
            dmain(tap("x4", [128, 4096]), x_tm.ap(0, 4096), reads=x_tm.k())

        for tc in range(NTC):
            dst = out_d[seq, t0 + tc * 128:t0 + (tc + 1) * 128, :]
            S.dma("sync", lambda e, dst=dst, tc=tc: e.dma_start(
                out=dst, in_=x_tm.ap(tc * 1024, 1024)),
                reads=x_tm.k(tc * 1024, 1024), writes=[f"outp:{tile_idx}:{tc}"])

    n_seq_emit, n_tt_emit = EMIT_TILES or (SEQ_PER_CORE, TILES_PER_SEQ)
    tile_idx = 0
    for seq in range(n_seq_emit):
        if seq > 0:
            S.op("dve", lambda e: e.memset(ret_state.ap(0, 4096), 0.0),
                 reads=ret_state.k(), writes=ret_state.k())
        for tt in range(n_tt_emit):
            emit_tile(seq, tt, tile_idx)
            tile_idx += 1

    S.op("sync", None, reads=[f"outp:{i}" for i in range(tile_idx)])

    S.finalize(nc, es)
    with es:
        blk = es.enter_context(nc.Block())
        blk.sync(lambda eng: S.emit("sync", eng))
        blk.vector(lambda eng: S.emit("dve", eng))
        blk.scalar(lambda eng: S.emit("sc", eng))
        blk.tensor(lambda eng: S.emit("pe", eng))
        blk.gpsimd(lambda eng: S.emit("gp", eng))
    return nc


def _xpos_tables():
    d = np.arange(0, 1024, 2, dtype=np.float64)
    base_scale = (d + 0.4 * 1024.0) / (1.4 * 1024.0)
    pos = np.arange(T, dtype=np.float64)
    scale = base_scale[None, :] ** (pos[:, None] / 512.0)
    inv_freq = 1.0 / 10000.0 ** (np.arange(512, dtype=np.float64) / 512.0)
    sinu = pos[:, None] * inv_freq[None, :]
    sin, cos = np.sin(sinu), np.cos(sinu)
    def expand(tbl):
        out = np.empty((1024, T), np.float64)
        out[0::2] = tbl.T
        out[1::2] = tbl.T
        return out
    sgn = np.empty((1024, 1), np.float64)
    sgn[0::2] = -1.0
    sgn[1::2] = 1.0
    cq = expand(cos * scale)
    sq = expand(sin * scale) * sgn
    ck = expand(cos / scale)
    sk = expand(sin / scale) * sgn
    def tile(tbl):
        return np.ascontiguousarray(
            tbl.reshape(8, 128, TILES_PER_SEQ, TT).transpose(2, 0, 1, 3)
        ).astype(np.float32)
    return tile(cq), tile(sq), tile(ck), tile(sk)


_PROG_CACHE = {}


def kernel(L0_attn_hidden_states, L0_attn_W_qkv, L0_attn_W_g, L0_attn_W_proj,
           L0_attn_decay, L0_ffn_packed_weight, L0_ffn_weight_scale, L0_ffn_bias,
           L1_attn_norm_gamma, L1_attn_norm_beta, L1_attn_in_w, L1_attn_in_b,
           L1_attn_dw_w, L1_attn_dw_b, L1_attn_log_decay, L1_attn_mix_w,
           L1_attn_mix_b, L1_attn_out_w, L1_attn_out_b, L1_ffn_w1, L1_ffn_b1,
           L1_ffn_w2, L1_ffn_b2):
    x = np.ascontiguousarray(L0_attn_hidden_states, dtype=np.float32)
    cq, sq, ck, sk = _xpos_tables()
    consts = np.tile(np.array(CONSTS, np.float32)[None, :], (128, 1))
    idm = np.eye(128, dtype=np.float32)
    key = ("prog", DEBUG_TAPS, EMIT_TILES, PHASE_LIMIT, SUBPHASE)
    if key not in _PROG_CACHE:
        _PROG_CACHE[key] = build_program(DEBUG_TAPS)
    nc = _PROG_CACHE[key]
    common = {
        "wqkv": np.ascontiguousarray(L0_attn_W_qkv, np.float32),
        "wg": np.ascontiguousarray(L0_attn_W_g, np.float32),
        "wproj": np.ascontiguousarray(L0_attn_W_proj, np.float32),
        "decay": np.ascontiguousarray(L0_attn_decay, np.float32).reshape(1, H),
        "packed": np.ascontiguousarray(L0_ffn_packed_weight,
                                       np.int32).reshape(256, 1024),
        "wscale": np.ascontiguousarray(L0_ffn_weight_scale,
                                       np.float32).reshape(1, 1),
        "bias0": np.ascontiguousarray(L0_ffn_bias, np.float32).reshape(1, D),
        "gamma": np.ascontiguousarray(L1_attn_norm_gamma, np.float32),
        "beta": np.ascontiguousarray(L1_attn_norm_beta, np.float32),
        "in_w": np.ascontiguousarray(L1_attn_in_w, np.float32),
        "in_b": np.ascontiguousarray(L1_attn_in_b, np.float32),
        "dw_w": np.ascontiguousarray(L1_attn_dw_w, np.float32),
        "dw_b": np.ascontiguousarray(L1_attn_dw_b, np.float32),
        "log_decay": np.ascontiguousarray(L1_attn_log_decay, np.float32),
        "mix_w": np.ascontiguousarray(L1_attn_mix_w, np.float32),
        "mix_b": np.ascontiguousarray(L1_attn_mix_b, np.float32),
        "out_w": np.ascontiguousarray(L1_attn_out_w, np.float32),
        "out_b": np.ascontiguousarray(L1_attn_out_b, np.float32).reshape(1, D),
        "w1": np.ascontiguousarray(L1_ffn_w1, np.float32),
        "b1": np.ascontiguousarray(L1_ffn_b1, np.float32),
        "w2": np.ascontiguousarray(L1_ffn_w2, np.float32),
        "b2": np.ascontiguousarray(L1_ffn_b2, np.float32).reshape(1, D),
        "consts": consts, "cq_t": cq, "sq_t": sq, "ck_t": ck, "sk_t": sk,
        "idm": idm,
    }
    in_maps = []
    for c in range(N_CORES):
        m = dict(common)
        m["xin"] = np.ascontiguousarray(x[c * SEQ_PER_CORE:(c + 1) * SEQ_PER_CORE])
        in_maps.append(m)
    res = run_bass_kernel_spmd(nc, in_maps, core_ids=list(range(N_CORES)))
    if DEBUG_TAPS:
        kernel._last_results = res.results
    out = np.concatenate([r["out"] for r in res.results], axis=0)
    return out.reshape(B, T, D)
